# version 03
# baseline (speedup 1.0000x reference)
import numpy as np
import ml_dtypes

import bass_rust
import concourse.bass as bass
import concourse.mybir as mybir
from concourse.bass import IndirectOffsetOnAxis
from concourse.bass_utils import run_bass_kernel_spmd

F32 = mybir.dt.float32
BF16 = mybir.dt.bfloat16
U32 = mybir.dt.uint32
AF = mybir.ActivationFunctionType
ALU = mybir.AluOpType
AXX = mybir.AxisListType.X

NCORES = 8
BIGIDX = 2097152.0
EPS = 1e-5
NEGINF = -1.0e30

_DBG = set()
_MOE_DBG = False


class Sem:
    __slots__ = ("h", "n")

    def __init__(self, h):
        self.h = h
        self.n = 0

    def inc(self, inst, k=1):
        if inst is not None:
            inst.then_inc(self.h, k)
        self.n += k
        return self.n


class KB:
    ENGINES = ("sync", "scalar", "vector", "tensor", "gpsimd")

    def __init__(self, cfg):
        self.cfg = cfg
        self.B, self.S, self.D = cfg["B"], cfg["S"], cfg["D"]
        self.NH, self.DH, self.FF, self.NE = cfg["NH"], cfg["DH"], cfg["FF"], cfg["NE"]
        self.SEQ = self.B // NCORES
        self.TOK = self.SEQ * self.S
        self.NT = self.S // 128
        self.NTT = self.TOK // 128
        self.DC = self.D // 128
        self.FC = self.FF // 128
        self.SEGW = 1024

        nc = bass.Bass("TRN2", target_bir_lowering=False)
        self.nc = nc
        self._sb = {}
        self._sems = {}
        self.en = None
        self.eng = None
        self.st = {}

        D, S, FF, NE, TOK, DC, FC, NT, NTT = (
            self.D, self.S, self.FF, self.NE, self.TOK, self.DC, self.FC,
            self.NT, self.NTT)

        self.inp = {}

        def di(name, shape, dt=F32):
            self.inp[name] = nc.dram_tensor(name, shape, dt, kind="ExternalInput").ap()

        di("x", [TOK, D])
        di("mx_Wz", [D, D]); di("mx_Wh", [D, D]); di("mx_Wg", [D, D])
        di("mx_bz", [128, DC]); di("mx_bh", [128, DC]); di("mx_bg", [128, DC])
        di("f_Wqkv", [D, 3 * D]); di("f_Wo", [D, D]); di("f_Wr", [D, D])
        di("f_ub", [128, DC]); di("f_vb", [128, DC])
        di("f_lng", [128, 3, D]); di("f_lnb", [128, 3, D])
        di("f_Wff1", [D, FF]); di("f_bff1", [128, FC])
        di("f_Wff2", [FF, D]); di("f_bff2", [1, D])
        di("f_Wg", [D, NE])
        di("f_We1", [NE, D, FF]); di("f_be1", [NE, 128, FC])
        di("f_We2", [NE, FF, D]); di("f_be2", [NE, 1, D])
        di("f_Wout", [D, D])
        di("peT", [D, S])
        di("ident", [128, 128], BF16)
        di("identf", [128, 128])
        di("cmask", [128, 128])
        di("rowidx", [128, NTT])

        self.y = nc.dram_tensor("y", [TOK, D], F32, kind="ExternalOutput").ap()

        def scr(name, shape, dt=F32):
            kind = "ExternalOutput" if name in _DBG else "Internal"
            return nc.dram_tensor(name, shape, dt, kind=kind).ap()

        self.xA = scr("xA", [TOK, D])
        self.xB = scr("xB", [TOK, D])
        self.xn_s = scr("xn_s", [TOK, D])
        self.qT_s = scr("qT_s", [self.SEQ * NT, 128, DC * 128], BF16)
        self.ctx_s = scr("ctx_s", [TOK, D])
        self.h1_s = scr("h1_s", [TOK, D])
        self.h1T_s = scr("h1T_s", [DC, 128, TOK], BF16)
        self.h2_s = scr("h2_s", [TOK, D])
        self.h2T_s = scr("h2T_s", [DC, 128, TOK], BF16)
        self.moe_s = scr("moe_s", [TOK, D])
        self.G_s = scr("G_s", [2, 128 * S], BF16)
        self.moe_dbg = scr("moe_dbg", [NE, TOK, D]) if _MOE_DBG else None

        self.S01 = nc.alloc_psum_tensor("S01", [128, 1024], F32).ap()
        self.S23 = nc.alloc_psum_tensor("S23", [128, 1024], F32).ap()
        self.S45 = nc.alloc_psum_tensor("S45", [128, 1024], F32).ap()
        self.S67 = nc.alloc_psum_tensor("S67", [128, 1024], F32).ap()
        self.PT = self.S67[:, 0:512].bitcast(BF16)
        self.PTF = self.S67[:, 0:128]
        self.CTXP = self.S67[:, 512:576]
        self.CTXP2 = self.S67[:, 576:640]
        self.GATEP = self.S67[:, 512:512 + NE]

        A = nc.alloc_sbuf_tensor

        def ap_(name, shape, dt=F32):
            return A(name, shape, dt).ap()

        self.ident_sb = ap_("ident_sb", [128, 128], BF16)
        self.identf_sb = ap_("identf_sb", [128, 128])
        self.cmask_sb = ap_("cmask_sb", [128, 128])
        self.rowidx_sb = ap_("rowidx_sb", [128, NTT])
        self.ub_sb = ap_("ub_sb", [128, DC])
        self.vb_sb = ap_("vb_sb", [128, DC])
        self.bz_sb = ap_("bz_sb", [128, DC])
        self.nbz_sb = ap_("nbz_sb", [128, DC])
        self.bh_sb = ap_("bh_sb", [128, DC])
        self.bg_sb = ap_("bg_sb", [128, DC])
        self.bff1_sb = ap_("bff1_sb", [128, FC])
        self.be1_sb = ap_("be1_sb", [128, 2, FC])
        self.bff2_sb = ap_("bff2_sb", [1, D], BF16)
        self.be2_sb = ap_("be2_sb", [1, 2, D], BF16)
        self.wg_sb = ap_("wg_sb", [128, DC, NE], BF16)
        self.ones_sb = ap_("ones_sb", [1, 128], BF16)
        self.ones8_sb = ap_("ones8_sb", [128, NE])
        self.st_sb = ap_("st_sb", [128, 64])
        self.mx2_sb = ap_("mx2_sb", [128, 4])
        self.nmx_sb = ap_("nmx_sb", [128, 2])
        self.den_sb = ap_("den_sb", [128, 2])
        self.rden_sb = ap_("rden_sb", [128, 2])
        self.epsv = ap_("epsv", [128, 1])
        self.mask8_sb = ap_("mask8_sb", [128, NTT, NE])
        self.mg8_sb = ap_("mg8_sb", [128, NTT, NE])
        self.idx_sb = ap_("idx_sb", [128, NTT, NE], U32)
        self.scr8_sb = ap_("scr8_sb", [128, 4 * NE])
        self.off_xr1 = nc.sbuf_base
        self.xr1 = ap_("xr1", [128, 2, D])
        self.off_xr2 = nc.sbuf_base
        self.xr2 = ap_("xr2", [128, 2, D])
        self.xnb = ap_("xnb", [128, 2, D], BF16)
        self.sq = ap_("sq", [128, D], BF16)
        self.lnt = ap_("lnt", [128, D])

        self.arena0 = nc.sbuf_base
        self.arena_kb = (nc.sbuf_top - nc.sbuf_base) // 1024
        assert self.arena_kb >= 168, f"arena too small: {self.arena_kb}KB"

    def sem(self, name):
        s = self._sems.get(name)
        if s is None:
            s = Sem(self.nc.alloc_semaphore(name))
            self._sems[name] = s
        return s

    @property
    def CHV(self):
        return self.sem("ch_vector")

    @property
    def CHS(self):
        return self.sem("ch_scalar")

    @property
    def CHG(self):
        return self.sem("ch_gpsimd")

    def sba(self, name, shape, dt, off_kb):
        t = self._sb.get(name)
        if t is None:
            t = self.nc.alloc_sbuf_tensor_at(
                name, shape, dt, offset=self.arena0 + off_kb * 1024).ap()
            self._sb[name] = t
        return t

    def begin_pass(self, en, eng):
        self.en = en
        self.eng = eng
        self.st = {}
        for s in self._sems.values():
            s.n = 0

    def on(self, en):
        return self.en == en

    def w(self, en, sem, val):
        if val > 0 and self.en == en:
            self.eng.wait_ge(sem.h, val)

    def barrier(self, deps):
        for sem, val in deps:
            if val > 0:
                self.eng.wait_ge(sem.h, val)

    def mm(self, out, lhsT, rhs, start=True, stop=True):
        if self.en == "tensor":
            return self.eng.matmul(out, lhsT, rhs, start=start, stop=stop)
        return None

    def tr(self, out, in_, fp32=False):
        if self.en == "tensor":
            return self.eng.transpose(
                out, in_, self.identf_sb if fp32 else self.ident_sb)
        return None

    def _chain_op(self, en_name, make):
        s = self.sem("ch_" + en_name)
        if s.n > 0 and self.en == en_name:
            self.eng.wait_ge(s.h, s.n)
        inst = make(self.eng) if self.en == en_name else None
        return s.inc(inst)

    def act(self, out, in_, func, **kw):
        return self._chain_op("scalar", lambda e: e.activation(out, in_, func, **kw))

    def dve(self, meth, *a, **kw):
        return self._chain_op("vector", lambda e: getattr(e, meth)(*a, **kw))

    def gp(self, meth, *a, **kw):
        return self._chain_op("gpsimd", lambda e: getattr(e, meth)(*a, **kw))

    def dma(self, q, out, in_, **kw):
        if self.en == q:
            return self.eng.dma_start(out=out, in_=in_, **kw)
        return None


def emit_ln(k, xin, xout, lng=None, lnb=None, affout=None):
    D = k.D
    m = k.st_sb[:, 0:1]; msq = k.st_sb[:, 1:2]; va = k.st_sb[:, 2:3]
    lv = k.st_sb[:, 3:4]; rstd = k.st_sb[:, 4:5]; nmr = k.st_sb[:, 5:6]
    ssum = k.st_sb[:, 6:7]; ssq = k.st_sb[:, 7:8]
    k.dve("scalar_tensor_tensor", out=k.sq[:], in0=xin, scalar=1.0,
          in1=xin, op0=ALU.mult, op1=ALU.mult, accum_out=ssq)
    k.dve("reduce_sum", out=ssum, in_=xin, axis=AXX)
    k.dve("tensor_scalar", out=m, in0=ssum, scalar1=1.0 / D, scalar2=None,
          op0=ALU.mult, op1=ALU.bypass)
    k.dve("tensor_tensor", out=msq, in0=m, in1=m, op=ALU.mult)
    tv = k.dve("tensor_scalar", out=va, in0=ssq, scalar1=1.0 / D, scalar2=msq,
               op0=ALU.mult, op1=ALU.subtract)
    k.w("scalar", k.CHV, tv)
    k.act(lv, va, AF.Ln, bias=k.epsv[:])
    ta = k.act(rstd, lv, AF.Exp, scale=-0.5)
    k.w("vector", k.CHS, ta)
    k.dve("scalar_tensor_tensor", out=nmr, in0=m, scalar=-1.0, in1=rstd,
          op0=ALU.mult, op1=ALU.mult)
    tn = k.dve("tensor_scalar", out=xout, in0=xin, scalar1=rstd, scalar2=nmr,
               op0=ALU.mult, op1=ALU.add)
    if lng is not None:
        k.dve("tensor_tensor", out=affout, in0=xout, in1=lng, op=ALU.mult)
        tn = k.dve("tensor_tensor", out=affout, in0=affout, in1=lnb, op=ALU.add)
    return tn


def emit_ct(k, src, dst, par, smt, war=()):
    DC = k.DC
    tc = k.dve("tensor_copy", out=k.xnb[:, par, :], in_=src)
    k.w("tensor", k.CHV, tc)
    k.w("tensor", k.CHV, k.st.get("ptfree", 0))
    last = None
    for c in range(DC):
        ii = k.tr(k.PT[:, c * 128:(c + 1) * 128],
                  k.xnb[:, par, :].rearrange("p (c f) -> p c f", c=DC)[:, c, :])
        if ii is not None:
            last = ii
    tt = smt.inc(last)
    k.w("vector", smt, tt)
    for sem, val in war:
        k.w("vector", sem, val)
    te = k.dve("tensor_copy", out=dst,
               in_=k.PT.rearrange("p (c f) -> p c f", c=DC))
    k.st["ptfree"] = te
    return te


def emit_stage0(k):
    sL = k.sem("c_ld")
    sG = k.sem("c_ldg")
    loads = [
        (k.ident_sb[:], k.inp["ident"]), (k.identf_sb[:], k.inp["identf"]),
        (k.cmask_sb[:], k.inp["cmask"]), (k.rowidx_sb[:], k.inp["rowidx"]),
        (k.ub_sb[:], k.inp["f_ub"]), (k.vb_sb[:], k.inp["f_vb"]),
        (k.bz_sb[:], k.inp["mx_bz"]), (k.bh_sb[:], k.inp["mx_bh"]),
        (k.bg_sb[:], k.inp["mx_bg"]), (k.bff1_sb[:], k.inp["f_bff1"]),
    ]
    t = 0
    for out, in_ in loads:
        i = k.dma("sync", out, in_)
        t = sL.inc(i, 16)
    i = k.dma("gpsimd", k.bff2_sb[:], k.inp["f_bff2"])
    sG.inc(i, 16)
    i = k.dma("gpsimd", k.wg_sb[:],
              k.inp["f_Wg"].rearrange("(c p) e -> p c e", p=128))
    tg = sG.inc(i, 16)
    k.w("gpsimd", sG, tg)
    k.gp("memset", k.ones_sb[:], 1.0)
    k.gp("memset", k.ones8_sb[:], 1.0)
    tm = k.gp("memset", k.epsv[:], EPS)
    if k.on("gpsimd"):
        k.st["breg"] = k.eng.to_reg(k.TOK - 1)
    k.w("vector", sL, t)
    tv = k.dve("tensor_scalar", out=k.nbz_sb[:], in0=k.bz_sb[:], scalar1=-1.0,
               scalar2=None, op0=ALU.mult, op1=ALU.bypass)
    k.barrier([(sL, t), (sG, tg), (k.CHG, tm), (k.CHV, tv)])


def emit_rt(k, call):
    S, DC = k.S, k.DC
    rT = k.sba("A_rT", [128, DC, S], BF16, 64)
    Wr = k.sba("A_wr", [128, DC, k.D], BF16, 96)
    peb = k.sba("A_peb", [128, 2, DC, 512], BF16, 112)
    sW = k.sem("r_wld")
    sPE = [k.sem("r_pe0"), k.sem("r_pe1")]
    smq = k.sem("r_mm")
    i = k.dma("gpsimd", Wr[:], k.inp["f_Wr"].rearrange("(c p) f -> p c f", p=128))
    tW = sW.inc(i, 16)
    ev = [0, 0]
    pe_users = [0, 0]
    tlast = 0
    for n in range(S // 512):
        par = n % 2
        k.w("gpsimd", smq, pe_users[par])
        i = k.dma("gpsimd", peb[:, par], k.inp["peT"].rearrange(
            "(c p) t -> p c t", p=128)[:, :, n * 512:(n + 1) * 512])
        tpe = sPE[par].inc(i, 16)
        k.w("tensor", sW, tW)
        k.w("tensor", sPE[par], tpe)
        for oc in range(DC):
            r = (n * DC + oc) % 2
            ps = k.S01[:, r * 512:(r + 1) * 512]
            k.w("tensor", k.CHV, ev[r])
            last = None
            for kc in range(DC):
                ii = k.mm(ps, Wr[:, kc, oc * 128:(oc + 1) * 128], peb[:, par, kc, :],
                          start=(kc == 0), stop=(kc == DC - 1))
                if ii is not None:
                    last = ii
            tg = smq.inc(last)
            k.w("vector", smq, tg)
            tlast = k.dve("tensor_copy", out=rT[:, oc, n * 512:(n + 1) * 512], in_=ps)
            ev[r] = tlast
        pe_users[par] = smq.n
    k.barrier([(k.CHV, tlast)])
    return rT


def emit_p0(k, call, s, x_in):
    S, D, DC, NT = k.S, k.D, k.DC, k.NT
    kT = k.sba("A_kT", [128, DC, S], BF16, 0)
    V = k.sba("A_v", [128, NT, D], BF16, 32)
    Wq = k.sba("A_wqkv", [128, DC, 3 * D], BF16, 96)
    xnT = k.sba("A_xnT", [128, 2, DC, 512], BF16, 144)
    qTb = k.sba("A_qTb", [128, 2, 512], BF16, 160)

    sX = [k.sem("p_x0"), k.sem("p_x1")]
    sW = k.sem("p_wld")
    smq = k.sem("p_mmq")
    smt = k.sem("p_mmt")
    sSX = [k.sem("p_sx0"), k.sem("p_sx1")]
    sSQ = [k.sem("p_sq0"), k.sem("p_sq1")]

    wsrc = k.inp["f_Wqkv"].rearrange("(c p) f -> p c f", p=128)
    i = k.dma("gpsimd", Wq[:, :, 0:2048], wsrc[:, :, 0:2048])
    sW.inc(i, 16)
    i = k.dma("gpsimd", Wq[:, :, 2048:3072], wsrc[:, :, 2048:3072])
    tW = sW.inc(i, 16)

    xnT_users = [0, 0]
    ps_ev = [0, 0, 0, 0]
    v_ev = [0, 0]
    qsp = [0, 0]
    xr1_rd = k.st.setdefault("p_xr1_rd", [0, 0])
    xr2_rd = k.st.setdefault("p_xr2_rd", [0, 0])
    for n in range(S // 512):
        par = n % 2
        for it in range(4):
            tau = n * 4 + it
            gtau = s * NT + tau
            slot = tau % 2
            k.w("sync", k.CHV, xr1_rd[slot])
            i = k.dma("sync", k.xr1[:, slot, :],
                      x_in[gtau * 128:(gtau + 1) * 128, :])
            tx = sX[slot].inc(i, 16)
            k.w("vector", sX[slot], tx)
            k.w("vector", sSX[slot], xr2_rd[slot])
            tn = emit_ln(k, k.xr1[:, slot, :], k.xr2[:, slot, :])
            xr1_rd[slot] = tn
            k.w("sync", k.CHV, tn)
            i = k.dma("sync", k.xn_s[gtau * 128:(gtau + 1) * 128, :],
                      k.xr2[:, slot, :])
            xr2_rd[slot] = sSX[slot].inc(i, 16)
            emit_ct(k, k.xr2[:, slot, :], xnT[:, par, :, it * 128:(it + 1) * 128],
                    slot, smt, war=[(smq, xnT_users[par])] if it == 0 else ())
        txn = k.st["ptfree"]
        k.w("tensor", k.CHV, txn)
        k.w("tensor", sW, tW)
        pi = 0
        for which, base in (("q", 0), ("k", D)):
            for oc in range(DC):
                r = pi % 4
                ps = [k.S01[:, 0:512], k.S01[:, 512:1024],
                      k.S23[:, 0:512], k.S23[:, 512:1024]][r]
                k.w("tensor", k.CHV, ps_ev[r])
                last = None
                for kc in range(DC):
                    ii = k.mm(ps, Wq[:, kc, base + oc * 128:base + oc * 128 + 128],
                              xnT[:, par, kc, :],
                              start=(kc == 0), stop=(kc == DC - 1))
                    if ii is not None:
                        last = ii
                tgm = smq.inc(last)
                k.w("vector", smq, tgm)
                if which == "k":
                    ps_ev[r] = k.dve("tensor_copy",
                                     out=kT[:, oc, n * 512:(n + 1) * 512], in_=ps)
                else:
                    qs = pi % 2
                    k.w("vector", sSQ[qs], qsp[qs])
                    tqe = k.dve("tensor_copy", out=qTb[:, qs, :], in_=ps)
                    ps_ev[r] = tqe
                    k.w("sync", k.CHV, tqe)
                    qdst = k.qT_s[s * NT + n * 4: s * NT + n * 4 + 4, :,
                                  oc * 128:(oc + 1) * 128].rearrange("t p f -> p t f")
                    i = k.dma("sync", qdst,
                              qTb[:, qs, :].rearrange("p (t f) -> p t f", t=4))
                    qsp[qs] = sSQ[qs].inc(i, 16)
                pi += 1
        for it in range(4):
            for half in range(2):
                ps = k.S45[:, half * 512:(half + 1) * 512]
                k.w("tensor", k.CHV, v_ev[half])
                last = None
                for kc in range(DC):
                    ii = k.mm(ps, xnT[:, par, kc, it * 128:(it + 1) * 128],
                              Wq[:, kc, 2 * D + half * 512:2 * D + (half + 1) * 512],
                              start=(kc == 0), stop=(kc == DC - 1))
                    if ii is not None:
                        last = ii
                tgm = smq.inc(last)
                k.w("vector", smq, tgm)
                v_ev[half] = k.dve(
                    "tensor_copy",
                    out=V[:, n * 4 + it, half * 512:(half + 1) * 512], in_=ps)
        xnT_users[par] = smq.n
    k.barrier([(k.CHV, k.CHV.n), (sSX[0], sSX[0].n), (sSX[1], sSX[1].n),
               (sSQ[0], sSQ[0].n), (sSQ[1], sSQ[1].n)])
    return kT, V


def emit_attn(k, call, s, kT, V, rT):
    S, D, DC, NT, NH = k.S, k.D, k.DC, k.NT, k.NH
    SEGW = k.SEGW
    qTt = k.sba("A_qTt", [128, 2, DC, 128], BF16, 96)
    qu = k.sba("A_qu", [128, 2, DC, 128], BF16, 100)
    qv = k.sba("A_qv", [128, 2, DC, 128], BF16, 104)
    Gsb = k.sba("A_G", [128, 2, S], BF16, 108)
    bd = k.sba("A_bd", [128, 2, S], BF16, 116)
    ssb = k.sba("A_s", [128, 2, S], F32, 124)
    Psb = k.sba("A_P", [128, 2, S], BF16, 140)
    PTs = k.sba("A_PTs", [128, NT, 128], BF16, 148)
    ctx = k.sba("A_ctx", [128, 2, D], F32, 152)

    sQ = [k.sem("a_q0"), k.sem("a_q1")]
    sG = k.sem("a_g")
    sSPL = [k.sem("a_spl0"), k.sem("a_spl1")]
    sBD = [k.sem("a_bd0"), k.sem("a_bd1")]
    sAC = k.sem("a_ac")
    sTR = k.sem("a_tr")
    sPV = k.sem("a_pv")
    sCS = [k.sem("a_cs0"), k.sem("a_cs1")]
    SC = [k.S01, k.S23, k.S45]
    stt = k.st

    for t in range(NT):
        KT = (t + 1) * 128
        nseg = (KT + SEGW - 1) // SEGW
        segs = [(si * SEGW, min(SEGW, KT - si * SEGW)) for si in range(nseg)]
        gtau = s * NT + t
        tp = t % 2
        k.w("sync", k.CHV, stt.get(("a_quv", tp), 0))
        i = k.dma("sync", qTt[:, tp],
                  k.qT_s[gtau].rearrange("p (c f) -> p c f", c=DC))
        tq = sQ[tp].inc(i, 16)
        k.w("vector", sQ[tp], tq)
        k.w("vector", sG, stt.get(("a_g_users", tp), 0))
        k.w("vector", sAC, stt.get(("a_ac_users", tp), 0))
        tquv = 0
        for c in range(DC):
            k.dve("tensor_scalar", out=qu[:, tp, c, :], in0=qTt[:, tp, c, :],
                  scalar1=k.ub_sb[:, c:c + 1], scalar2=0.125,
                  op0=ALU.add, op1=ALU.mult)
            tquv = k.dve("tensor_scalar", out=qv[:, tp, c, :], in0=qTt[:, tp, c, :],
                         scalar1=k.vb_sb[:, c:c + 1], scalar2=0.125,
                         op0=ALU.add, op1=ALU.mult)
        stt[("a_quv", tp)] = tquv

        units = [(h, si) for h in range(NH) for si in range(nseg)]
        NU = len(units)
        tG = [0] * NU; tGE = [0] * NU; tAC = [0] * NU; tSE = [0] * NU
        tBDh = [0] * NH; tMKh = [0] * NH; tSPLh = [0] * NH
        tEXh = [0] * NH; tNMh = [0] * NH; tRDh = [0] * NH
        tCEh = [0] * NH; tTRh = [0] * NH

        def emit_G(u):
            h, si = units[u]
            base, w_ = segs[si]
            hc, poff = h // 2, 64 * (h % 2)
            slot = SC[u % 3]
            k.w("tensor", k.CHV, tSE[u - 3] if u >= 3 else stt.get("a_se_prev", 0))
            k.w("tensor", k.CHV, tquv)
            last = None
            for cc in range(0, w_, 512):
                cw = min(512, w_ - cc)
                ii = k.mm(slot[:, cc:cc + cw], qv[poff:poff + 64, tp, hc, :],
                          rT[poff:poff + 64, hc,
                             S - KT + base + cc:S - KT + base + cc + cw],
                          start=True, stop=True)
                if ii is not None:
                    last = ii
            tG[u] = sG.inc(last)
            k.w("vector", sG, tG[u])
            k.w("vector", sSPL[h % 2],
                tSPLh[h - 2] if h >= 2 else stt.get(("a_spl_prev", h % 2), 0))
            tGE[u] = k.dve("tensor_copy", out=Gsb[:, h % 2, base:base + w_],
                           in_=slot[:, 0:w_])

        def emit_SPILL(h):
            KTl = KT
            qh = "gpsimd" if h % 2 == 0 else "sync"
            k.w(qh, k.CHV, tGE[(h + 1) * nseg - 1])
            k.w(qh, sBD[h % 2],
                tBDh[h - 2] if h >= 2 else stt.get(("a_bd_prev", h % 2), 0))
            gslot = k.G_s[h % 2]
            wr = gslot.copy()
            base_off = wr.offset
            wr.ap = bass_rust.VecI64Pair([[KTl, 128], [1, KTl]])
            wr.offset = base_off
            i = k.dma(qh, wr, Gsb[:, h % 2, 0:KTl])
            tSPLh[h] = sSPL[h % 2].inc(i, 16)
            k.w(qh, sSPL[h % 2], tSPLh[h])
            k.w(qh, sAC,
                tAC[(h - 1) * nseg - 1] if h >= 2 else stt.get("a_ac_prev", 0))
            rd = gslot.copy()
            rd.ap = bass_rust.VecI64Pair([[KTl - 1, 128], [1, KTl]])
            rd.offset = base_off + 127
            i = k.dma(qh, bd[:, h % 2, 0:KTl], rd)
            tBDh[h] = sBD[h % 2].inc(i, 16)
            k.w("vector", sBD[h % 2], tBDh[h])
            tMKh[h] = k.dve("tensor_tensor", out=bd[:, h % 2, KTl - 128:KTl],
                            in0=bd[:, h % 2, KTl - 128:KTl], in1=k.cmask_sb[:],
                            op=ALU.add)

        def emit_AC(u):
            h, si = units[u]
            base, w_ = segs[si]
            hc, poff = h // 2, 64 * (h % 2)
            slot = SC[u % 3]
            k.w("tensor", k.CHV, max(tGE[u], tMKh[h]))
            last = None
            for cc in range(0, w_, 512):
                cw = min(512, w_ - cc)
                ii = k.mm(slot[:, cc:cc + cw], qu[poff:poff + 64, tp, hc, :],
                          kT[poff:poff + 64, hc, base + cc:base + cc + cw],
                          start=True, stop=False)
                if ii is not None:
                    last = ii
                ii = k.mm(slot[:, cc:cc + cw], k.ident_sb[:],
                          bd[:, h % 2, base + cc:base + cc + cw],
                          start=False, stop=True)
                if ii is not None:
                    last = ii
            tAC[u] = sAC.inc(last)
            k.w("vector", sAC, tAC[u])
            if h >= 2:
                k.w("vector", k.CHS, tEXh[h - 2])
            tSE[u] = k.dve("tensor_scalar", out=ssb[:, h % 2, base:base + w_],
                           in0=slot[:, 0:w_], scalar1=1.0, scalar2=None,
                           op0=ALU.mult, op1=ALU.max,
                           accum_out=k.mx2_sb[:, si:si + 1])
            if si == nseg - 1:
                tNMh[h] = k.dve("reduce_max", out=k.nmx_sb[:, h % 2:h % 2 + 1],
                                in_=k.mx2_sb[:, 0:nseg], axis=AXX, negate=True)
                k.w("scalar", k.CHV, tNMh[h])
                if h >= 2:
                    k.w("scalar", sTR, tTRh[h - 2])
                tEXh[h] = k.act(Psb[:, h % 2, 0:KT], ssb[:, h % 2, 0:KT], AF.Exp,
                                bias=k.nmx_sb[:, h % 2:h % 2 + 1],
                                accum_out=k.den_sb[:, h % 2:h % 2 + 1])
                k.w("vector", k.CHS, tEXh[h])
                tRDh[h] = k.dve("reciprocal", out=k.rden_sb[:, h % 2:h % 2 + 1],
                                in_=k.den_sb[:, h % 2:h % 2 + 1])

        def emit_TPV(h):
            ntile = t + 1
            ngrp = (ntile + 7) // 8
            cp = k.CTXP if h % 2 == 0 else k.CTXP2
            k.w("tensor", k.CHS, max(tEXh[h], tCEh[h - 2] if h >= 2 else
                                     stt.get(("a_ce_prev", h % 2), 0)))
            last_pv = None
            for g in range(ngrp):
                j0, j1 = g * 8, min(g * 8 + 8, ntile)
                k.w("tensor", k.CHV, k.st.get("ptfree", 0))
                last = None
                for j in range(j0, j1):
                    ii = k.tr(k.PT[:, (j - j0) * 128:(j - j0 + 1) * 128],
                              Psb[:, h % 2, j * 128:(j + 1) * 128])
                    if ii is not None:
                        last = ii
                tg = sTR.inc(last)
                k.w("vector", sTR, tg)
                te = k.dve("tensor_copy", out=PTs[:, j0:j1, :],
                           in_=k.PT[:, 0:(j1 - j0) * 128].rearrange(
                               "p (j f) -> p j f", f=128))
                k.st["ptfree"] = te
                k.w("tensor", k.CHV, te)
                for j in range(j0, j1):
                    ii = k.mm(cp, PTs[:, j, :], V[:, j, h * 64:(h + 1) * 64],
                              start=(j == 0), stop=(j == ntile - 1))
                    if ii is not None:
                        last_pv = ii
            tTRh[h] = sTR.n
            tPVh = sPV.inc(last_pv)
            k.w("scalar", sPV, tPVh)
            k.w("scalar", k.CHV, tRDh[h])
            k.w("scalar", sCS[t % 2], stt.get(("a_cs", t % 2), 0))
            tCEh[h] = k.act(ctx[:, t % 2, h * 64:(h + 1) * 64], cp, AF.Copy,
                            scale=k.rden_sb[:, h % 2:h % 2 + 1])

        emit_G(0)
        if NU > 1:
            emit_G(1)
        for uu in range(min(2, NU)):
            hh, ss = units[uu]
            if ss == nseg - 1:
                emit_SPILL(hh)
        for u in range(NU):
            if u + 2 < NU:
                emit_G(u + 2)
            emit_AC(u)
            if u + 2 < NU and units[u + 2][1] == nseg - 1:
                emit_SPILL(units[u + 2][0])
            h, si = units[u]
            if si == nseg - 1 and h >= 1:
                emit_TPV(h - 1)
        emit_TPV(NH - 1)
        k.w("sync", k.CHS, tCEh[NH - 1])
        i = k.dma("sync", k.ctx_s[gtau * 128:(gtau + 1) * 128, :], ctx[:, t % 2, :])
        stt[("a_cs", t % 2)] = sCS[t % 2].inc(i, 16)
        stt["a_se_prev"] = tSE[NU - 1]
        stt[("a_spl_prev", 0)] = tSPLh[NH - 2]
        stt[("a_spl_prev", 1)] = tSPLh[NH - 1]
        stt[("a_bd_prev", 0)] = tBDh[NH - 2]
        stt[("a_bd_prev", 1)] = tBDh[NH - 1]
        stt["a_ac_prev"] = tAC[NU - 1]
        stt[("a_ce_prev", 0)] = tCEh[NH - 2]
        stt[("a_ce_prev", 1)] = tCEh[NH - 1]
        stt[("a_g_users", t % 2)] = tG[NU - 1]
        stt[("a_ac_users", t % 2)] = tAC[NU - 1]
    k.barrier([(sCS[0], sCS[0].n), (sCS[1], sCS[1].n)])


def emit_B(k, call):
    D, DC, NTT = k.D, k.DC, k.NTT
    Wo = k.sba("B_wo", [128, DC, D], BF16, 0)
    lng = k.sba("B_lng", [128, 2, D], F32, 16)
    ctr = k.sba("B_ctr", [128, 2, D], F32, 24)
    cT = k.sba("B_cT", [128, DC, 128], BF16, 32)
    h1r = k.sba("B_h1r", [128, 2, D], F32, 36)
    h1Tb = k.sba("B_h1T", [128, DC, 128], BF16, 44)

    sC = [k.sem("b_c0"), k.sem("b_c1")]
    sX = [k.sem("b_x0"), k.sem("b_x1")]
    sLg = k.sem("b_lng"); sW = k.sem("b_wld")
    smq = k.sem("b_mm"); smt = k.sem("b_mt")
    sS1 = [k.sem("b_s10"), k.sem("b_s11")]
    sST = k.sem("b_sT")

    i = k.dma("gpsimd", Wo[:], k.inp["f_Wo"].rearrange("(c p) f -> p c f", p=128))
    tW = sW.inc(i, 16)
    i = k.dma("sync", lng[:, 0, :], k.inp["f_lng"][:, 0, :])
    sLg.inc(i, 16)
    i = k.dma("sync", lng[:, 1, :], k.inp["f_lnb"][:, 0, :])
    tlg = sLg.inc(i, 16)

    stt = k.st
    for tau in range(NTT):
        sl = tau % 2
        k.w("sync", k.CHV, stt.get(("b_ctr", sl), 0))
        i = k.dma("sync", ctr[:, sl, :], k.ctx_s[tau * 128:(tau + 1) * 128, :])
        tc = sC[sl].inc(i, 16)
        k.w("sync", k.CHV, stt.get(("b_xr1", sl), 0))
        i = k.dma("sync", k.xr1[:, sl, :], k.xn_s[tau * 128:(tau + 1) * 128, :])
        tx = sX[sl].inc(i, 16)
        k.w("vector", sC[sl], tc)
        tct = emit_ct(k, ctr[:, sl, :], cT[:], sl, smt,
                      war=[(smq, stt.get("b_ct_users", 0))])
        stt[("b_ctr", sl)] = tct
        k.w("tensor", k.CHV, tct)
        k.w("tensor", sW, tW)
        for half in range(2):
            ps = k.S01[:, half * 512:(half + 1) * 512]
            k.w("tensor", k.CHV, stt.get(("b_ps", half), 0))
            last = None
            for kc in range(DC):
                ii = k.mm(ps, cT[:, kc, :], Wo[:, kc, half * 512:(half + 1) * 512],
                          start=(kc == 0), stop=(kc == DC - 1))
                if ii is not None:
                    last = ii
            tgm = smq.inc(last)
            k.w("vector", smq, tgm)
            k.w("vector", sX[sl], tx)
            tu = k.dve("tensor_tensor",
                       out=k.xr2[:, sl, half * 512:(half + 1) * 512], in0=ps,
                       in1=k.xr1[:, sl, half * 512:(half + 1) * 512], op=ALU.add)
            stt[("b_ps", half)] = tu
            stt[("b_xr1", sl)] = tu
        stt["b_ct_users"] = smq.n
        k.w("vector", sLg, tlg)
        k.w("vector", sS1[sl], stt.get(("b_h1r", sl), 0))
        tn = emit_ln(k, k.xr2[:, sl, :], k.lnt[:],
                     lng=lng[:, 0, :], lnb=lng[:, 1, :], affout=h1r[:, sl, :])
        k.w("sync", k.CHV, tn)
        i = k.dma("sync", k.h1_s[tau * 128:(tau + 1) * 128, :], h1r[:, sl, :])
        tsp = sS1[sl].inc(i, 16)
        stt[("b_h1r", sl)] = tsp
        the = emit_ct(k, h1r[:, sl, :], h1Tb[:], sl, smt,
                      war=[(sST, stt.get("b_h1T_sp", 0))])
        k.w("sync", k.CHV, the)
        i = k.dma("sync",
                  k.h1T_s.rearrange("c p t -> p c t")[:, :, tau * 128:(tau + 1) * 128],
                  h1Tb[:])
        stt["b_h1T_sp"] = sST.inc(i, 16)
    k.barrier([(sS1[0], sS1[0].n), (sS1[1], sS1[1].n), (sST, sST.n)])


def emit_B2(k, call):
    D, DC, FC, NE, TOK = k.D, k.DC, k.FC, k.NE, k.TOK
    W1 = k.sba("F_w1", [128, DC, k.FF], BF16, 0)
    W2 = k.sba("F_w2", [128, FC, D], BF16, 64)
    hidT = k.sba("F_hid", [128, FC, 256], BF16, 128)
    h1T = k.sba("F_h1T", [128, 2, DC, 256], BF16, 144)
    lng = k.sba("F_lng", [128, 2, D], F32, 152)
    h2Tb = k.sba("F_h2T", [128, DC, 128], BF16, 160)

    sH = [k.sem("f_h0"), k.sem("f_h1")]
    sX = [k.sem("f_x0"), k.sem("f_x1")]
    sW1s = k.sem("f_w1"); sW2s = k.sem("f_w2"); sLg = k.sem("f_lng")
    smq = k.sem("f_mm"); sm2 = k.sem("f_mm2")
    smt = k.sem("f_mt"); smg = k.sem("f_gmm")
    sS2 = [k.sem("f_s20"), k.sem("f_s21")]
    sST = k.sem("f_sT")

    w1src = k.inp["f_Wff1"].rearrange("(c p) f -> p c f", p=128)
    i = k.dma("gpsimd", W1[:, :, 0:2048], w1src[:, :, 0:2048])
    sW1s.inc(i, 16)
    i = k.dma("gpsimd", W1[:, :, 2048:4096], w1src[:, :, 2048:4096])
    tW1 = sW1s.inc(i, 16)
    i = k.dma("gpsimd", W2[:], k.inp["f_Wff2"].rearrange("(c p) f -> p c f", p=128))
    tW2 = sW2s.inc(i, 16)
    i = k.dma("sync", lng[:, 0, :], k.inp["f_lng"][:, 1, :])
    sLg.inc(i, 16)
    i = k.dma("sync", lng[:, 1, :], k.inp["f_lnb"][:, 1, :])
    tlg = sLg.inc(i, 16)

    stt = k.st
    stt[("f_ps", 0)] = 0
    stt[("f_ps", 1)] = 0
    for st_ in range(TOK // 256):
        hpar = st_ % 2
        k.w("sync", smq, stt.get(("f_h1T_users", hpar), 0))
        i = k.dma("sync", h1T[:, hpar],
                  k.h1T_s.rearrange("c p t -> p c t")[:, :, st_ * 256:(st_ + 1) * 256])
        th = sH[hpar].inc(i, 16)
        k.w("tensor", sH[hpar], th)
        k.w("tensor", sW1s, tW1)
        for fc in range(FC):
            r = fc % 2
            ps = k.S01[:, r * 512:r * 512 + 256]
            k.w("tensor", k.CHS, stt[("f_ps", r)])
            last = None
            for kc in range(DC):
                ii = k.mm(ps, W1[:, kc, fc * 128:(fc + 1) * 128], h1T[:, hpar, kc, :],
                          start=(kc == 0), stop=(kc == DC - 1))
                if ii is not None:
                    last = ii
            tg = smq.inc(last)
            k.w("scalar", smq, tg)
            k.w("scalar", sm2, stt.get("f_hid_users", 0))
            stt[("f_ps", r)] = k.act(hidT[:, fc, :], ps, AF.Relu,
                                     bias=k.bff1_sb[:, fc:fc + 1])
        stt[("f_h1T_users", hpar)] = smq.n
        k.w("tensor", k.CHS, max(stt[("f_ps", 0)], stt[("f_ps", 1)]))
        k.w("tensor", sW2s, tW2)
        for it in range(2):
            tau = st_ * 2 + it
            sl = tau % 2
            ps2 = k.S23 if it % 2 == 0 else k.S45
            k.w("sync", k.CHV, stt.get(("f_xr1", sl), 0))
            k.w("sync", sS2[sl], stt.get(("f_xr1s", sl), 0))
            i = k.dma("sync", k.xr1[:, sl, :], k.h1_s[tau * 128:(tau + 1) * 128, :])
            tx = sX[sl].inc(i, 16)
            k.w("tensor", k.CHV, stt.get(("f_ps2", it % 2), 0))
            last = None
            for half in range(2):
                for sl2 in range(FC):
                    ii = k.mm(ps2[:, half * 512:(half + 1) * 512],
                              hidT[:, sl2, it * 128:(it + 1) * 128],
                              W2[:, sl2, half * 512:(half + 1) * 512],
                              start=(sl2 == 0), stop=False)
                    if ii is not None:
                        last = ii
                ii = k.mm(ps2[:, half * 512:(half + 1) * 512], k.ones_sb[:],
                          k.bff2_sb[:, half * 512:(half + 1) * 512],
                          start=False, stop=True)
                if ii is not None:
                    last = ii
            tgm = sm2.inc(last)
            k.w("vector", sm2, tgm)
            k.w("vector", sX[sl], tx)
            k.w("vector", sS2[sl], stt.get(("f_xr2", sl), 0))
            tu2 = k.dve("tensor_tensor", out=k.xr2[:, sl, :], in0=ps2,
                        in1=k.xr1[:, sl, :], op=ALU.add)
            stt[("f_ps2", it % 2)] = tu2
            stt[("f_xr1", sl)] = tu2
            k.w("vector", sLg, tlg)
            tn = emit_ln(k, k.xr2[:, sl, :], k.lnt[:],
                         lng=lng[:, 0, :], lnb=lng[:, 1, :],
                         affout=k.xr1[:, sl, :])
            stt[("f_xr1", sl)] = tn
            k.w("sync", k.CHV, tn)
            i = k.dma("sync", k.h2_s[tau * 128:(tau + 1) * 128, :], k.xr1[:, sl, :])
            th2 = sS2[sl].inc(i, 16)
            stt[("f_xr2", sl)] = th2
            stt[("f_xr1s", sl)] = th2
            the = emit_ct(k, k.xr1[:, sl, :], h2Tb[:], sl, smt,
                          war=[(sST, stt.get("f_h2T_sp", 0)),
                               (smg, stt.get("f_h2T_users", 0))])
            stt[("f_xr1", sl)] = max(stt[("f_xr1", sl)], the)
            k.w("sync", k.CHV, the)
            i = k.dma("sync",
                      k.h2T_s.rearrange("c p t -> p c t")[:, :, tau * 128:(tau + 1) * 128],
                      h2Tb[:])
            stt["f_h2T_sp"] = sST.inc(i, 16)
            stt["f_hid_users"] = sm2.n
            k.w("tensor", k.CHV, max(the, stt.get("f_gate_done", 0)))
            last = None
            for kc in range(DC):
                ii = k.mm(k.GATEP, h2Tb[:, kc, :], k.wg_sb[:, kc, :],
                          start=(kc == 0), stop=(kc == DC - 1))
                if ii is not None:
                    last = ii
            tgm = smg.inc(last)
            stt["f_h2T_users"] = tgm
            k.w("vector", smg, tgm)
            nl = k.st_sb[:, 8:9]; dsum = k.st_sb[:, 9:10]; gate = k.st_sb[:, 10:11]
            eq8 = k.scr8_sb[:, 0:NE]; cum8 = k.scr8_sb[:, NE:2 * NE]
            c18 = k.scr8_sb[:, 2 * NE:3 * NE]; pr8 = k.scr8_sb[:, 3 * NE:4 * NE]
            tv1 = k.dve("reduce_max", out=nl, in_=k.GATEP, axis=AXX, negate=True)
            k.w("scalar", k.CHV, tv1)
            ta1 = k.act(pr8, k.GATEP, AF.Exp, bias=nl, accum_out=dsum)
            k.w("vector", k.CHS, ta1)
            k.dve("reciprocal", out=gate, in_=dsum)
            k.dve("tensor_scalar", out=eq8, in0=k.GATEP, scalar1=nl,
                  scalar2=0.0, op0=ALU.add, op1=ALU.is_equal)
            k.dve("tensor_tensor_scan", out=cum8, data0=k.ones8_sb[:],
                  data1=eq8, initial=0.0, op0=ALU.mult, op1=ALU.add)
            k.dve("tensor_scalar", out=c18, in0=cum8, scalar1=1.0,
                  scalar2=None, op0=ALU.is_equal, op1=ALU.bypass)
            k.dve("tensor_tensor", out=k.mask8_sb[:, tau, :], in0=eq8,
                  in1=c18, op=ALU.mult)
            k.dve("tensor_scalar", out=k.mg8_sb[:, tau, :],
                  in0=k.mask8_sb[:, tau, :], scalar1=gate, scalar2=None,
                  op0=ALU.mult, op1=ALU.bypass)
            k.dve("tensor_scalar", out=pr8, in0=k.mask8_sb[:, tau, :],
                  scalar1=-BIGIDX, scalar2=k.rowidx_sb[:, tau:tau + 1],
                  op0=ALU.mult, op1=ALU.add)
            stt["f_gate_done"] = k.dve("tensor_copy", out=k.idx_sb[:, tau, :],
                                       in_=pr8)
    k.barrier([(sS2[0], sS2[0].n), (sS2[1], sS2[1].n), (sST, sST.n),
               (k.CHV, stt["f_gate_done"])])


def emit_C(k, call):
    D, DC, FC, NE, TOK = k.D, k.DC, k.FC, k.NE, k.TOK
    We1 = k.sba("C_w1", [128, DC, k.FF], BF16, 0)
    We2 = k.sba("C_w2", [128, FC, D], BF16, 64)
    hidT = k.sba("C_hid", [128, FC, 512], BF16, 128)
    if "C_h2T" not in k._sb:
        k._sb["C_h2T"] = k.nc.alloc_sbuf_tensor_at(
            "C_h2T", [128, DC, 512], BF16, offset=k.off_xr1).ap()
    h2T = k._sb["C_h2T"]
    eob = k.xr2

    sL = k.sem("c2_ld"); sW1 = k.sem("c2_w1"); sW2 = k.sem("c2_w2")
    sB = k.sem("c2_b")
    smq = k.sem("c2_mm"); sm2 = k.sem("c2_mm2")
    sSC = [k.sem("c2_sc0"), k.sem("c2_sc1")]
    stt = k.st
    stt[("c_ps", 0)] = 0
    stt[("c_ps", 1)] = 0

    def ldw1(e):
        src = k.inp["f_We1"][e].rearrange("(c p) f -> p c f", p=128)
        i = k.dma("gpsimd", We1[:, :, 0:2048], src[:, :, 0:2048])
        sW1.inc(i, 16)
        i = k.dma("gpsimd", We1[:, :, 2048:4096], src[:, :, 2048:4096])
        return sW1.inc(i, 16)

    def ldw2(e):
        i = k.dma("gpsimd", We2[:],
                  k.inp["f_We2"][e].rearrange("(c p) f -> p c f", p=128))
        return sW2.inc(i, 16)

    def ldb(e):
        i = k.dma("gpsimd", k.be1_sb[:, e % 2, :], k.inp["f_be1"][e])
        sB.inc(i, 16)
        i = k.dma("gpsimd", k.be2_sb[:, e % 2, :], k.inp["f_be2"][e])
        return sB.inc(i, 16)

    tw1 = ldw1(0)
    tb = ldb(0)
    for e in range(NE):
        k.w("gpsimd", sm2, stt.get("c_eout_done", 0))
        tw2 = ldw2(e)
        for st_ in range(TOK // 512):
            k.w("sync", smq, stt.get("c_h2T_users", 0))
            i = k.dma("sync", h2T[:],
                      k.h2T_s.rearrange("c p t -> p c t")[:, :, st_ * 512:(st_ + 1) * 512])
            th = sL.inc(i, 16)
            k.w("tensor", sL, th)
            k.w("tensor", sW1, tw1)
            k.w("tensor", sB, tb)
            for fc in range(FC):
                r = fc % 2
                ps = k.S01[:, r * 512:(r + 1) * 512]
                k.w("tensor", k.CHS, stt[("c_ps", r)])
                last = None
                for kc in range(DC):
                    ii = k.mm(ps, We1[:, kc, fc * 128:(fc + 1) * 128], h2T[:, kc, :],
                              start=(kc == 0), stop=(kc == DC - 1))
                    if ii is not None:
                        last = ii
                tg = smq.inc(last)
                k.w("scalar", smq, tg)
                k.w("scalar", sm2, stt.get("c_hid_users", 0))
                stt[("c_ps", r)] = k.act(hidT[:, fc, :], ps, AF.Relu,
                                         bias=k.be1_sb[:, e % 2, fc:fc + 1])
            stt["c_h2T_users"] = smq.n
            k.w("tensor", k.CHS, max(stt[("c_ps", 0)], stt[("c_ps", 1)]))
            k.w("tensor", sW2, tw2)
            for it in range(4):
                tau = st_ * 4 + it
                sl = tau % 2
                ps2 = k.S23 if it % 2 == 0 else k.S45
                k.w("tensor", k.CHV, stt.get(("c_ps2", it % 2), 0))
                last = None
                for half in range(2):
                    for sl2 in range(FC):
                        ii = k.mm(ps2[:, half * 512:(half + 1) * 512],
                                  hidT[:, sl2, it * 128:(it + 1) * 128],
                                  We2[:, sl2, half * 512:(half + 1) * 512],
                                  start=(sl2 == 0), stop=False)
                        if ii is not None:
                            last = ii
                    ii = k.mm(ps2[:, half * 512:(half + 1) * 512], k.ones_sb[:],
                              k.be2_sb[:, e % 2, half * 512:(half + 1) * 512],
                              start=False, stop=True)
                    if ii is not None:
                        last = ii
                tgm = sm2.inc(last)
                k.w("vector", sm2, tgm)
                k.w("vector", sSC[sl], stt.get(("c_eo", sl), 0))
                teo = k.dve("tensor_scalar", out=eob[:, sl, :], in0=ps2,
                            scalar1=k.mg8_sb[:, tau, e:e + 1], scalar2=None,
                            op0=ALU.mult, op1=ALU.bypass)
                stt[("c_ps2", it % 2)] = teo
                k.w("gpsimd", k.CHV, teo)
                i = None
                if k.on("gpsimd") and _MOE_DBG:
                    di = k.eng.indirect_dma_start(
                        out=k.moe_dbg.rearrange("e t d -> (e t) d"),
                        out_offset=IndirectOffsetOnAxis(
                            ap=k.idx_sb[:, tau, e:e + 1], axis=0),
                        in_=eob[:, sl, :], in_offset=None,
                        element_offset=e * k.TOK * k.D,
                        bounds_check=k.st["breg"], oob_is_err=False)
                else:
                    di = None
                if _MOE_DBG:
                    sSC[sl].inc(di, 16)
                if k.on("gpsimd"):
                    i = k.eng.indirect_dma_start(
                        out=k.moe_s,
                        out_offset=IndirectOffsetOnAxis(
                            ap=k.idx_sb[:, tau, e:e + 1], axis=0),
                        in_=eob[:, sl, :], in_offset=None,
                        bounds_check=k.st["breg"], oob_is_err=False)
                stt[("c_eo", sl)] = sSC[sl].inc(i, 16)
            stt["c_eout_done"] = sm2.n
        if e + 1 < NE:
            k.w("gpsimd", smq, stt["c_h2T_users"])
            tw1 = ldw1(e + 1)
            k.w("gpsimd", k.CHS, max(stt[("c_ps", 0)], stt[("c_ps", 1)]))
            tb = ldb(e + 1)
    k.barrier([(sSC[0], sSC[0].n), (sSC[1], sSC[1].n)])


def emit_D(k, call, x_in, x_out):
    D, DC, NTT = k.D, k.DC, k.NTT
    Wout = k.sba("D_wout", [128, DC, D], BF16, 0)
    lng = k.sba("D_lng", [128, 2, D], F32, 16)
    x3r = k.sba("D_x3", [128, 2, D], F32, 24)
    u3r = k.sba("D_u3", [128, 2, D], F32, 32)
    h3r = k.sba("D_h3", [128, 2, D], F32, 40)
    h3Tb = k.sba("D_h3T", [128, DC, 128], BF16, 48)

    s1 = [k.sem("d_a0"), k.sem("d_a1")]
    s2 = [k.sem("d_b0"), k.sem("d_b1")]
    s3 = [k.sem("d_c0"), k.sem("d_c1")]
    sLg = k.sem("d_lng"); sW = k.sem("d_wld")
    smq = k.sem("d_mm"); smt = k.sem("d_mt")
    sSP = [k.sem("d_o0"), k.sem("d_o1")]

    i = k.dma("gpsimd", Wout[:], k.inp["f_Wout"].rearrange("(c p) f -> p c f", p=128))
    tW = sW.inc(i, 16)
    i = k.dma("sync", lng[:, 0, :], k.inp["f_lng"][:, 2, :])
    sLg.inc(i, 16)
    i = k.dma("sync", lng[:, 1, :], k.inp["f_lnb"][:, 2, :])
    tlg = sLg.inc(i, 16)

    stt = k.st
    for tau in range(NTT):
        sl = tau % 2
        k.w("sync", k.CHV, stt.get(("d_xr", sl), 0))
        i = k.dma("sync", k.xr1[:, sl, :], k.h2_s[tau * 128:(tau + 1) * 128, :])
        t1 = s1[sl].inc(i, 16)
        i = k.dma("sync", k.xr2[:, sl, :], k.moe_s[tau * 128:(tau + 1) * 128, :])
        t2 = s2[sl].inc(i, 16)
        k.w("sync", k.CHV, stt.get(("d_x3", sl), 0))
        i = k.dma("sync", x3r[:, sl, :], x_in[tau * 128:(tau + 1) * 128, :])
        t3 = s3[sl].inc(i, 16)
        k.w("vector", s1[sl], t1)
        k.w("vector", s2[sl], t2)
        k.w("vector", sSP[sl], stt.get(("d_u3", sl), 0))
        tu3 = k.dve("tensor_tensor", out=u3r[:, sl, :], in0=k.xr1[:, sl, :],
                    in1=k.xr2[:, sl, :], op=ALU.add)
        stt[("d_xr", sl)] = tu3
        k.w("vector", sLg, tlg)
        emit_ln(k, u3r[:, sl, :], k.lnt[:],
                lng=lng[:, 0, :], lnb=lng[:, 1, :], affout=h3r[:, sl, :])
        the = emit_ct(k, h3r[:, sl, :], h3Tb[:], sl, smt,
                      war=[(smq, stt.get("d_h3T_users", 0))])
        k.w("tensor", k.CHV, the)
        k.w("tensor", sW, tW)
        tf = 0
        for half in range(2):
            ps = k.S01[:, half * 512:(half + 1) * 512]
            k.w("tensor", k.CHV, stt.get(("d_ps", half), 0))
            last = None
            for kc in range(DC):
                ii = k.mm(ps, h3Tb[:, kc, :],
                          Wout[:, kc, half * 512:(half + 1) * 512],
                          start=(kc == 0), stop=(kc == DC - 1))
                if ii is not None:
                    last = ii
            tgm = smq.inc(last)
            k.w("vector", smq, tgm)
            k.w("vector", s3[sl], t3)
            tf = k.dve("scalar_tensor_tensor",
                       out=u3r[:, sl, half * 512:(half + 1) * 512], in0=ps,
                       scalar=0.5, in1=x3r[:, sl, half * 512:(half + 1) * 512],
                       op0=ALU.mult, op1=ALU.add)
            stt[("d_ps", half)] = tf
            stt[("d_x3", sl)] = tf
        stt["d_h3T_users"] = smq.n
        k.w("sync", k.CHV, tf)
        i = k.dma("sync", x_out[tau * 128:(tau + 1) * 128, :], u3r[:, sl, :])
        stt[("d_u3", sl)] = sSP[sl].inc(i, 16)
    k.barrier([(sSP[0], sSP[0].n), (sSP[1], sSP[1].n)])


def emit_ffn(k, call, x_in, x_out):
    rT = emit_rt(k, call)
    for s in range(k.SEQ):
        kT, V = emit_p0(k, call, s, x_in)
        emit_attn(k, call, s, kT, V, rT)
    emit_B(k, call)
    emit_B2(k, call)
    emit_C(k, call)
    emit_D(k, call, x_in, x_out)


def emit_mixer(k, x_in, x_out):
    S, D, DC, NT = k.S, k.D, k.DC, k.NT
    Wz = k.sba("M_wz", [128, DC, D], BF16, 0)
    Wh = k.sba("M_wh", [128, DC, D], BF16, 16)
    xnT = k.sba("M_xnT", [128, 2, DC, 512], BF16, 32)
    asb = k.sba("M_a", [128, 2, 512], F32, 48)
    zsb = k.sba("M_z", [128, 2, 512], F32, 52)
    hsb = k.sba("M_h", [128, DC, S], F32, 56)
    hbf = k.sba("M_hbf", [128, DC, S], BF16, 120)
    Wg = k.sba("M_wg", [128, DC, D], BF16, 0)
    gsb = k.sba("M_g", [128, 2, 512], F32, 16)
    mox = k.sba("M_mo", [128, 4, D], F32, 24)

    sX = [k.sem("m_x0"), k.sem("m_x1")]
    sWzh = k.sem("m_wzh"); sWg = k.sem("m_wgl")
    smq = k.sem("m_mm"); smt = k.sem("m_mt")
    sSP = [k.sem("m_o0"), k.sem("m_o1"), k.sem("m_o2"), k.sem("m_o3")]
    stt = k.st

    for s in range(k.SEQ):
        k.w("gpsimd", smq, stt.get("m_wg_users", 0))
        k.w("gpsimd", smt, stt.get("m_gate_tr", 0))
        k.w("gpsimd", k.CHV, stt.get("m_gate_dve", 0))
        k.w("gpsimd", k.CHS, stt.get("m_gate_act", 0))
        for it_ in range(4):
            k.w("gpsimd", sSP[it_], stt.get(("m_mo", it_), 0))
        i = k.dma("gpsimd", Wz[:], k.inp["mx_Wz"].rearrange("(c p) f -> p c f", p=128))
        sWzh.inc(i, 16)
        i = k.dma("gpsimd", Wh[:], k.inp["mx_Wh"].rearrange("(c p) f -> p c f", p=128))
        tWzh = sWzh.inc(i, 16)
        xnT_users = [0, 0]
        for n in range(S // 512):
            par = n % 2
            for it in range(4):
                tau = n * 4 + it
                gtau = s * NT + tau
                slot = tau % 2
                k.w("sync", k.CHV, stt.get(("m_xr1", slot), 0))
                i = k.dma("sync", k.xr1[:, slot, :],
                          x_in[gtau * 128:(gtau + 1) * 128, :])
                tx = sX[slot].inc(i, 16)
                k.w("vector", sX[slot], tx)
                tn = emit_ln(k, k.xr1[:, slot, :], k.xr2[:, slot, :])
                stt[("m_xr1", slot)] = tn
                war_ = [(smq, xnT_users[par])] if it == 0 else []
                if n == 0 and it == 0:
                    war_ += [(sSP[2], stt.get(("m_mo", 2), 0)),
                             (sSP[3], stt.get(("m_mo", 3), 0))]
                emit_ct(k, k.xr2[:, slot, :],
                        xnT[:, par, :, it * 128:(it + 1) * 128], slot, smt,
                        war=war_)
            txn = k.st["ptfree"]
            k.w("tensor", k.CHV, txn)
            k.w("tensor", sWzh, tWzh)
            for oc in range(DC):
                r = oc % 2
                psz = k.S01[:, r * 512:(r + 1) * 512]
                psh = k.S23[:, r * 512:(r + 1) * 512]
                k.w("tensor", k.CHV, stt.get(("m_ps", r), 0))
                lastz = None
                for kc in range(DC):
                    ii = k.mm(psz, Wz[:, kc, oc * 128:(oc + 1) * 128],
                              xnT[:, par, kc, :], start=(kc == 0), stop=(kc == DC - 1))
                    if ii is not None:
                        lastz = ii
                tz = smq.inc(lastz)
                lasth = None
                for kc in range(DC):
                    ii = k.mm(psh, Wh[:, kc, oc * 128:(oc + 1) * 128],
                              xnT[:, par, kc, :], start=(kc == 0), stop=(kc == DC - 1))
                    if ii is not None:
                        lasth = ii
                th_ = smq.inc(lasth)
                k.w("scalar", smq, tz)
                k.w("scalar", k.CHV, stt.get(("m_ab", r), 0))
                k.act(asb[:, r, :], psz, AF.Sigmoid, scale=-1.0,
                      bias=k.nbz_sb[:, oc:oc + 1])
                ta2 = k.act(zsb[:, r, :], psz, AF.Sigmoid, bias=k.bz_sb[:, oc:oc + 1])
                k.w("vector", k.CHS, ta2)
                k.w("vector", smq, th_)
                k.dve("scalar_tensor_tensor", out=zsb[:, r, :], in0=psh,
                      scalar=k.bh_sb[:, oc:oc + 1], in1=zsb[:, r, :],
                      op0=ALU.add, op1=ALU.mult)
                init = 0.0 if n == 0 else hsb[:, oc, n * 512 - 1:n * 512]
                tsc = k.dve("tensor_tensor_scan",
                            out=hsb[:, oc, n * 512:(n + 1) * 512],
                            data0=asb[:, r, :], data1=zsb[:, r, :],
                            initial=init, op0=ALU.mult, op1=ALU.add)
                stt[("m_ps", r)] = tsc
                stt[("m_ab", r)] = tsc
            xnT_users[par] = smq.n
        tcast = 0
        for oc in range(DC):
            tcast = k.dve("tensor_copy", out=hbf[:, oc, :], in_=hsb[:, oc, :])
        k.w("gpsimd", smq, smq.n)
        i = k.dma("gpsimd", Wg[:], k.inp["mx_Wg"].rearrange("(c p) f -> p c f", p=128))
        tWg = sWg.inc(i, 16)
        for n in range(S // 512):
            for oc in range(DC):
                r = oc % 2
                psz = k.S45[:, r * 512:(r + 1) * 512]
                k.w("tensor", k.CHV, max(tcast, stt.get(("m_g", r), 0)))
                k.w("tensor", sWg, tWg)
                last = None
                for kc in range(DC):
                    ii = k.mm(psz, Wg[:, kc, oc * 128:(oc + 1) * 128],
                              hbf[:, kc, n * 512:(n + 1) * 512],
                              start=(kc == 0), stop=(kc == DC - 1))
                    if ii is not None:
                        last = ii
                tg = smq.inc(last)
                stt["m_wg_users"] = tg
                k.w("scalar", smq, tg)
                k.w("scalar", k.CHV, stt.get(("m_g", r), 0))
                tga = k.act(gsb[:, r, :], psz, AF.Sigmoid, bias=k.bg_sb[:, oc:oc + 1])
                k.w("vector", k.CHS, tga)
                tgh = k.dve("tensor_tensor", out=gsb[:, r, :], in0=gsb[:, r, :],
                            in1=hsb[:, oc, n * 512:(n + 1) * 512], op=ALU.mult)
                k.w("tensor", k.CHV, tgh)
                te = 0
                for it in range(4):
                    tau = n * 4 + it
                    k.w("tensor", k.CHV, k.st.get("ptfree", 0))
                    ii = k.tr(k.PTF, gsb[:, r, it * 128:(it + 1) * 128], fp32=True)
                    tt = smt.inc(ii)
                    k.w("vector", smt, tt)
                    if oc == 0:
                        k.w("vector", sSP[it], stt.get(("m_mo", it), 0))
                    te = k.dve("tensor_copy",
                               out=mox[:, it, oc * 128:(oc + 1) * 128], in_=k.PTF)
                    k.st["ptfree"] = te
                stt[("m_g", r)] = te
            for it in range(4):
                tau = n * 4 + it
                gtau = s * NT + tau
                slot = tau % 2
                k.w("sync", k.CHV, stt.get(("m_xr1", slot), 0))
                i = k.dma("sync", k.xr1[:, slot, :],
                          x_in[gtau * 128:(gtau + 1) * 128, :])
                tx = sX[slot].inc(i, 16)
                k.w("vector", sX[slot], tx)
                tmo = k.dve("tensor_tensor", out=mox[:, it, :], in0=mox[:, it, :],
                            in1=k.xr1[:, slot, :], op=ALU.add)
                stt[("m_xr1", slot)] = tmo
                k.w("sync", k.CHV, tmo)
                i = k.dma("sync", x_out[gtau * 128:(gtau + 1) * 128, :],
                          mox[:, it, :])
                stt[("m_mo", it)] = sSP[it].inc(i, 16)
        stt["m_gate_dve"] = k.CHV.n
        stt["m_gate_act"] = k.CHS.n
        stt["m_gate_tr"] = smt.n
    k.barrier([(sSP[0], sSP[0].n), (sSP[1], sSP[1].n),
               (sSP[2], sSP[2].n), (sSP[3], sSP[3].n)])


_NSTAGE = 3


def emit_program(k):
    emit_stage0(k)
    emit_ffn(k, 0, k.inp["x"], k.xA)
    if _NSTAGE >= 2:
        emit_mixer(k, k.xA, k.xB)
    if _NSTAGE >= 3:
        emit_ffn(k, 1, k.xB, k.y)


def _pe_table(S, D):
    pos = np.arange(S - 1, -1, -1, dtype=np.float64)
    inv_freq = 1.0 / 10000.0 ** (np.arange(0, D, 2, dtype=np.float64) / D)
    sinusoid = pos[:, None] * inv_freq[None, :]
    pe = np.concatenate([np.sin(sinusoid), np.cos(sinusoid)], axis=-1)
    return np.ascontiguousarray(pe.T.astype(np.float32))


def prep_inputs(cfg, inputs):
    B, S, D = cfg["B"], cfg["S"], cfg["D"]
    NE = cfg["NE"]
    SEQ = B // NCORES
    TOK = SEQ * S
    NTT = TOK // 128
    f32 = np.float32

    def asnp(v, dt=f32):
        return np.ascontiguousarray(np.asarray(v), dtype=dt)

    def vec_pc(v):
        return np.ascontiguousarray(v.reshape(-1, 128).T)

    Wz = asnp(inputs["L0_attn_Wz"])
    Wzf = np.ascontiguousarray(np.transpose(Wz, (1, 0, 2)).reshape(D, D))
    Wh = asnp(inputs["L0_attn_Wh"])
    Whf = np.ascontiguousarray(np.transpose(Wh, (1, 0, 2)).reshape(D, D))
    lng = np.ascontiguousarray(np.stack(
        [np.broadcast_to(asnp(inputs[f"L0_ffn_ln{i}_g"]), (128, D))
         for i in (1, 2, 3)], axis=1))
    lnb = np.ascontiguousarray(np.stack(
        [np.broadcast_to(asnp(inputs[f"L0_ffn_ln{i}_b"]), (128, D))
         for i in (1, 2, 3)], axis=1))
    be1 = asnp(inputs["L0_ffn_be1"])
    be1pc = np.ascontiguousarray(
        np.stack([vec_pc(be1[e]) for e in range(NE)], axis=0))
    ident = np.eye(128, dtype=ml_dtypes.bfloat16)
    identf = np.eye(128, dtype=f32)
    cm = np.where(np.arange(128)[None, :] <= np.arange(128)[:, None],
                  0.0, NEGINF).astype(f32)
    rowidx = (BIGIDX + (np.arange(NTT)[None, :] * 128
                        + np.arange(128)[:, None])).astype(f32)
    shared = {
        "mx_Wz": Wzf, "mx_Wh": Whf, "mx_Wg": asnp(inputs["L0_attn_Wg"]),
        "mx_bz": vec_pc(asnp(inputs["L0_attn_bz"]).reshape(-1)),
        "mx_bh": vec_pc(asnp(inputs["L0_attn_bh"]).reshape(-1)),
        "mx_bg": vec_pc(asnp(inputs["L0_attn_bg"]).reshape(-1)),
        "f_Wqkv": asnp(inputs["L0_ffn_Wqkv"]), "f_Wo": asnp(inputs["L0_ffn_Wo"]),
        "f_Wr": asnp(inputs["L0_ffn_Wr"]),
        "f_ub": vec_pc(asnp(inputs["L0_ffn_u_bias"]).reshape(-1)),
        "f_vb": vec_pc(asnp(inputs["L0_ffn_v_bias"]).reshape(-1)),
        "f_lng": lng, "f_lnb": lnb,
        "f_Wff1": asnp(inputs["L0_ffn_Wff1"]),
        "f_bff1": vec_pc(asnp(inputs["L0_ffn_bff1"]).reshape(-1)),
        "f_Wff2": asnp(inputs["L0_ffn_Wff2"]),
        "f_bff2": asnp(inputs["L0_ffn_bff2"]).reshape(1, D),
        "f_Wg": asnp(inputs["L0_ffn_Wg"]),
        "f_We1": asnp(inputs["L0_ffn_We1"]), "f_be1": be1pc,
        "f_We2": asnp(inputs["L0_ffn_We2"]),
        "f_be2": asnp(inputs["L0_ffn_be2"]).reshape(NE, 1, D),
        "f_Wout": asnp(inputs["L0_ffn_Wout"]),
        "peT": _pe_table(S, D),
        "ident": ident, "identf": identf, "cmask": cm, "rowidx": rowidx,
    }
    x = asnp(inputs["L0_attn_x"])
    in_maps = []
    for c in range(NCORES):
        m = dict(shared)
        m["x"] = np.ascontiguousarray(x[c * SEQ:(c + 1) * SEQ].reshape(TOK, D))
        in_maps.append(m)
    return in_maps


def build(cfg):
    k = KB(cfg)
    with k.nc.Block() as block:
        for en in KB.ENGINES:
            def mk(en=en):
                def f(eng):
                    k.begin_pass(en, eng)
                    emit_program(k)
                return f
            getattr(block, en)(mk())
    return k


def kernel(**inputs):
    cfg = dict(B=16, S=2048, D=1024, NH=16, DH=64, FF=4096, NE=8)
    in_maps = prep_inputs(cfg, inputs)
    k = build(cfg)
    res = run_bass_kernel_spmd(k.nc, in_maps, core_ids=list(range(NCORES)))
    SEQ = cfg["B"] // NCORES
    out = np.concatenate(
        [r["y"].reshape(SEQ, cfg["S"], cfg["D"]) for r in res.results], axis=0)
    return out


# version 06
# speedup vs baseline: 1.0494x; 1.0494x over previous
import numpy as np
import ml_dtypes

import bass_rust
import concourse.bass as bass
import concourse.mybir as mybir
from concourse.bass import IndirectOffsetOnAxis
from concourse.bass_utils import run_bass_kernel_spmd

F32 = mybir.dt.float32
BF16 = mybir.dt.bfloat16
U32 = mybir.dt.uint32
AF = mybir.ActivationFunctionType
ALU = mybir.AluOpType
AXX = mybir.AxisListType.X

NCORES = 8
BIGIDX = 2097152.0
EPS = 1e-5
NEGINF = -1.0e30

_DBG = set()
_MOE_DBG = False


class Sem:
    __slots__ = ("h", "n")

    def __init__(self, h):
        self.h = h
        self.n = 0

    def inc(self, inst, k=1):
        if inst is not None:
            inst.then_inc(self.h, k)
        self.n += k
        return self.n


class KB:
    ENGINES = ("sync", "scalar", "vector", "tensor", "gpsimd")

    def __init__(self, cfg):
        self.cfg = cfg
        self.B, self.S, self.D = cfg["B"], cfg["S"], cfg["D"]
        self.NH, self.DH, self.FF, self.NE = cfg["NH"], cfg["DH"], cfg["FF"], cfg["NE"]
        self.SEQ = self.B // NCORES
        self.TOK = self.SEQ * self.S
        self.NT = self.S // 128
        self.NTT = self.TOK // 128
        self.DC = self.D // 128
        self.FC = self.FF // 128
        self.SEGW = 1024

        nc = bass.Bass("TRN2", target_bir_lowering=False)
        self.nc = nc
        self._sb = {}
        self._sems = {}
        self.en = None
        self.eng = None
        self.st = {}

        D, S, FF, NE, TOK, DC, FC, NT, NTT = (
            self.D, self.S, self.FF, self.NE, self.TOK, self.DC, self.FC,
            self.NT, self.NTT)

        self.inp = {}

        def di(name, shape, dt=F32):
            self.inp[name] = nc.dram_tensor(name, shape, dt, kind="ExternalInput").ap()

        di("x", [TOK, D])
        di("mx_Wz", [D, D]); di("mx_Wh", [D, D]); di("mx_Wg", [D, D])
        di("mx_bz", [128, DC]); di("mx_bh", [128, DC]); di("mx_bg", [128, DC])
        di("f_Wqkv", [D, 3 * D]); di("f_Wo", [D, D]); di("f_Wr", [D, D])
        di("f_ub", [128, DC]); di("f_vb", [128, DC])
        di("f_lng", [128, 3, D]); di("f_lnb", [128, 3, D])
        di("f_Wff1", [D, FF]); di("f_bff1", [128, FC])
        di("f_Wff2", [FF, D]); di("f_bff2", [1, D])
        di("f_Wg", [D, NE])
        di("f_We1", [NE, D, FF]); di("f_be1", [NE, 128, FC])
        di("f_We2", [NE, FF, D]); di("f_be2", [NE, 1, D])
        di("f_Wout", [D, D])
        di("peT", [D, S])
        di("ident", [128, 128], BF16)
        di("identf", [128, 128])
        di("cmask", [128, 512], BF16)
        di("rowidx", [128, NTT])

        self.y = nc.dram_tensor("y", [TOK, D], F32, kind="ExternalOutput").ap()

        def scr(name, shape, dt=F32):
            kind = "ExternalOutput" if name in _DBG else "Internal"
            return nc.dram_tensor(name, shape, dt, kind=kind).ap()

        self.xA = scr("xA", [TOK, D])
        self.xB = scr("xB", [TOK, D])
        self.xn_s = scr("xn_s", [TOK, D])
        self.qT_s = scr("qT_s", [self.SEQ * NT, 128, DC * 128], BF16)
        self.ctx_s = scr("ctx_s", [TOK, D])
        self.h1_s = scr("h1_s", [TOK, D])
        self.h1T_s = scr("h1T_s", [DC, 128, TOK], BF16)
        self.h2_s = scr("h2_s", [TOK, D])
        self.h2T_s = scr("h2T_s", [DC, 128, TOK], BF16)
        self.moe_s = scr("moe_s", [TOK, D])
        self.G_s = scr("G_s", [2, 128 * S], BF16)
        self.moe_dbg = scr("moe_dbg", [NE, TOK, D]) if _MOE_DBG else None

        self.S01 = nc.alloc_psum_tensor("S01", [128, 1024], F32).ap()
        self.S23 = nc.alloc_psum_tensor("S23", [128, 1024], F32).ap()
        self.S45 = nc.alloc_psum_tensor("S45", [128, 1024], F32).ap()
        self.S67 = nc.alloc_psum_tensor("S67", [128, 1024], F32).ap()
        self.PT = self.S67[:, 0:512].bitcast(BF16)
        self.PTF = self.S67[:, 0:128]
        self.CTXP = self.S67[:, 512:576]
        self.CTXP2 = self.S67[:, 576:640]
        self.GATEP = self.S67[:, 512:512 + NE]

        A = nc.alloc_sbuf_tensor

        def ap_(name, shape, dt=F32):
            return A(name, shape, dt).ap()

        self.ident_sb = ap_("ident_sb", [128, 128], BF16)
        self.identf_sb = ap_("identf_sb", [128, 128])
        self.cmask_sb = ap_("cmask_sb", [128, 512], BF16)
        self.rowidx_sb = ap_("rowidx_sb", [128, NTT])
        self.ub_sb = ap_("ub_sb", [128, DC])
        self.vb_sb = ap_("vb_sb", [128, DC])
        self.bz_sb = ap_("bz_sb", [128, DC])
        self.nbz_sb = ap_("nbz_sb", [128, DC])
        self.bh_sb = ap_("bh_sb", [128, DC])
        self.bg_sb = ap_("bg_sb", [128, DC])
        self.bff1_sb = ap_("bff1_sb", [128, FC])
        self.be1_sb = ap_("be1_sb", [128, 2, FC])
        self.bff2_sb = ap_("bff2_sb", [1, D], BF16)
        self.be2_sb = ap_("be2_sb", [1, 2, D], BF16)
        self.wg_sb = ap_("wg_sb", [128, DC, NE], BF16)
        self.ones_sb = ap_("ones_sb", [1, 128], BF16)
        self.ones8_sb = ap_("ones8_sb", [128, NE])
        self.st_sb = ap_("st_sb", [128, 64])
        self.mx2_sb = ap_("mx2_sb", [128, 4])
        self.nmx_sb = ap_("nmx_sb", [128, 2])
        self.den_sb = ap_("den_sb", [128, 2])
        self.rden_sb = ap_("rden_sb", [128, 2])
        self.epsv = ap_("epsv", [128, 1])
        self.mask8_sb = ap_("mask8_sb", [128, NTT, NE])
        self.mg8_sb = ap_("mg8_sb", [128, NTT, NE])
        self.idx_sb = ap_("idx_sb", [128, NTT, NE], U32)
        self.scr8_sb = ap_("scr8_sb", [128, 4 * NE])
        self.off_xr1 = nc.sbuf_base
        self.xr1 = ap_("xr1", [128, 2, D])
        self.off_xr2 = nc.sbuf_base
        self.xr2 = ap_("xr2", [128, 2, D])
        self.xnb = ap_("xnb", [128, 2, D], BF16)
        self.sq = ap_("sq", [128, D], BF16)
        self.lnt = ap_("lnt", [128, D])

        self.arena0 = nc.sbuf_base
        self.arena_kb = (nc.sbuf_top - nc.sbuf_base) // 1024
        assert self.arena_kb >= 168, f"arena too small: {self.arena_kb}KB"

    def sem(self, name):
        s = self._sems.get(name)
        if s is None:
            s = Sem(self.nc.alloc_semaphore(name))
            self._sems[name] = s
        return s

    @property
    def CHV(self):
        return self.sem("ch_vector")

    @property
    def CHS(self):
        return self.sem("ch_scalar")

    @property
    def CHG(self):
        return self.sem("ch_gpsimd")

    def sba(self, name, shape, dt, off_kb):
        t = self._sb.get(name)
        if t is None:
            t = self.nc.alloc_sbuf_tensor_at(
                name, shape, dt, offset=self.arena0 + off_kb * 1024).ap()
            self._sb[name] = t
        return t

    def begin_pass(self, en, eng):
        self.en = en
        self.eng = eng
        self.st = {}
        for s in self._sems.values():
            s.n = 0

    def on(self, en):
        return self.en == en

    def w(self, en, sem, val):
        if val > 0 and self.en == en:
            self.eng.wait_ge(sem.h, val)

    def barrier(self, deps):
        for sem, val in deps:
            if val > 0:
                self.eng.wait_ge(sem.h, val)

    def mm(self, out, lhsT, rhs, start=True, stop=True):
        if self.en == "tensor":
            return self.eng.matmul(out, lhsT, rhs, start=start, stop=stop)
        return None

    def tr(self, out, in_, fp32=False):
        if self.en == "tensor":
            return self.eng.transpose(
                out, in_, self.identf_sb if fp32 else self.ident_sb)
        return None

    def _chain_op(self, en_name, make):
        s = self.sem("ch_" + en_name)
        if s.n > 0 and self.en == en_name:
            self.eng.wait_ge(s.h, s.n)
        inst = make(self.eng) if self.en == en_name else None
        return s.inc(inst)

    def act(self, out, in_, func, **kw):
        return self._chain_op("scalar", lambda e: e.activation(out, in_, func, **kw))

    def dve(self, meth, *a, **kw):
        return self._chain_op("vector", lambda e: getattr(e, meth)(*a, **kw))

    def gp(self, meth, *a, **kw):
        return self._chain_op("gpsimd", lambda e: getattr(e, meth)(*a, **kw))

    def dma(self, q, out, in_, **kw):
        if self.en == q:
            return self.eng.dma_start(out=out, in_=in_, **kw)
        return None


def emit_ln(k, xin, xout, lng=None, lnb=None, affout=None):
    D = k.D
    m = k.st_sb[:, 0:1]; msq = k.st_sb[:, 1:2]; va = k.st_sb[:, 2:3]
    lv = k.st_sb[:, 3:4]; rstd = k.st_sb[:, 4:5]; nmr = k.st_sb[:, 5:6]
    ssum = k.st_sb[:, 6:7]; ssq = k.st_sb[:, 7:8]
    k.dve("scalar_tensor_tensor", out=k.sq[:], in0=xin, scalar=1.0,
          in1=xin, op0=ALU.mult, op1=ALU.mult, accum_out=ssq)
    k.dve("reduce_sum", out=ssum, in_=xin, axis=AXX)
    k.dve("tensor_scalar", out=m, in0=ssum, scalar1=1.0 / D, scalar2=None,
          op0=ALU.mult, op1=ALU.bypass)
    k.dve("tensor_tensor", out=msq, in0=m, in1=m, op=ALU.mult)
    tv = k.dve("tensor_scalar", out=va, in0=ssq, scalar1=1.0 / D, scalar2=msq,
               op0=ALU.mult, op1=ALU.subtract)
    k.w("scalar", k.CHV, tv)
    k.act(lv, va, AF.Ln, bias=k.epsv[:])
    ta = k.act(rstd, lv, AF.Exp, scale=-0.5)
    k.w("vector", k.CHS, ta)
    k.dve("scalar_tensor_tensor", out=nmr, in0=m, scalar=-1.0, in1=rstd,
          op0=ALU.mult, op1=ALU.mult)
    tn = k.dve("tensor_scalar", out=xout, in0=xin, scalar1=rstd, scalar2=nmr,
               op0=ALU.mult, op1=ALU.add)
    if lng is not None:
        k.dve("tensor_tensor", out=affout, in0=xout, in1=lng, op=ALU.mult)
        tn = k.dve("tensor_tensor", out=affout, in0=affout, in1=lnb, op=ALU.add)
    return tn


def emit_ct(k, src, dst, par, smt, war=()):
    DC = k.DC
    tc = k.dve("tensor_copy", out=k.xnb[:, par, :], in_=src)
    k.w("tensor", k.CHV, tc)
    k.w("tensor", k.CHV, k.st.get("ptfree", 0))
    last = None
    for c in range(DC):
        ii = k.tr(k.PT[:, c * 128:(c + 1) * 128],
                  k.xnb[:, par, :].rearrange("p (c f) -> p c f", c=DC)[:, c, :])
        if ii is not None:
            last = ii
    tt = smt.inc(last)
    k.w("vector", smt, tt)
    for sem, val in war:
        k.w("vector", sem, val)
    te = k.dve("tensor_copy", out=dst,
               in_=k.PT.rearrange("p (c f) -> p c f", c=DC))
    k.st["ptfree"] = te
    return te


def emit_stage0(k):
    sL = k.sem("c_ld")
    sG = k.sem("c_ldg")
    loads = [
        (k.ident_sb[:], k.inp["ident"]), (k.identf_sb[:], k.inp["identf"]),
        (k.cmask_sb[:], k.inp["cmask"]), (k.rowidx_sb[:], k.inp["rowidx"]),
        (k.ub_sb[:], k.inp["f_ub"]), (k.vb_sb[:], k.inp["f_vb"]),
        (k.bz_sb[:], k.inp["mx_bz"]), (k.bh_sb[:], k.inp["mx_bh"]),
        (k.bg_sb[:], k.inp["mx_bg"]), (k.bff1_sb[:], k.inp["f_bff1"]),
    ]
    t = 0
    for out, in_ in loads:
        i = k.dma("sync", out, in_)
        t = sL.inc(i, 16)
    i = k.dma("gpsimd", k.bff2_sb[:], k.inp["f_bff2"])
    sG.inc(i, 16)
    i = k.dma("gpsimd", k.wg_sb[:],
              k.inp["f_Wg"].rearrange("(c p) e -> p c e", p=128))
    tg = sG.inc(i, 16)
    k.w("gpsimd", sG, tg)
    k.gp("memset", k.ones_sb[:], 1.0)
    k.gp("memset", k.ones8_sb[:], 1.0)
    tm = k.gp("memset", k.epsv[:], EPS)
    if k.on("gpsimd"):
        k.st["breg"] = k.eng.to_reg(k.TOK - 1)
    k.w("vector", sL, t)
    tv = k.dve("tensor_scalar", out=k.nbz_sb[:], in0=k.bz_sb[:], scalar1=-1.0,
               scalar2=None, op0=ALU.mult, op1=ALU.bypass)
    k.barrier([(sL, t), (sG, tg), (k.CHG, tm), (k.CHV, tv)])


def emit_rt(k, call):
    S, DC = k.S, k.DC
    rT = k.sba("A_rT", [128, DC, S], BF16, 64)
    Wr = k.sba("A_wr", [128, DC, k.D], BF16, 96)
    peb = k.sba("A_peb", [128, 2, DC, 512], BF16, 112)
    sW = k.sem("r_wld")
    sPE = [k.sem("r_pe0"), k.sem("r_pe1")]
    smq = k.sem("r_mm")
    i = k.dma("gpsimd", Wr[:], k.inp["f_Wr"].rearrange("(c p) f -> p c f", p=128))
    tW = sW.inc(i, 16)
    ev = [0, 0]
    pe_users = [0, 0]
    tlast = 0
    for n in range(S // 512):
        par = n % 2
        k.w("gpsimd", smq, pe_users[par])
        i = k.dma("gpsimd", peb[:, par], k.inp["peT"].rearrange(
            "(c p) t -> p c t", p=128)[:, :, n * 512:(n + 1) * 512])
        tpe = sPE[par].inc(i, 16)
        k.w("tensor", sW, tW)
        k.w("tensor", sPE[par], tpe)
        for oc in range(DC):
            r = (n * DC + oc) % 2
            ps = k.S01[:, r * 512:(r + 1) * 512]
            k.w("tensor", k.CHV, ev[r])
            last = None
            for kc in range(DC):
                ii = k.mm(ps, Wr[:, kc, oc * 128:(oc + 1) * 128], peb[:, par, kc, :],
                          start=(kc == 0), stop=(kc == DC - 1))
                if ii is not None:
                    last = ii
            tg = smq.inc(last)
            k.w("vector", smq, tg)
            tlast = k.dve("tensor_copy", out=rT[:, oc, n * 512:(n + 1) * 512], in_=ps)
            ev[r] = tlast
        pe_users[par] = smq.n
    k.barrier([(k.CHV, tlast)])
    return rT


def emit_p0(k, call, s, x_in):
    S, D, DC, NT = k.S, k.D, k.DC, k.NT
    kT = k.sba("A_kT", [128, DC, S], BF16, 0)
    V = k.sba("A_v", [128, NT, D], BF16, 32)
    Wq = k.sba("A_wqkv", [128, DC, 3 * D], BF16, 96)
    xnT = k.sba("A_xnT", [128, 2, DC, 512], BF16, 144)
    qTb = k.sba("A_qTb", [128, 2, 512], BF16, 160)

    sX = [k.sem("p_x0"), k.sem("p_x1")]
    sW = k.sem("p_wld")
    smq = k.sem("p_mmq")
    smt = k.sem("p_mmt")
    sSX = [k.sem("p_sx0"), k.sem("p_sx1")]
    sSQ = [k.sem("p_sq0"), k.sem("p_sq1")]

    wsrc = k.inp["f_Wqkv"].rearrange("(c p) f -> p c f", p=128)
    i = k.dma("gpsimd", Wq[:, :, 0:2048], wsrc[:, :, 0:2048])
    sW.inc(i, 16)
    i = k.dma("gpsimd", Wq[:, :, 2048:3072], wsrc[:, :, 2048:3072])
    tW = sW.inc(i, 16)

    xnT_users = [0, 0]
    ps_ev = [0, 0, 0, 0]
    v_ev = [0, 0]
    qsp = [0, 0]
    xr1_rd = k.st.setdefault("p_xr1_rd", [0, 0])
    xr2_rd = k.st.setdefault("p_xr2_rd", [0, 0])
    for n in range(S // 512):
        par = n % 2
        for it in range(4):
            tau = n * 4 + it
            gtau = s * NT + tau
            slot = tau % 2
            k.w("sync", k.CHV, xr1_rd[slot])
            i = k.dma("sync", k.xr1[:, slot, :],
                      x_in[gtau * 128:(gtau + 1) * 128, :])
            tx = sX[slot].inc(i, 16)
            k.w("vector", sX[slot], tx)
            k.w("vector", sSX[slot], xr2_rd[slot])
            tn = emit_ln(k, k.xr1[:, slot, :], k.xr2[:, slot, :])
            xr1_rd[slot] = tn
            k.w("sync", k.CHV, tn)
            i = k.dma("sync", k.xn_s[gtau * 128:(gtau + 1) * 128, :],
                      k.xr2[:, slot, :])
            xr2_rd[slot] = sSX[slot].inc(i, 16)
            emit_ct(k, k.xr2[:, slot, :], xnT[:, par, :, it * 128:(it + 1) * 128],
                    slot, smt, war=[(smq, xnT_users[par])] if it == 0 else ())
        txn = k.st["ptfree"]
        k.w("tensor", k.CHV, txn)
        k.w("tensor", sW, tW)
        pi = 0
        for which, base in (("q", 0), ("k", D)):
            for oc in range(DC):
                r = pi % 4
                ps = [k.S01[:, 0:512], k.S01[:, 512:1024],
                      k.S23[:, 0:512], k.S23[:, 512:1024]][r]
                k.w("tensor", k.CHV, ps_ev[r])
                last = None
                for kc in range(DC):
                    ii = k.mm(ps, Wq[:, kc, base + oc * 128:base + oc * 128 + 128],
                              xnT[:, par, kc, :],
                              start=(kc == 0), stop=(kc == DC - 1))
                    if ii is not None:
                        last = ii
                tgm = smq.inc(last)
                k.w("vector", smq, tgm)
                if which == "k":
                    ps_ev[r] = k.dve("tensor_copy",
                                     out=kT[:, oc, n * 512:(n + 1) * 512], in_=ps)
                else:
                    qs = pi % 2
                    k.w("vector", sSQ[qs], qsp[qs])
                    tqe = k.dve("tensor_copy", out=qTb[:, qs, :], in_=ps)
                    ps_ev[r] = tqe
                    k.w("sync", k.CHV, tqe)
                    qdst = k.qT_s[s * NT + n * 4: s * NT + n * 4 + 4, :,
                                  oc * 128:(oc + 1) * 128].rearrange("t p f -> p t f")
                    i = k.dma("sync", qdst,
                              qTb[:, qs, :].rearrange("p (t f) -> p t f", t=4))
                    qsp[qs] = sSQ[qs].inc(i, 16)
                pi += 1
        for it in range(4):
            for half in range(2):
                ps = k.S45[:, half * 512:(half + 1) * 512]
                k.w("tensor", k.CHV, v_ev[half])
                last = None
                for kc in range(DC):
                    ii = k.mm(ps, xnT[:, par, kc, it * 128:(it + 1) * 128],
                              Wq[:, kc, 2 * D + half * 512:2 * D + (half + 1) * 512],
                              start=(kc == 0), stop=(kc == DC - 1))
                    if ii is not None:
                        last = ii
                tgm = smq.inc(last)
                k.w("vector", smq, tgm)
                v_ev[half] = k.dve(
                    "tensor_copy",
                    out=V[:, n * 4 + it, half * 512:(half + 1) * 512], in_=ps)
        xnT_users[par] = smq.n
    k.barrier([(k.CHV, k.CHV.n), (sSX[0], sSX[0].n), (sSX[1], sSX[1].n),
               (sSQ[0], sSQ[0].n), (sSQ[1], sSQ[1].n)])
    return kT, V


def emit_attn(k, call, s, kT, V, rT):
    S, D, DC, NT, NH = k.S, k.D, k.DC, k.NT, k.NH
    SEGW = k.SEGW
    qTt = k.sba("A_qTt", [128, 2, DC, 128], BF16, 96)
    qu = k.sba("A_qu", [128, 2, DC, 128], BF16, 100)
    qv = k.sba("A_qv", [128, 2, DC, 128], BF16, 104)
    Gsb = k.sba("A_G", [128, 2, S], BF16, 108)
    bd = k.sba("A_bd", [128, 2, S], BF16, 116)
    ssb = k.sba("A_s", [128, 2, S], F32, 124)
    Psb = k.sba("A_P", [128, 2, S], BF16, 140)
    PTs = k.sba("A_PTs", [128, NT, 128], BF16, 148)
    ctx = k.sba("A_ctx", [128, 2, D], F32, 152)

    sQ = [k.sem("a_q0"), k.sem("a_q1")]
    sG = k.sem("a_g")
    sSPL = [k.sem("a_spl0"), k.sem("a_spl1")]
    sBD = [k.sem("a_bd0"), k.sem("a_bd1")]
    sAC = k.sem("a_ac")
    sTR = k.sem("a_tr")
    sPV = k.sem("a_pv")
    sCS = [k.sem("a_cs0"), k.sem("a_cs1")]
    SC = [k.S01, k.S23, k.S45]
    stt = k.st

    for t in range(NT):
        KT = (t + 1) * 128
        nseg = (KT + SEGW - 1) // SEGW
        segs = [(si * SEGW, min(SEGW, KT - si * SEGW)) for si in range(nseg)]
        gtau = s * NT + t
        tp = t % 2
        k.w("sync", k.CHV, stt.get(("a_quv", tp), 0))
        i = k.dma("sync", qTt[:, tp],
                  k.qT_s[gtau].rearrange("p (c f) -> p c f", c=DC))
        tq = sQ[tp].inc(i, 16)
        k.w("vector", sQ[tp], tq)
        k.w("vector", sG, stt.get(("a_g_users", tp), 0))
        k.w("vector", sAC, stt.get(("a_ac_users", tp), 0))
        tquv = 0
        for c in range(DC):
            k.dve("tensor_scalar", out=qu[:, tp, c, :], in0=qTt[:, tp, c, :],
                  scalar1=k.ub_sb[:, c:c + 1], scalar2=0.125,
                  op0=ALU.add, op1=ALU.mult)
            tquv = k.dve("tensor_scalar", out=qv[:, tp, c, :], in0=qTt[:, tp, c, :],
                         scalar1=k.vb_sb[:, c:c + 1], scalar2=0.125,
                         op0=ALU.add, op1=ALU.mult)
        stt[("a_quv", tp)] = tquv

        units = [(h, si) for h in range(NH) for si in range(nseg)]
        NU = len(units)
        tG = [0] * NU; tGE = [0] * NU; tAC = [0] * NU; tSE = [0] * NU
        tBDh = [0] * NH; tMKh = [0] * NH; tSPLh = [0] * NH
        tEXh = [0] * NH; tNMh = [0] * NH; tRDh = [0] * NH
        tCEh = [0] * NH; tTRh = [0] * NH

        def emit_G(u):
            h, si = units[u]
            base, w_ = segs[si]
            hc, poff = h // 2, 64 * (h % 2)
            slot = SC[u % 3]
            k.w("tensor", k.CHV, tSE[u - 3] if u >= 3 else stt.get("a_se_prev", 0))
            k.w("tensor", k.CHV, tquv)
            last = None
            for cc in range(0, w_, 512):
                cw = min(512, w_ - cc)
                ii = k.mm(slot[:, cc:cc + cw], qv[poff:poff + 64, tp, hc, :],
                          rT[poff:poff + 64, hc,
                             S - KT + base + cc:S - KT + base + cc + cw],
                          start=True, stop=True)
                if ii is not None:
                    last = ii
            tG[u] = sG.inc(last)
            k.w("vector", sG, tG[u])
            k.w("vector", sSPL[h % 2],
                tSPLh[h - 2] if h >= 2 else stt.get(("a_spl_prev", h % 2), 0))
            tGE[u] = k.dve("tensor_copy", out=Gsb[:, h % 2, base:base + w_],
                           in_=slot[:, 0:w_])

        def emit_SPILL(h):
            KTl = KT
            qh = "gpsimd" if h % 2 == 0 else "sync"
            k.w(qh, k.CHV, tGE[(h + 1) * nseg - 1])
            k.w(qh, sBD[h % 2],
                tBDh[h - 2] if h >= 2 else stt.get(("a_bd_prev", h % 2), 0))
            gslot = k.G_s[h % 2]
            wr = gslot.copy()
            base_off = wr.offset
            wr.ap = bass_rust.VecI64Pair([[KTl, 128], [1, KTl]])
            wr.offset = base_off
            i = k.dma(qh, wr, Gsb[:, h % 2, 0:KTl])
            tSPLh[h] = sSPL[h % 2].inc(i, 16)
            k.w(qh, sSPL[h % 2], tSPLh[h])
            k.w(qh, sAC,
                tAC[(h - 1) * nseg - 1] if h >= 2 else stt.get("a_ac_prev", 0))
            rd = gslot.copy()
            rd.ap = bass_rust.VecI64Pair([[KTl - 1, 128], [1, KTl]])
            rd.offset = base_off + 127
            i = k.dma(qh, bd[:, h % 2, 0:KTl], rd)
            tBDh[h] = sBD[h % 2].inc(i, 16)

        def emit_AC(u):
            h, si = units[u]
            base, w_ = segs[si]
            hc, poff = h // 2, 64 * (h % 2)
            slot = SC[u % 3]
            k.w("tensor", k.CHV, tGE[u])
            k.w("tensor", sBD[h % 2], tBDh[h])
            last = None
            for cc in range(0, w_, 512):
                cw = min(512, w_ - cc)
                is_last = (si == nseg - 1) and (cc + cw == w_)
                ii = k.mm(slot[:, cc:cc + cw], qu[poff:poff + 64, tp, hc, :],
                          kT[poff:poff + 64, hc, base + cc:base + cc + cw],
                          start=True, stop=False)
                if ii is not None:
                    last = ii
                ii = k.mm(slot[:, cc:cc + cw], k.ident_sb[:],
                          bd[:, h % 2, base + cc:base + cc + cw],
                          start=False, stop=not is_last)
                if ii is not None:
                    last = ii
                if is_last:
                    ii = k.mm(slot[:, cc:cc + cw], k.ident_sb[:],
                              k.cmask_sb[:, 512 - cw:512],
                              start=False, stop=True)
                    if ii is not None:
                        last = ii
            tAC[u] = sAC.inc(last)
            k.w("vector", sAC, tAC[u])
            if h >= 2:
                k.w("vector", k.CHS, tEXh[h - 2])
            tSE[u] = k.dve("tensor_scalar", out=ssb[:, h % 2, base:base + w_],
                           in0=slot[:, 0:w_], scalar1=1.0, scalar2=None,
                           op0=ALU.mult, op1=ALU.max,
                           accum_out=k.mx2_sb[:, si:si + 1])
            if si == nseg - 1:
                tNMh[h] = k.dve("reduce_max", out=k.nmx_sb[:, h % 2:h % 2 + 1],
                                in_=k.mx2_sb[:, 0:nseg], axis=AXX, negate=True)
                k.w("scalar", k.CHV, tNMh[h])
                if h >= 2:
                    k.w("scalar", sTR, tTRh[h - 2])
                tEXh[h] = k.act(Psb[:, h % 2, 0:KT], ssb[:, h % 2, 0:KT], AF.Exp,
                                bias=k.nmx_sb[:, h % 2:h % 2 + 1],
                                accum_out=k.den_sb[:, h % 2:h % 2 + 1])

        def emit_TPV(h):
            ntile = t + 1
            ngrp = (ntile + 7) // 8
            cp = k.CTXP if h % 2 == 0 else k.CTXP2
            k.w("tensor", k.CHS, max(tEXh[h], tCEh[h - 2] if h >= 2 else
                                     stt.get(("a_ce_prev", h % 2), 0)))
            last_pv = None
            for g in range(ngrp):
                j0, j1 = g * 8, min(g * 8 + 8, ntile)
                k.w("tensor", k.CHV, k.st.get("ptfree", 0))
                last = None
                for j in range(j0, j1):
                    ii = k.tr(k.PT[:, (j - j0) * 128:(j - j0 + 1) * 128],
                              Psb[:, h % 2, j * 128:(j + 1) * 128])
                    if ii is not None:
                        last = ii
                tg = sTR.inc(last)
                k.w("vector", sTR, tg)
                te = k.dve("tensor_copy", out=PTs[:, j0:j1, :],
                           in_=k.PT[:, 0:(j1 - j0) * 128].rearrange(
                               "p (j f) -> p j f", f=128))
                k.st["ptfree"] = te
                k.w("tensor", k.CHV, te)
                for j in range(j0, j1):
                    ii = k.mm(cp, PTs[:, j, :], V[:, j, h * 64:(h + 1) * 64],
                              start=(j == 0), stop=(j == ntile - 1))
                    if ii is not None:
                        last_pv = ii
            tTRh[h] = sTR.n
            tPVh = sPV.inc(last_pv)
            k.w("vector", k.CHS, tEXh[h])
            tRDh[h] = k.dve("reciprocal", out=k.rden_sb[:, h % 2:h % 2 + 1],
                            in_=k.den_sb[:, h % 2:h % 2 + 1])
            k.w("scalar", sPV, tPVh)
            k.w("scalar", k.CHV, tRDh[h])
            k.w("scalar", sCS[t % 2], stt.get(("a_cs", t % 2), 0))
            tCEh[h] = k.act(ctx[:, t % 2, h * 64:(h + 1) * 64], cp, AF.Copy,
                            scale=k.rden_sb[:, h % 2:h % 2 + 1])

        emit_G(0)
        if NU > 1:
            emit_G(1)
        for uu in range(min(2, NU)):
            hh, ss = units[uu]
            if ss == nseg - 1:
                emit_SPILL(hh)
        for u in range(NU):
            if u + 2 < NU:
                emit_G(u + 2)
            emit_AC(u)
            if u + 2 < NU and units[u + 2][1] == nseg - 1:
                emit_SPILL(units[u + 2][0])
            h, si = units[u]
            if si == nseg - 1 and h >= 1:
                emit_TPV(h - 1)
        emit_TPV(NH - 1)
        k.w("sync", k.CHS, tCEh[NH - 1])
        i = k.dma("sync", k.ctx_s[gtau * 128:(gtau + 1) * 128, :], ctx[:, t % 2, :])
        stt[("a_cs", t % 2)] = sCS[t % 2].inc(i, 16)
        stt["a_se_prev"] = tSE[NU - 1]
        stt[("a_spl_prev", 0)] = tSPLh[NH - 2]
        stt[("a_spl_prev", 1)] = tSPLh[NH - 1]
        stt[("a_bd_prev", 0)] = tBDh[NH - 2]
        stt[("a_bd_prev", 1)] = tBDh[NH - 1]
        stt["a_ac_prev"] = tAC[NU - 1]
        stt[("a_ce_prev", 0)] = tCEh[NH - 2]
        stt[("a_ce_prev", 1)] = tCEh[NH - 1]
        stt[("a_g_users", t % 2)] = tG[NU - 1]
        stt[("a_ac_users", t % 2)] = tAC[NU - 1]
    k.barrier([(sCS[0], sCS[0].n), (sCS[1], sCS[1].n)])


def emit_B(k, call):
    D, DC, NTT = k.D, k.DC, k.NTT
    Wo = k.sba("B_wo", [128, DC, D], BF16, 0)
    lng = k.sba("B_lng", [128, 2, D], F32, 16)
    ctr = k.sba("B_ctr", [128, 2, D], F32, 24)
    cT = k.sba("B_cT", [128, DC, 128], BF16, 32)
    h1r = k.sba("B_h1r", [128, 2, D], F32, 36)
    h1Tb = k.sba("B_h1T", [128, DC, 128], BF16, 44)

    sC = [k.sem("b_c0"), k.sem("b_c1")]
    sX = [k.sem("b_x0"), k.sem("b_x1")]
    sLg = k.sem("b_lng"); sW = k.sem("b_wld")
    smq = k.sem("b_mm"); smt = k.sem("b_mt")
    sS1 = [k.sem("b_s10"), k.sem("b_s11")]
    sST = k.sem("b_sT")

    i = k.dma("gpsimd", Wo[:], k.inp["f_Wo"].rearrange("(c p) f -> p c f", p=128))
    tW = sW.inc(i, 16)
    i = k.dma("sync", lng[:, 0, :], k.inp["f_lng"][:, 0, :])
    sLg.inc(i, 16)
    i = k.dma("sync", lng[:, 1, :], k.inp["f_lnb"][:, 0, :])
    tlg = sLg.inc(i, 16)

    stt = k.st
    for tau in range(NTT):
        sl = tau % 2
        k.w("sync", k.CHV, stt.get(("b_ctr", sl), 0))
        i = k.dma("sync", ctr[:, sl, :], k.ctx_s[tau * 128:(tau + 1) * 128, :])
        tc = sC[sl].inc(i, 16)
        k.w("sync", k.CHV, stt.get(("b_xr1", sl), 0))
        i = k.dma("sync", k.xr1[:, sl, :], k.xn_s[tau * 128:(tau + 1) * 128, :])
        tx = sX[sl].inc(i, 16)
        k.w("vector", sC[sl], tc)
        tct = emit_ct(k, ctr[:, sl, :], cT[:], sl, smt,
                      war=[(smq, stt.get("b_ct_users", 0))])
        stt[("b_ctr", sl)] = tct
        k.w("tensor", k.CHV, tct)
        k.w("tensor", sW, tW)
        for half in range(2):
            ps = k.S01[:, half * 512:(half + 1) * 512]
            k.w("tensor", k.CHV, stt.get(("b_ps", half), 0))
            last = None
            for kc in range(DC):
                ii = k.mm(ps, cT[:, kc, :], Wo[:, kc, half * 512:(half + 1) * 512],
                          start=(kc == 0), stop=(kc == DC - 1))
                if ii is not None:
                    last = ii
            tgm = smq.inc(last)
            k.w("vector", smq, tgm)
            k.w("vector", sX[sl], tx)
            tu = k.dve("tensor_tensor",
                       out=k.xr2[:, sl, half * 512:(half + 1) * 512], in0=ps,
                       in1=k.xr1[:, sl, half * 512:(half + 1) * 512], op=ALU.add)
            stt[("b_ps", half)] = tu
            stt[("b_xr1", sl)] = tu
        stt["b_ct_users"] = smq.n
        k.w("vector", sLg, tlg)
        k.w("vector", sS1[sl], stt.get(("b_h1r", sl), 0))
        tn = emit_ln(k, k.xr2[:, sl, :], k.lnt[:],
                     lng=lng[:, 0, :], lnb=lng[:, 1, :], affout=h1r[:, sl, :])
        k.w("sync", k.CHV, tn)
        i = k.dma("sync", k.h1_s[tau * 128:(tau + 1) * 128, :], h1r[:, sl, :])
        tsp = sS1[sl].inc(i, 16)
        stt[("b_h1r", sl)] = tsp
        the = emit_ct(k, h1r[:, sl, :], h1Tb[:], sl, smt,
                      war=[(sST, stt.get("b_h1T_sp", 0))])
        k.w("sync", k.CHV, the)
        i = k.dma("sync",
                  k.h1T_s.rearrange("c p t -> p c t")[:, :, tau * 128:(tau + 1) * 128],
                  h1Tb[:])
        stt["b_h1T_sp"] = sST.inc(i, 16)
    k.barrier([(sS1[0], sS1[0].n), (sS1[1], sS1[1].n), (sST, sST.n)])


def emit_B2(k, call):
    D, DC, FC, NE, TOK = k.D, k.DC, k.FC, k.NE, k.TOK
    W1 = k.sba("F_w1", [128, DC, k.FF], BF16, 0)
    W2 = k.sba("F_w2", [128, FC, D], BF16, 64)
    hidT = k.sba("F_hid", [128, FC, 256], BF16, 128)
    h1T = k.sba("F_h1T", [128, 2, DC, 256], BF16, 144)
    lng = k.sba("F_lng", [128, 2, D], F32, 152)
    h2Tb = k.sba("F_h2T", [128, DC, 128], BF16, 160)

    sH = [k.sem("f_h0"), k.sem("f_h1")]
    sX = [k.sem("f_x0"), k.sem("f_x1")]
    sW1s = k.sem("f_w1"); sW2s = k.sem("f_w2"); sLg = k.sem("f_lng")
    smq = k.sem("f_mm"); sm2 = k.sem("f_mm2")
    smt = k.sem("f_mt"); smg = k.sem("f_gmm")
    sS2 = [k.sem("f_s20"), k.sem("f_s21")]
    sST = k.sem("f_sT")

    w1src = k.inp["f_Wff1"].rearrange("(c p) f -> p c f", p=128)
    i = k.dma("gpsimd", W1[:, :, 0:2048], w1src[:, :, 0:2048])
    sW1s.inc(i, 16)
    i = k.dma("gpsimd", W1[:, :, 2048:4096], w1src[:, :, 2048:4096])
    tW1 = sW1s.inc(i, 16)
    i = k.dma("gpsimd", W2[:], k.inp["f_Wff2"].rearrange("(c p) f -> p c f", p=128))
    tW2 = sW2s.inc(i, 16)
    i = k.dma("sync", lng[:, 0, :], k.inp["f_lng"][:, 1, :])
    sLg.inc(i, 16)
    i = k.dma("sync", lng[:, 1, :], k.inp["f_lnb"][:, 1, :])
    tlg = sLg.inc(i, 16)

    stt = k.st
    stt[("f_ps", 0)] = 0
    stt[("f_ps", 1)] = 0
    for st_ in range(TOK // 256):
        hpar = st_ % 2
        k.w("sync", smq, stt.get(("f_h1T_users", hpar), 0))
        i = k.dma("sync", h1T[:, hpar],
                  k.h1T_s.rearrange("c p t -> p c t")[:, :, st_ * 256:(st_ + 1) * 256])
        th = sH[hpar].inc(i, 16)
        k.w("tensor", sH[hpar], th)
        k.w("tensor", sW1s, tW1)
        for fc in range(FC):
            r = fc % 2
            ps = k.S01[:, r * 512:r * 512 + 256]
            k.w("tensor", k.CHS, stt[("f_ps", r)])
            last = None
            for kc in range(DC):
                ii = k.mm(ps, W1[:, kc, fc * 128:(fc + 1) * 128], h1T[:, hpar, kc, :],
                          start=(kc == 0), stop=(kc == DC - 1))
                if ii is not None:
                    last = ii
            tg = smq.inc(last)
            k.w("scalar", smq, tg)
            k.w("scalar", sm2, stt.get("f_hid_users", 0))
            stt[("f_ps", r)] = k.act(hidT[:, fc, :], ps, AF.Relu,
                                     bias=k.bff1_sb[:, fc:fc + 1])
        stt[("f_h1T_users", hpar)] = smq.n
        k.w("tensor", k.CHS, max(stt[("f_ps", 0)], stt[("f_ps", 1)]))
        k.w("tensor", sW2s, tW2)
        for it in range(2):
            tau = st_ * 2 + it
            sl = tau % 2
            ps2 = k.S23 if it % 2 == 0 else k.S45
            k.w("sync", k.CHV, stt.get(("f_xr1", sl), 0))
            k.w("sync", sS2[sl], stt.get(("f_xr1s", sl), 0))
            i = k.dma("sync", k.xr1[:, sl, :], k.h1_s[tau * 128:(tau + 1) * 128, :])
            tx = sX[sl].inc(i, 16)
            k.w("tensor", k.CHV, stt.get(("f_ps2", it % 2), 0))
            last = None
            for half in range(2):
                for sl2 in range(FC):
                    ii = k.mm(ps2[:, half * 512:(half + 1) * 512],
                              hidT[:, sl2, it * 128:(it + 1) * 128],
                              W2[:, sl2, half * 512:(half + 1) * 512],
                              start=(sl2 == 0), stop=False)
                    if ii is not None:
                        last = ii
                ii = k.mm(ps2[:, half * 512:(half + 1) * 512], k.ones_sb[:],
                          k.bff2_sb[:, half * 512:(half + 1) * 512],
                          start=False, stop=True)
                if ii is not None:
                    last = ii
            tgm = sm2.inc(last)
            k.w("vector", sm2, tgm)
            k.w("vector", sX[sl], tx)
            k.w("vector", sS2[sl], stt.get(("f_xr2", sl), 0))
            tu2 = k.dve("tensor_tensor", out=k.xr2[:, sl, :], in0=ps2,
                        in1=k.xr1[:, sl, :], op=ALU.add)
            stt[("f_ps2", it % 2)] = tu2
            stt[("f_xr1", sl)] = tu2
            k.w("vector", sLg, tlg)
            tn = emit_ln(k, k.xr2[:, sl, :], k.lnt[:],
                         lng=lng[:, 0, :], lnb=lng[:, 1, :],
                         affout=k.xr1[:, sl, :])
            stt[("f_xr1", sl)] = tn
            k.w("sync", k.CHV, tn)
            i = k.dma("sync", k.h2_s[tau * 128:(tau + 1) * 128, :], k.xr1[:, sl, :])
            th2 = sS2[sl].inc(i, 16)
            stt[("f_xr2", sl)] = th2
            stt[("f_xr1s", sl)] = th2
            the = emit_ct(k, k.xr1[:, sl, :], h2Tb[:], sl, smt,
                          war=[(sST, stt.get("f_h2T_sp", 0)),
                               (smg, stt.get("f_h2T_users", 0))])
            stt[("f_xr1", sl)] = max(stt[("f_xr1", sl)], the)
            k.w("sync", k.CHV, the)
            i = k.dma("sync",
                      k.h2T_s.rearrange("c p t -> p c t")[:, :, tau * 128:(tau + 1) * 128],
                      h2Tb[:])
            stt["f_h2T_sp"] = sST.inc(i, 16)
            stt["f_hid_users"] = sm2.n
            k.w("tensor", k.CHV, max(the, stt.get("f_gate_done", 0)))
            last = None
            for kc in range(DC):
                ii = k.mm(k.GATEP, h2Tb[:, kc, :], k.wg_sb[:, kc, :],
                          start=(kc == 0), stop=(kc == DC - 1))
                if ii is not None:
                    last = ii
            tgm = smg.inc(last)
            stt["f_h2T_users"] = tgm
            k.w("vector", smg, tgm)
            nl = k.st_sb[:, 8:9]; dsum = k.st_sb[:, 9:10]; gate = k.st_sb[:, 10:11]
            eq8 = k.scr8_sb[:, 0:NE]; cum8 = k.scr8_sb[:, NE:2 * NE]
            c18 = k.scr8_sb[:, 2 * NE:3 * NE]; pr8 = k.scr8_sb[:, 3 * NE:4 * NE]
            tv1 = k.dve("reduce_max", out=nl, in_=k.GATEP, axis=AXX, negate=True)
            k.w("scalar", k.CHV, tv1)
            ta1 = k.act(pr8, k.GATEP, AF.Exp, bias=nl, accum_out=dsum)
            k.w("vector", k.CHS, ta1)
            k.dve("reciprocal", out=gate, in_=dsum)
            k.dve("tensor_scalar", out=eq8, in0=k.GATEP, scalar1=nl,
                  scalar2=0.0, op0=ALU.add, op1=ALU.is_equal)
            k.dve("tensor_tensor_scan", out=cum8, data0=k.ones8_sb[:],
                  data1=eq8, initial=0.0, op0=ALU.mult, op1=ALU.add)
            k.dve("tensor_scalar", out=c18, in0=cum8, scalar1=1.0,
                  scalar2=None, op0=ALU.is_equal, op1=ALU.bypass)
            k.dve("tensor_tensor", out=k.mask8_sb[:, tau, :], in0=eq8,
                  in1=c18, op=ALU.mult)
            k.dve("tensor_scalar", out=k.mg8_sb[:, tau, :],
                  in0=k.mask8_sb[:, tau, :], scalar1=gate, scalar2=None,
                  op0=ALU.mult, op1=ALU.bypass)
            k.dve("tensor_scalar", out=pr8, in0=k.mask8_sb[:, tau, :],
                  scalar1=-BIGIDX, scalar2=k.rowidx_sb[:, tau:tau + 1],
                  op0=ALU.mult, op1=ALU.add)
            stt["f_gate_done"] = k.dve("tensor_copy", out=k.idx_sb[:, tau, :],
                                       in_=pr8)
    k.barrier([(sS2[0], sS2[0].n), (sS2[1], sS2[1].n), (sST, sST.n),
               (k.CHV, stt["f_gate_done"])])


def emit_C(k, call):
    D, DC, FC, NE, TOK = k.D, k.DC, k.FC, k.NE, k.TOK
    We1 = k.sba("C_w1", [128, DC, k.FF], BF16, 0)
    We2 = k.sba("C_w2", [128, FC, D], BF16, 64)
    hidT = k.sba("C_hid", [128, FC, 512], BF16, 128)
    if "C_h2T" not in k._sb:
        k._sb["C_h2T"] = k.nc.alloc_sbuf_tensor_at(
            "C_h2T", [128, DC, 512], BF16, offset=k.off_xr1).ap()
    h2T = k._sb["C_h2T"]
    eob = k.xr2

    sL = k.sem("c2_ld"); sW1 = k.sem("c2_w1"); sW2 = k.sem("c2_w2")
    sB = k.sem("c2_b")
    smq = k.sem("c2_mm"); sm2 = k.sem("c2_mm2")
    sSC = [k.sem("c2_sc0"), k.sem("c2_sc1")]
    stt = k.st
    stt[("c_ps", 0)] = 0
    stt[("c_ps", 1)] = 0

    def ldw1(e):
        src = k.inp["f_We1"][e].rearrange("(c p) f -> p c f", p=128)
        i = k.dma("gpsimd", We1[:, :, 0:2048], src[:, :, 0:2048])
        sW1.inc(i, 16)
        i = k.dma("gpsimd", We1[:, :, 2048:4096], src[:, :, 2048:4096])
        return sW1.inc(i, 16)

    def ldw2(e):
        i = k.dma("gpsimd", We2[:],
                  k.inp["f_We2"][e].rearrange("(c p) f -> p c f", p=128))
        return sW2.inc(i, 16)

    def ldb(e):
        i = k.dma("gpsimd", k.be1_sb[:, e % 2, :], k.inp["f_be1"][e])
        sB.inc(i, 16)
        i = k.dma("gpsimd", k.be2_sb[:, e % 2, :], k.inp["f_be2"][e])
        return sB.inc(i, 16)

    tw1 = ldw1(0)
    tb = ldb(0)
    for e in range(NE):
        k.w("gpsimd", sm2, stt.get("c_eout_done", 0))
        tw2 = ldw2(e)
        for st_ in range(TOK // 512):
            k.w("sync", smq, stt.get("c_h2T_users", 0))
            i = k.dma("sync", h2T[:],
                      k.h2T_s.rearrange("c p t -> p c t")[:, :, st_ * 512:(st_ + 1) * 512])
            th = sL.inc(i, 16)
            k.w("tensor", sL, th)
            k.w("tensor", sW1, tw1)
            k.w("tensor", sB, tb)
            for fc in range(FC):
                r = fc % 2
                ps = k.S01[:, r * 512:(r + 1) * 512]
                k.w("tensor", k.CHS, stt[("c_ps", r)])
                last = None
                for kc in range(DC):
                    ii = k.mm(ps, We1[:, kc, fc * 128:(fc + 1) * 128], h2T[:, kc, :],
                              start=(kc == 0), stop=(kc == DC - 1))
                    if ii is not None:
                        last = ii
                tg = smq.inc(last)
                k.w("scalar", smq, tg)
                k.w("scalar", sm2, stt.get("c_hid_users", 0))
                stt[("c_ps", r)] = k.act(hidT[:, fc, :], ps, AF.Relu,
                                         bias=k.be1_sb[:, e % 2, fc:fc + 1])
            stt["c_h2T_users"] = smq.n
            k.w("tensor", k.CHS, max(stt[("c_ps", 0)], stt[("c_ps", 1)]))
            k.w("tensor", sW2, tw2)
            for it in range(4):
                tau = st_ * 4 + it
                sl = tau % 2
                ps2 = k.S23 if it % 2 == 0 else k.S45
                k.w("tensor", k.CHV, stt.get(("c_ps2", it % 2), 0))
                last = None
                for half in range(2):
                    for sl2 in range(FC):
                        ii = k.mm(ps2[:, half * 512:(half + 1) * 512],
                                  hidT[:, sl2, it * 128:(it + 1) * 128],
                                  We2[:, sl2, half * 512:(half + 1) * 512],
                                  start=(sl2 == 0), stop=False)
                        if ii is not None:
                            last = ii
                    ii = k.mm(ps2[:, half * 512:(half + 1) * 512], k.ones_sb[:],
                              k.be2_sb[:, e % 2, half * 512:(half + 1) * 512],
                              start=False, stop=True)
                    if ii is not None:
                        last = ii
                tgm = sm2.inc(last)
                k.w("vector", sm2, tgm)
                k.w("vector", sSC[sl], stt.get(("c_eo", sl), 0))
                teo = k.dve("tensor_scalar", out=eob[:, sl, :], in0=ps2,
                            scalar1=k.mg8_sb[:, tau, e:e + 1], scalar2=None,
                            op0=ALU.mult, op1=ALU.bypass)
                stt[("c_ps2", it % 2)] = teo
                k.w("gpsimd", k.CHV, teo)
                i = None
                if k.on("gpsimd") and _MOE_DBG:
                    di = k.eng.indirect_dma_start(
                        out=k.moe_dbg.rearrange("e t d -> (e t) d"),
                        out_offset=IndirectOffsetOnAxis(
                            ap=k.idx_sb[:, tau, e:e + 1], axis=0),
                        in_=eob[:, sl, :], in_offset=None,
                        element_offset=e * k.TOK * k.D,
                        bounds_check=k.st["breg"], oob_is_err=False)
                else:
                    di = None
                if _MOE_DBG:
                    sSC[sl].inc(di, 16)
                if k.on("gpsimd"):
                    i = k.eng.indirect_dma_start(
                        out=k.moe_s,
                        out_offset=IndirectOffsetOnAxis(
                            ap=k.idx_sb[:, tau, e:e + 1], axis=0),
                        in_=eob[:, sl, :], in_offset=None,
                        bounds_check=k.st["breg"], oob_is_err=False)
                stt[("c_eo", sl)] = sSC[sl].inc(i, 16)
            stt["c_eout_done"] = sm2.n
        if e + 1 < NE:
            k.w("gpsimd", smq, stt["c_h2T_users"])
            tw1 = ldw1(e + 1)
            k.w("gpsimd", k.CHS, max(stt[("c_ps", 0)], stt[("c_ps", 1)]))
            tb = ldb(e + 1)
    k.barrier([(sSC[0], sSC[0].n), (sSC[1], sSC[1].n)])


def emit_D(k, call, x_in, x_out):
    D, DC, NTT = k.D, k.DC, k.NTT
    Wout = k.sba("D_wout", [128, DC, D], BF16, 0)
    lng = k.sba("D_lng", [128, 2, D], F32, 16)
    x3r = k.sba("D_x3", [128, 2, D], F32, 24)
    u3r = k.sba("D_u3", [128, 2, D], F32, 32)
    h3r = k.sba("D_h3", [128, 2, D], F32, 40)
    h3Tb = k.sba("D_h3T", [128, DC, 128], BF16, 48)

    s1 = [k.sem("d_a0"), k.sem("d_a1")]
    s2 = [k.sem("d_b0"), k.sem("d_b1")]
    s3 = [k.sem("d_c0"), k.sem("d_c1")]
    sLg = k.sem("d_lng"); sW = k.sem("d_wld")
    smq = k.sem("d_mm"); smt = k.sem("d_mt")
    sSP = [k.sem("d_o0"), k.sem("d_o1")]

    i = k.dma("gpsimd", Wout[:], k.inp["f_Wout"].rearrange("(c p) f -> p c f", p=128))
    tW = sW.inc(i, 16)
    i = k.dma("sync", lng[:, 0, :], k.inp["f_lng"][:, 2, :])
    sLg.inc(i, 16)
    i = k.dma("sync", lng[:, 1, :], k.inp["f_lnb"][:, 2, :])
    tlg = sLg.inc(i, 16)

    stt = k.st
    for tau in range(NTT):
        sl = tau % 2
        k.w("sync", k.CHV, stt.get(("d_xr", sl), 0))
        i = k.dma("sync", k.xr1[:, sl, :], k.h2_s[tau * 128:(tau + 1) * 128, :])
        t1 = s1[sl].inc(i, 16)
        i = k.dma("sync", k.xr2[:, sl, :], k.moe_s[tau * 128:(tau + 1) * 128, :])
        t2 = s2[sl].inc(i, 16)
        k.w("sync", k.CHV, stt.get(("d_x3", sl), 0))
        i = k.dma("sync", x3r[:, sl, :], x_in[tau * 128:(tau + 1) * 128, :])
        t3 = s3[sl].inc(i, 16)
        k.w("vector", s1[sl], t1)
        k.w("vector", s2[sl], t2)
        k.w("vector", sSP[sl], stt.get(("d_u3", sl), 0))
        tu3 = k.dve("tensor_tensor", out=u3r[:, sl, :], in0=k.xr1[:, sl, :],
                    in1=k.xr2[:, sl, :], op=ALU.add)
        stt[("d_xr", sl)] = tu3
        k.w("vector", sLg, tlg)
        emit_ln(k, u3r[:, sl, :], k.lnt[:],
                lng=lng[:, 0, :], lnb=lng[:, 1, :], affout=h3r[:, sl, :])
        the = emit_ct(k, h3r[:, sl, :], h3Tb[:], sl, smt,
                      war=[(smq, stt.get("d_h3T_users", 0))])
        k.w("tensor", k.CHV, the)
        k.w("tensor", sW, tW)
        tf = 0
        for half in range(2):
            ps = k.S01[:, half * 512:(half + 1) * 512]
            k.w("tensor", k.CHV, stt.get(("d_ps", half), 0))
            last = None
            for kc in range(DC):
                ii = k.mm(ps, h3Tb[:, kc, :],
                          Wout[:, kc, half * 512:(half + 1) * 512],
                          start=(kc == 0), stop=(kc == DC - 1))
                if ii is not None:
                    last = ii
            tgm = smq.inc(last)
            k.w("vector", smq, tgm)
            k.w("vector", s3[sl], t3)
            tf = k.dve("scalar_tensor_tensor",
                       out=u3r[:, sl, half * 512:(half + 1) * 512], in0=ps,
                       scalar=0.5, in1=x3r[:, sl, half * 512:(half + 1) * 512],
                       op0=ALU.mult, op1=ALU.add)
            stt[("d_ps", half)] = tf
            stt[("d_x3", sl)] = tf
        stt["d_h3T_users"] = smq.n
        k.w("sync", k.CHV, tf)
        i = k.dma("sync", x_out[tau * 128:(tau + 1) * 128, :], u3r[:, sl, :])
        stt[("d_u3", sl)] = sSP[sl].inc(i, 16)
    k.barrier([(sSP[0], sSP[0].n), (sSP[1], sSP[1].n)])


def emit_ffn(k, call, x_in, x_out):
    rT = emit_rt(k, call)
    for s in range(k.SEQ):
        kT, V = emit_p0(k, call, s, x_in)
        emit_attn(k, call, s, kT, V, rT)
    emit_B(k, call)
    emit_B2(k, call)
    emit_C(k, call)
    emit_D(k, call, x_in, x_out)


def emit_mixer(k, x_in, x_out):
    S, D, DC, NT = k.S, k.D, k.DC, k.NT
    Wz = k.sba("M_wz", [128, DC, D], BF16, 0)
    Wh = k.sba("M_wh", [128, DC, D], BF16, 16)
    xnT = k.sba("M_xnT", [128, 2, DC, 512], BF16, 32)
    asb = k.sba("M_a", [128, 2, 512], F32, 48)
    zsb = k.sba("M_z", [128, 2, 512], F32, 52)
    hsb = k.sba("M_h", [128, DC, S], F32, 56)
    hbf = k.sba("M_hbf", [128, DC, S], BF16, 120)
    Wg = k.sba("M_wg", [128, DC, D], BF16, 0)
    gsb = k.sba("M_g", [128, 2, 512], F32, 16)
    mox = k.sba("M_mo", [128, 4, D], F32, 24)

    sX = [k.sem("m_x0"), k.sem("m_x1")]
    sWzh = k.sem("m_wzh"); sWg = k.sem("m_wgl")
    smq = k.sem("m_mm"); smt = k.sem("m_mt")
    sSP = [k.sem("m_o0"), k.sem("m_o1"), k.sem("m_o2"), k.sem("m_o3")]
    stt = k.st

    for s in range(k.SEQ):
        k.w("gpsimd", smq, stt.get("m_wg_users", 0))
        k.w("gpsimd", smt, stt.get("m_gate_tr", 0))
        k.w("gpsimd", k.CHV, stt.get("m_gate_dve", 0))
        k.w("gpsimd", k.CHS, stt.get("m_gate_act", 0))
        for it_ in range(4):
            k.w("gpsimd", sSP[it_], stt.get(("m_mo", it_), 0))
        i = k.dma("gpsimd", Wz[:], k.inp["mx_Wz"].rearrange("(c p) f -> p c f", p=128))
        sWzh.inc(i, 16)
        i = k.dma("gpsimd", Wh[:], k.inp["mx_Wh"].rearrange("(c p) f -> p c f", p=128))
        tWzh = sWzh.inc(i, 16)
        xnT_users = [0, 0]
        for n in range(S // 512):
            par = n % 2
            for it in range(4):
                tau = n * 4 + it
                gtau = s * NT + tau
                slot = tau % 2
                k.w("sync", k.CHV, stt.get(("m_xr1", slot), 0))
                i = k.dma("sync", k.xr1[:, slot, :],
                          x_in[gtau * 128:(gtau + 1) * 128, :])
                tx = sX[slot].inc(i, 16)
                k.w("vector", sX[slot], tx)
                tn = emit_ln(k, k.xr1[:, slot, :], k.xr2[:, slot, :])
                stt[("m_xr1", slot)] = tn
                war_ = [(smq, xnT_users[par])] if it == 0 else []
                if n == 0 and it == 0:
                    war_ += [(sSP[2], stt.get(("m_mo", 2), 0)),
                             (sSP[3], stt.get(("m_mo", 3), 0))]
                emit_ct(k, k.xr2[:, slot, :],
                        xnT[:, par, :, it * 128:(it + 1) * 128], slot, smt,
                        war=war_)
            txn = k.st["ptfree"]
            k.w("tensor", k.CHV, txn)
            k.w("tensor", sWzh, tWzh)
            for oc in range(DC):
                r = oc % 2
                psz = k.S01[:, r * 512:(r + 1) * 512]
                psh = k.S23[:, r * 512:(r + 1) * 512]
                k.w("tensor", k.CHV, stt.get(("m_ps", r), 0))
                lastz = None
                for kc in range(DC):
                    ii = k.mm(psz, Wz[:, kc, oc * 128:(oc + 1) * 128],
                              xnT[:, par, kc, :], start=(kc == 0), stop=(kc == DC - 1))
                    if ii is not None:
                        lastz = ii
                tz = smq.inc(lastz)
                lasth = None
                for kc in range(DC):
                    ii = k.mm(psh, Wh[:, kc, oc * 128:(oc + 1) * 128],
                              xnT[:, par, kc, :], start=(kc == 0), stop=(kc == DC - 1))
                    if ii is not None:
                        lasth = ii
                th_ = smq.inc(lasth)
                k.w("scalar", smq, tz)
                k.w("scalar", k.CHV, stt.get(("m_ab", r), 0))
                k.act(asb[:, r, :], psz, AF.Sigmoid, scale=-1.0,
                      bias=k.nbz_sb[:, oc:oc + 1])
                ta2 = k.act(zsb[:, r, :], psz, AF.Sigmoid, bias=k.bz_sb[:, oc:oc + 1])
                k.w("vector", k.CHS, ta2)
                k.w("vector", smq, th_)
                k.dve("scalar_tensor_tensor", out=zsb[:, r, :], in0=psh,
                      scalar=k.bh_sb[:, oc:oc + 1], in1=zsb[:, r, :],
                      op0=ALU.add, op1=ALU.mult)
                init = 0.0 if n == 0 else hsb[:, oc, n * 512 - 1:n * 512]
                tsc = k.dve("tensor_tensor_scan",
                            out=hsb[:, oc, n * 512:(n + 1) * 512],
                            data0=asb[:, r, :], data1=zsb[:, r, :],
                            initial=init, op0=ALU.mult, op1=ALU.add)
                stt[("m_ps", r)] = tsc
                stt[("m_ab", r)] = tsc
            xnT_users[par] = smq.n
        tcast = 0
        for oc in range(DC):
            tcast = k.dve("tensor_copy", out=hbf[:, oc, :], in_=hsb[:, oc, :])
        k.w("gpsimd", smq, smq.n)
        i = k.dma("gpsimd", Wg[:], k.inp["mx_Wg"].rearrange("(c p) f -> p c f", p=128))
        tWg = sWg.inc(i, 16)
        for n in range(S // 512):
            for oc in range(DC):
                r = oc % 2
                psz = k.S45[:, r * 512:(r + 1) * 512]
                k.w("tensor", k.CHV, max(tcast, stt.get(("m_g", r), 0)))
                k.w("tensor", sWg, tWg)
                last = None
                for kc in range(DC):
                    ii = k.mm(psz, Wg[:, kc, oc * 128:(oc + 1) * 128],
                              hbf[:, kc, n * 512:(n + 1) * 512],
                              start=(kc == 0), stop=(kc == DC - 1))
                    if ii is not None:
                        last = ii
                tg = smq.inc(last)
                stt["m_wg_users"] = tg
                k.w("scalar", smq, tg)
                k.w("scalar", k.CHV, stt.get(("m_g", r), 0))
                tga = k.act(gsb[:, r, :], psz, AF.Sigmoid, bias=k.bg_sb[:, oc:oc + 1])
                k.w("vector", k.CHS, tga)
                tgh = k.dve("tensor_tensor", out=gsb[:, r, :], in0=gsb[:, r, :],
                            in1=hsb[:, oc, n * 512:(n + 1) * 512], op=ALU.mult)
                k.w("tensor", k.CHV, tgh)
                te = 0
                for it in range(4):
                    tau = n * 4 + it
                    k.w("tensor", k.CHV, k.st.get("ptfree", 0))
                    ii = k.tr(k.PTF, gsb[:, r, it * 128:(it + 1) * 128], fp32=True)
                    tt = smt.inc(ii)
                    k.w("vector", smt, tt)
                    if oc == 0:
                        k.w("vector", sSP[it], stt.get(("m_mo", it), 0))
                    te = k.dve("tensor_copy",
                               out=mox[:, it, oc * 128:(oc + 1) * 128], in_=k.PTF)
                    k.st["ptfree"] = te
                stt[("m_g", r)] = te
            for it in range(4):
                tau = n * 4 + it
                gtau = s * NT + tau
                slot = tau % 2
                k.w("sync", k.CHV, stt.get(("m_xr1", slot), 0))
                i = k.dma("sync", k.xr1[:, slot, :],
                          x_in[gtau * 128:(gtau + 1) * 128, :])
                tx = sX[slot].inc(i, 16)
                k.w("vector", sX[slot], tx)
                tmo = k.dve("tensor_tensor", out=mox[:, it, :], in0=mox[:, it, :],
                            in1=k.xr1[:, slot, :], op=ALU.add)
                stt[("m_xr1", slot)] = tmo
                k.w("sync", k.CHV, tmo)
                i = k.dma("sync", x_out[gtau * 128:(gtau + 1) * 128, :],
                          mox[:, it, :])
                stt[("m_mo", it)] = sSP[it].inc(i, 16)
        stt["m_gate_dve"] = k.CHV.n
        stt["m_gate_act"] = k.CHS.n
        stt["m_gate_tr"] = smt.n
    k.barrier([(sSP[0], sSP[0].n), (sSP[1], sSP[1].n),
               (sSP[2], sSP[2].n), (sSP[3], sSP[3].n)])


_NSTAGE = 3


def emit_program(k):
    emit_stage0(k)
    emit_ffn(k, 0, k.inp["x"], k.xA)
    if _NSTAGE >= 2:
        emit_mixer(k, k.xA, k.xB)
    if _NSTAGE >= 3:
        emit_ffn(k, 1, k.xB, k.y)


def _pe_table(S, D):
    pos = np.arange(S - 1, -1, -1, dtype=np.float64)
    inv_freq = 1.0 / 10000.0 ** (np.arange(0, D, 2, dtype=np.float64) / D)
    sinusoid = pos[:, None] * inv_freq[None, :]
    pe = np.concatenate([np.sin(sinusoid), np.cos(sinusoid)], axis=-1)
    return np.ascontiguousarray(pe.T.astype(np.float32))


def prep_inputs(cfg, inputs):
    B, S, D = cfg["B"], cfg["S"], cfg["D"]
    NE = cfg["NE"]
    SEQ = B // NCORES
    TOK = SEQ * S
    NTT = TOK // 128
    f32 = np.float32

    def asnp(v, dt=f32):
        return np.ascontiguousarray(np.asarray(v), dtype=dt)

    def vec_pc(v):
        return np.ascontiguousarray(v.reshape(-1, 128).T)

    Wz = asnp(inputs["L0_attn_Wz"])
    Wzf = np.ascontiguousarray(np.transpose(Wz, (1, 0, 2)).reshape(D, D))
    Wh = asnp(inputs["L0_attn_Wh"])
    Whf = np.ascontiguousarray(np.transpose(Wh, (1, 0, 2)).reshape(D, D))
    lng = np.ascontiguousarray(np.stack(
        [np.broadcast_to(asnp(inputs[f"L0_ffn_ln{i}_g"]), (128, D))
         for i in (1, 2, 3)], axis=1))
    lnb = np.ascontiguousarray(np.stack(
        [np.broadcast_to(asnp(inputs[f"L0_ffn_ln{i}_b"]), (128, D))
         for i in (1, 2, 3)], axis=1))
    be1 = asnp(inputs["L0_ffn_be1"])
    be1pc = np.ascontiguousarray(
        np.stack([vec_pc(be1[e]) for e in range(NE)], axis=0))
    ident = np.eye(128, dtype=ml_dtypes.bfloat16)
    identf = np.eye(128, dtype=f32)
    cm = np.zeros((128, 512), dtype=ml_dtypes.bfloat16)
    cm[:, 384:512] = np.where(np.arange(128)[None, :] <= np.arange(128)[:, None],
                              0.0, NEGINF).astype(ml_dtypes.bfloat16)
    rowidx = (BIGIDX + (np.arange(NTT)[None, :] * 128
                        + np.arange(128)[:, None])).astype(f32)
    shared = {
        "mx_Wz": Wzf, "mx_Wh": Whf, "mx_Wg": asnp(inputs["L0_attn_Wg"]),
        "mx_bz": vec_pc(asnp(inputs["L0_attn_bz"]).reshape(-1)),
        "mx_bh": vec_pc(asnp(inputs["L0_attn_bh"]).reshape(-1)),
        "mx_bg": vec_pc(asnp(inputs["L0_attn_bg"]).reshape(-1)),
        "f_Wqkv": asnp(inputs["L0_ffn_Wqkv"]), "f_Wo": asnp(inputs["L0_ffn_Wo"]),
        "f_Wr": asnp(inputs["L0_ffn_Wr"]),
        "f_ub": vec_pc(asnp(inputs["L0_ffn_u_bias"]).reshape(-1)),
        "f_vb": vec_pc(asnp(inputs["L0_ffn_v_bias"]).reshape(-1)),
        "f_lng": lng, "f_lnb": lnb,
        "f_Wff1": asnp(inputs["L0_ffn_Wff1"]),
        "f_bff1": vec_pc(asnp(inputs["L0_ffn_bff1"]).reshape(-1)),
        "f_Wff2": asnp(inputs["L0_ffn_Wff2"]),
        "f_bff2": asnp(inputs["L0_ffn_bff2"]).reshape(1, D),
        "f_Wg": asnp(inputs["L0_ffn_Wg"]),
        "f_We1": asnp(inputs["L0_ffn_We1"]), "f_be1": be1pc,
        "f_We2": asnp(inputs["L0_ffn_We2"]),
        "f_be2": asnp(inputs["L0_ffn_be2"]).reshape(NE, 1, D),
        "f_Wout": asnp(inputs["L0_ffn_Wout"]),
        "peT": _pe_table(S, D),
        "ident": ident, "identf": identf, "cmask": cm, "rowidx": rowidx,
    }
    x = asnp(inputs["L0_attn_x"])
    in_maps = []
    for c in range(NCORES):
        m = dict(shared)
        m["x"] = np.ascontiguousarray(x[c * SEQ:(c + 1) * SEQ].reshape(TOK, D))
        in_maps.append(m)
    return in_maps


def build(cfg):
    k = KB(cfg)
    with k.nc.Block() as block:
        for en in KB.ENGINES:
            def mk(en=en):
                def f(eng):
                    k.begin_pass(en, eng)
                    emit_program(k)
                return f
            getattr(block, en)(mk())
    return k


def kernel(**inputs):
    cfg = dict(B=16, S=2048, D=1024, NH=16, DH=64, FF=4096, NE=8)
    in_maps = prep_inputs(cfg, inputs)
    k = build(cfg)
    res = run_bass_kernel_spmd(k.nc, in_maps, core_ids=list(range(NCORES)))
    SEQ = cfg["B"] // NCORES
    out = np.concatenate(
        [r["y"].reshape(SEQ, cfg["S"], cfg["D"]) for r in res.results], axis=0)
    return out


# version 15
# speedup vs baseline: 1.4649x; 1.3959x over previous
import numpy as np
import ml_dtypes

import bass_rust
import concourse.bass as bass
import concourse.mybir as mybir
from concourse.bass import IndirectOffsetOnAxis
from concourse.bass_utils import run_bass_kernel_spmd

F32 = mybir.dt.float32
BF16 = mybir.dt.bfloat16
U32 = mybir.dt.uint32
AF = mybir.ActivationFunctionType
ALU = mybir.AluOpType
AXX = mybir.AxisListType.X

NCORES = 8
BIGIDX = 2097152.0
EPS = 1e-5
NEGINF = -1.0e30
NU_MOE = 15
PAD = NU_MOE * 512

_DBG = set()
_MOE_DBG = False


class Sem:
    __slots__ = ("h", "n")

    def __init__(self, h):
        self.h = h
        self.n = 0

    def inc(self, inst, k=1):
        if inst is not None:
            inst.then_inc(self.h, k)
        self.n += k
        return self.n


class KB:
    ENGINES = ("sync", "scalar", "vector", "tensor", "gpsimd")

    def __init__(self, cfg):
        self.cfg = cfg
        self.B, self.S, self.D = cfg["B"], cfg["S"], cfg["D"]
        self.NH, self.DH, self.FF, self.NE = cfg["NH"], cfg["DH"], cfg["FF"], cfg["NE"]
        self.SEQ = self.B // NCORES
        self.TOK = self.SEQ * self.S
        self.NT = self.S // 128
        self.NTT = self.TOK // 128
        self.DC = self.D // 128
        self.FC = self.FF // 128
        self.SEGW = 1024

        nc = bass.Bass("TRN2", target_bir_lowering=False)
        self.nc = nc
        self._sb = {}
        self._sems = {}
        self.en = None
        self.eng = None
        self.st = {}

        D, S, FF, NE, TOK, DC, FC, NT, NTT = (
            self.D, self.S, self.FF, self.NE, self.TOK, self.DC, self.FC,
            self.NT, self.NTT)

        self.inp = {}

        def di(name, shape, dt=F32):
            self.inp[name] = nc.dram_tensor(name, shape, dt, kind="ExternalInput").ap()

        di("x", [TOK, D])
        di("mx_Wz", [D, D]); di("mx_Wh", [D, D]); di("mx_Wg", [D, D])
        di("mx_bz", [128, DC]); di("mx_bh", [128, DC]); di("mx_bg", [128, DC])
        di("f_Wqkv", [D, 3 * D]); di("f_Wo", [D, D]); di("f_Wr", [D, D])
        di("f_ub", [128, DC]); di("f_vb", [128, DC])
        di("f_lng", [128, 3, D]); di("f_lnb", [128, 3, D])
        di("f_Wff1", [D, FF]); di("f_bff1", [128, FC])
        di("f_Wff2", [FF, D]); di("f_bff2", [1, D])
        di("f_Wg", [D, NE])
        di("f_We1", [NE, D, FF]); di("f_be1", [NE, 128, FC])
        di("f_We2", [NE, FF, D]); di("f_be2", [NE, 1, D])
        di("f_Wout", [D, D])
        di("peT", [D, S])
        di("ident", [128, 128], BF16)
        di("identf", [128, 128])
        di("cmask", [128, 512], BF16)
        di("rowidx", [128, NTT])
        di("trilS", [128, 128], BF16)
        di("iota128", [128, 1])
        di("thr8", [8, 8])
        di("iotaU", [1, 16])
        di("rowplain", [128, NTT])
        di("ordinit", [128, 2 * (PAD // 128)])

        self.y = nc.dram_tensor("y", [TOK, D], F32, kind="ExternalOutput").ap()

        def scr(name, shape, dt=F32):
            kind = "ExternalOutput" if name in _DBG else "Internal"
            return nc.dram_tensor(name, shape, dt, kind=kind).ap()

        self.xA = scr("xA", [TOK, D])
        self.xB = scr("xB", [TOK, D])
        self.xn_s = scr("xn_s", [TOK, D])
        self.qT_s = scr("qT_s", [self.SEQ * NT, 128, DC * 128], BF16)
        self.ctx_s = scr("ctx_s", [TOK, D])
        self.h1_s = scr("h1_s", [TOK, D])
        self.h1T_s = scr("h1T_s", [DC, 128, TOK], BF16)
        self.h2_s = scr("h2_s", [TOK, D])
        self.h2T_s = scr("h2T_s", [DC, 128, TOK], BF16)
        self.moe_s = scr("moe_s", [TOK, D])
        self.G_s = scr("G_s", [2, 128 * S], BF16)
        self.moe_dbg = scr("moe_dbg", [NE, TOK, D]) if _MOE_DBG else None
        self.We1b = scr("We1b", [NE, 8, 128, FF], BF16)
        self.We2b = scr("We2b", [NE, 8, 128, FF], BF16)
        self.ord = scr("ord_s", [PAD, 2])
        self.cnt_d = scr("cnt_d", [1, 2 * NTT * NE])
        self.B_d = scr("B_d", [1, NTT * NE])
        self.padd_d = scr("padd_d", [NE, 1])
        self.pb_d = scr("pb_d", [1, 16])

        self.S01 = nc.alloc_psum_tensor("S01", [128, 1024], F32).ap()
        self.S23 = nc.alloc_psum_tensor("S23", [128, 1024], F32).ap()
        self.S45 = nc.alloc_psum_tensor("S45", [128, 1024], F32).ap()
        self.S67 = nc.alloc_psum_tensor("S67", [128, 1024], F32).ap()
        self.PT = self.S67[:, 0:512].bitcast(BF16)
        self.PTF = self.S67[:, 0:128]
        self.CTXP = self.S67[:, 512:576]
        self.CTXP2 = self.S67[:, 576:640]
        self.GATEP = self.S67[:, 512:512 + NE]

        A = nc.alloc_sbuf_tensor

        def ap_(name, shape, dt=F32):
            return A(name, shape, dt).ap()

        self.ident_sb = ap_("ident_sb", [128, 128], BF16)
        self.identf_sb = ap_("identf_sb", [128, 128])
        self.cmask_sb = ap_("cmask_sb", [128, 512], BF16)
        self.rowidx_sb = ap_("rowidx_sb", [128, NTT])
        self.ub_sb = ap_("ub_sb", [128, DC])
        self.vb_sb = ap_("vb_sb", [128, DC])
        self.bz_sb = ap_("bz_sb", [128, DC])
        self.nbz_sb = ap_("nbz_sb", [128, DC])
        self.bh_sb = ap_("bh_sb", [128, DC])
        self.bg_sb = ap_("bg_sb", [128, DC])
        self.bff1_sb = ap_("bff1_sb", [128, FC])
        self.be1_sb = ap_("be1_sb", [128, 2, FC])
        self.bff2_sb = ap_("bff2_sb", [1, D], BF16)
        self.be2_sb = ap_("be2_sb", [1, 2, D], BF16)
        self.wg_sb = ap_("wg_sb", [128, DC, NE], BF16)
        self.ones_sb = ap_("ones_sb", [1, 128], BF16)
        self.ones8_sb = ap_("ones8_sb", [128, NE])
        self.st_sb = ap_("st_sb", [128, 64])
        self.mx2_sb = ap_("mx2_sb", [128, 4])
        self.nmx_sb = ap_("nmx_sb", [128, 2])
        self.den_sb = ap_("den_sb", [128, 2])
        self.rden_sb = ap_("rden_sb", [128, 2])
        self.epsv = ap_("epsv", [128, 1])
        self.mask8_sb = ap_("mask8_sb", [128, NTT, NE])
        self.mg8_sb = ap_("mg8_sb", [128, NTT, NE])
        self.idx_sb = ap_("idx_sb", [128, NTT, NE], U32)
        self.scr8_sb = ap_("scr8_sb", [128, 4 * NE])
        self.off_xr1 = nc.sbuf_base
        self.xr1 = ap_("xr1", [128, 2, D])
        self.off_xr2 = nc.sbuf_base
        self.xr2 = ap_("xr2", [128, 2, D])
        self.xnb = ap_("xnb", [128, 2, D], BF16)
        self.off_sq = nc.sbuf_base
        self.sq = ap_("sq", [128, D], BF16)
        self.off_lnt = nc.sbuf_base
        self.lnt = ap_("lnt", [128, D])
        self.gates_sb = ap_("gates_sb", [128, NTT])
        self.widx_u32 = ap_("widx_u32", [128, 16], U32)
        self.wq_u32 = ap_("wq_u32", [128, 8, 16], U32)
        self.uecol_u32 = ap_("uecol_u32", [128, 16], U32)
        self.be1g = ap_("be1g", [128, FC])
        self.osb = ap_("osb", [128, 2, 4])
        self.oidx_u32 = ap_("oidx_u32", [128, 8], U32)
        self.gcol = ap_("gcol", [128, 8])

        self.arena0 = nc.sbuf_base
        self.arena_kb = (nc.sbuf_top - nc.sbuf_base) // 1024
        assert self.arena_kb >= 168, f"arena too small: {self.arena_kb}KB"

    def sem(self, name):
        s = self._sems.get(name)
        if s is None:
            s = Sem(self.nc.alloc_semaphore(name))
            self._sems[name] = s
        return s

    @property
    def CHV(self):
        return self.sem("ch_vector")

    @property
    def CHS(self):
        return self.sem("ch_scalar")

    @property
    def CHG(self):
        return self.sem("ch_gpsimd")

    def sba(self, name, shape, dt, off_kb):
        t = self._sb.get(name)
        if t is None:
            t = self.nc.alloc_sbuf_tensor_at(
                name, shape, dt, offset=self.arena0 + off_kb * 1024).ap()
            self._sb[name] = t
        return t

    def begin_pass(self, en, eng):
        self.en = en
        self.eng = eng
        self.st = {}
        for s in self._sems.values():
            s.n = 0

    def on(self, en):
        return self.en == en

    def w(self, en, sem, val):
        if val > 0 and self.en == en:
            self.eng.wait_ge(sem.h, val)

    def barrier(self, deps):
        for sem, val in deps:
            if val > 0:
                self.eng.wait_ge(sem.h, val)

    def mm(self, out, lhsT, rhs, start=True, stop=True):
        if self.en == "tensor":
            return self.eng.matmul(out, lhsT, rhs, start=start, stop=stop)
        return None

    def tr(self, out, in_, fp32=False):
        if self.en == "tensor":
            return self.eng.transpose(
                out, in_, self.identf_sb if fp32 else self.ident_sb)
        return None

    def _chain_op(self, en_name, make):
        s = self.sem("ch_" + en_name)
        if s.n > 0 and self.en == en_name:
            self.eng.wait_ge(s.h, s.n)
        inst = make(self.eng) if self.en == en_name else None
        return s.inc(inst)

    def act(self, out, in_, func, **kw):
        return self._chain_op("scalar", lambda e: e.activation(out, in_, func, **kw))

    def dve(self, meth, *a, **kw):
        return self._chain_op("vector", lambda e: getattr(e, meth)(*a, **kw))

    def gp(self, meth, *a, **kw):
        return self._chain_op("gpsimd", lambda e: getattr(e, meth)(*a, **kw))

    def dma(self, q, out, in_, **kw):
        if self.en == q:
            return self.eng.dma_start(out=out, in_=in_, **kw)
        return None


def emit_ln(k, xin, xout, lng=None, lnb=None, affout=None):
    D = k.D
    m = k.st_sb[:, 0:1]; msq = k.st_sb[:, 1:2]; va = k.st_sb[:, 2:3]
    lv = k.st_sb[:, 3:4]; rstd = k.st_sb[:, 4:5]; nmr = k.st_sb[:, 5:6]
    ssum = k.st_sb[:, 6:7]; ssq = k.st_sb[:, 7:8]
    k.dve("scalar_tensor_tensor", out=k.sq[:], in0=xin, scalar=1.0,
          in1=xin, op0=ALU.mult, op1=ALU.mult, accum_out=ssq)
    k.dve("reduce_sum", out=ssum, in_=xin, axis=AXX)
    k.dve("tensor_scalar", out=m, in0=ssum, scalar1=1.0 / D, scalar2=None,
          op0=ALU.mult, op1=ALU.bypass)
    k.dve("tensor_tensor", out=msq, in0=m, in1=m, op=ALU.mult)
    tv = k.dve("tensor_scalar", out=va, in0=ssq, scalar1=1.0 / D, scalar2=msq,
               op0=ALU.mult, op1=ALU.subtract)
    k.w("scalar", k.CHV, tv)
    k.act(lv, va, AF.Ln, bias=k.epsv[:])
    ta = k.act(rstd, lv, AF.Exp, scale=-0.5)
    k.w("vector", k.CHS, ta)
    k.dve("scalar_tensor_tensor", out=nmr, in0=m, scalar=-1.0, in1=rstd,
          op0=ALU.mult, op1=ALU.mult)
    tn = k.dve("tensor_scalar", out=xout, in0=xin, scalar1=rstd, scalar2=nmr,
               op0=ALU.mult, op1=ALU.add)
    if lng is not None:
        k.dve("tensor_tensor", out=affout, in0=xout, in1=lng, op=ALU.mult)
        tn = k.dve("tensor_tensor", out=affout, in0=affout, in1=lnb, op=ALU.add)
    return tn


def emit_ct(k, src, dst, par, smt, war=()):
    DC = k.DC
    tc = k.dve("tensor_copy", out=k.xnb[:, par, :], in_=src)
    k.w("tensor", k.CHV, tc)
    k.w("tensor", k.CHV, k.st.get("ptfree", 0))
    last = None
    for c in range(DC):
        ii = k.tr(k.PT[:, c * 128:(c + 1) * 128],
                  k.xnb[:, par, :].rearrange("p (c f) -> p c f", c=DC)[:, c, :])
        if ii is not None:
            last = ii
    tt = smt.inc(last)
    k.w("vector", smt, tt)
    for sem, val in war:
        k.w("vector", sem, val)
    te = k.dve("tensor_copy", out=dst,
               in_=k.PT.rearrange("p (c f) -> p c f", c=DC))
    k.st["ptfree"] = te
    return te


def emit_stage0(k):
    sL = k.sem("c_ld")
    sG = k.sem("c_ldg")
    loads = [
        (k.ident_sb[:], k.inp["ident"]), (k.identf_sb[:], k.inp["identf"]),
        (k.cmask_sb[:], k.inp["cmask"]), (k.rowidx_sb[:], k.inp["rowidx"]),
        (k.ub_sb[:], k.inp["f_ub"]), (k.vb_sb[:], k.inp["f_vb"]),
        (k.bz_sb[:], k.inp["mx_bz"]), (k.bh_sb[:], k.inp["mx_bh"]),
        (k.bg_sb[:], k.inp["mx_bg"]), (k.bff1_sb[:], k.inp["f_bff1"]),
    ]
    t = 0
    for out, in_ in loads:
        i = k.dma("sync", out, in_)
        t = sL.inc(i, 16)
    i = k.dma("gpsimd", k.bff2_sb[:], k.inp["f_bff2"])
    sG.inc(i, 16)
    i = k.dma("gpsimd", k.wg_sb[:],
              k.inp["f_Wg"].rearrange("(c p) e -> p c e", p=128))
    tg = sG.inc(i, 16)
    k.w("gpsimd", sG, tg)
    k.gp("memset", k.ones_sb[:], 1.0)
    k.gp("memset", k.ones8_sb[:], 1.0)
    tm = k.gp("memset", k.epsv[:], EPS)
    if k.on("gpsimd"):
        k.st["breg"] = k.eng.to_reg(k.TOK - 1)
        k.st["wreg"] = k.eng.to_reg(k.NE * 128 - 1)
        k.st["wreg2"] = k.eng.to_reg(k.NE * 1024 - 1)
        k.st["ereg"] = k.eng.to_reg(k.NE - 1)
        k.st["oreg"] = k.eng.to_reg(PAD - 1)
    k.w("vector", sL, t)
    tv = k.dve("tensor_scalar", out=k.nbz_sb[:], in0=k.bz_sb[:], scalar1=-1.0,
               scalar2=None, op0=ALU.mult, op1=ALU.bypass)
    k.barrier([(sL, t), (sG, tg), (k.CHG, tm), (k.CHV, tv)])


def emit_wconv(k):
    sWC = k.sem("wc")
    t = 0
    for e in range(k.NE):
        s1 = k.inp["f_We1"][e].rearrange("(c p) f -> p c f", p=128)
        s2 = k.inp["f_We2"][e].rearrange("(c p) f -> p c f", p=128)
        for g in range(8):
            i = k.dma("gpsimd", k.We1b[e, g], s1[:, g, :])
            t = sWC.inc(i, 16)
            i = k.dma("gpsimd",
                      k.We2b[e, g].rearrange("p (c f) -> p c f", c=4),
                      s2[:, g * 4:(g + 1) * 4, :])
            t = sWC.inc(i, 16)
    k.st["wconv_t"] = t


def emit_ord(k, call):
    D, DC, FC, NE, NTT, TOK = k.D, k.DC, k.FC, k.NE, k.NTT, k.TOK
    UB = PAD // 128
    trilS = k.sba("O_tril", [128, 128], BF16, 0)
    ordini = k.sba("O_ordini", [128, 2 * UB], F32, 1)
    iota = k.sba("O_iota", [128, 1], F32, 2)
    thr8 = k.sba("O_thr8", [8, 8], F32, 24)
    iotaU = k.sba("O_iotaU", [1, 16], F32, 25)
    onesf = k.sba("O_onesf", [1, 128], F32, 3)
    onesb = k.sba("O_onesb", [128, 1], BF16, 26)
    ones832 = k.sba("O_ones", [8, 32], F32, 27)
    rowpl = k.sba("O_rowpl", [128, NTT], F32, 4)
    maskb = k.sba("O_maskb", [128, NTT * NE], BF16, 5)
    cntrow = k.sba("O_cntrow", [1, NTT * NE], F32, 6)
    Brow = k.sba("O_Brow", [1, NTT * NE], F32, 8)
    ranks = k.sba("O_ranks", [128, NTT * NE], F32, 10)
    ptall = k.sba("O_ptall", [128, NTT * NE], F32, 12)
    pt = k.sba("O_pt", [128, NTT], F32, 14)
    pcol = k.sba("O_pcol", [128, NTT], U32, 15)
    ordsrc = k.sba("O_osrc", [128, NTT, 2], F32, 16)
    cntT = k.sba("O_cntT", [8, 32], F32, 17)
    sh = k.sba("O_sh", [8, 32], F32, 18)
    xbase = k.sba("O_xbase", [8, 32], F32, 19)
    ge8 = k.sba("O_ge8", [8, 8], F32, 20)
    small = k.sba("O_small", [8, 8], F32, 21)
    rows1 = k.sba("O_rows1", [1, 64], F32, 22)
    uecolf = k.sba("O_uecolf", [128, 16], F32, 23)

    sL = k.sem("o_ld")
    sB = k.sem("o_bounce")
    smq = k.sem("o_mm")
    sSC = k.sem("o_sc")

    c_col = small[0:8, 0:1]
    pad_col = small[0:8, 1:2]
    pad_row = rows1[0:1, 0:8]
    cumpad = rows1[0:1, 8:16]
    pbase = rows1[0:1, 16:24]
    ue_row = rows1[0:1, 24:40]

    loads = [
        (trilS[:], k.inp["trilS"]), (ordini[:], k.inp["ordinit"]),
        (iota[:], k.inp["iota128"]), (thr8[0:8, :], k.inp["thr8"]),
        (iotaU[0:1, :], k.inp["iotaU"]), (rowpl[:], k.inp["rowplain"]),
    ]
    tld = 0
    for out, in_ in loads:
        i = k.dma("sync", out, in_)
        tld = sL.inc(i, 16)
    tms = k.gp("memset", onesf[0:1, :], 1.0)
    k.gp("memset", onesb[:], 1.0)
    tms = k.gp("memset", ones832[0:8, :], 1.0)
    k.w("sync", sL, tld)
    i = k.dma("sync", k.ord.rearrange("(u p) c -> p u c", p=128),
              ordini[:].rearrange("p (u c) -> p u c", c=2))
    tpre = sL.inc(i, 16)

    tmb = k.dve("tensor_copy", out=maskb[:],
                in_=k.mask8_sb.rearrange("p t e -> p (t e)"))
    k.w("tensor", k.CHV, tmb)
    k.w("tensor", sL, tld)
    k.w("tensor", k.CHG, tms)
    i1 = k.mm(k.S01[:, 0:NTT * NE], trilS[:], maskb[:], start=True, stop=True)
    i2 = k.mm(k.S23[0:1, 0:NTT * NE].rearrange("o (e t) -> o e t", e=NE),
              onesb[:],
              maskb.rearrange("p (t e) -> p e t", e=NE), start=True, stop=True)
    tmm = smq.inc(i2)
    k.w("vector", smq, tmm)
    k.dve("tensor_copy", out=ranks[:], in_=k.S01[:, 0:NTT * NE])
    tcv = k.dve("tensor_copy", out=cntrow[0:1, :], in_=k.S23[0:1, 0:NTT * NE])
    k.w("sync", k.CHV, tcv)
    i = k.dma("sync", k.cnt_d[:, 0:NTT * NE], cntrow[0:1, :])
    tb = sB.inc(i, 16)
    k.w("sync", sB, tb)
    i = k.dma("sync", cntT[0:8, :],
              k.cnt_d[:, 0:NTT * NE].rearrange("o (e t) -> (o e) t", e=NE))
    tb = sB.inc(i, 16)
    k.w("vector", sB, tb)
    k.w("vector", k.CHG, tms)
    k.dve("memset", sh[0:8, 0:1], 0.0)
    k.dve("tensor_copy", out=sh[0:8, 1:32], in_=cntT[0:8, 0:31])
    k.dve("tensor_tensor_scan", out=xbase[0:8, :], data0=ones832[0:8, :],
          data1=sh[0:8, :], initial=0.0, op0=ALU.mult, op1=ALU.add)
    k.dve("reduce_sum", out=c_col, in_=cntT[0:8, :], axis=AXX)
    k.dve("tensor_scalar", out=ge8[0:8, :], in0=thr8[0:8, :], scalar1=c_col,
          scalar2=None, op0=ALU.is_lt, op1=ALU.bypass)
    k.dve("reduce_sum", out=pad_col, in_=ge8[0:8, :], axis=AXX)
    tpc = k.dve("tensor_scalar", out=pad_col, in0=pad_col, scalar1=512.0,
                scalar2=None, op0=ALU.mult, op1=ALU.bypass)
    k.w("sync", k.CHV, tpc)
    i = k.dma("sync", k.padd_d, pad_col)
    tb = sB.inc(i, 16)
    k.w("sync", sB, tb)
    i = k.dma("sync", pad_row, k.padd_d.rearrange("e o -> o e"))
    tb = sB.inc(i, 16)
    k.w("vector", sB, tb)
    k.dve("tensor_tensor_scan", out=cumpad, data0=ones832[0:1, 0:8],
          data1=pad_row, initial=0.0, op0=ALU.mult, op1=ALU.add)
    k.dve("tensor_tensor", out=pbase, in0=cumpad, in1=pad_row, op=ALU.subtract)
    k.dve("memset", ue_row, 0.0)
    tue = 0
    for e in range(NE):
        tue = k.dve("scalar_tensor_tensor", out=ue_row, in0=iotaU[0:1, 0:16],
                    scalar=cumpad[0:1, e:e + 1], in1=ue_row,
                    op0=ALU.is_ge, op1=ALU.add)
    k.w("sync", k.CHV, tue)
    i = k.dma("sync", k.pb_d[:, 0:8], pbase)
    tb = sB.inc(i, 16)
    k.w("sync", sB, tb)
    i = k.dma("sync", pad_col, k.pb_d.rearrange("o f -> f o")[0:8, :])
    tb = sB.inc(i, 16)
    k.w("vector", sB, tb)
    tB = k.dve("tensor_scalar", out=xbase[0:8, :], in0=xbase[0:8, :],
               scalar1=pad_col, scalar2=None, op0=ALU.add, op1=ALU.bypass)
    k.w("sync", k.CHV, tB)
    i = k.dma("sync", k.B_d.rearrange("o (t e) -> e o t", e=NE),
              xbase[0:8, :].rearrange("e (o t) -> e o t", o=1),
              allow_slow_non_contiguous=True)
    tb = sB.inc(i, 16)
    k.w("sync", sB, tb)
    i = k.dma("sync", Brow[0:1, :], k.B_d)
    tb = sB.inc(i, 16)
    k.w("tensor", sB, tb)
    k.w("tensor", k.CHG, tms)
    i3 = k.mm(k.S45[:, 0:NTT * NE], onesf[0:1, :], Brow[0:1, :],
              start=True, stop=True)
    tmm = smq.inc(i3)
    k.w("vector", smq, tmm)
    k.dve("tensor_tensor", out=ptall[:], in0=k.S45[:, 0:NTT * NE],
          in1=ranks[:], op=ALU.add)
    k.dve("tensor_tensor", out=ptall[:], in0=ptall[:],
          in1=k.mask8_sb.rearrange("p t e -> p (t e)"), op=ALU.mult)
    pv = ptall.rearrange("p (t e) -> p t e", e=NE)
    k.dve("tensor_copy", out=pt[:], in_=pv[:, :, 0])
    for e in range(1, NE):
        k.dve("tensor_tensor", out=pt[:], in0=pt[:], in1=pv[:, :, e],
              op=ALU.add)
    tpt = k.dve("tensor_copy", out=pcol[:], in_=pt[:])
    k.dve("tensor_copy", out=ordsrc[:, :, 0], in_=rowpl[:])
    tos = k.dve("tensor_copy", out=ordsrc[:, :, 1], in_=k.gates_sb[:])
    k.w("gpsimd", k.CHV, max(tpt, tos))
    k.w("gpsimd", sL, tpre)
    for tau in range(NTT):
        i = None
        if k.on("gpsimd"):
            i = k.eng.indirect_dma_start(
                out=k.ord,
                out_offset=IndirectOffsetOnAxis(ap=pcol[:, tau:tau + 1], axis=0),
                in_=ordsrc[:, tau, :], in_offset=None,
                bounds_check=k.st["oreg"], oob_is_err=False)
        sSC.inc(i, 16)
    k.w("tensor", k.CHV, tue)
    i4 = k.mm(k.S45[:, 512:528], onesf[0:1, :], ue_row, start=True, stop=True)
    tmm = smq.inc(i4)
    k.w("vector", smq, tmm)
    wf = k.sba("O_wf", [128, 16], F32, 28)
    k.dve("tensor_copy", out=uecolf[:], in_=k.S45[:, 512:528])
    k.dve("tensor_copy", out=k.uecol_u32[:], in_=uecolf[:])
    k.dve("tensor_scalar", out=wf[:], in0=uecolf[:], scalar1=128.0,
          scalar2=iota[:, 0:1], op0=ALU.mult, op1=ALU.add)
    tw = k.dve("tensor_copy", out=k.widx_u32[:], in_=wf[:])
    for g in range(8):
        k.dve("tensor_scalar", out=wf[:], in0=uecolf[:], scalar1=1024.0,
              scalar2=iota[:, 0:1], op0=ALU.mult, op1=ALU.add)
        k.dve("tensor_scalar", out=wf[:], in0=wf[:], scalar1=float(g * 128),
              scalar2=None, op0=ALU.add, op1=ALU.bypass)
        tw = k.dve("tensor_copy", out=k.wq_u32[:, g, :], in_=wf[:])
    k.barrier([(sSC, sSC.n), (k.CHV, tw), (sB, sB.n), (sL, sL.n)])


def emit_Cs(k, call):
    D, DC, FC, NE, TOK = k.D, k.DC, k.FC, k.NE, k.TOK
    Wu1 = k.sba("S_w1", [128, DC, k.FF], BF16, 0)
    Wu2 = k.sba("S_w2", [128, FC, D], BF16, 64)
    hidT = k.sba("S_hid", [128, FC, 512], BF16, 128)
    h2row = k.sba("S_h2r", [128, 2, D], F32, 160)
    if "S_h2T" not in k._sb:
        k._sb["S_h2T"] = k.nc.alloc_sbuf_tensor_at(
            "S_h2T", [128, DC, 512], BF16, offset=k.off_xr1).ap()
        k._sb["S_be2g"] = k.nc.alloc_sbuf_tensor_at(
            "S_be2g", [128, D], F32, offset=k.off_lnt).ap()
        k._sb["S_be2b"] = k.nc.alloc_sbuf_tensor_at(
            "S_be2b", [1, D], BF16, offset=k.off_sq).ap()
    h2T = k._sb["S_h2T"]
    be2g = k._sb["S_be2g"]
    be2b = k._sb["S_be2b"]
    eob = k.xr2

    sW1 = k.sem("s_w1"); sW2 = k.sem("s_w2"); sBE = k.sem("s_be")
    sO = k.sem("s_ord"); sH = k.sem("s_h2")
    smq = k.sem("s_mm"); sm2 = k.sem("s_mm2"); smt = k.sem("s_mt")
    sSC = [k.sem("s_sc0"), k.sem("s_sc1")]
    stt = k.st
    stt[("s_ps", 0)] = 0
    stt[("s_ps", 1)] = 0

    We1v = k.We1b.rearrange("e g p f -> (e g p) f")
    We2v = k.We2b.rearrange("e g p f -> (e g p) f")
    be1v = k.inp["f_be1"].rearrange("e p f -> (e p) f")
    be2v = k.inp["f_be2"].rearrange("e o d -> (e o) d")

    for j in range(NU_MOE):
        k.w("gpsimd", smq, stt.get("s_w1_users", 0))
        if j == 0 and call == 0:
            k.w("gpsimd", k.sem("wc"), k.st.get("wconv_t", 0))
        tw1a = tw1b = tw2 = tbe = None
        for g in range(8):
            i = None
            if k.on("gpsimd"):
                i = k.eng.indirect_dma_start(
                    out=Wu1[:, g, :], out_offset=None, in_=We1v,
                    in_offset=IndirectOffsetOnAxis(
                        ap=k.wq_u32[:, g, j:j + 1], axis=0),
                    bounds_check=k.st["wreg2"], oob_is_err=False)
            tt = sW1.inc(i, 16)
            if g == 3:
                tw1a = tt
        tw1b = sW1.n
        k.w("gpsimd", sm2, stt.get("s_w2_users", 0))
        for g in range(8):
            i = None
            if k.on("gpsimd"):
                i = k.eng.indirect_dma_start(
                    out=Wu2[:, g * 4:(g + 1) * 4, :].rearrange(
                        "p c f -> p (c f)"),
                    out_offset=None, in_=We2v,
                    in_offset=IndirectOffsetOnAxis(
                        ap=k.wq_u32[:, g, j:j + 1], axis=0),
                    bounds_check=k.st["wreg2"], oob_is_err=False)
            tw2 = sW2.inc(i, 16)
        k.w("gpsimd", k.CHS, stt.get("s_be_users", 0))
        k.w("gpsimd", k.CHV, stt.get("s_be2_users", 0))
        i = None
        if k.on("gpsimd"):
            i = k.eng.indirect_dma_start(
                out=k.be1g[:], out_offset=None, in_=be1v,
                in_offset=IndirectOffsetOnAxis(ap=k.widx_u32[:, j:j + 1], axis=0),
                bounds_check=k.st["wreg"], oob_is_err=False)
        sBE.inc(i, 16)
        i = None
        if k.on("gpsimd"):
            i = k.eng.indirect_dma_start(
                out=be2g[:], out_offset=None, in_=be2v,
                in_offset=IndirectOffsetOnAxis(ap=k.uecol_u32[:, j:j + 1], axis=0),
                bounds_check=k.st["ereg"], oob_is_err=False)
        tbe = sBE.inc(i, 16)
        k.w("vector", sBE, tbe)
        k.w("vector", sm2, stt.get("s_w2_users", 0))
        tb2 = k.dve("tensor_copy", out=be2b[0:1, :], in_=be2g[0:1, :])
        stt["s_be2_users"] = tb2

        for it in range(4):
            pos = j * 4 + it
            sl = it % 2
            k.w("sync", k.CHV, stt.get(("s_osb", sl), 0))
            i = k.dma("sync", k.osb[:, sl, 0:2],
                      k.ord[pos * 128:(pos + 1) * 128, :])
            to = sO.inc(i, 16)
            k.w("vector", sO, to)
            k.w("vector", sSC[it % 2], stt.get(("s_eo_sc", it), 0))
            k.dve("tensor_copy", out=k.oidx_u32[:, it:it + 1],
                  in_=k.osb[:, sl, 0:1])
            tcv = k.dve("tensor_copy", out=k.gcol[:, it:it + 1],
                        in_=k.osb[:, sl, 1:2])
            stt[("s_osb", sl)] = tcv
            k.w("gpsimd", k.CHV, tcv)
            k.w("gpsimd", k.CHV, stt.get(("s_h2row", sl), 0))
            i = None
            if k.on("gpsimd"):
                i = k.eng.indirect_dma_start(
                    out=h2row[:, sl, :], out_offset=None, in_=k.h2_s,
                    in_offset=IndirectOffsetOnAxis(ap=k.oidx_u32[:, it:it + 1],
                                                   axis=0),
                    bounds_check=k.st["breg"], oob_is_err=False)
            th = sH.inc(i, 16)
            k.w("vector", sH, th)
            war_ = [(smq, stt.get("s_h2T_users", 0))] if it == 0 else []
            the = emit_ct(k, h2row[:, sl, :],
                          h2T[:, :, it * 128:(it + 1) * 128], sl, smt, war=war_)
            stt[("s_h2row", sl)] = the
        txn = k.st["ptfree"]

        k.w("tensor", k.CHV, txn)
        for fc in range(FC):
            r = fc % 2
            ps = k.S01[:, r * 512:(r + 1) * 512]
            k.w("tensor", sW1, tw1b)
            if fc == 0:
                k.w("tensor", sBE, tbe)
            k.w("tensor", k.CHS, stt[("s_ps", r)])
            last = None
            for kc in range(DC):
                ii = k.mm(ps, Wu1[:, kc, fc * 128:(fc + 1) * 128],
                          h2T[:, kc, :], start=(kc == 0), stop=(kc == DC - 1))
                if ii is not None:
                    last = ii
            tg = smq.inc(last)
            k.w("scalar", smq, tg)
            k.w("scalar", sBE, tbe)
            k.w("scalar", sm2, stt.get("s_hid_users", 0))
            stt[("s_ps", r)] = k.act(hidT[:, fc, :], ps, AF.Relu,
                                     bias=k.be1g[:, fc:fc + 1])
        stt["s_w1_users"] = smq.n
        stt["s_h2T_users"] = smq.n
        stt["s_be_users"] = k.CHS.n

        k.w("tensor", k.CHS, max(stt[("s_ps", 0)], stt[("s_ps", 1)]))
        k.w("tensor", sW2, tw2)
        k.w("tensor", k.CHV, tb2)
        for it in range(4):
            sl = it % 2
            ps2 = k.S23 if it % 2 == 0 else k.S45
            k.w("tensor", k.CHV, stt.get(("s_ps2", it % 2), 0))
            last = None
            for half in range(2):
                for sl2 in range(FC):
                    ii = k.mm(ps2[:, half * 512:(half + 1) * 512],
                              hidT[:, sl2, it * 128:(it + 1) * 128],
                              Wu2[:, sl2, half * 512:(half + 1) * 512],
                              start=(sl2 == 0), stop=False)
                    if ii is not None:
                        last = ii
                ii = k.mm(ps2[:, half * 512:(half + 1) * 512], k.ones_sb[:],
                          be2b[0:1, half * 512:(half + 1) * 512],
                          start=False, stop=True)
                if ii is not None:
                    last = ii
            tgm = sm2.inc(last)
            k.w("vector", sm2, tgm)
            k.w("vector", sSC[sl], stt.get(("s_eo", sl), 0))
            teo = k.dve("tensor_scalar", out=eob[:, sl, :], in0=ps2,
                        scalar1=k.gcol[:, it:it + 1], scalar2=None,
                        op0=ALU.mult, op1=ALU.bypass)
            stt[("s_ps2", it % 2)] = teo
            k.w("gpsimd", k.CHV, teo)
            i = None
            if k.on("gpsimd"):
                i = k.eng.indirect_dma_start(
                    out=k.moe_s,
                    out_offset=IndirectOffsetOnAxis(ap=k.oidx_u32[:, it:it + 1],
                                                    axis=0),
                    in_=eob[:, sl, :], in_offset=None,
                    bounds_check=k.st["breg"], oob_is_err=False)
            tsc = sSC[sl].inc(i, 16)
            stt[("s_eo", sl)] = tsc
            stt[("s_eo_sc", it)] = tsc
        stt["s_hid_users"] = sm2.n
        stt["s_w2_users"] = sm2.n
    k.barrier([(sSC[0], sSC[0].n), (sSC[1], sSC[1].n)])


def emit_rt(k, call):
    S, DC = k.S, k.DC
    rT = k.sba("A_rT", [128, DC, S], BF16, 64)
    Wr = k.sba("A_wr", [128, DC, k.D], BF16, 96)
    peb = k.sba("A_peb", [128, 2, DC, 512], BF16, 112)
    sW = k.sem("r_wld")
    sPE = [k.sem("r_pe0"), k.sem("r_pe1")]
    smq = k.sem("r_mm")
    i = k.dma("gpsimd", Wr[:], k.inp["f_Wr"].rearrange("(c p) f -> p c f", p=128))
    tW = sW.inc(i, 16)
    ev = [0, 0]
    pe_users = [0, 0]
    tlast = 0
    for n in range(S // 512):
        par = n % 2
        k.w("gpsimd", smq, pe_users[par])
        i = k.dma("gpsimd", peb[:, par], k.inp["peT"].rearrange(
            "(c p) t -> p c t", p=128)[:, :, n * 512:(n + 1) * 512])
        tpe = sPE[par].inc(i, 16)
        k.w("tensor", sW, tW)
        k.w("tensor", sPE[par], tpe)
        for oc in range(DC):
            r = (n * DC + oc) % 2
            ps = k.S01[:, r * 512:(r + 1) * 512]
            k.w("tensor", k.CHV, ev[r])
            last = None
            for kc in range(DC):
                ii = k.mm(ps, Wr[:, kc, oc * 128:(oc + 1) * 128], peb[:, par, kc, :],
                          start=(kc == 0), stop=(kc == DC - 1))
                if ii is not None:
                    last = ii
            tg = smq.inc(last)
            k.w("vector", smq, tg)
            tlast = k.dve("tensor_copy", out=rT[:, oc, n * 512:(n + 1) * 512], in_=ps)
            ev[r] = tlast
        pe_users[par] = smq.n
    k.barrier([(k.CHV, tlast)])
    return rT


def emit_p0(k, call, s, x_in):
    S, D, DC, NT = k.S, k.D, k.DC, k.NT
    kT = k.sba("A_kT", [128, DC, S], BF16, 0)
    V = k.sba("A_v", [128, NT, D], BF16, 32)
    Wq = k.sba("A_wqkv", [128, DC, 3 * D], BF16, 96)
    xnT = k.sba("A_xnT", [128, 2, DC, 512], BF16, 144)
    qTb = k.sba("A_qTb", [128, 2, 512], BF16, 160)

    sX = [k.sem("p_x0"), k.sem("p_x1")]
    sW = k.sem("p_wld")
    smq = k.sem("p_mmq")
    smt = k.sem("p_mmt")
    sSX = [k.sem("p_sx0"), k.sem("p_sx1")]
    sSQ = [k.sem("p_sq0"), k.sem("p_sq1")]

    wsrc = k.inp["f_Wqkv"].rearrange("(c p) f -> p c f", p=128)
    i = k.dma("gpsimd", Wq[:, :, 0:2048], wsrc[:, :, 0:2048])
    sW.inc(i, 16)
    i = k.dma("gpsimd", Wq[:, :, 2048:3072], wsrc[:, :, 2048:3072])
    tW = sW.inc(i, 16)
    if call == 0 and "wconv_t" not in k.st:
        emit_wconv(k)

    xnT_users = [0, 0]
    ps_ev = [0, 0, 0, 0]
    v_ev = [0, 0]
    qsp = [0, 0]
    xr1_rd = k.st.setdefault("p_xr1_rd", [0, 0])
    xr2_rd = k.st.setdefault("p_xr2_rd", [0, 0])
    for n in range(S // 512):
        par = n % 2
        for it in range(4):
            tau = n * 4 + it
            gtau = s * NT + tau
            slot = tau % 2
            k.w("sync", k.CHV, xr1_rd[slot])
            i = k.dma("sync", k.xr1[:, slot, :],
                      x_in[gtau * 128:(gtau + 1) * 128, :])
            tx = sX[slot].inc(i, 16)
            k.w("vector", sX[slot], tx)
            k.w("vector", sSX[slot], xr2_rd[slot])
            tn = emit_ln(k, k.xr1[:, slot, :], k.xr2[:, slot, :])
            xr1_rd[slot] = tn
            k.w("sync", k.CHV, tn)
            i = k.dma("sync", k.xn_s[gtau * 128:(gtau + 1) * 128, :],
                      k.xr2[:, slot, :])
            xr2_rd[slot] = sSX[slot].inc(i, 16)
            emit_ct(k, k.xr2[:, slot, :], xnT[:, par, :, it * 128:(it + 1) * 128],
                    slot, smt, war=[(smq, xnT_users[par])] if it == 0 else ())
        txn = k.st["ptfree"]
        k.w("tensor", k.CHV, txn)
        k.w("tensor", sW, tW)
        pi = 0
        for which, base in (("q", 0), ("k", D)):
            for oc in range(DC):
                r = pi % 4
                ps = [k.S01[:, 0:512], k.S01[:, 512:1024],
                      k.S23[:, 0:512], k.S23[:, 512:1024]][r]
                k.w("tensor", k.CHV, ps_ev[r])
                last = None
                for kc in range(DC):
                    ii = k.mm(ps, Wq[:, kc, base + oc * 128:base + oc * 128 + 128],
                              xnT[:, par, kc, :],
                              start=(kc == 0), stop=(kc == DC - 1))
                    if ii is not None:
                        last = ii
                tgm = smq.inc(last)
                k.w("vector", smq, tgm)
                if which == "k":
                    ps_ev[r] = k.dve("tensor_copy",
                                     out=kT[:, oc, n * 512:(n + 1) * 512], in_=ps)
                else:
                    qs = pi % 2
                    k.w("vector", sSQ[qs], qsp[qs])
                    tqe = k.dve("tensor_copy", out=qTb[:, qs, :], in_=ps)
                    ps_ev[r] = tqe
                    k.w("sync", k.CHV, tqe)
                    qdst = k.qT_s[s * NT + n * 4: s * NT + n * 4 + 4, :,
                                  oc * 128:(oc + 1) * 128].rearrange("t p f -> p t f")
                    i = k.dma("sync", qdst,
                              qTb[:, qs, :].rearrange("p (t f) -> p t f", t=4))
                    qsp[qs] = sSQ[qs].inc(i, 16)
                pi += 1
        for it in range(4):
            for half in range(2):
                ps = k.S45[:, half * 512:(half + 1) * 512]
                k.w("tensor", k.CHV, v_ev[half])
                last = None
                for kc in range(DC):
                    ii = k.mm(ps, xnT[:, par, kc, it * 128:(it + 1) * 128],
                              Wq[:, kc, 2 * D + half * 512:2 * D + (half + 1) * 512],
                              start=(kc == 0), stop=(kc == DC - 1))
                    if ii is not None:
                        last = ii
                tgm = smq.inc(last)
                k.w("vector", smq, tgm)
                v_ev[half] = k.dve(
                    "tensor_copy",
                    out=V[:, n * 4 + it, half * 512:(half + 1) * 512], in_=ps)
        xnT_users[par] = smq.n
    k.barrier([(k.CHV, k.CHV.n), (sSX[0], sSX[0].n), (sSX[1], sSX[1].n),
               (sSQ[0], sSQ[0].n), (sSQ[1], sSQ[1].n)])
    return kT, V


def emit_attn(k, call, s, kT, V, rT):
    S, D, DC, NT, NH = k.S, k.D, k.DC, k.NT, k.NH
    SEGW = k.SEGW
    qTt = k.sba("A_qTt", [128, 2, DC, 128], BF16, 96)
    qu = k.sba("A_qu", [128, 2, DC, 128], BF16, 100)
    qv = k.sba("A_qv", [128, 2, DC, 128], BF16, 104)
    Gsb = k.sba("A_G", [128, 2, S], BF16, 108)
    bd = k.sba("A_bd", [128, 2, S], BF16, 116)
    ssb = k.sba("A_s", [128, 2, S], F32, 124)
    Psb = k.sba("A_P", [128, 2, S], BF16, 140)
    PTs = k.sba("A_PTs", [128, NT, 128], BF16, 148)
    ctx = k.sba("A_ctx", [128, 2, D], F32, 152)

    sQ = [k.sem("a_q0"), k.sem("a_q1")]
    sG = k.sem("a_g")
    sSPL = [k.sem("a_spl0"), k.sem("a_spl1")]
    sBD = [k.sem("a_bd0"), k.sem("a_bd1")]
    sAC = k.sem("a_ac")
    sTR = k.sem("a_tr")
    sPV = k.sem("a_pv")
    sCS = [k.sem("a_cs0"), k.sem("a_cs1")]
    SC = [k.S01, k.S23, k.S45]
    stt = k.st

    for t in range(NT):
        KT = (t + 1) * 128
        nseg = (KT + SEGW - 1) // SEGW
        segs = [(si * SEGW, min(SEGW, KT - si * SEGW)) for si in range(nseg)]
        gtau = s * NT + t
        tp = t % 2
        k.w("sync", k.CHV, stt.get(("a_quv", tp), 0))
        i = k.dma("sync", qTt[:, tp],
                  k.qT_s[gtau].rearrange("p (c f) -> p c f", c=DC))
        tq = sQ[tp].inc(i, 16)
        k.w("vector", sQ[tp], tq)
        k.w("vector", sG, stt.get(("a_g_users", tp), 0))
        k.w("vector", sAC, stt.get(("a_ac_users", tp), 0))
        tquv = 0
        for c in range(DC):
            k.dve("tensor_scalar", out=qu[:, tp, c, :], in0=qTt[:, tp, c, :],
                  scalar1=k.ub_sb[:, c:c + 1], scalar2=0.125,
                  op0=ALU.add, op1=ALU.mult)
            tquv = k.dve("tensor_scalar", out=qv[:, tp, c, :], in0=qTt[:, tp, c, :],
                         scalar1=k.vb_sb[:, c:c + 1], scalar2=0.125,
                         op0=ALU.add, op1=ALU.mult)
        stt[("a_quv", tp)] = tquv

        units = [(h, si) for h in range(NH) for si in range(nseg)]
        NU = len(units)
        tG = [0] * NU; tGE = [0] * NU; tAC = [0] * NU; tSE = [0] * NU
        tBDh = [0] * NH; tMKh = [0] * NH; tSPLh = [0] * NH
        tEXh = [0] * NH; tNMh = [0] * NH; tRDh = [0] * NH
        tCEh = [0] * NH; tTRh = [0] * NH

        def emit_G(u):
            h, si = units[u]
            base, w_ = segs[si]
            hc, poff = h // 2, 64 * (h % 2)
            slot = SC[u % 3]
            k.w("tensor", k.CHV, tSE[u - 3] if u >= 3 else stt.get("a_se_prev", 0))
            k.w("tensor", k.CHV, tquv)
            last = None
            for cc in range(0, w_, 512):
                cw = min(512, w_ - cc)
                ii = k.mm(slot[:, cc:cc + cw], qv[poff:poff + 64, tp, hc, :],
                          rT[poff:poff + 64, hc,
                             S - KT + base + cc:S - KT + base + cc + cw],
                          start=True, stop=True)
                if ii is not None:
                    last = ii
            tG[u] = sG.inc(last)
            k.w("vector", sG, tG[u])
            k.w("vector", sSPL[h % 2],
                tSPLh[h - 2] if h >= 2 else stt.get(("a_spl_prev", h % 2), 0))
            tGE[u] = k.dve("tensor_copy", out=Gsb[:, h % 2, base:base + w_],
                           in_=slot[:, 0:w_])

        def emit_SPILL(h):
            KTl = KT
            qh = "gpsimd" if h % 2 == 0 else "sync"
            k.w(qh, k.CHV, tGE[(h + 1) * nseg - 1])
            k.w(qh, sBD[h % 2],
                tBDh[h - 2] if h >= 2 else stt.get(("a_bd_prev", h % 2), 0))
            gslot = k.G_s[h % 2]
            wr = gslot.copy()
            base_off = wr.offset
            wr.ap = bass_rust.VecI64Pair([[KTl, 128], [1, KTl]])
            wr.offset = base_off
            i = k.dma(qh, wr, Gsb[:, h % 2, 0:KTl])
            tSPLh[h] = sSPL[h % 2].inc(i, 16)
            k.w(qh, sSPL[h % 2], tSPLh[h])
            k.w(qh, sAC,
                tAC[(h - 1) * nseg - 1] if h >= 2 else stt.get("a_ac_prev", 0))
            rd = gslot.copy()
            rd.ap = bass_rust.VecI64Pair([[KTl - 1, 128], [1, KTl]])
            rd.offset = base_off + 127
            i = k.dma(qh, bd[:, h % 2, 0:KTl], rd)
            tBDh[h] = sBD[h % 2].inc(i, 16)

        def emit_AC(u):
            h, si = units[u]
            base, w_ = segs[si]
            hc, poff = h // 2, 64 * (h % 2)
            slot = SC[u % 3]
            k.w("tensor", k.CHV, tGE[u])
            k.w("tensor", sBD[h % 2], tBDh[h])
            last = None
            for cc in range(0, w_, 512):
                cw = min(512, w_ - cc)
                is_last = (si == nseg - 1) and (cc + cw == w_)
                ii = k.mm(slot[:, cc:cc + cw], qu[poff:poff + 64, tp, hc, :],
                          kT[poff:poff + 64, hc, base + cc:base + cc + cw],
                          start=True, stop=False)
                if ii is not None:
                    last = ii
                ii = k.mm(slot[:, cc:cc + cw], k.ident_sb[:],
                          bd[:, h % 2, base + cc:base + cc + cw],
                          start=False, stop=not is_last)
                if ii is not None:
                    last = ii
                if is_last:
                    ii = k.mm(slot[:, cc:cc + cw], k.ident_sb[:],
                              k.cmask_sb[:, 512 - cw:512],
                              start=False, stop=True)
                    if ii is not None:
                        last = ii
            tAC[u] = sAC.inc(last)
            k.w("vector", sAC, tAC[u])
            if h >= 2:
                k.w("vector", k.CHS, tEXh[h - 2])
            tSE[u] = k.dve("tensor_scalar", out=ssb[:, h % 2, base:base + w_],
                           in0=slot[:, 0:w_], scalar1=1.0, scalar2=None,
                           op0=ALU.mult, op1=ALU.max,
                           accum_out=k.mx2_sb[:, si:si + 1])
            if si == nseg - 1:
                tNMh[h] = k.dve("reduce_max", out=k.nmx_sb[:, h % 2:h % 2 + 1],
                                in_=k.mx2_sb[:, 0:nseg], axis=AXX, negate=True)
                k.w("scalar", k.CHV, tNMh[h])
                if h >= 2:
                    k.w("scalar", sTR, tTRh[h - 2])
                tEXh[h] = k.act(Psb[:, h % 2, 0:KT], ssb[:, h % 2, 0:KT], AF.Exp,
                                bias=k.nmx_sb[:, h % 2:h % 2 + 1],
                                accum_out=k.den_sb[:, h % 2:h % 2 + 1])

        def emit_TPV(h):
            ntile = t + 1
            ngrp = (ntile + 7) // 8
            cp = k.CTXP if h % 2 == 0 else k.CTXP2
            k.w("tensor", k.CHS, max(tEXh[h], tCEh[h - 2] if h >= 2 else
                                     stt.get(("a_ce_prev", h % 2), 0)))
            last_pv = None
            for g in range(ngrp):
                j0, j1 = g * 8, min(g * 8 + 8, ntile)
                k.w("tensor", k.CHV, k.st.get("ptfree", 0))
                last = None
                for j in range(j0, j1):
                    ii = k.tr(k.PT[:, (j - j0) * 128:(j - j0 + 1) * 128],
                              Psb[:, h % 2, j * 128:(j + 1) * 128])
                    if ii is not None:
                        last = ii
                tg = sTR.inc(last)
                k.w("vector", sTR, tg)
                te = k.dve("tensor_copy", out=PTs[:, j0:j1, :],
                           in_=k.PT[:, 0:(j1 - j0) * 128].rearrange(
                               "p (j f) -> p j f", f=128))
                k.st["ptfree"] = te
                k.w("tensor", k.CHV, te)
                for j in range(j0, j1):
                    ii = k.mm(cp, PTs[:, j, :], V[:, j, h * 64:(h + 1) * 64],
                              start=(j == 0), stop=(j == ntile - 1))
                    if ii is not None:
                        last_pv = ii
            tTRh[h] = sTR.n
            tPVh = sPV.inc(last_pv)
            k.w("vector", k.CHS, tEXh[h])
            tRDh[h] = k.dve("reciprocal", out=k.rden_sb[:, h % 2:h % 2 + 1],
                            in_=k.den_sb[:, h % 2:h % 2 + 1])
            k.w("scalar", sPV, tPVh)
            k.w("scalar", k.CHV, tRDh[h])
            k.w("scalar", sCS[t % 2], stt.get(("a_cs", t % 2), 0))
            tCEh[h] = k.act(ctx[:, t % 2, h * 64:(h + 1) * 64], cp, AF.Copy,
                            scale=k.rden_sb[:, h % 2:h % 2 + 1])

        emit_G(0)
        if NU > 1:
            emit_G(1)
        for uu in range(min(2, NU)):
            hh, ss = units[uu]
            if ss == nseg - 1:
                emit_SPILL(hh)
        for u in range(NU):
            if u + 2 < NU:
                emit_G(u + 2)
            emit_AC(u)
            if u + 2 < NU and units[u + 2][1] == nseg - 1:
                emit_SPILL(units[u + 2][0])
            h, si = units[u]
            if si == nseg - 1 and h >= 1:
                emit_TPV(h - 1)
        emit_TPV(NH - 1)
        k.w("sync", k.CHS, tCEh[NH - 1])
        i = k.dma("sync", k.ctx_s[gtau * 128:(gtau + 1) * 128, :], ctx[:, t % 2, :])
        stt[("a_cs", t % 2)] = sCS[t % 2].inc(i, 16)
        stt["a_se_prev"] = tSE[NU - 1]
        stt[("a_spl_prev", 0)] = tSPLh[NH - 2]
        stt[("a_spl_prev", 1)] = tSPLh[NH - 1]
        stt[("a_bd_prev", 0)] = tBDh[NH - 2]
        stt[("a_bd_prev", 1)] = tBDh[NH - 1]
        stt["a_ac_prev"] = tAC[NU - 1]
        stt[("a_ce_prev", 0)] = tCEh[NH - 2]
        stt[("a_ce_prev", 1)] = tCEh[NH - 1]
        stt[("a_g_users", t % 2)] = tG[NU - 1]
        stt[("a_ac_users", t % 2)] = tAC[NU - 1]
    k.barrier([(sCS[0], sCS[0].n), (sCS[1], sCS[1].n)])


def emit_B(k, call):
    D, DC, NTT = k.D, k.DC, k.NTT
    Wo = k.sba("B_wo", [128, DC, D], BF16, 0)
    lng = k.sba("B_lng", [128, 2, D], F32, 16)
    ctr = k.sba("B_ctr", [128, 2, D], F32, 24)
    cT = k.sba("B_cT", [128, DC, 128], BF16, 32)
    h1r = k.sba("B_h1r", [128, 2, D], F32, 36)
    h1Tb = k.sba("B_h1T", [128, DC, 128], BF16, 44)

    sC = [k.sem("b_c0"), k.sem("b_c1")]
    sX = [k.sem("b_x0"), k.sem("b_x1")]
    sLg = k.sem("b_lng"); sW = k.sem("b_wld")
    smq = k.sem("b_mm"); smt = k.sem("b_mt")
    sS1 = [k.sem("b_s10"), k.sem("b_s11")]
    sST = k.sem("b_sT")

    i = k.dma("gpsimd", Wo[:], k.inp["f_Wo"].rearrange("(c p) f -> p c f", p=128))
    tW = sW.inc(i, 16)
    i = k.dma("sync", lng[:, 0, :], k.inp["f_lng"][:, 0, :])
    sLg.inc(i, 16)
    i = k.dma("sync", lng[:, 1, :], k.inp["f_lnb"][:, 0, :])
    tlg = sLg.inc(i, 16)

    stt = k.st
    for tau in range(NTT):
        sl = tau % 2
        k.w("sync", k.CHV, stt.get(("b_ctr", sl), 0))
        i = k.dma("sync", ctr[:, sl, :], k.ctx_s[tau * 128:(tau + 1) * 128, :])
        tc = sC[sl].inc(i, 16)
        k.w("sync", k.CHV, stt.get(("b_xr1", sl), 0))
        i = k.dma("sync", k.xr1[:, sl, :], k.xn_s[tau * 128:(tau + 1) * 128, :])
        tx = sX[sl].inc(i, 16)
        k.w("vector", sC[sl], tc)
        tct = emit_ct(k, ctr[:, sl, :], cT[:], sl, smt,
                      war=[(smq, stt.get("b_ct_users", 0))])
        stt[("b_ctr", sl)] = tct
        k.w("tensor", k.CHV, tct)
        k.w("tensor", sW, tW)
        for half in range(2):
            ps = k.S01[:, half * 512:(half + 1) * 512]
            k.w("tensor", k.CHV, stt.get(("b_ps", half), 0))
            last = None
            for kc in range(DC):
                ii = k.mm(ps, cT[:, kc, :], Wo[:, kc, half * 512:(half + 1) * 512],
                          start=(kc == 0), stop=(kc == DC - 1))
                if ii is not None:
                    last = ii
            tgm = smq.inc(last)
            k.w("vector", smq, tgm)
            k.w("vector", sX[sl], tx)
            tu = k.dve("tensor_tensor",
                       out=k.xr2[:, sl, half * 512:(half + 1) * 512], in0=ps,
                       in1=k.xr1[:, sl, half * 512:(half + 1) * 512], op=ALU.add)
            stt[("b_ps", half)] = tu
            stt[("b_xr1", sl)] = tu
        stt["b_ct_users"] = smq.n
        k.w("vector", sLg, tlg)
        k.w("vector", sS1[sl], stt.get(("b_h1r", sl), 0))
        tn = emit_ln(k, k.xr2[:, sl, :], k.lnt[:],
                     lng=lng[:, 0, :], lnb=lng[:, 1, :], affout=h1r[:, sl, :])
        k.w("sync", k.CHV, tn)
        i = k.dma("sync", k.h1_s[tau * 128:(tau + 1) * 128, :], h1r[:, sl, :])
        tsp = sS1[sl].inc(i, 16)
        stt[("b_h1r", sl)] = tsp
        the = emit_ct(k, h1r[:, sl, :], h1Tb[:], sl, smt,
                      war=[(sST, stt.get("b_h1T_sp", 0))])
        k.w("sync", k.CHV, the)
        i = k.dma("sync",
                  k.h1T_s.rearrange("c p t -> p c t")[:, :, tau * 128:(tau + 1) * 128],
                  h1Tb[:])
        stt["b_h1T_sp"] = sST.inc(i, 16)
    k.barrier([(sS1[0], sS1[0].n), (sS1[1], sS1[1].n), (sST, sST.n)])


def emit_B2(k, call):
    D, DC, FC, NE, TOK = k.D, k.DC, k.FC, k.NE, k.TOK
    W1 = k.sba("F_w1", [128, DC, k.FF], BF16, 0)
    W2 = k.sba("F_w2", [128, FC, D], BF16, 64)
    hidT = k.sba("F_hid", [128, FC, 256], BF16, 128)
    h1T = k.sba("F_h1T", [128, 2, DC, 256], BF16, 144)
    lng = k.sba("F_lng", [128, 2, D], F32, 152)
    h2Tb = k.sba("F_h2T", [128, DC, 128], BF16, 160)

    sH = [k.sem("f_h0"), k.sem("f_h1")]
    sX = [k.sem("f_x0"), k.sem("f_x1")]
    sW1s = k.sem("f_w1"); sW2s = k.sem("f_w2"); sLg = k.sem("f_lng")
    smq = k.sem("f_mm"); sm2 = k.sem("f_mm2")
    smt = k.sem("f_mt"); smg = k.sem("f_gmm")
    sS2 = [k.sem("f_s20"), k.sem("f_s21")]
    sST = k.sem("f_sT")

    w1src = k.inp["f_Wff1"].rearrange("(c p) f -> p c f", p=128)
    i = k.dma("gpsimd", W1[:, :, 0:2048], w1src[:, :, 0:2048])
    sW1s.inc(i, 16)
    i = k.dma("gpsimd", W1[:, :, 2048:4096], w1src[:, :, 2048:4096])
    tW1 = sW1s.inc(i, 16)
    i = k.dma("gpsimd", W2[:], k.inp["f_Wff2"].rearrange("(c p) f -> p c f", p=128))
    tW2 = sW2s.inc(i, 16)
    i = k.dma("sync", lng[:, 0, :], k.inp["f_lng"][:, 1, :])
    sLg.inc(i, 16)
    i = k.dma("sync", lng[:, 1, :], k.inp["f_lnb"][:, 1, :])
    tlg = sLg.inc(i, 16)

    stt = k.st
    stt[("f_ps", 0)] = 0
    stt[("f_ps", 1)] = 0
    for st_ in range(TOK // 256):
        hpar = st_ % 2
        k.w("sync", smq, stt.get(("f_h1T_users", hpar), 0))
        i = k.dma("sync", h1T[:, hpar],
                  k.h1T_s.rearrange("c p t -> p c t")[:, :, st_ * 256:(st_ + 1) * 256])
        th = sH[hpar].inc(i, 16)
        k.w("tensor", sH[hpar], th)
        k.w("tensor", sW1s, tW1)
        for fc in range(FC):
            r = fc % 2
            ps = k.S01[:, r * 512:r * 512 + 256]
            k.w("tensor", k.CHS, stt[("f_ps", r)])
            last = None
            for kc in range(DC):
                ii = k.mm(ps, W1[:, kc, fc * 128:(fc + 1) * 128], h1T[:, hpar, kc, :],
                          start=(kc == 0), stop=(kc == DC - 1))
                if ii is not None:
                    last = ii
            tg = smq.inc(last)
            k.w("scalar", smq, tg)
            k.w("scalar", sm2, stt.get("f_hid_users", 0))
            stt[("f_ps", r)] = k.act(hidT[:, fc, :], ps, AF.Relu,
                                     bias=k.bff1_sb[:, fc:fc + 1])
        stt[("f_h1T_users", hpar)] = smq.n
        k.w("tensor", k.CHS, max(stt[("f_ps", 0)], stt[("f_ps", 1)]))
        k.w("tensor", sW2s, tW2)
        for it in range(2):
            tau = st_ * 2 + it
            sl = tau % 2
            ps2 = k.S23 if it % 2 == 0 else k.S45
            k.w("sync", k.CHV, stt.get(("f_xr1", sl), 0))
            k.w("sync", sS2[sl], stt.get(("f_xr1s", sl), 0))
            i = k.dma("sync", k.xr1[:, sl, :], k.h1_s[tau * 128:(tau + 1) * 128, :])
            tx = sX[sl].inc(i, 16)
            k.w("tensor", k.CHV, stt.get(("f_ps2", it % 2), 0))
            last = None
            for half in range(2):
                for sl2 in range(FC):
                    ii = k.mm(ps2[:, half * 512:(half + 1) * 512],
                              hidT[:, sl2, it * 128:(it + 1) * 128],
                              W2[:, sl2, half * 512:(half + 1) * 512],
                              start=(sl2 == 0), stop=False)
                    if ii is not None:
                        last = ii
                ii = k.mm(ps2[:, half * 512:(half + 1) * 512], k.ones_sb[:],
                          k.bff2_sb[:, half * 512:(half + 1) * 512],
                          start=False, stop=True)
                if ii is not None:
                    last = ii
            tgm = sm2.inc(last)
            k.w("vector", sm2, tgm)
            k.w("vector", sX[sl], tx)
            k.w("vector", sS2[sl], stt.get(("f_xr2", sl), 0))
            tu2 = k.dve("tensor_tensor", out=k.xr2[:, sl, :], in0=ps2,
                        in1=k.xr1[:, sl, :], op=ALU.add)
            stt[("f_ps2", it % 2)] = tu2
            stt[("f_xr1", sl)] = tu2
            k.w("vector", sLg, tlg)
            tn = emit_ln(k, k.xr2[:, sl, :], k.lnt[:],
                         lng=lng[:, 0, :], lnb=lng[:, 1, :],
                         affout=k.xr1[:, sl, :])
            stt[("f_xr1", sl)] = tn
            k.w("sync", k.CHV, tn)
            i = k.dma("sync", k.h2_s[tau * 128:(tau + 1) * 128, :], k.xr1[:, sl, :])
            th2 = sS2[sl].inc(i, 16)
            stt[("f_xr2", sl)] = th2
            stt[("f_xr1s", sl)] = th2
            the = emit_ct(k, k.xr1[:, sl, :], h2Tb[:], sl, smt,
                          war=[(sST, stt.get("f_h2T_sp", 0)),
                               (smg, stt.get("f_h2T_users", 0))])
            stt[("f_xr1", sl)] = max(stt[("f_xr1", sl)], the)
            k.w("sync", k.CHV, the)
            i = k.dma("sync",
                      k.h2T_s.rearrange("c p t -> p c t")[:, :, tau * 128:(tau + 1) * 128],
                      h2Tb[:])
            stt["f_h2T_sp"] = sST.inc(i, 16)
            stt["f_hid_users"] = sm2.n
            k.w("tensor", k.CHV, max(the, stt.get("f_gate_done", 0)))
            last = None
            for kc in range(DC):
                ii = k.mm(k.GATEP, h2Tb[:, kc, :], k.wg_sb[:, kc, :],
                          start=(kc == 0), stop=(kc == DC - 1))
                if ii is not None:
                    last = ii
            tgm = smg.inc(last)
            stt["f_h2T_users"] = tgm
            k.w("vector", smg, tgm)
            nl = k.st_sb[:, 8:9]; dsum = k.st_sb[:, 9:10]; gate = k.st_sb[:, 10:11]
            eq8 = k.scr8_sb[:, 0:NE]; cum8 = k.scr8_sb[:, NE:2 * NE]
            c18 = k.scr8_sb[:, 2 * NE:3 * NE]; pr8 = k.scr8_sb[:, 3 * NE:4 * NE]
            tv1 = k.dve("reduce_max", out=nl, in_=k.GATEP, axis=AXX, negate=True)
            k.w("scalar", k.CHV, tv1)
            ta1 = k.act(pr8, k.GATEP, AF.Exp, bias=nl, accum_out=dsum)
            k.w("vector", k.CHS, ta1)
            k.dve("reciprocal", out=gate, in_=dsum)
            k.dve("tensor_scalar", out=eq8, in0=k.GATEP, scalar1=nl,
                  scalar2=0.0, op0=ALU.add, op1=ALU.is_equal)
            k.dve("tensor_tensor_scan", out=cum8, data0=k.ones8_sb[:],
                  data1=eq8, initial=0.0, op0=ALU.mult, op1=ALU.add)
            k.dve("tensor_scalar", out=c18, in0=cum8, scalar1=1.0,
                  scalar2=None, op0=ALU.is_equal, op1=ALU.bypass)
            k.dve("tensor_tensor", out=k.mask8_sb[:, tau, :], in0=eq8,
                  in1=c18, op=ALU.mult)
            stt["f_gate_done"] = k.dve(
                "tensor_copy", out=k.gates_sb[:, tau:tau + 1], in_=gate)
    k.barrier([(sS2[0], sS2[0].n), (sS2[1], sS2[1].n), (sST, sST.n),
               (k.CHV, stt["f_gate_done"])])


def emit_C(k, call):
    D, DC, FC, NE, TOK = k.D, k.DC, k.FC, k.NE, k.TOK
    We1 = k.sba("C_w1", [128, DC, k.FF], BF16, 0)
    We2 = k.sba("C_w2", [128, FC, D], BF16, 64)
    hidT = k.sba("C_hid", [128, FC, 512], BF16, 128)
    if "C_h2T" not in k._sb:
        k._sb["C_h2T"] = k.nc.alloc_sbuf_tensor_at(
            "C_h2T", [128, DC, 512], BF16, offset=k.off_xr1).ap()
    h2T = k._sb["C_h2T"]
    eob = k.xr2

    sL = k.sem("c2_ld"); sW1 = k.sem("c2_w1"); sW2 = k.sem("c2_w2")
    sB = k.sem("c2_b")
    smq = k.sem("c2_mm"); sm2 = k.sem("c2_mm2")
    sSC = [k.sem("c2_sc0"), k.sem("c2_sc1")]
    stt = k.st
    stt[("c_ps", 0)] = 0
    stt[("c_ps", 1)] = 0

    def ldw1(e):
        src = k.inp["f_We1"][e].rearrange("(c p) f -> p c f", p=128)
        i = k.dma("gpsimd", We1[:, :, 0:2048], src[:, :, 0:2048])
        sW1.inc(i, 16)
        i = k.dma("gpsimd", We1[:, :, 2048:4096], src[:, :, 2048:4096])
        return sW1.inc(i, 16)

    def ldw2(e):
        i = k.dma("gpsimd", We2[:],
                  k.inp["f_We2"][e].rearrange("(c p) f -> p c f", p=128))
        return sW2.inc(i, 16)

    def ldb(e):
        i = k.dma("gpsimd", k.be1_sb[:, e % 2, :], k.inp["f_be1"][e])
        sB.inc(i, 16)
        i = k.dma("gpsimd", k.be2_sb[:, e % 2, :], k.inp["f_be2"][e])
        return sB.inc(i, 16)

    tw1 = ldw1(0)
    tb = ldb(0)
    for e in range(NE):
        k.w("gpsimd", sm2, stt.get("c_eout_done", 0))
        tw2 = ldw2(e)
        for st_ in range(TOK // 512):
            k.w("sync", smq, stt.get("c_h2T_users", 0))
            i = k.dma("sync", h2T[:],
                      k.h2T_s.rearrange("c p t -> p c t")[:, :, st_ * 512:(st_ + 1) * 512])
            th = sL.inc(i, 16)
            k.w("tensor", sL, th)
            k.w("tensor", sW1, tw1)
            k.w("tensor", sB, tb)
            for fc in range(FC):
                r = fc % 2
                ps = k.S01[:, r * 512:(r + 1) * 512]
                k.w("tensor", k.CHS, stt[("c_ps", r)])
                last = None
                for kc in range(DC):
                    ii = k.mm(ps, We1[:, kc, fc * 128:(fc + 1) * 128], h2T[:, kc, :],
                              start=(kc == 0), stop=(kc == DC - 1))
                    if ii is not None:
                        last = ii
                tg = smq.inc(last)
                k.w("scalar", smq, tg)
                k.w("scalar", sm2, stt.get("c_hid_users", 0))
                stt[("c_ps", r)] = k.act(hidT[:, fc, :], ps, AF.Relu,
                                         bias=k.be1_sb[:, e % 2, fc:fc + 1])
            stt["c_h2T_users"] = smq.n
            k.w("tensor", k.CHS, max(stt[("c_ps", 0)], stt[("c_ps", 1)]))
            k.w("tensor", sW2, tw2)
            for it in range(4):
                tau = st_ * 4 + it
                sl = tau % 2
                ps2 = k.S23 if it % 2 == 0 else k.S45
                k.w("tensor", k.CHV, stt.get(("c_ps2", it % 2), 0))
                last = None
                for half in range(2):
                    for sl2 in range(FC):
                        ii = k.mm(ps2[:, half * 512:(half + 1) * 512],
                                  hidT[:, sl2, it * 128:(it + 1) * 128],
                                  We2[:, sl2, half * 512:(half + 1) * 512],
                                  start=(sl2 == 0), stop=False)
                        if ii is not None:
                            last = ii
                    ii = k.mm(ps2[:, half * 512:(half + 1) * 512], k.ones_sb[:],
                              k.be2_sb[:, e % 2, half * 512:(half + 1) * 512],
                              start=False, stop=True)
                    if ii is not None:
                        last = ii
                tgm = sm2.inc(last)
                k.w("vector", sm2, tgm)
                k.w("vector", sSC[sl], stt.get(("c_eo", sl), 0))
                teo = k.dve("tensor_scalar", out=eob[:, sl, :], in0=ps2,
                            scalar1=k.mg8_sb[:, tau, e:e + 1], scalar2=None,
                            op0=ALU.mult, op1=ALU.bypass)
                stt[("c_ps2", it % 2)] = teo
                k.w("gpsimd", k.CHV, teo)
                i = None
                if k.on("gpsimd") and _MOE_DBG:
                    di = k.eng.indirect_dma_start(
                        out=k.moe_dbg.rearrange("e t d -> (e t) d"),
                        out_offset=IndirectOffsetOnAxis(
                            ap=k.idx_sb[:, tau, e:e + 1], axis=0),
                        in_=eob[:, sl, :], in_offset=None,
                        element_offset=e * k.TOK * k.D,
                        bounds_check=k.st["breg"], oob_is_err=False)
                else:
                    di = None
                if _MOE_DBG:
                    sSC[sl].inc(di, 16)
                if k.on("gpsimd"):
                    i = k.eng.indirect_dma_start(
                        out=k.moe_s,
                        out_offset=IndirectOffsetOnAxis(
                            ap=k.idx_sb[:, tau, e:e + 1], axis=0),
                        in_=eob[:, sl, :], in_offset=None,
                        bounds_check=k.st["breg"], oob_is_err=False)
                stt[("c_eo", sl)] = sSC[sl].inc(i, 16)
            stt["c_eout_done"] = sm2.n
        if e + 1 < NE:
            k.w("gpsimd", smq, stt["c_h2T_users"])
            tw1 = ldw1(e + 1)
            k.w("gpsimd", k.CHS, max(stt[("c_ps", 0)], stt[("c_ps", 1)]))
            tb = ldb(e + 1)
    k.barrier([(sSC[0], sSC[0].n), (sSC[1], sSC[1].n)])


def emit_D(k, call, x_in, x_out):
    D, DC, NTT = k.D, k.DC, k.NTT
    Wout = k.sba("D_wout", [128, DC, D], BF16, 0)
    lng = k.sba("D_lng", [128, 2, D], F32, 16)
    x3r = k.sba("D_x3", [128, 2, D], F32, 24)
    u3r = k.sba("D_u3", [128, 2, D], F32, 32)
    h3r = k.sba("D_h3", [128, 2, D], F32, 40)
    h3Tb = k.sba("D_h3T", [128, DC, 128], BF16, 48)

    s1 = [k.sem("d_a0"), k.sem("d_a1")]
    s2 = [k.sem("d_b0"), k.sem("d_b1")]
    s3 = [k.sem("d_c0"), k.sem("d_c1")]
    sLg = k.sem("d_lng"); sW = k.sem("d_wld")
    smq = k.sem("d_mm"); smt = k.sem("d_mt")
    sSP = [k.sem("d_o0"), k.sem("d_o1")]

    i = k.dma("gpsimd", Wout[:], k.inp["f_Wout"].rearrange("(c p) f -> p c f", p=128))
    tW = sW.inc(i, 16)
    i = k.dma("sync", lng[:, 0, :], k.inp["f_lng"][:, 2, :])
    sLg.inc(i, 16)
    i = k.dma("sync", lng[:, 1, :], k.inp["f_lnb"][:, 2, :])
    tlg = sLg.inc(i, 16)

    stt = k.st
    for tau in range(NTT):
        sl = tau % 2
        k.w("sync", k.CHV, stt.get(("d_xr", sl), 0))
        i = k.dma("sync", k.xr1[:, sl, :], k.h2_s[tau * 128:(tau + 1) * 128, :])
        t1 = s1[sl].inc(i, 16)
        i = k.dma("sync", k.xr2[:, sl, :], k.moe_s[tau * 128:(tau + 1) * 128, :])
        t2 = s2[sl].inc(i, 16)
        k.w("sync", k.CHV, stt.get(("d_x3", sl), 0))
        i = k.dma("sync", x3r[:, sl, :], x_in[tau * 128:(tau + 1) * 128, :])
        t3 = s3[sl].inc(i, 16)
        k.w("vector", s1[sl], t1)
        k.w("vector", s2[sl], t2)
        k.w("vector", sSP[sl], stt.get(("d_u3", sl), 0))
        tu3 = k.dve("tensor_tensor", out=u3r[:, sl, :], in0=k.xr1[:, sl, :],
                    in1=k.xr2[:, sl, :], op=ALU.add)
        stt[("d_xr", sl)] = tu3
        k.w("vector", sLg, tlg)
        emit_ln(k, u3r[:, sl, :], k.lnt[:],
                lng=lng[:, 0, :], lnb=lng[:, 1, :], affout=h3r[:, sl, :])
        the = emit_ct(k, h3r[:, sl, :], h3Tb[:], sl, smt,
                      war=[(smq, stt.get("d_h3T_users", 0))])
        k.w("tensor", k.CHV, the)
        k.w("tensor", sW, tW)
        tf = 0
        for half in range(2):
            ps = k.S01[:, half * 512:(half + 1) * 512]
            k.w("tensor", k.CHV, stt.get(("d_ps", half), 0))
            last = None
            for kc in range(DC):
                ii = k.mm(ps, h3Tb[:, kc, :],
                          Wout[:, kc, half * 512:(half + 1) * 512],
                          start=(kc == 0), stop=(kc == DC - 1))
                if ii is not None:
                    last = ii
            tgm = smq.inc(last)
            k.w("vector", smq, tgm)
            k.w("vector", s3[sl], t3)
            tf = k.dve("scalar_tensor_tensor",
                       out=u3r[:, sl, half * 512:(half + 1) * 512], in0=ps,
                       scalar=0.5, in1=x3r[:, sl, half * 512:(half + 1) * 512],
                       op0=ALU.mult, op1=ALU.add)
            stt[("d_ps", half)] = tf
            stt[("d_x3", sl)] = tf
        stt["d_h3T_users"] = smq.n
        k.w("sync", k.CHV, tf)
        i = k.dma("sync", x_out[tau * 128:(tau + 1) * 128, :], u3r[:, sl, :])
        stt[("d_u3", sl)] = sSP[sl].inc(i, 16)
    k.barrier([(sSP[0], sSP[0].n), (sSP[1], sSP[1].n)])


def emit_ffn(k, call, x_in, x_out):
    rT = emit_rt(k, call)
    for s in range(k.SEQ):
        kT, V = emit_p0(k, call, s, x_in)
        emit_attn(k, call, s, kT, V, rT)
    emit_B(k, call)
    emit_B2(k, call)
    emit_ord(k, call)
    emit_Cs(k, call)
    emit_D(k, call, x_in, x_out)


def emit_mixer(k, x_in, x_out):
    S, D, DC, NT = k.S, k.D, k.DC, k.NT
    Wz = k.sba("M_wz", [128, DC, D], BF16, 0)
    Wh = k.sba("M_wh", [128, DC, D], BF16, 16)
    xnT = k.sba("M_xnT", [128, 2, DC, 512], BF16, 32)
    asb = k.sba("M_a", [128, 2, 512], F32, 48)
    zsb = k.sba("M_z", [128, 2, 512], F32, 52)
    hsb = k.sba("M_h", [128, DC, S], F32, 56)
    hbf = k.sba("M_hbf", [128, DC, S], BF16, 120)
    Wg = k.sba("M_wg", [128, DC, D], BF16, 0)
    gsb = k.sba("M_g", [128, 2, 512], F32, 16)
    mox = k.sba("M_mo", [128, 4, D], F32, 24)

    sX = [k.sem("m_x0"), k.sem("m_x1")]
    sWzh = k.sem("m_wzh"); sWg = k.sem("m_wgl")
    smq = k.sem("m_mm"); smt = k.sem("m_mt")
    sSP = [k.sem("m_o0"), k.sem("m_o1"), k.sem("m_o2"), k.sem("m_o3")]
    stt = k.st

    for s in range(k.SEQ):
        k.w("gpsimd", smq, stt.get("m_wg_users", 0))
        k.w("gpsimd", smt, stt.get("m_gate_tr", 0))
        k.w("gpsimd", k.CHV, stt.get("m_gate_dve", 0))
        k.w("gpsimd", k.CHS, stt.get("m_gate_act", 0))
        for it_ in range(4):
            k.w("gpsimd", sSP[it_], stt.get(("m_mo", it_), 0))
        i = k.dma("gpsimd", Wz[:], k.inp["mx_Wz"].rearrange("(c p) f -> p c f", p=128))
        sWzh.inc(i, 16)
        i = k.dma("gpsimd", Wh[:], k.inp["mx_Wh"].rearrange("(c p) f -> p c f", p=128))
        tWzh = sWzh.inc(i, 16)
        xnT_users = [0, 0]
        for n in range(S // 512):
            par = n % 2
            for it in range(4):
                tau = n * 4 + it
                gtau = s * NT + tau
                slot = tau % 2
                k.w("sync", k.CHV, stt.get(("m_xr1", slot), 0))
                i = k.dma("sync", k.xr1[:, slot, :],
                          x_in[gtau * 128:(gtau + 1) * 128, :])
                tx = sX[slot].inc(i, 16)
                k.w("vector", sX[slot], tx)
                tn = emit_ln(k, k.xr1[:, slot, :], k.xr2[:, slot, :])
                stt[("m_xr1", slot)] = tn
                war_ = [(smq, xnT_users[par])] if it == 0 else []
                if n == 0 and it == 0:
                    war_ += [(sSP[2], stt.get(("m_mo", 2), 0)),
                             (sSP[3], stt.get(("m_mo", 3), 0))]
                emit_ct(k, k.xr2[:, slot, :],
                        xnT[:, par, :, it * 128:(it + 1) * 128], slot, smt,
                        war=war_)
            txn = k.st["ptfree"]
            k.w("tensor", k.CHV, txn)
            k.w("tensor", sWzh, tWzh)
            for oc in range(DC):
                r = oc % 2
                psz = k.S01[:, r * 512:(r + 1) * 512]
                psh = k.S23[:, r * 512:(r + 1) * 512]
                k.w("tensor", k.CHV, stt.get(("m_ps", r), 0))
                lastz = None
                for kc in range(DC):
                    ii = k.mm(psz, Wz[:, kc, oc * 128:(oc + 1) * 128],
                              xnT[:, par, kc, :], start=(kc == 0), stop=(kc == DC - 1))
                    if ii is not None:
                        lastz = ii
                tz = smq.inc(lastz)
                lasth = None
                for kc in range(DC):
                    ii = k.mm(psh, Wh[:, kc, oc * 128:(oc + 1) * 128],
                              xnT[:, par, kc, :], start=(kc == 0), stop=(kc == DC - 1))
                    if ii is not None:
                        lasth = ii
                th_ = smq.inc(lasth)
                k.w("scalar", smq, tz)
                k.w("scalar", k.CHV, stt.get(("m_ab", r), 0))
                k.act(asb[:, r, :], psz, AF.Sigmoid, scale=-1.0,
                      bias=k.nbz_sb[:, oc:oc + 1])
                ta2 = k.act(zsb[:, r, :], psz, AF.Sigmoid, bias=k.bz_sb[:, oc:oc + 1])
                k.w("vector", k.CHS, ta2)
                k.w("vector", smq, th_)
                k.dve("scalar_tensor_tensor", out=zsb[:, r, :], in0=psh,
                      scalar=k.bh_sb[:, oc:oc + 1], in1=zsb[:, r, :],
                      op0=ALU.add, op1=ALU.mult)
                init = 0.0 if n == 0 else hsb[:, oc, n * 512 - 1:n * 512]
                tsc = k.dve("tensor_tensor_scan",
                            out=hsb[:, oc, n * 512:(n + 1) * 512],
                            data0=asb[:, r, :], data1=zsb[:, r, :],
                            initial=init, op0=ALU.mult, op1=ALU.add)
                stt[("m_ps", r)] = tsc
                stt[("m_ab", r)] = tsc
            xnT_users[par] = smq.n
        tcast = 0
        for oc in range(DC):
            tcast = k.dve("tensor_copy", out=hbf[:, oc, :], in_=hsb[:, oc, :])
        k.w("gpsimd", smq, smq.n)
        i = k.dma("gpsimd", Wg[:], k.inp["mx_Wg"].rearrange("(c p) f -> p c f", p=128))
        tWg = sWg.inc(i, 16)
        for n in range(S // 512):
            for oc in range(DC):
                r = oc % 2
                psz = k.S45[:, r * 512:(r + 1) * 512]
                k.w("tensor", k.CHV, max(tcast, stt.get(("m_g", r), 0)))
                k.w("tensor", sWg, tWg)
                last = None
                for kc in range(DC):
                    ii = k.mm(psz, Wg[:, kc, oc * 128:(oc + 1) * 128],
                              hbf[:, kc, n * 512:(n + 1) * 512],
                              start=(kc == 0), stop=(kc == DC - 1))
                    if ii is not None:
                        last = ii
                tg = smq.inc(last)
                stt["m_wg_users"] = tg
                k.w("scalar", smq, tg)
                k.w("scalar", k.CHV, stt.get(("m_g", r), 0))
                tga = k.act(gsb[:, r, :], psz, AF.Sigmoid, bias=k.bg_sb[:, oc:oc + 1])
                k.w("vector", k.CHS, tga)
                tgh = k.dve("tensor_tensor", out=gsb[:, r, :], in0=gsb[:, r, :],
                            in1=hsb[:, oc, n * 512:(n + 1) * 512], op=ALU.mult)
                k.w("tensor", k.CHV, tgh)
                te = 0
                for it in range(4):
                    tau = n * 4 + it
                    k.w("tensor", k.CHV, k.st.get("ptfree", 0))
                    ii = k.tr(k.PTF, gsb[:, r, it * 128:(it + 1) * 128], fp32=True)
                    tt = smt.inc(ii)
                    k.w("vector", smt, tt)
                    if oc == 0:
                        k.w("vector", sSP[it], stt.get(("m_mo", it), 0))
                    te = k.dve("tensor_copy",
                               out=mox[:, it, oc * 128:(oc + 1) * 128], in_=k.PTF)
                    k.st["ptfree"] = te
                stt[("m_g", r)] = te
            for it in range(4):
                tau = n * 4 + it
                gtau = s * NT + tau
                slot = tau % 2
                k.w("sync", k.CHV, stt.get(("m_xr1", slot), 0))
                i = k.dma("sync", k.xr1[:, slot, :],
                          x_in[gtau * 128:(gtau + 1) * 128, :])
                tx = sX[slot].inc(i, 16)
                k.w("vector", sX[slot], tx)
                tmo = k.dve("tensor_tensor", out=mox[:, it, :], in0=mox[:, it, :],
                            in1=k.xr1[:, slot, :], op=ALU.add)
                stt[("m_xr1", slot)] = tmo
                k.w("sync", k.CHV, tmo)
                i = k.dma("sync", x_out[gtau * 128:(gtau + 1) * 128, :],
                          mox[:, it, :])
                stt[("m_mo", it)] = sSP[it].inc(i, 16)
        stt["m_gate_dve"] = k.CHV.n
        stt["m_gate_act"] = k.CHS.n
        stt["m_gate_tr"] = smt.n
    k.barrier([(sSP[0], sSP[0].n), (sSP[1], sSP[1].n),
               (sSP[2], sSP[2].n), (sSP[3], sSP[3].n)])


_NSTAGE = 3


def emit_program(k):
    emit_stage0(k)
    emit_ffn(k, 0, k.inp["x"], k.xA)
    if _NSTAGE >= 2:
        emit_mixer(k, k.xA, k.xB)
    if _NSTAGE >= 3:
        emit_ffn(k, 1, k.xB, k.y)


def _pe_table(S, D):
    pos = np.arange(S - 1, -1, -1, dtype=np.float64)
    inv_freq = 1.0 / 10000.0 ** (np.arange(0, D, 2, dtype=np.float64) / D)
    sinusoid = pos[:, None] * inv_freq[None, :]
    pe = np.concatenate([np.sin(sinusoid), np.cos(sinusoid)], axis=-1)
    return np.ascontiguousarray(pe.T.astype(np.float32))


def prep_inputs(cfg, inputs):
    B, S, D = cfg["B"], cfg["S"], cfg["D"]
    NE = cfg["NE"]
    SEQ = B // NCORES
    TOK = SEQ * S
    NTT = TOK // 128
    f32 = np.float32

    def asnp(v, dt=f32):
        return np.ascontiguousarray(np.asarray(v), dtype=dt)

    def vec_pc(v):
        return np.ascontiguousarray(v.reshape(-1, 128).T)

    Wz = asnp(inputs["L0_attn_Wz"])
    Wzf = np.ascontiguousarray(np.transpose(Wz, (1, 0, 2)).reshape(D, D))
    Wh = asnp(inputs["L0_attn_Wh"])
    Whf = np.ascontiguousarray(np.transpose(Wh, (1, 0, 2)).reshape(D, D))
    lng = np.ascontiguousarray(np.stack(
        [np.broadcast_to(asnp(inputs[f"L0_ffn_ln{i}_g"]), (128, D))
         for i in (1, 2, 3)], axis=1))
    lnb = np.ascontiguousarray(np.stack(
        [np.broadcast_to(asnp(inputs[f"L0_ffn_ln{i}_b"]), (128, D))
         for i in (1, 2, 3)], axis=1))
    be1 = asnp(inputs["L0_ffn_be1"])
    be1pc = np.ascontiguousarray(
        np.stack([vec_pc(be1[e]) for e in range(NE)], axis=0))
    ident = np.eye(128, dtype=ml_dtypes.bfloat16)
    identf = np.eye(128, dtype=f32)
    cm = np.zeros((128, 512), dtype=ml_dtypes.bfloat16)
    cm[:, 384:512] = np.where(np.arange(128)[None, :] <= np.arange(128)[:, None],
                              0.0, NEGINF).astype(ml_dtypes.bfloat16)
    rowidx = (BIGIDX + (np.arange(NTT)[None, :] * 128
                        + np.arange(128)[:, None])).astype(f32)
    trilS = np.triu(np.ones((128, 128)), 1).astype(ml_dtypes.bfloat16)
    iota128 = np.arange(128, dtype=f32)[:, None]
    thr8 = np.tile((np.arange(8, dtype=f32) * 512.0)[None, :], (8, 1))
    iotaU = (np.arange(16, dtype=f32) * 512.0)[None, :]
    rowplain = (np.arange(NTT)[None, :] * 128
                + np.arange(128)[:, None]).astype(f32)
    UB = PAD // 128
    ordinit = np.zeros((128, 2 * UB), dtype=f32)
    ordinit[:, 0::2] = (BIGIDX + np.arange(UB)[None, :] * 128
                        + np.arange(128)[:, None])
    shared = {
        "mx_Wz": Wzf, "mx_Wh": Whf, "mx_Wg": asnp(inputs["L0_attn_Wg"]),
        "mx_bz": vec_pc(asnp(inputs["L0_attn_bz"]).reshape(-1)),
        "mx_bh": vec_pc(asnp(inputs["L0_attn_bh"]).reshape(-1)),
        "mx_bg": vec_pc(asnp(inputs["L0_attn_bg"]).reshape(-1)),
        "f_Wqkv": asnp(inputs["L0_ffn_Wqkv"]), "f_Wo": asnp(inputs["L0_ffn_Wo"]),
        "f_Wr": asnp(inputs["L0_ffn_Wr"]),
        "f_ub": vec_pc(asnp(inputs["L0_ffn_u_bias"]).reshape(-1)),
        "f_vb": vec_pc(asnp(inputs["L0_ffn_v_bias"]).reshape(-1)),
        "f_lng": lng, "f_lnb": lnb,
        "f_Wff1": asnp(inputs["L0_ffn_Wff1"]),
        "f_bff1": vec_pc(asnp(inputs["L0_ffn_bff1"]).reshape(-1)),
        "f_Wff2": asnp(inputs["L0_ffn_Wff2"]),
        "f_bff2": asnp(inputs["L0_ffn_bff2"]).reshape(1, D),
        "f_Wg": asnp(inputs["L0_ffn_Wg"]),
        "f_We1": asnp(inputs["L0_ffn_We1"]), "f_be1": be1pc,
        "f_We2": asnp(inputs["L0_ffn_We2"]),
        "f_be2": asnp(inputs["L0_ffn_be2"]).reshape(NE, 1, D),
        "f_Wout": asnp(inputs["L0_ffn_Wout"]),
        "peT": _pe_table(S, D),
        "ident": ident, "identf": identf, "cmask": cm, "rowidx": rowidx,
        "trilS": trilS, "iota128": iota128, "thr8": thr8, "iotaU": iotaU,
        "rowplain": rowplain, "ordinit": ordinit,
    }
    x = asnp(inputs["L0_attn_x"])
    in_maps = []
    for c in range(NCORES):
        m = dict(shared)
        m["x"] = np.ascontiguousarray(x[c * SEQ:(c + 1) * SEQ].reshape(TOK, D))
        in_maps.append(m)
    return in_maps


def build(cfg):
    k = KB(cfg)
    with k.nc.Block() as block:
        for en in KB.ENGINES:
            def mk(en=en):
                def f(eng):
                    k.begin_pass(en, eng)
                    emit_program(k)
                return f
            getattr(block, en)(mk())
    return k


def kernel(**inputs):
    cfg = dict(B=16, S=2048, D=1024, NH=16, DH=64, FF=4096, NE=8)
    in_maps = prep_inputs(cfg, inputs)
    k = build(cfg)
    res = run_bass_kernel_spmd(k.nc, in_maps, core_ids=list(range(NCORES)))
    SEQ = cfg["B"] // NCORES
    out = np.concatenate(
        [r["y"].reshape(SEQ, cfg["S"], cfg["D"]) for r in res.results], axis=0)
    return out


# version 16
# speedup vs baseline: 1.4976x; 1.0223x over previous
import numpy as np
import ml_dtypes

import bass_rust
import concourse.bass as bass
import concourse.mybir as mybir
from concourse.bass import IndirectOffsetOnAxis
from concourse.bass_utils import run_bass_kernel_spmd

F32 = mybir.dt.float32
BF16 = mybir.dt.bfloat16
U32 = mybir.dt.uint32
AF = mybir.ActivationFunctionType
ALU = mybir.AluOpType
AXX = mybir.AxisListType.X

NCORES = 8
BIGIDX = 2097152.0
EPS = 1e-5
NEGINF = -1.0e30
NU_MOE = 15
PAD = NU_MOE * 512

_DBG = set()
_MOE_DBG = False


class Sem:
    __slots__ = ("h", "n")

    def __init__(self, h):
        self.h = h
        self.n = 0

    def inc(self, inst, k=1):
        if inst is not None:
            inst.then_inc(self.h, k)
        self.n += k
        return self.n


class KB:
    ENGINES = ("sync", "scalar", "vector", "tensor", "gpsimd")

    def __init__(self, cfg):
        self.cfg = cfg
        self.B, self.S, self.D = cfg["B"], cfg["S"], cfg["D"]
        self.NH, self.DH, self.FF, self.NE = cfg["NH"], cfg["DH"], cfg["FF"], cfg["NE"]
        self.SEQ = self.B // NCORES
        self.TOK = self.SEQ * self.S
        self.NT = self.S // 128
        self.NTT = self.TOK // 128
        self.DC = self.D // 128
        self.FC = self.FF // 128
        self.SEGW = 1024

        nc = bass.Bass("TRN2", target_bir_lowering=False)
        self.nc = nc
        self._sb = {}
        self._sems = {}
        self.en = None
        self.eng = None
        self.st = {}

        D, S, FF, NE, TOK, DC, FC, NT, NTT = (
            self.D, self.S, self.FF, self.NE, self.TOK, self.DC, self.FC,
            self.NT, self.NTT)

        self.inp = {}

        def di(name, shape, dt=F32):
            self.inp[name] = nc.dram_tensor(name, shape, dt, kind="ExternalInput").ap()

        di("x", [TOK, D])
        di("mx_Wz", [D, D]); di("mx_Wh", [D, D]); di("mx_Wg", [D, D])
        di("mx_bz", [128, DC]); di("mx_bh", [128, DC]); di("mx_bg", [128, DC])
        di("f_Wqkv", [D, 3 * D]); di("f_Wo", [D, D]); di("f_Wr", [D, D])
        di("f_ub", [128, DC]); di("f_vb", [128, DC])
        di("f_lng", [128, 3, D]); di("f_lnb", [128, 3, D])
        di("f_Wff1", [D, FF]); di("f_bff1", [128, FC])
        di("f_Wff2", [FF, D]); di("f_bff2", [1, D])
        di("f_Wg", [D, NE])
        di("f_We1", [NE, D, FF]); di("f_be1", [NE, 128, FC])
        di("f_We2", [NE, FF, D]); di("f_be2", [NE, 1, D])
        di("f_Wout", [D, D])
        di("peT", [D, S])
        di("ident", [128, 128], BF16)
        di("identf", [128, 128])
        di("cmask", [128, 512], BF16)
        di("rowidx", [128, NTT])
        di("trilS", [128, 128], BF16)
        di("iota128", [128, 1])
        di("thr8", [8, 8])
        di("iotaU", [1, 16])
        di("rowplain", [128, NTT])
        di("ordinit", [128, 2 * (PAD // 128)])

        self.y = nc.dram_tensor("y", [TOK, D], F32, kind="ExternalOutput").ap()

        def scr(name, shape, dt=F32):
            kind = "ExternalOutput" if name in _DBG else "Internal"
            return nc.dram_tensor(name, shape, dt, kind=kind).ap()

        self.xA = scr("xA", [TOK, D])
        self.xB = scr("xB", [TOK, D])
        self.xn_s = scr("xn_s", [TOK, D])
        self.qT_s = scr("qT_s", [self.SEQ * NT, 128, DC * 128], BF16)
        self.ctx_s = scr("ctx_s", [TOK, D])
        self.h1_s = scr("h1_s", [TOK, D])
        self.h1T_s = scr("h1T_s", [DC, 128, TOK], BF16)
        self.h2_s = scr("h2_s", [TOK, D])
        self.h2T_s = scr("h2T_s", [DC, 128, TOK], BF16)
        self.moe_s = scr("moe_s", [TOK, D])
        self.G_s = scr("G_s", [2, 128 * S], BF16)
        self.moe_dbg = scr("moe_dbg", [NE, TOK, D]) if _MOE_DBG else None
        self.We1b = scr("We1b", [NE, 8, 128, FF], BF16)
        self.We2b = scr("We2b", [NE, 8, 128, FF], BF16)
        self.ord = scr("ord_s", [PAD, 2])
        self.cnt_d = scr("cnt_d", [1, 2 * NTT * NE])
        self.B_d = scr("B_d", [1, NTT * NE])
        self.padd_d = scr("padd_d", [NE, 1])
        self.pb_d = scr("pb_d", [1, 16])

        self.S01 = nc.alloc_psum_tensor("S01", [128, 1024], F32).ap()
        self.S23 = nc.alloc_psum_tensor("S23", [128, 1024], F32).ap()
        self.S45 = nc.alloc_psum_tensor("S45", [128, 1024], F32).ap()
        self.S67 = nc.alloc_psum_tensor("S67", [128, 1024], F32).ap()
        self.PT = self.S67[:, 0:512].bitcast(BF16)
        self.PTF = self.S67[:, 0:128]
        self.CTXP = self.S67[:, 512:576]
        self.CTXP2 = self.S67[:, 576:640]
        self.GATEP = self.S67[:, 512:512 + NE]

        A = nc.alloc_sbuf_tensor

        def ap_(name, shape, dt=F32):
            return A(name, shape, dt).ap()

        self.ident_sb = ap_("ident_sb", [128, 128], BF16)
        self.identf_sb = ap_("identf_sb", [128, 128])
        self.cmask_sb = ap_("cmask_sb", [128, 512], BF16)
        self.rowidx_sb = ap_("rowidx_sb", [128, NTT])
        self.ub_sb = ap_("ub_sb", [128, DC])
        self.vb_sb = ap_("vb_sb", [128, DC])
        self.bz_sb = ap_("bz_sb", [128, DC])
        self.nbz_sb = ap_("nbz_sb", [128, DC])
        self.bh_sb = ap_("bh_sb", [128, DC])
        self.bg_sb = ap_("bg_sb", [128, DC])
        self.bff1_sb = ap_("bff1_sb", [128, FC])
        self.be1_sb = ap_("be1_sb", [128, 2, FC])
        self.bff2_sb = ap_("bff2_sb", [1, D], BF16)
        self.be2_sb = ap_("be2_sb", [1, 2, D], BF16)
        self.wg_sb = ap_("wg_sb", [128, DC, NE], BF16)
        self.ones_sb = ap_("ones_sb", [1, 128], BF16)
        self.ones8_sb = ap_("ones8_sb", [128, NE])
        self.st_sb = ap_("st_sb", [128, 64])
        self.mx2_sb = ap_("mx2_sb", [128, 4])
        self.nmx_sb = ap_("nmx_sb", [128, 2])
        self.den_sb = ap_("den_sb", [128, 2])
        self.rden_sb = ap_("rden_sb", [128, 2])
        self.epsv = ap_("epsv", [128, 1])
        self.mask8_sb = ap_("mask8_sb", [128, NTT, NE])
        self.mg8_sb = ap_("mg8_sb", [128, NTT, NE])
        self.idx_sb = ap_("idx_sb", [128, NTT, NE], U32)
        self.scr8_sb = ap_("scr8_sb", [128, 4 * NE])
        self.off_xr1 = nc.sbuf_base
        self.xr1 = ap_("xr1", [128, 2, D])
        self.off_xr2 = nc.sbuf_base
        self.xr2 = ap_("xr2", [128, 2, D])
        self.xnb = ap_("xnb", [128, 2, D], BF16)
        self.off_sq = nc.sbuf_base
        self.sq = ap_("sq", [128, D], BF16)
        self.off_lnt = nc.sbuf_base
        self.lnt = ap_("lnt", [128, D])
        self.gates_sb = ap_("gates_sb", [128, NTT])
        self.widx_u32 = ap_("widx_u32", [128, 16], U32)
        self.wq_u32 = ap_("wq_u32", [128, 8, 16], U32)
        self.uecol_u32 = ap_("uecol_u32", [128, 16], U32)
        self.be1g = ap_("be1g", [128, FC])
        self.osb = ap_("osb", [128, 2, 4])
        self.oidx_u32 = ap_("oidx_u32", [128, 8], U32)
        self.gcol = ap_("gcol", [128, 8])

        self.arena0 = nc.sbuf_base
        self.arena_kb = (nc.sbuf_top - nc.sbuf_base) // 1024
        assert self.arena_kb >= 168, f"arena too small: {self.arena_kb}KB"

    def sem(self, name):
        s = self._sems.get(name)
        if s is None:
            s = Sem(self.nc.alloc_semaphore(name))
            self._sems[name] = s
        return s

    @property
    def CHV(self):
        return self.sem("ch_vector")

    @property
    def CHS(self):
        return self.sem("ch_scalar")

    @property
    def CHG(self):
        return self.sem("ch_gpsimd")

    def sba(self, name, shape, dt, off_kb):
        t = self._sb.get(name)
        if t is None:
            t = self.nc.alloc_sbuf_tensor_at(
                name, shape, dt, offset=self.arena0 + off_kb * 1024).ap()
            self._sb[name] = t
        return t

    def begin_pass(self, en, eng):
        self.en = en
        self.eng = eng
        self.st = {}
        for s in self._sems.values():
            s.n = 0

    def on(self, en):
        return self.en == en

    def w(self, en, sem, val):
        if val > 0 and self.en == en:
            self.eng.wait_ge(sem.h, val)

    def barrier(self, deps):
        for sem, val in deps:
            if val > 0:
                self.eng.wait_ge(sem.h, val)

    def mm(self, out, lhsT, rhs, start=True, stop=True):
        if self.en == "tensor":
            return self.eng.matmul(out, lhsT, rhs, start=start, stop=stop)
        return None

    def tr(self, out, in_, fp32=False):
        if self.en == "tensor":
            return self.eng.transpose(
                out, in_, self.identf_sb if fp32 else self.ident_sb)
        return None

    def _chain_op(self, en_name, make):
        s = self.sem("ch_" + en_name)
        if s.n > 0 and self.en == en_name:
            self.eng.wait_ge(s.h, s.n)
        inst = make(self.eng) if self.en == en_name else None
        return s.inc(inst)

    def act(self, out, in_, func, **kw):
        return self._chain_op("scalar", lambda e: e.activation(out, in_, func, **kw))

    def dve(self, meth, *a, **kw):
        return self._chain_op("vector", lambda e: getattr(e, meth)(*a, **kw))

    def gp(self, meth, *a, **kw):
        return self._chain_op("gpsimd", lambda e: getattr(e, meth)(*a, **kw))

    def dma(self, q, out, in_, **kw):
        if self.en == q:
            return self.eng.dma_start(out=out, in_=in_, **kw)
        return None


def emit_ln(k, xin, xout, lng=None, lnb=None, affout=None):
    D = k.D
    m = k.st_sb[:, 0:1]; msq = k.st_sb[:, 1:2]; va = k.st_sb[:, 2:3]
    lv = k.st_sb[:, 3:4]; rstd = k.st_sb[:, 4:5]; nmr = k.st_sb[:, 5:6]
    ssum = k.st_sb[:, 6:7]; ssq = k.st_sb[:, 7:8]
    k.dve("scalar_tensor_tensor", out=k.sq[:], in0=xin, scalar=1.0,
          in1=xin, op0=ALU.mult, op1=ALU.mult, accum_out=ssq)
    k.dve("reduce_sum", out=ssum, in_=xin, axis=AXX)
    k.dve("tensor_scalar", out=m, in0=ssum, scalar1=1.0 / D, scalar2=None,
          op0=ALU.mult, op1=ALU.bypass)
    k.dve("tensor_tensor", out=msq, in0=m, in1=m, op=ALU.mult)
    tv = k.dve("tensor_scalar", out=va, in0=ssq, scalar1=1.0 / D, scalar2=msq,
               op0=ALU.mult, op1=ALU.subtract)
    k.w("scalar", k.CHV, tv)
    k.act(lv, va, AF.Ln, bias=k.epsv[:])
    ta = k.act(rstd, lv, AF.Exp, scale=-0.5)
    k.w("vector", k.CHS, ta)
    k.dve("scalar_tensor_tensor", out=nmr, in0=m, scalar=-1.0, in1=rstd,
          op0=ALU.mult, op1=ALU.mult)
    tn = k.dve("tensor_scalar", out=xout, in0=xin, scalar1=rstd, scalar2=nmr,
               op0=ALU.mult, op1=ALU.add)
    if lng is not None:
        k.dve("tensor_tensor", out=affout, in0=xout, in1=lng, op=ALU.mult)
        tn = k.dve("tensor_tensor", out=affout, in0=affout, in1=lnb, op=ALU.add)
    return tn


def emit_ct(k, src, dst, par, smt, war=()):
    DC = k.DC
    tc = k.dve("tensor_copy", out=k.xnb[:, par, :], in_=src)
    k.w("tensor", k.CHV, tc)
    k.w("tensor", k.CHV, k.st.get("ptfree", 0))
    last = None
    for c in range(DC):
        ii = k.tr(k.PT[:, c * 128:(c + 1) * 128],
                  k.xnb[:, par, :].rearrange("p (c f) -> p c f", c=DC)[:, c, :])
        if ii is not None:
            last = ii
    tt = smt.inc(last)
    k.w("vector", smt, tt)
    for sem, val in war:
        k.w("vector", sem, val)
    te = k.dve("tensor_copy", out=dst,
               in_=k.PT.rearrange("p (c f) -> p c f", c=DC))
    k.st["ptfree"] = te
    return te


def emit_stage0(k):
    sL = k.sem("c_ld")
    sG = k.sem("c_ldg")
    loads = [
        (k.ident_sb[:], k.inp["ident"]), (k.identf_sb[:], k.inp["identf"]),
        (k.cmask_sb[:], k.inp["cmask"]), (k.rowidx_sb[:], k.inp["rowidx"]),
        (k.ub_sb[:], k.inp["f_ub"]), (k.vb_sb[:], k.inp["f_vb"]),
        (k.bz_sb[:], k.inp["mx_bz"]), (k.bh_sb[:], k.inp["mx_bh"]),
        (k.bg_sb[:], k.inp["mx_bg"]), (k.bff1_sb[:], k.inp["f_bff1"]),
    ]
    t = 0
    for out, in_ in loads:
        i = k.dma("sync", out, in_)
        t = sL.inc(i, 16)
    i = k.dma("gpsimd", k.bff2_sb[:], k.inp["f_bff2"])
    sG.inc(i, 16)
    i = k.dma("gpsimd", k.wg_sb[:],
              k.inp["f_Wg"].rearrange("(c p) e -> p c e", p=128))
    tg = sG.inc(i, 16)
    k.w("gpsimd", sG, tg)
    k.gp("memset", k.ones_sb[:], 1.0)
    k.gp("memset", k.ones8_sb[:], 1.0)
    tm = k.gp("memset", k.epsv[:], EPS)
    if k.on("gpsimd"):
        k.st["breg"] = k.eng.to_reg(k.TOK - 1)
        k.st["wreg"] = k.eng.to_reg(k.NE * 128 - 1)
        k.st["wreg2"] = k.eng.to_reg(k.NE * 1024 - 1)
        k.st["ereg"] = k.eng.to_reg(k.NE - 1)
        k.st["oreg"] = k.eng.to_reg(PAD - 1)
    k.w("vector", sL, t)
    tv = k.dve("tensor_scalar", out=k.nbz_sb[:], in0=k.bz_sb[:], scalar1=-1.0,
               scalar2=None, op0=ALU.mult, op1=ALU.bypass)
    k.barrier([(sL, t), (sG, tg), (k.CHG, tm), (k.CHV, tv)])


def emit_wconv(k):
    sWC = k.sem("wc")
    t = 0
    for e in range(k.NE):
        s1 = k.inp["f_We1"][e].rearrange("(c p) f -> p c f", p=128)
        s2 = k.inp["f_We2"][e].rearrange("(c p) f -> p c f", p=128)
        for g in range(8):
            i = k.dma("gpsimd", k.We1b[e, g], s1[:, g, :])
            t = sWC.inc(i, 16)
            i = k.dma("gpsimd",
                      k.We2b[e, g].rearrange("p (c f) -> p c f", c=4),
                      s2[:, g * 4:(g + 1) * 4, :])
            t = sWC.inc(i, 16)
    k.st["wconv_t"] = t


def emit_ord(k, call):
    D, DC, FC, NE, NTT, TOK = k.D, k.DC, k.FC, k.NE, k.NTT, k.TOK
    UB = PAD // 128
    trilS = k.sba("O_tril", [128, 128], BF16, 0)
    ordini = k.sba("O_ordini", [128, 2 * UB], F32, 1)
    iota = k.sba("O_iota", [128, 1], F32, 2)
    thr8 = k.sba("O_thr8", [8, 8], F32, 24)
    iotaU = k.sba("O_iotaU", [1, 16], F32, 25)
    onesf = k.sba("O_onesf", [1, 128], F32, 3)
    onesb = k.sba("O_onesb", [128, 1], BF16, 26)
    ones832 = k.sba("O_ones", [8, 32], F32, 27)
    rowpl = k.sba("O_rowpl", [128, NTT], F32, 4)
    maskb = k.sba("O_maskb", [128, NTT * NE], BF16, 5)
    cntrow = k.sba("O_cntrow", [1, NTT * NE], F32, 6)
    Brow = k.sba("O_Brow", [1, NTT * NE], F32, 8)
    ranks = k.sba("O_ranks", [128, NTT * NE], F32, 10)
    ptall = k.sba("O_ptall", [128, NTT * NE], F32, 12)
    pt = k.sba("O_pt", [128, NTT], F32, 14)
    pcol = k.sba("O_pcol", [128, NTT], U32, 15)
    ordsrc = k.sba("O_osrc", [128, NTT, 2], F32, 16)
    cntT = k.sba("O_cntT", [8, 32], F32, 17)
    sh = k.sba("O_sh", [8, 32], F32, 18)
    xbase = k.sba("O_xbase", [8, 32], F32, 19)
    ge8 = k.sba("O_ge8", [8, 8], F32, 20)
    small = k.sba("O_small", [8, 8], F32, 21)
    rows1 = k.sba("O_rows1", [1, 64], F32, 22)
    uecolf = k.sba("O_uecolf", [128, 16], F32, 23)

    sL = k.sem("o_ld")
    sB = k.sem("o_bounce")
    smq = k.sem("o_mm")
    sSC = k.sem("o_sc")

    c_col = small[0:8, 0:1]
    pad_col = small[0:8, 1:2]
    pad_row = rows1[0:1, 0:8]
    cumpad = rows1[0:1, 8:16]
    pbase = rows1[0:1, 16:24]
    ue_row = rows1[0:1, 24:40]

    loads = [
        (trilS[:], k.inp["trilS"]), (ordini[:], k.inp["ordinit"]),
        (iota[:], k.inp["iota128"]), (thr8[0:8, :], k.inp["thr8"]),
        (iotaU[0:1, :], k.inp["iotaU"]), (rowpl[:], k.inp["rowplain"]),
    ]
    tld = 0
    for out, in_ in loads:
        i = k.dma("sync", out, in_)
        tld = sL.inc(i, 16)
    tms = k.gp("memset", onesf[0:1, :], 1.0)
    k.gp("memset", onesb[:], 1.0)
    tms = k.gp("memset", ones832[0:8, :], 1.0)
    k.w("sync", sL, tld)
    i = k.dma("sync", k.ord.rearrange("(u p) c -> p u c", p=128),
              ordini[:].rearrange("p (u c) -> p u c", c=2))
    tpre = sL.inc(i, 16)

    tmb = k.dve("tensor_copy", out=maskb[:],
                in_=k.mask8_sb.rearrange("p t e -> p (t e)"))
    k.w("tensor", k.CHV, tmb)
    k.w("tensor", sL, tld)
    k.w("tensor", k.CHG, tms)
    i1 = k.mm(k.S01[:, 0:NTT * NE], trilS[:], maskb[:], start=True, stop=True)
    i2 = k.mm(k.S23[0:1, 0:NTT * NE].rearrange("o (e t) -> o e t", e=NE),
              onesb[:],
              maskb.rearrange("p (t e) -> p e t", e=NE), start=True, stop=True)
    tmm = smq.inc(i2)
    k.w("vector", smq, tmm)
    k.dve("tensor_copy", out=ranks[:], in_=k.S01[:, 0:NTT * NE])
    tcv = k.dve("tensor_copy", out=cntrow[0:1, :], in_=k.S23[0:1, 0:NTT * NE])
    k.w("sync", k.CHV, tcv)
    i = k.dma("sync", k.cnt_d[:, 0:NTT * NE], cntrow[0:1, :])
    tb = sB.inc(i, 16)
    k.w("sync", sB, tb)
    i = k.dma("sync", cntT[0:8, :],
              k.cnt_d[:, 0:NTT * NE].rearrange("o (e t) -> (o e) t", e=NE))
    tb = sB.inc(i, 16)
    k.w("vector", sB, tb)
    k.w("vector", k.CHG, tms)
    k.dve("memset", sh[0:8, 0:1], 0.0)
    k.dve("tensor_copy", out=sh[0:8, 1:32], in_=cntT[0:8, 0:31])
    k.dve("tensor_tensor_scan", out=xbase[0:8, :], data0=ones832[0:8, :],
          data1=sh[0:8, :], initial=0.0, op0=ALU.mult, op1=ALU.add)
    k.dve("reduce_sum", out=c_col, in_=cntT[0:8, :], axis=AXX)
    k.dve("tensor_scalar", out=ge8[0:8, :], in0=thr8[0:8, :], scalar1=c_col,
          scalar2=None, op0=ALU.is_lt, op1=ALU.bypass)
    k.dve("reduce_sum", out=pad_col, in_=ge8[0:8, :], axis=AXX)
    tpc = k.dve("tensor_scalar", out=pad_col, in0=pad_col, scalar1=512.0,
                scalar2=None, op0=ALU.mult, op1=ALU.bypass)
    k.w("sync", k.CHV, tpc)
    i = k.dma("sync", k.padd_d, pad_col)
    tb = sB.inc(i, 16)
    k.w("sync", sB, tb)
    i = k.dma("sync", pad_row, k.padd_d.rearrange("e o -> o e"))
    tb = sB.inc(i, 16)
    k.w("vector", sB, tb)
    k.dve("tensor_tensor_scan", out=cumpad, data0=ones832[0:1, 0:8],
          data1=pad_row, initial=0.0, op0=ALU.mult, op1=ALU.add)
    k.dve("tensor_tensor", out=pbase, in0=cumpad, in1=pad_row, op=ALU.subtract)
    k.dve("memset", ue_row, 0.0)
    tue = 0
    for e in range(NE):
        tue = k.dve("scalar_tensor_tensor", out=ue_row, in0=iotaU[0:1, 0:16],
                    scalar=cumpad[0:1, e:e + 1], in1=ue_row,
                    op0=ALU.is_ge, op1=ALU.add)
    k.w("sync", k.CHV, tue)
    i = k.dma("sync", k.pb_d[:, 0:8], pbase)
    tb = sB.inc(i, 16)
    k.w("sync", sB, tb)
    i = k.dma("sync", pad_col, k.pb_d.rearrange("o f -> f o")[0:8, :])
    tb = sB.inc(i, 16)
    k.w("vector", sB, tb)
    tB = k.dve("tensor_scalar", out=xbase[0:8, :], in0=xbase[0:8, :],
               scalar1=pad_col, scalar2=None, op0=ALU.add, op1=ALU.bypass)
    k.w("sync", k.CHV, tB)
    i = k.dma("sync", k.B_d.rearrange("o (t e) -> e o t", e=NE),
              xbase[0:8, :].rearrange("e (o t) -> e o t", o=1),
              allow_slow_non_contiguous=True)
    tb = sB.inc(i, 16)
    k.w("sync", sB, tb)
    i = k.dma("sync", Brow[0:1, :], k.B_d)
    tb = sB.inc(i, 16)
    k.w("tensor", sB, tb)
    k.w("tensor", k.CHG, tms)
    i3 = k.mm(k.S45[:, 0:NTT * NE], onesf[0:1, :], Brow[0:1, :],
              start=True, stop=True)
    tmm = smq.inc(i3)
    k.w("vector", smq, tmm)
    k.dve("tensor_tensor", out=ptall[:], in0=k.S45[:, 0:NTT * NE],
          in1=ranks[:], op=ALU.add)
    k.dve("tensor_tensor", out=ptall[:], in0=ptall[:],
          in1=k.mask8_sb.rearrange("p t e -> p (t e)"), op=ALU.mult)
    pv = ptall.rearrange("p (t e) -> p t e", e=NE)
    k.dve("tensor_copy", out=pt[:], in_=pv[:, :, 0])
    for e in range(1, NE):
        k.dve("tensor_tensor", out=pt[:], in0=pt[:], in1=pv[:, :, e],
              op=ALU.add)
    tpt = k.dve("tensor_copy", out=pcol[:], in_=pt[:])
    k.dve("tensor_copy", out=ordsrc[:, :, 0], in_=rowpl[:])
    tos = k.dve("tensor_copy", out=ordsrc[:, :, 1], in_=k.gates_sb[:])
    k.w("gpsimd", k.CHV, max(tpt, tos))
    k.w("gpsimd", sL, tpre)
    for tau in range(NTT):
        i = None
        if k.on("gpsimd"):
            i = k.eng.indirect_dma_start(
                out=k.ord,
                out_offset=IndirectOffsetOnAxis(ap=pcol[:, tau:tau + 1], axis=0),
                in_=ordsrc[:, tau, :], in_offset=None,
                bounds_check=k.st["oreg"], oob_is_err=False)
        sSC.inc(i, 16)
    k.w("tensor", k.CHV, tue)
    i4 = k.mm(k.S45[:, 512:528], onesf[0:1, :], ue_row, start=True, stop=True)
    tmm = smq.inc(i4)
    k.w("vector", smq, tmm)
    wf = k.sba("O_wf", [128, 16], F32, 28)
    k.dve("tensor_copy", out=uecolf[:], in_=k.S45[:, 512:528])
    k.dve("tensor_copy", out=k.uecol_u32[:], in_=uecolf[:])
    k.dve("tensor_scalar", out=wf[:], in0=uecolf[:], scalar1=128.0,
          scalar2=iota[:, 0:1], op0=ALU.mult, op1=ALU.add)
    tw = k.dve("tensor_copy", out=k.widx_u32[:], in_=wf[:])
    for g in range(8):
        k.dve("tensor_scalar", out=wf[:], in0=uecolf[:], scalar1=1024.0,
              scalar2=iota[:, 0:1], op0=ALU.mult, op1=ALU.add)
        k.dve("tensor_scalar", out=wf[:], in0=wf[:], scalar1=float(g * 128),
              scalar2=None, op0=ALU.add, op1=ALU.bypass)
        tw = k.dve("tensor_copy", out=k.wq_u32[:, g, :], in_=wf[:])
    k.barrier([(sSC, sSC.n), (k.CHV, tw), (sB, sB.n), (sL, sL.n)])


def emit_Cs(k, call):
    D, DC, FC, NE, TOK = k.D, k.DC, k.FC, k.NE, k.TOK
    Wu1 = k.sba("S_w1", [128, DC, k.FF], BF16, 0)
    Wu2 = k.sba("S_w2", [128, FC, D], BF16, 64)
    hidT = k.sba("S_hid", [128, FC, 512], BF16, 128)
    h2row = k.sba("S_h2r", [128, 2, D], F32, 160)
    if "S_h2T" not in k._sb:
        k._sb["S_h2T"] = k.nc.alloc_sbuf_tensor_at(
            "S_h2T", [128, DC, 512], BF16, offset=k.off_xr1).ap()
        k._sb["S_be2g"] = k.nc.alloc_sbuf_tensor_at(
            "S_be2g", [128, D], F32, offset=k.off_lnt).ap()
        k._sb["S_be2b"] = k.nc.alloc_sbuf_tensor_at(
            "S_be2b", [1, D], BF16, offset=k.off_sq).ap()
    h2T = k._sb["S_h2T"]
    be2g = k._sb["S_be2g"]
    be2b = k._sb["S_be2b"]
    eob = k.xr2

    sW1 = k.sem("s_w1"); sW2 = k.sem("s_w2"); sBE = k.sem("s_be")
    sO = k.sem("s_ord"); sH = k.sem("s_h2")
    smq = k.sem("s_mm"); sm2 = k.sem("s_mm2"); smt = k.sem("s_mt")
    sSC = [k.sem("s_sc0"), k.sem("s_sc1")]
    stt = k.st
    stt[("s_ps", 0)] = 0
    stt[("s_ps", 1)] = 0

    We1v = k.We1b.rearrange("e g p f -> (e g p) f")
    We2v = k.We2b.rearrange("e g p f -> (e g p) f")
    be1v = k.inp["f_be1"].rearrange("e p f -> (e p) f")
    be2v = k.inp["f_be2"].rearrange("e o d -> (e o) d")

    def flush_scatters(pend):
        for col, sl, teo in pend:
            k.w("gpsimd", k.CHV, teo)
            i = None
            if k.on("gpsimd"):
                i = k.eng.indirect_dma_start(
                    out=k.moe_s,
                    out_offset=IndirectOffsetOnAxis(
                        ap=k.oidx_u32[:, col:col + 1], axis=0),
                    in_=eob[:, sl, :], in_offset=None,
                    bounds_check=k.st["breg"], oob_is_err=False)
            tsc = sSC[sl].inc(i, 16)
            stt[("s_eo", sl)] = tsc
            stt[("s_eo_sc", col)] = tsc
        del pend[:]

    pending = []
    for j in range(NU_MOE):
        k.w("gpsimd", smq, stt.get("s_w1_users", 0))
        if j == 0 and call == 0:
            k.w("gpsimd", k.sem("wc"), k.st.get("wconv_t", 0))
        tw1a = tw1b = tw2 = tbe = None
        for g in range(8):
            i = None
            if k.on("gpsimd"):
                i = k.eng.indirect_dma_start(
                    out=Wu1[:, g, :], out_offset=None, in_=We1v,
                    in_offset=IndirectOffsetOnAxis(
                        ap=k.wq_u32[:, g, j:j + 1], axis=0),
                    bounds_check=k.st["wreg2"], oob_is_err=False)
            tt = sW1.inc(i, 16)
            if g == 3:
                tw1a = tt
        tw1b = sW1.n
        k.w("gpsimd", sm2, stt.get("s_w2_users", 0))
        for g in range(8):
            i = None
            if k.on("gpsimd"):
                i = k.eng.indirect_dma_start(
                    out=Wu2[:, g * 4:(g + 1) * 4, :].rearrange(
                        "p c f -> p (c f)"),
                    out_offset=None, in_=We2v,
                    in_offset=IndirectOffsetOnAxis(
                        ap=k.wq_u32[:, g, j:j + 1], axis=0),
                    bounds_check=k.st["wreg2"], oob_is_err=False)
            tw2 = sW2.inc(i, 16)
        k.w("gpsimd", k.CHS, stt.get("s_be_users", 0))
        k.w("gpsimd", k.CHV, stt.get("s_be2_users", 0))
        i = None
        if k.on("gpsimd"):
            i = k.eng.indirect_dma_start(
                out=k.be1g[:], out_offset=None, in_=be1v,
                in_offset=IndirectOffsetOnAxis(ap=k.widx_u32[:, j:j + 1], axis=0),
                bounds_check=k.st["wreg"], oob_is_err=False)
        sBE.inc(i, 16)
        i = None
        if k.on("gpsimd"):
            i = k.eng.indirect_dma_start(
                out=be2g[:], out_offset=None, in_=be2v,
                in_offset=IndirectOffsetOnAxis(ap=k.uecol_u32[:, j:j + 1], axis=0),
                bounds_check=k.st["ereg"], oob_is_err=False)
        tbe = sBE.inc(i, 16)
        k.w("vector", sBE, tbe)
        k.w("vector", sm2, stt.get("s_w2_users", 0))
        tb2 = k.dve("tensor_copy", out=be2b[0:1, :], in_=be2g[0:1, :])
        stt["s_be2_users"] = tb2

        flush_scatters(pending)

        for it in range(4):
            pos = j * 4 + it
            sl = it % 2
            col = (j % 2) * 4 + it
            k.w("sync", k.CHV, stt.get(("s_osb", sl), 0))
            i = k.dma("sync", k.osb[:, sl, 0:2],
                      k.ord[pos * 128:(pos + 1) * 128, :])
            to = sO.inc(i, 16)
            k.w("vector", sO, to)
            k.w("vector", sSC[it % 2], stt.get(("s_eo_sc", col), 0))
            k.dve("tensor_copy", out=k.oidx_u32[:, col:col + 1],
                  in_=k.osb[:, sl, 0:1])
            tcv = k.dve("tensor_copy", out=k.gcol[:, col:col + 1],
                        in_=k.osb[:, sl, 1:2])
            stt[("s_osb", sl)] = tcv
            k.w("gpsimd", k.CHV, tcv)
            k.w("gpsimd", k.CHV, stt.get(("s_h2row", sl), 0))
            i = None
            if k.on("gpsimd"):
                i = k.eng.indirect_dma_start(
                    out=h2row[:, sl, :], out_offset=None, in_=k.h2_s,
                    in_offset=IndirectOffsetOnAxis(
                        ap=k.oidx_u32[:, col:col + 1], axis=0),
                    bounds_check=k.st["breg"], oob_is_err=False)
            th = sH.inc(i, 16)
            k.w("vector", sH, th)
            war_ = [(smq, stt.get("s_h2T_users", 0))] if it == 0 else []
            the = emit_ct(k, h2row[:, sl, :],
                          h2T[:, :, it * 128:(it + 1) * 128], sl, smt, war=war_)
            stt[("s_h2row", sl)] = the
        txn = k.st["ptfree"]

        k.w("tensor", k.CHV, txn)
        for fc in range(FC):
            r = fc % 2
            ps = k.S01[:, r * 512:(r + 1) * 512]
            k.w("tensor", sW1, tw1b)
            if fc == 0:
                k.w("tensor", sBE, tbe)
            k.w("tensor", k.CHS, stt[("s_ps", r)])
            last = None
            for kc in range(DC):
                ii = k.mm(ps, Wu1[:, kc, fc * 128:(fc + 1) * 128],
                          h2T[:, kc, :], start=(kc == 0), stop=(kc == DC - 1))
                if ii is not None:
                    last = ii
            tg = smq.inc(last)
            k.w("scalar", smq, tg)
            k.w("scalar", sBE, tbe)
            k.w("scalar", sm2, stt.get("s_hid_users", 0))
            stt[("s_ps", r)] = k.act(hidT[:, fc, :], ps, AF.Relu,
                                     bias=k.be1g[:, fc:fc + 1])
        stt["s_w1_users"] = smq.n
        stt["s_h2T_users"] = smq.n
        stt["s_be_users"] = k.CHS.n

        k.w("tensor", k.CHS, max(stt[("s_ps", 0)], stt[("s_ps", 1)]))
        k.w("tensor", sW2, tw2)
        k.w("tensor", k.CHV, tb2)
        for it in range(4):
            sl = it % 2
            col = (j % 2) * 4 + it
            ps2 = k.S23 if it % 2 == 0 else k.S45
            k.w("tensor", k.CHV, stt.get(("s_ps2", it % 2), 0))
            last = None
            for half in range(2):
                for sl2 in range(FC):
                    ii = k.mm(ps2[:, half * 512:(half + 1) * 512],
                              hidT[:, sl2, it * 128:(it + 1) * 128],
                              Wu2[:, sl2, half * 512:(half + 1) * 512],
                              start=(sl2 == 0), stop=False)
                    if ii is not None:
                        last = ii
                ii = k.mm(ps2[:, half * 512:(half + 1) * 512], k.ones_sb[:],
                          be2b[0:1, half * 512:(half + 1) * 512],
                          start=False, stop=True)
                if ii is not None:
                    last = ii
            tgm = sm2.inc(last)
            k.w("vector", sm2, tgm)
            k.w("vector", sSC[sl], stt.get(("s_eo", sl), 0))
            teo = k.dve("tensor_scalar", out=eob[:, sl, :], in0=ps2,
                        scalar1=k.gcol[:, col:col + 1], scalar2=None,
                        op0=ALU.mult, op1=ALU.bypass)
            stt[("s_ps2", it % 2)] = teo
            if it < 2:
                flush_scatters([(col, sl, teo)])
            else:
                pending.append((col, sl, teo))
        stt["s_hid_users"] = sm2.n
        stt["s_w2_users"] = sm2.n
    flush_scatters(pending)
    k.barrier([(sSC[0], sSC[0].n), (sSC[1], sSC[1].n)])


def emit_rt(k, call):
    S, DC = k.S, k.DC
    rT = k.sba("A_rT", [128, DC, S], BF16, 64)
    Wr = k.sba("A_wr", [128, DC, k.D], BF16, 96)
    peb = k.sba("A_peb", [128, 2, DC, 512], BF16, 112)
    sW = k.sem("r_wld")
    sPE = [k.sem("r_pe0"), k.sem("r_pe1")]
    smq = k.sem("r_mm")
    i = k.dma("gpsimd", Wr[:], k.inp["f_Wr"].rearrange("(c p) f -> p c f", p=128))
    tW = sW.inc(i, 16)
    ev = [0, 0]
    pe_users = [0, 0]
    tlast = 0
    for n in range(S // 512):
        par = n % 2
        k.w("gpsimd", smq, pe_users[par])
        i = k.dma("gpsimd", peb[:, par], k.inp["peT"].rearrange(
            "(c p) t -> p c t", p=128)[:, :, n * 512:(n + 1) * 512])
        tpe = sPE[par].inc(i, 16)
        k.w("tensor", sW, tW)
        k.w("tensor", sPE[par], tpe)
        for oc in range(DC):
            r = (n * DC + oc) % 2
            ps = k.S01[:, r * 512:(r + 1) * 512]
            k.w("tensor", k.CHV, ev[r])
            last = None
            for kc in range(DC):
                ii = k.mm(ps, Wr[:, kc, oc * 128:(oc + 1) * 128], peb[:, par, kc, :],
                          start=(kc == 0), stop=(kc == DC - 1))
                if ii is not None:
                    last = ii
            tg = smq.inc(last)
            k.w("vector", smq, tg)
            tlast = k.dve("tensor_copy", out=rT[:, oc, n * 512:(n + 1) * 512], in_=ps)
            ev[r] = tlast
        pe_users[par] = smq.n
    k.barrier([(k.CHV, tlast)])
    return rT


def emit_p0(k, call, s, x_in):
    S, D, DC, NT = k.S, k.D, k.DC, k.NT
    kT = k.sba("A_kT", [128, DC, S], BF16, 0)
    V = k.sba("A_v", [128, NT, D], BF16, 32)
    Wq = k.sba("A_wqkv", [128, DC, 3 * D], BF16, 96)
    xnT = k.sba("A_xnT", [128, 2, DC, 512], BF16, 144)
    qTb = k.sba("A_qTb", [128, 2, 512], BF16, 160)

    sX = [k.sem("p_x0"), k.sem("p_x1")]
    sW = k.sem("p_wld")
    smq = k.sem("p_mmq")
    smt = k.sem("p_mmt")
    sSX = [k.sem("p_sx0"), k.sem("p_sx1")]
    sSQ = [k.sem("p_sq0"), k.sem("p_sq1")]

    wsrc = k.inp["f_Wqkv"].rearrange("(c p) f -> p c f", p=128)
    i = k.dma("gpsimd", Wq[:, :, 0:2048], wsrc[:, :, 0:2048])
    sW.inc(i, 16)
    i = k.dma("gpsimd", Wq[:, :, 2048:3072], wsrc[:, :, 2048:3072])
    tW = sW.inc(i, 16)
    if call == 0 and "wconv_t" not in k.st:
        emit_wconv(k)

    xnT_users = [0, 0]
    ps_ev = [0, 0, 0, 0]
    v_ev = [0, 0]
    qsp = [0, 0]
    xr1_rd = k.st.setdefault("p_xr1_rd", [0, 0])
    xr2_rd = k.st.setdefault("p_xr2_rd", [0, 0])
    for n in range(S // 512):
        par = n % 2
        for it in range(4):
            tau = n * 4 + it
            gtau = s * NT + tau
            slot = tau % 2
            k.w("sync", k.CHV, xr1_rd[slot])
            i = k.dma("sync", k.xr1[:, slot, :],
                      x_in[gtau * 128:(gtau + 1) * 128, :])
            tx = sX[slot].inc(i, 16)
            k.w("vector", sX[slot], tx)
            k.w("vector", sSX[slot], xr2_rd[slot])
            tn = emit_ln(k, k.xr1[:, slot, :], k.xr2[:, slot, :])
            xr1_rd[slot] = tn
            k.w("sync", k.CHV, tn)
            i = k.dma("sync", k.xn_s[gtau * 128:(gtau + 1) * 128, :],
                      k.xr2[:, slot, :])
            xr2_rd[slot] = sSX[slot].inc(i, 16)
            emit_ct(k, k.xr2[:, slot, :], xnT[:, par, :, it * 128:(it + 1) * 128],
                    slot, smt, war=[(smq, xnT_users[par])] if it == 0 else ())
        txn = k.st["ptfree"]
        k.w("tensor", k.CHV, txn)
        k.w("tensor", sW, tW)
        pi = 0
        for which, base in (("q", 0), ("k", D)):
            for oc in range(DC):
                r = pi % 4
                ps = [k.S01[:, 0:512], k.S01[:, 512:1024],
                      k.S23[:, 0:512], k.S23[:, 512:1024]][r]
                k.w("tensor", k.CHV, ps_ev[r])
                last = None
                for kc in range(DC):
                    ii = k.mm(ps, Wq[:, kc, base + oc * 128:base + oc * 128 + 128],
                              xnT[:, par, kc, :],
                              start=(kc == 0), stop=(kc == DC - 1))
                    if ii is not None:
                        last = ii
                tgm = smq.inc(last)
                k.w("vector", smq, tgm)
                if which == "k":
                    ps_ev[r] = k.dve("tensor_copy",
                                     out=kT[:, oc, n * 512:(n + 1) * 512], in_=ps)
                else:
                    qs = pi % 2
                    k.w("vector", sSQ[qs], qsp[qs])
                    tqe = k.dve("tensor_copy", out=qTb[:, qs, :], in_=ps)
                    ps_ev[r] = tqe
                    k.w("sync", k.CHV, tqe)
                    qdst = k.qT_s[s * NT + n * 4: s * NT + n * 4 + 4, :,
                                  oc * 128:(oc + 1) * 128].rearrange("t p f -> p t f")
                    i = k.dma("sync", qdst,
                              qTb[:, qs, :].rearrange("p (t f) -> p t f", t=4))
                    qsp[qs] = sSQ[qs].inc(i, 16)
                pi += 1
        for it in range(4):
            for half in range(2):
                ps = k.S45[:, half * 512:(half + 1) * 512]
                k.w("tensor", k.CHV, v_ev[half])
                last = None
                for kc in range(DC):
                    ii = k.mm(ps, xnT[:, par, kc, it * 128:(it + 1) * 128],
                              Wq[:, kc, 2 * D + half * 512:2 * D + (half + 1) * 512],
                              start=(kc == 0), stop=(kc == DC - 1))
                    if ii is not None:
                        last = ii
                tgm = smq.inc(last)
                k.w("vector", smq, tgm)
                v_ev[half] = k.dve(
                    "tensor_copy",
                    out=V[:, n * 4 + it, half * 512:(half + 1) * 512], in_=ps)
        xnT_users[par] = smq.n
    k.barrier([(k.CHV, k.CHV.n), (sSX[0], sSX[0].n), (sSX[1], sSX[1].n),
               (sSQ[0], sSQ[0].n), (sSQ[1], sSQ[1].n)])
    return kT, V


def emit_attn(k, call, s, kT, V, rT):
    S, D, DC, NT, NH = k.S, k.D, k.DC, k.NT, k.NH
    SEGW = k.SEGW
    qTt = k.sba("A_qTt", [128, 2, DC, 128], BF16, 96)
    qu = k.sba("A_qu", [128, 2, DC, 128], BF16, 100)
    qv = k.sba("A_qv", [128, 2, DC, 128], BF16, 104)
    Gsb = k.sba("A_G", [128, 2, S], BF16, 108)
    bd = k.sba("A_bd", [128, 2, S], BF16, 116)
    ssb = k.sba("A_s", [128, 2, S], F32, 124)
    Psb = k.sba("A_P", [128, 2, S], BF16, 140)
    PTs = k.sba("A_PTs", [128, NT, 128], BF16, 148)
    ctx = k.sba("A_ctx", [128, 2, D], F32, 152)

    sQ = [k.sem("a_q0"), k.sem("a_q1")]
    sG = k.sem("a_g")
    sSPL = [k.sem("a_spl0"), k.sem("a_spl1")]
    sBD = [k.sem("a_bd0"), k.sem("a_bd1")]
    sAC = k.sem("a_ac")
    sTR = k.sem("a_tr")
    sPV = k.sem("a_pv")
    sCS = [k.sem("a_cs0"), k.sem("a_cs1")]
    SC = [k.S01, k.S23, k.S45]
    stt = k.st

    for t in range(NT):
        KT = (t + 1) * 128
        nseg = (KT + SEGW - 1) // SEGW
        segs = [(si * SEGW, min(SEGW, KT - si * SEGW)) for si in range(nseg)]
        gtau = s * NT + t
        tp = t % 2
        k.w("sync", k.CHV, stt.get(("a_quv", tp), 0))
        i = k.dma("sync", qTt[:, tp],
                  k.qT_s[gtau].rearrange("p (c f) -> p c f", c=DC))
        tq = sQ[tp].inc(i, 16)
        k.w("vector", sQ[tp], tq)
        k.w("vector", sG, stt.get(("a_g_users", tp), 0))
        k.w("vector", sAC, stt.get(("a_ac_users", tp), 0))
        tquv = 0
        for c in range(DC):
            k.dve("tensor_scalar", out=qu[:, tp, c, :], in0=qTt[:, tp, c, :],
                  scalar1=k.ub_sb[:, c:c + 1], scalar2=0.125,
                  op0=ALU.add, op1=ALU.mult)
            tquv = k.dve("tensor_scalar", out=qv[:, tp, c, :], in0=qTt[:, tp, c, :],
                         scalar1=k.vb_sb[:, c:c + 1], scalar2=0.125,
                         op0=ALU.add, op1=ALU.mult)
        stt[("a_quv", tp)] = tquv

        units = [(h, si) for h in range(NH) for si in range(nseg)]
        NU = len(units)
        tG = [0] * NU; tGE = [0] * NU; tAC = [0] * NU; tSE = [0] * NU
        tBDh = [0] * NH; tMKh = [0] * NH; tSPLh = [0] * NH
        tEXh = [0] * NH; tNMh = [0] * NH; tRDh = [0] * NH
        tCEh = [0] * NH; tTRh = [0] * NH

        def emit_G(u):
            h, si = units[u]
            base, w_ = segs[si]
            hc, poff = h // 2, 64 * (h % 2)
            slot = SC[u % 3]
            k.w("tensor", k.CHV, tSE[u - 3] if u >= 3 else stt.get("a_se_prev", 0))
            k.w("tensor", k.CHV, tquv)
            last = None
            for cc in range(0, w_, 512):
                cw = min(512, w_ - cc)
                ii = k.mm(slot[:, cc:cc + cw], qv[poff:poff + 64, tp, hc, :],
                          rT[poff:poff + 64, hc,
                             S - KT + base + cc:S - KT + base + cc + cw],
                          start=True, stop=True)
                if ii is not None:
                    last = ii
            tG[u] = sG.inc(last)
            k.w("vector", sG, tG[u])
            k.w("vector", sSPL[h % 2],
                tSPLh[h - 2] if h >= 2 else stt.get(("a_spl_prev", h % 2), 0))
            tGE[u] = k.dve("tensor_copy", out=Gsb[:, h % 2, base:base + w_],
                           in_=slot[:, 0:w_])

        def emit_SPILL(h):
            KTl = KT
            qh = "gpsimd" if h % 2 == 0 else "sync"
            k.w(qh, k.CHV, tGE[(h + 1) * nseg - 1])
            k.w(qh, sBD[h % 2],
                tBDh[h - 2] if h >= 2 else stt.get(("a_bd_prev", h % 2), 0))
            gslot = k.G_s[h % 2]
            wr = gslot.copy()
            base_off = wr.offset
            wr.ap = bass_rust.VecI64Pair([[KTl, 128], [1, KTl]])
            wr.offset = base_off
            i = k.dma(qh, wr, Gsb[:, h % 2, 0:KTl])
            tSPLh[h] = sSPL[h % 2].inc(i, 16)
            k.w(qh, sSPL[h % 2], tSPLh[h])
            k.w(qh, sAC,
                tAC[(h - 1) * nseg - 1] if h >= 2 else stt.get("a_ac_prev", 0))
            rd = gslot.copy()
            rd.ap = bass_rust.VecI64Pair([[KTl - 1, 128], [1, KTl]])
            rd.offset = base_off + 127
            i = k.dma(qh, bd[:, h % 2, 0:KTl], rd)
            tBDh[h] = sBD[h % 2].inc(i, 16)

        def emit_AC(u):
            h, si = units[u]
            base, w_ = segs[si]
            hc, poff = h // 2, 64 * (h % 2)
            slot = SC[u % 3]
            k.w("tensor", k.CHV, tGE[u])
            k.w("tensor", sBD[h % 2], tBDh[h])
            last = None
            for cc in range(0, w_, 512):
                cw = min(512, w_ - cc)
                is_last = (si == nseg - 1) and (cc + cw == w_)
                ii = k.mm(slot[:, cc:cc + cw], qu[poff:poff + 64, tp, hc, :],
                          kT[poff:poff + 64, hc, base + cc:base + cc + cw],
                          start=True, stop=False)
                if ii is not None:
                    last = ii
                ii = k.mm(slot[:, cc:cc + cw], k.ident_sb[:],
                          bd[:, h % 2, base + cc:base + cc + cw],
                          start=False, stop=not is_last)
                if ii is not None:
                    last = ii
                if is_last:
                    ii = k.mm(slot[:, cc:cc + cw], k.ident_sb[:],
                              k.cmask_sb[:, 512 - cw:512],
                              start=False, stop=True)
                    if ii is not None:
                        last = ii
            tAC[u] = sAC.inc(last)
            k.w("vector", sAC, tAC[u])
            if h >= 2:
                k.w("vector", k.CHS, tEXh[h - 2])
            tSE[u] = k.dve("tensor_scalar", out=ssb[:, h % 2, base:base + w_],
                           in0=slot[:, 0:w_], scalar1=1.0, scalar2=None,
                           op0=ALU.mult, op1=ALU.max,
                           accum_out=k.mx2_sb[:, si:si + 1])
            if si == nseg - 1:
                tNMh[h] = k.dve("reduce_max", out=k.nmx_sb[:, h % 2:h % 2 + 1],
                                in_=k.mx2_sb[:, 0:nseg], axis=AXX, negate=True)
                k.w("scalar", k.CHV, tNMh[h])
                if h >= 2:
                    k.w("scalar", sTR, tTRh[h - 2])
                tEXh[h] = k.act(Psb[:, h % 2, 0:KT], ssb[:, h % 2, 0:KT], AF.Exp,
                                bias=k.nmx_sb[:, h % 2:h % 2 + 1],
                                accum_out=k.den_sb[:, h % 2:h % 2 + 1])

        def emit_TPV(h):
            ntile = t + 1
            ngrp = (ntile + 7) // 8
            cp = k.CTXP if h % 2 == 0 else k.CTXP2
            k.w("tensor", k.CHS, max(tEXh[h], tCEh[h - 2] if h >= 2 else
                                     stt.get(("a_ce_prev", h % 2), 0)))
            last_pv = None
            for g in range(ngrp):
                j0, j1 = g * 8, min(g * 8 + 8, ntile)
                k.w("tensor", k.CHV, k.st.get("ptfree", 0))
                last = None
                for j in range(j0, j1):
                    ii = k.tr(k.PT[:, (j - j0) * 128:(j - j0 + 1) * 128],
                              Psb[:, h % 2, j * 128:(j + 1) * 128])
                    if ii is not None:
                        last = ii
                tg = sTR.inc(last)
                k.w("vector", sTR, tg)
                te = k.dve("tensor_copy", out=PTs[:, j0:j1, :],
                           in_=k.PT[:, 0:(j1 - j0) * 128].rearrange(
                               "p (j f) -> p j f", f=128))
                k.st["ptfree"] = te
                k.w("tensor", k.CHV, te)
                for j in range(j0, j1):
                    ii = k.mm(cp, PTs[:, j, :], V[:, j, h * 64:(h + 1) * 64],
                              start=(j == 0), stop=(j == ntile - 1))
                    if ii is not None:
                        last_pv = ii
            tTRh[h] = sTR.n
            tPVh = sPV.inc(last_pv)
            k.w("vector", k.CHS, tEXh[h])
            tRDh[h] = k.dve("reciprocal", out=k.rden_sb[:, h % 2:h % 2 + 1],
                            in_=k.den_sb[:, h % 2:h % 2 + 1])
            k.w("scalar", sPV, tPVh)
            k.w("scalar", k.CHV, tRDh[h])
            k.w("scalar", sCS[t % 2], stt.get(("a_cs", t % 2), 0))
            tCEh[h] = k.act(ctx[:, t % 2, h * 64:(h + 1) * 64], cp, AF.Copy,
                            scale=k.rden_sb[:, h % 2:h % 2 + 1])

        emit_G(0)
        if NU > 1:
            emit_G(1)
        for uu in range(min(2, NU)):
            hh, ss = units[uu]
            if ss == nseg - 1:
                emit_SPILL(hh)
        for u in range(NU):
            if u + 2 < NU:
                emit_G(u + 2)
            emit_AC(u)
            if u + 2 < NU and units[u + 2][1] == nseg - 1:
                emit_SPILL(units[u + 2][0])
            h, si = units[u]
            if si == nseg - 1 and h >= 1:
                emit_TPV(h - 1)
        emit_TPV(NH - 1)
        k.w("sync", k.CHS, tCEh[NH - 1])
        i = k.dma("sync", k.ctx_s[gtau * 128:(gtau + 1) * 128, :], ctx[:, t % 2, :])
        stt[("a_cs", t % 2)] = sCS[t % 2].inc(i, 16)
        stt["a_se_prev"] = tSE[NU - 1]
        stt[("a_spl_prev", 0)] = tSPLh[NH - 2]
        stt[("a_spl_prev", 1)] = tSPLh[NH - 1]
        stt[("a_bd_prev", 0)] = tBDh[NH - 2]
        stt[("a_bd_prev", 1)] = tBDh[NH - 1]
        stt["a_ac_prev"] = tAC[NU - 1]
        stt[("a_ce_prev", 0)] = tCEh[NH - 2]
        stt[("a_ce_prev", 1)] = tCEh[NH - 1]
        stt[("a_g_users", t % 2)] = tG[NU - 1]
        stt[("a_ac_users", t % 2)] = tAC[NU - 1]
    k.barrier([(sCS[0], sCS[0].n), (sCS[1], sCS[1].n)])


def emit_B(k, call):
    D, DC, NTT = k.D, k.DC, k.NTT
    Wo = k.sba("B_wo", [128, DC, D], BF16, 0)
    lng = k.sba("B_lng", [128, 2, D], F32, 16)
    ctr = k.sba("B_ctr", [128, 2, D], F32, 24)
    cT = k.sba("B_cT", [128, DC, 128], BF16, 32)
    h1r = k.sba("B_h1r", [128, 2, D], F32, 36)
    h1Tb = k.sba("B_h1T", [128, DC, 128], BF16, 44)

    sC = [k.sem("b_c0"), k.sem("b_c1")]
    sX = [k.sem("b_x0"), k.sem("b_x1")]
    sLg = k.sem("b_lng"); sW = k.sem("b_wld")
    smq = k.sem("b_mm"); smt = k.sem("b_mt")
    sS1 = [k.sem("b_s10"), k.sem("b_s11")]
    sST = k.sem("b_sT")

    i = k.dma("gpsimd", Wo[:], k.inp["f_Wo"].rearrange("(c p) f -> p c f", p=128))
    tW = sW.inc(i, 16)
    i = k.dma("sync", lng[:, 0, :], k.inp["f_lng"][:, 0, :])
    sLg.inc(i, 16)
    i = k.dma("sync", lng[:, 1, :], k.inp["f_lnb"][:, 0, :])
    tlg = sLg.inc(i, 16)

    stt = k.st
    for tau in range(NTT):
        sl = tau % 2
        k.w("sync", k.CHV, stt.get(("b_ctr", sl), 0))
        i = k.dma("sync", ctr[:, sl, :], k.ctx_s[tau * 128:(tau + 1) * 128, :])
        tc = sC[sl].inc(i, 16)
        k.w("sync", k.CHV, stt.get(("b_xr1", sl), 0))
        i = k.dma("sync", k.xr1[:, sl, :], k.xn_s[tau * 128:(tau + 1) * 128, :])
        tx = sX[sl].inc(i, 16)
        k.w("vector", sC[sl], tc)
        tct = emit_ct(k, ctr[:, sl, :], cT[:], sl, smt,
                      war=[(smq, stt.get("b_ct_users", 0))])
        stt[("b_ctr", sl)] = tct
        k.w("tensor", k.CHV, tct)
        k.w("tensor", sW, tW)
        for half in range(2):
            ps = k.S01[:, half * 512:(half + 1) * 512]
            k.w("tensor", k.CHV, stt.get(("b_ps", half), 0))
            last = None
            for kc in range(DC):
                ii = k.mm(ps, cT[:, kc, :], Wo[:, kc, half * 512:(half + 1) * 512],
                          start=(kc == 0), stop=(kc == DC - 1))
                if ii is not None:
                    last = ii
            tgm = smq.inc(last)
            k.w("vector", smq, tgm)
            k.w("vector", sX[sl], tx)
            tu = k.dve("tensor_tensor",
                       out=k.xr2[:, sl, half * 512:(half + 1) * 512], in0=ps,
                       in1=k.xr1[:, sl, half * 512:(half + 1) * 512], op=ALU.add)
            stt[("b_ps", half)] = tu
            stt[("b_xr1", sl)] = tu
        stt["b_ct_users"] = smq.n
        k.w("vector", sLg, tlg)
        k.w("vector", sS1[sl], stt.get(("b_h1r", sl), 0))
        tn = emit_ln(k, k.xr2[:, sl, :], k.lnt[:],
                     lng=lng[:, 0, :], lnb=lng[:, 1, :], affout=h1r[:, sl, :])
        k.w("sync", k.CHV, tn)
        i = k.dma("sync", k.h1_s[tau * 128:(tau + 1) * 128, :], h1r[:, sl, :])
        tsp = sS1[sl].inc(i, 16)
        stt[("b_h1r", sl)] = tsp
        the = emit_ct(k, h1r[:, sl, :], h1Tb[:], sl, smt,
                      war=[(sST, stt.get("b_h1T_sp", 0))])
        k.w("sync", k.CHV, the)
        i = k.dma("sync",
                  k.h1T_s.rearrange("c p t -> p c t")[:, :, tau * 128:(tau + 1) * 128],
                  h1Tb[:])
        stt["b_h1T_sp"] = sST.inc(i, 16)
    k.barrier([(sS1[0], sS1[0].n), (sS1[1], sS1[1].n), (sST, sST.n)])


def emit_B2(k, call):
    D, DC, FC, NE, TOK = k.D, k.DC, k.FC, k.NE, k.TOK
    W1 = k.sba("F_w1", [128, DC, k.FF], BF16, 0)
    W2 = k.sba("F_w2", [128, FC, D], BF16, 64)
    hidT = k.sba("F_hid", [128, FC, 256], BF16, 128)
    h1T = k.sba("F_h1T", [128, 2, DC, 256], BF16, 144)
    lng = k.sba("F_lng", [128, 2, D], F32, 152)
    h2Tb = k.sba("F_h2T", [128, DC, 128], BF16, 160)

    sH = [k.sem("f_h0"), k.sem("f_h1")]
    sX = [k.sem("f_x0"), k.sem("f_x1")]
    sW1s = k.sem("f_w1"); sW2s = k.sem("f_w2"); sLg = k.sem("f_lng")
    smq = k.sem("f_mm"); sm2 = k.sem("f_mm2")
    smt = k.sem("f_mt"); smg = k.sem("f_gmm")
    sS2 = [k.sem("f_s20"), k.sem("f_s21")]
    sST = k.sem("f_sT")

    w1src = k.inp["f_Wff1"].rearrange("(c p) f -> p c f", p=128)
    i = k.dma("gpsimd", W1[:, :, 0:2048], w1src[:, :, 0:2048])
    sW1s.inc(i, 16)
    i = k.dma("gpsimd", W1[:, :, 2048:4096], w1src[:, :, 2048:4096])
    tW1 = sW1s.inc(i, 16)
    i = k.dma("gpsimd", W2[:], k.inp["f_Wff2"].rearrange("(c p) f -> p c f", p=128))
    tW2 = sW2s.inc(i, 16)
    i = k.dma("sync", lng[:, 0, :], k.inp["f_lng"][:, 1, :])
    sLg.inc(i, 16)
    i = k.dma("sync", lng[:, 1, :], k.inp["f_lnb"][:, 1, :])
    tlg = sLg.inc(i, 16)

    stt = k.st
    stt[("f_ps", 0)] = 0
    stt[("f_ps", 1)] = 0
    for st_ in range(TOK // 256):
        hpar = st_ % 2
        k.w("sync", smq, stt.get(("f_h1T_users", hpar), 0))
        i = k.dma("sync", h1T[:, hpar],
                  k.h1T_s.rearrange("c p t -> p c t")[:, :, st_ * 256:(st_ + 1) * 256])
        th = sH[hpar].inc(i, 16)
        k.w("tensor", sH[hpar], th)
        k.w("tensor", sW1s, tW1)
        for fc in range(FC):
            r = fc % 2
            ps = k.S01[:, r * 512:r * 512 + 256]
            k.w("tensor", k.CHS, stt[("f_ps", r)])
            last = None
            for kc in range(DC):
                ii = k.mm(ps, W1[:, kc, fc * 128:(fc + 1) * 128], h1T[:, hpar, kc, :],
                          start=(kc == 0), stop=(kc == DC - 1))
                if ii is not None:
                    last = ii
            tg = smq.inc(last)
            k.w("scalar", smq, tg)
            k.w("scalar", sm2, stt.get("f_hid_users", 0))
            stt[("f_ps", r)] = k.act(hidT[:, fc, :], ps, AF.Relu,
                                     bias=k.bff1_sb[:, fc:fc + 1])
        stt[("f_h1T_users", hpar)] = smq.n
        k.w("tensor", k.CHS, max(stt[("f_ps", 0)], stt[("f_ps", 1)]))
        k.w("tensor", sW2s, tW2)
        for it in range(2):
            tau = st_ * 2 + it
            sl = tau % 2
            ps2 = k.S23 if it % 2 == 0 else k.S45
            k.w("sync", k.CHV, stt.get(("f_xr1", sl), 0))
            k.w("sync", sS2[sl], stt.get(("f_xr1s", sl), 0))
            i = k.dma("sync", k.xr1[:, sl, :], k.h1_s[tau * 128:(tau + 1) * 128, :])
            tx = sX[sl].inc(i, 16)
            k.w("tensor", k.CHV, stt.get(("f_ps2", it % 2), 0))
            last = None
            for half in range(2):
                for sl2 in range(FC):
                    ii = k.mm(ps2[:, half * 512:(half + 1) * 512],
                              hidT[:, sl2, it * 128:(it + 1) * 128],
                              W2[:, sl2, half * 512:(half + 1) * 512],
                              start=(sl2 == 0), stop=False)
                    if ii is not None:
                        last = ii
                ii = k.mm(ps2[:, half * 512:(half + 1) * 512], k.ones_sb[:],
                          k.bff2_sb[:, half * 512:(half + 1) * 512],
                          start=False, stop=True)
                if ii is not None:
                    last = ii
            tgm = sm2.inc(last)
            k.w("vector", sm2, tgm)
            k.w("vector", sX[sl], tx)
            k.w("vector", sS2[sl], stt.get(("f_xr2", sl), 0))
            tu2 = k.dve("tensor_tensor", out=k.xr2[:, sl, :], in0=ps2,
                        in1=k.xr1[:, sl, :], op=ALU.add)
            stt[("f_ps2", it % 2)] = tu2
            stt[("f_xr1", sl)] = tu2
            k.w("vector", sLg, tlg)
            tn = emit_ln(k, k.xr2[:, sl, :], k.lnt[:],
                         lng=lng[:, 0, :], lnb=lng[:, 1, :],
                         affout=k.xr1[:, sl, :])
            stt[("f_xr1", sl)] = tn
            k.w("sync", k.CHV, tn)
            i = k.dma("sync", k.h2_s[tau * 128:(tau + 1) * 128, :], k.xr1[:, sl, :])
            th2 = sS2[sl].inc(i, 16)
            stt[("f_xr2", sl)] = th2
            stt[("f_xr1s", sl)] = th2
            the = emit_ct(k, k.xr1[:, sl, :], h2Tb[:], sl, smt,
                          war=[(sST, stt.get("f_h2T_sp", 0)),
                               (smg, stt.get("f_h2T_users", 0))])
            stt[("f_xr1", sl)] = max(stt[("f_xr1", sl)], the)
            k.w("sync", k.CHV, the)
            i = k.dma("sync",
                      k.h2T_s.rearrange("c p t -> p c t")[:, :, tau * 128:(tau + 1) * 128],
                      h2Tb[:])
            stt["f_h2T_sp"] = sST.inc(i, 16)
            stt["f_hid_users"] = sm2.n
            k.w("tensor", k.CHV, max(the, stt.get("f_gate_done", 0)))
            last = None
            for kc in range(DC):
                ii = k.mm(k.GATEP, h2Tb[:, kc, :], k.wg_sb[:, kc, :],
                          start=(kc == 0), stop=(kc == DC - 1))
                if ii is not None:
                    last = ii
            tgm = smg.inc(last)
            stt["f_h2T_users"] = tgm
            k.w("vector", smg, tgm)
            nl = k.st_sb[:, 8:9]; dsum = k.st_sb[:, 9:10]; gate = k.st_sb[:, 10:11]
            eq8 = k.scr8_sb[:, 0:NE]; cum8 = k.scr8_sb[:, NE:2 * NE]
            c18 = k.scr8_sb[:, 2 * NE:3 * NE]; pr8 = k.scr8_sb[:, 3 * NE:4 * NE]
            tv1 = k.dve("reduce_max", out=nl, in_=k.GATEP, axis=AXX, negate=True)
            k.w("scalar", k.CHV, tv1)
            ta1 = k.act(pr8, k.GATEP, AF.Exp, bias=nl, accum_out=dsum)
            k.w("vector", k.CHS, ta1)
            k.dve("reciprocal", out=gate, in_=dsum)
            k.dve("tensor_scalar", out=eq8, in0=k.GATEP, scalar1=nl,
                  scalar2=0.0, op0=ALU.add, op1=ALU.is_equal)
            k.dve("tensor_tensor_scan", out=cum8, data0=k.ones8_sb[:],
                  data1=eq8, initial=0.0, op0=ALU.mult, op1=ALU.add)
            k.dve("tensor_scalar", out=c18, in0=cum8, scalar1=1.0,
                  scalar2=None, op0=ALU.is_equal, op1=ALU.bypass)
            k.dve("tensor_tensor", out=k.mask8_sb[:, tau, :], in0=eq8,
                  in1=c18, op=ALU.mult)
            stt["f_gate_done"] = k.dve(
                "tensor_copy", out=k.gates_sb[:, tau:tau + 1], in_=gate)
    k.barrier([(sS2[0], sS2[0].n), (sS2[1], sS2[1].n), (sST, sST.n),
               (k.CHV, stt["f_gate_done"])])


def emit_C(k, call):
    D, DC, FC, NE, TOK = k.D, k.DC, k.FC, k.NE, k.TOK
    We1 = k.sba("C_w1", [128, DC, k.FF], BF16, 0)
    We2 = k.sba("C_w2", [128, FC, D], BF16, 64)
    hidT = k.sba("C_hid", [128, FC, 512], BF16, 128)
    if "C_h2T" not in k._sb:
        k._sb["C_h2T"] = k.nc.alloc_sbuf_tensor_at(
            "C_h2T", [128, DC, 512], BF16, offset=k.off_xr1).ap()
    h2T = k._sb["C_h2T"]
    eob = k.xr2

    sL = k.sem("c2_ld"); sW1 = k.sem("c2_w1"); sW2 = k.sem("c2_w2")
    sB = k.sem("c2_b")
    smq = k.sem("c2_mm"); sm2 = k.sem("c2_mm2")
    sSC = [k.sem("c2_sc0"), k.sem("c2_sc1")]
    stt = k.st
    stt[("c_ps", 0)] = 0
    stt[("c_ps", 1)] = 0

    def ldw1(e):
        src = k.inp["f_We1"][e].rearrange("(c p) f -> p c f", p=128)
        i = k.dma("gpsimd", We1[:, :, 0:2048], src[:, :, 0:2048])
        sW1.inc(i, 16)
        i = k.dma("gpsimd", We1[:, :, 2048:4096], src[:, :, 2048:4096])
        return sW1.inc(i, 16)

    def ldw2(e):
        i = k.dma("gpsimd", We2[:],
                  k.inp["f_We2"][e].rearrange("(c p) f -> p c f", p=128))
        return sW2.inc(i, 16)

    def ldb(e):
        i = k.dma("gpsimd", k.be1_sb[:, e % 2, :], k.inp["f_be1"][e])
        sB.inc(i, 16)
        i = k.dma("gpsimd", k.be2_sb[:, e % 2, :], k.inp["f_be2"][e])
        return sB.inc(i, 16)

    tw1 = ldw1(0)
    tb = ldb(0)
    for e in range(NE):
        k.w("gpsimd", sm2, stt.get("c_eout_done", 0))
        tw2 = ldw2(e)
        for st_ in range(TOK // 512):
            k.w("sync", smq, stt.get("c_h2T_users", 0))
            i = k.dma("sync", h2T[:],
                      k.h2T_s.rearrange("c p t -> p c t")[:, :, st_ * 512:(st_ + 1) * 512])
            th = sL.inc(i, 16)
            k.w("tensor", sL, th)
            k.w("tensor", sW1, tw1)
            k.w("tensor", sB, tb)
            for fc in range(FC):
                r = fc % 2
                ps = k.S01[:, r * 512:(r + 1) * 512]
                k.w("tensor", k.CHS, stt[("c_ps", r)])
                last = None
                for kc in range(DC):
                    ii = k.mm(ps, We1[:, kc, fc * 128:(fc + 1) * 128], h2T[:, kc, :],
                              start=(kc == 0), stop=(kc == DC - 1))
                    if ii is not None:
                        last = ii
                tg = smq.inc(last)
                k.w("scalar", smq, tg)
                k.w("scalar", sm2, stt.get("c_hid_users", 0))
                stt[("c_ps", r)] = k.act(hidT[:, fc, :], ps, AF.Relu,
                                         bias=k.be1_sb[:, e % 2, fc:fc + 1])
            stt["c_h2T_users"] = smq.n
            k.w("tensor", k.CHS, max(stt[("c_ps", 0)], stt[("c_ps", 1)]))
            k.w("tensor", sW2, tw2)
            for it in range(4):
                tau = st_ * 4 + it
                sl = tau % 2
                ps2 = k.S23 if it % 2 == 0 else k.S45
                k.w("tensor", k.CHV, stt.get(("c_ps2", it % 2), 0))
                last = None
                for half in range(2):
                    for sl2 in range(FC):
                        ii = k.mm(ps2[:, half * 512:(half + 1) * 512],
                                  hidT[:, sl2, it * 128:(it + 1) * 128],
                                  We2[:, sl2, half * 512:(half + 1) * 512],
                                  start=(sl2 == 0), stop=False)
                        if ii is not None:
                            last = ii
                    ii = k.mm(ps2[:, half * 512:(half + 1) * 512], k.ones_sb[:],
                              k.be2_sb[:, e % 2, half * 512:(half + 1) * 512],
                              start=False, stop=True)
                    if ii is not None:
                        last = ii
                tgm = sm2.inc(last)
                k.w("vector", sm2, tgm)
                k.w("vector", sSC[sl], stt.get(("c_eo", sl), 0))
                teo = k.dve("tensor_scalar", out=eob[:, sl, :], in0=ps2,
                            scalar1=k.mg8_sb[:, tau, e:e + 1], scalar2=None,
                            op0=ALU.mult, op1=ALU.bypass)
                stt[("c_ps2", it % 2)] = teo
                k.w("gpsimd", k.CHV, teo)
                i = None
                if k.on("gpsimd") and _MOE_DBG:
                    di = k.eng.indirect_dma_start(
                        out=k.moe_dbg.rearrange("e t d -> (e t) d"),
                        out_offset=IndirectOffsetOnAxis(
                            ap=k.idx_sb[:, tau, e:e + 1], axis=0),
                        in_=eob[:, sl, :], in_offset=None,
                        element_offset=e * k.TOK * k.D,
                        bounds_check=k.st["breg"], oob_is_err=False)
                else:
                    di = None
                if _MOE_DBG:
                    sSC[sl].inc(di, 16)
                if k.on("gpsimd"):
                    i = k.eng.indirect_dma_start(
                        out=k.moe_s,
                        out_offset=IndirectOffsetOnAxis(
                            ap=k.idx_sb[:, tau, e:e + 1], axis=0),
                        in_=eob[:, sl, :], in_offset=None,
                        bounds_check=k.st["breg"], oob_is_err=False)
                stt[("c_eo", sl)] = sSC[sl].inc(i, 16)
            stt["c_eout_done"] = sm2.n
        if e + 1 < NE:
            k.w("gpsimd", smq, stt["c_h2T_users"])
            tw1 = ldw1(e + 1)
            k.w("gpsimd", k.CHS, max(stt[("c_ps", 0)], stt[("c_ps", 1)]))
            tb = ldb(e + 1)
    k.barrier([(sSC[0], sSC[0].n), (sSC[1], sSC[1].n)])


def emit_D(k, call, x_in, x_out):
    D, DC, NTT = k.D, k.DC, k.NTT
    Wout = k.sba("D_wout", [128, DC, D], BF16, 0)
    lng = k.sba("D_lng", [128, 2, D], F32, 16)
    x3r = k.sba("D_x3", [128, 2, D], F32, 24)
    u3r = k.sba("D_u3", [128, 2, D], F32, 32)
    h3r = k.sba("D_h3", [128, 2, D], F32, 40)
    h3Tb = k.sba("D_h3T", [128, DC, 128], BF16, 48)

    s1 = [k.sem("d_a0"), k.sem("d_a1")]
    s2 = [k.sem("d_b0"), k.sem("d_b1")]
    s3 = [k.sem("d_c0"), k.sem("d_c1")]
    sLg = k.sem("d_lng"); sW = k.sem("d_wld")
    smq = k.sem("d_mm"); smt = k.sem("d_mt")
    sSP = [k.sem("d_o0"), k.sem("d_o1")]

    i = k.dma("gpsimd", Wout[:], k.inp["f_Wout"].rearrange("(c p) f -> p c f", p=128))
    tW = sW.inc(i, 16)
    i = k.dma("sync", lng[:, 0, :], k.inp["f_lng"][:, 2, :])
    sLg.inc(i, 16)
    i = k.dma("sync", lng[:, 1, :], k.inp["f_lnb"][:, 2, :])
    tlg = sLg.inc(i, 16)

    stt = k.st
    for tau in range(NTT):
        sl = tau % 2
        k.w("sync", k.CHV, stt.get(("d_xr", sl), 0))
        i = k.dma("sync", k.xr1[:, sl, :], k.h2_s[tau * 128:(tau + 1) * 128, :])
        t1 = s1[sl].inc(i, 16)
        i = k.dma("sync", k.xr2[:, sl, :], k.moe_s[tau * 128:(tau + 1) * 128, :])
        t2 = s2[sl].inc(i, 16)
        k.w("sync", k.CHV, stt.get(("d_x3", sl), 0))
        i = k.dma("sync", x3r[:, sl, :], x_in[tau * 128:(tau + 1) * 128, :])
        t3 = s3[sl].inc(i, 16)
        k.w("vector", s1[sl], t1)
        k.w("vector", s2[sl], t2)
        k.w("vector", sSP[sl], stt.get(("d_u3", sl), 0))
        tu3 = k.dve("tensor_tensor", out=u3r[:, sl, :], in0=k.xr1[:, sl, :],
                    in1=k.xr2[:, sl, :], op=ALU.add)
        stt[("d_xr", sl)] = tu3
        k.w("vector", sLg, tlg)
        emit_ln(k, u3r[:, sl, :], k.lnt[:],
                lng=lng[:, 0, :], lnb=lng[:, 1, :], affout=h3r[:, sl, :])
        the = emit_ct(k, h3r[:, sl, :], h3Tb[:], sl, smt,
                      war=[(smq, stt.get("d_h3T_users", 0))])
        k.w("tensor", k.CHV, the)
        k.w("tensor", sW, tW)
        tf = 0
        for half in range(2):
            ps = k.S01[:, half * 512:(half + 1) * 512]
            k.w("tensor", k.CHV, stt.get(("d_ps", half), 0))
            last = None
            for kc in range(DC):
                ii = k.mm(ps, h3Tb[:, kc, :],
                          Wout[:, kc, half * 512:(half + 1) * 512],
                          start=(kc == 0), stop=(kc == DC - 1))
                if ii is not None:
                    last = ii
            tgm = smq.inc(last)
            k.w("vector", smq, tgm)
            k.w("vector", s3[sl], t3)
            tf = k.dve("scalar_tensor_tensor",
                       out=u3r[:, sl, half * 512:(half + 1) * 512], in0=ps,
                       scalar=0.5, in1=x3r[:, sl, half * 512:(half + 1) * 512],
                       op0=ALU.mult, op1=ALU.add)
            stt[("d_ps", half)] = tf
            stt[("d_x3", sl)] = tf
        stt["d_h3T_users"] = smq.n
        k.w("sync", k.CHV, tf)
        i = k.dma("sync", x_out[tau * 128:(tau + 1) * 128, :], u3r[:, sl, :])
        stt[("d_u3", sl)] = sSP[sl].inc(i, 16)
    k.barrier([(sSP[0], sSP[0].n), (sSP[1], sSP[1].n)])


def emit_ffn(k, call, x_in, x_out):
    rT = emit_rt(k, call)
    for s in range(k.SEQ):
        kT, V = emit_p0(k, call, s, x_in)
        emit_attn(k, call, s, kT, V, rT)
    emit_B(k, call)
    emit_B2(k, call)
    emit_ord(k, call)
    emit_Cs(k, call)
    emit_D(k, call, x_in, x_out)


def emit_mixer(k, x_in, x_out):
    S, D, DC, NT = k.S, k.D, k.DC, k.NT
    Wz = k.sba("M_wz", [128, DC, D], BF16, 0)
    Wh = k.sba("M_wh", [128, DC, D], BF16, 16)
    xnT = k.sba("M_xnT", [128, 2, DC, 512], BF16, 32)
    asb = k.sba("M_a", [128, 2, 512], F32, 48)
    zsb = k.sba("M_z", [128, 2, 512], F32, 52)
    hsb = k.sba("M_h", [128, DC, S], F32, 56)
    hbf = k.sba("M_hbf", [128, DC, S], BF16, 120)
    Wg = k.sba("M_wg", [128, DC, D], BF16, 0)
    gsb = k.sba("M_g", [128, 2, 512], F32, 16)
    mox = k.sba("M_mo", [128, 4, D], F32, 24)

    sX = [k.sem("m_x0"), k.sem("m_x1")]
    sWzh = k.sem("m_wzh"); sWg = k.sem("m_wgl")
    smq = k.sem("m_mm"); smt = k.sem("m_mt")
    sSP = [k.sem("m_o0"), k.sem("m_o1"), k.sem("m_o2"), k.sem("m_o3")]
    stt = k.st

    for s in range(k.SEQ):
        k.w("gpsimd", smq, stt.get("m_wg_users", 0))
        k.w("gpsimd", smt, stt.get("m_gate_tr", 0))
        k.w("gpsimd", k.CHV, stt.get("m_gate_dve", 0))
        k.w("gpsimd", k.CHS, stt.get("m_gate_act", 0))
        for it_ in range(4):
            k.w("gpsimd", sSP[it_], stt.get(("m_mo", it_), 0))
        i = k.dma("gpsimd", Wz[:], k.inp["mx_Wz"].rearrange("(c p) f -> p c f", p=128))
        sWzh.inc(i, 16)
        i = k.dma("gpsimd", Wh[:], k.inp["mx_Wh"].rearrange("(c p) f -> p c f", p=128))
        tWzh = sWzh.inc(i, 16)
        xnT_users = [0, 0]
        for n in range(S // 512):
            par = n % 2
            for it in range(4):
                tau = n * 4 + it
                gtau = s * NT + tau
                slot = tau % 2
                k.w("sync", k.CHV, stt.get(("m_xr1", slot), 0))
                i = k.dma("sync", k.xr1[:, slot, :],
                          x_in[gtau * 128:(gtau + 1) * 128, :])
                tx = sX[slot].inc(i, 16)
                k.w("vector", sX[slot], tx)
                tn = emit_ln(k, k.xr1[:, slot, :], k.xr2[:, slot, :])
                stt[("m_xr1", slot)] = tn
                war_ = [(smq, xnT_users[par])] if it == 0 else []
                if n == 0 and it == 0:
                    war_ += [(sSP[2], stt.get(("m_mo", 2), 0)),
                             (sSP[3], stt.get(("m_mo", 3), 0))]
                emit_ct(k, k.xr2[:, slot, :],
                        xnT[:, par, :, it * 128:(it + 1) * 128], slot, smt,
                        war=war_)
            txn = k.st["ptfree"]
            k.w("tensor", k.CHV, txn)
            k.w("tensor", sWzh, tWzh)
            for oc in range(DC):
                r = oc % 2
                psz = k.S01[:, r * 512:(r + 1) * 512]
                psh = k.S23[:, r * 512:(r + 1) * 512]
                k.w("tensor", k.CHV, stt.get(("m_ps", r), 0))
                lastz = None
                for kc in range(DC):
                    ii = k.mm(psz, Wz[:, kc, oc * 128:(oc + 1) * 128],
                              xnT[:, par, kc, :], start=(kc == 0), stop=(kc == DC - 1))
                    if ii is not None:
                        lastz = ii
                tz = smq.inc(lastz)
                lasth = None
                for kc in range(DC):
                    ii = k.mm(psh, Wh[:, kc, oc * 128:(oc + 1) * 128],
                              xnT[:, par, kc, :], start=(kc == 0), stop=(kc == DC - 1))
                    if ii is not None:
                        lasth = ii
                th_ = smq.inc(lasth)
                k.w("scalar", smq, tz)
                k.w("scalar", k.CHV, stt.get(("m_ab", r), 0))
                k.act(asb[:, r, :], psz, AF.Sigmoid, scale=-1.0,
                      bias=k.nbz_sb[:, oc:oc + 1])
                ta2 = k.act(zsb[:, r, :], psz, AF.Sigmoid, bias=k.bz_sb[:, oc:oc + 1])
                k.w("vector", k.CHS, ta2)
                k.w("vector", smq, th_)
                k.dve("scalar_tensor_tensor", out=zsb[:, r, :], in0=psh,
                      scalar=k.bh_sb[:, oc:oc + 1], in1=zsb[:, r, :],
                      op0=ALU.add, op1=ALU.mult)
                init = 0.0 if n == 0 else hsb[:, oc, n * 512 - 1:n * 512]
                tsc = k.dve("tensor_tensor_scan",
                            out=hsb[:, oc, n * 512:(n + 1) * 512],
                            data0=asb[:, r, :], data1=zsb[:, r, :],
                            initial=init, op0=ALU.mult, op1=ALU.add)
                stt[("m_ps", r)] = tsc
                stt[("m_ab", r)] = tsc
            xnT_users[par] = smq.n
        tcast = 0
        for oc in range(DC):
            tcast = k.dve("tensor_copy", out=hbf[:, oc, :], in_=hsb[:, oc, :])
        k.w("gpsimd", smq, smq.n)
        i = k.dma("gpsimd", Wg[:], k.inp["mx_Wg"].rearrange("(c p) f -> p c f", p=128))
        tWg = sWg.inc(i, 16)
        for n in range(S // 512):
            for oc in range(DC):
                r = oc % 2
                psz = k.S45[:, r * 512:(r + 1) * 512]
                k.w("tensor", k.CHV, max(tcast, stt.get(("m_g", r), 0)))
                k.w("tensor", sWg, tWg)
                last = None
                for kc in range(DC):
                    ii = k.mm(psz, Wg[:, kc, oc * 128:(oc + 1) * 128],
                              hbf[:, kc, n * 512:(n + 1) * 512],
                              start=(kc == 0), stop=(kc == DC - 1))
                    if ii is not None:
                        last = ii
                tg = smq.inc(last)
                stt["m_wg_users"] = tg
                k.w("scalar", smq, tg)
                k.w("scalar", k.CHV, stt.get(("m_g", r), 0))
                tga = k.act(gsb[:, r, :], psz, AF.Sigmoid, bias=k.bg_sb[:, oc:oc + 1])
                k.w("vector", k.CHS, tga)
                tgh = k.dve("tensor_tensor", out=gsb[:, r, :], in0=gsb[:, r, :],
                            in1=hsb[:, oc, n * 512:(n + 1) * 512], op=ALU.mult)
                k.w("tensor", k.CHV, tgh)
                te = 0
                for it in range(4):
                    tau = n * 4 + it
                    k.w("tensor", k.CHV, k.st.get("ptfree", 0))
                    ii = k.tr(k.PTF, gsb[:, r, it * 128:(it + 1) * 128], fp32=True)
                    tt = smt.inc(ii)
                    k.w("vector", smt, tt)
                    if oc == 0:
                        k.w("vector", sSP[it], stt.get(("m_mo", it), 0))
                    te = k.dve("tensor_copy",
                               out=mox[:, it, oc * 128:(oc + 1) * 128], in_=k.PTF)
                    k.st["ptfree"] = te
                stt[("m_g", r)] = te
            for it in range(4):
                tau = n * 4 + it
                gtau = s * NT + tau
                slot = tau % 2
                k.w("sync", k.CHV, stt.get(("m_xr1", slot), 0))
                i = k.dma("sync", k.xr1[:, slot, :],
                          x_in[gtau * 128:(gtau + 1) * 128, :])
                tx = sX[slot].inc(i, 16)
                k.w("vector", sX[slot], tx)
                tmo = k.dve("tensor_tensor", out=mox[:, it, :], in0=mox[:, it, :],
                            in1=k.xr1[:, slot, :], op=ALU.add)
                stt[("m_xr1", slot)] = tmo
                k.w("sync", k.CHV, tmo)
                i = k.dma("sync", x_out[gtau * 128:(gtau + 1) * 128, :],
                          mox[:, it, :])
                stt[("m_mo", it)] = sSP[it].inc(i, 16)
        stt["m_gate_dve"] = k.CHV.n
        stt["m_gate_act"] = k.CHS.n
        stt["m_gate_tr"] = smt.n
    k.barrier([(sSP[0], sSP[0].n), (sSP[1], sSP[1].n),
               (sSP[2], sSP[2].n), (sSP[3], sSP[3].n)])


_NSTAGE = 3


def emit_program(k):
    emit_stage0(k)
    emit_ffn(k, 0, k.inp["x"], k.xA)
    if _NSTAGE >= 2:
        emit_mixer(k, k.xA, k.xB)
    if _NSTAGE >= 3:
        emit_ffn(k, 1, k.xB, k.y)


def _pe_table(S, D):
    pos = np.arange(S - 1, -1, -1, dtype=np.float64)
    inv_freq = 1.0 / 10000.0 ** (np.arange(0, D, 2, dtype=np.float64) / D)
    sinusoid = pos[:, None] * inv_freq[None, :]
    pe = np.concatenate([np.sin(sinusoid), np.cos(sinusoid)], axis=-1)
    return np.ascontiguousarray(pe.T.astype(np.float32))


def prep_inputs(cfg, inputs):
    B, S, D = cfg["B"], cfg["S"], cfg["D"]
    NE = cfg["NE"]
    SEQ = B // NCORES
    TOK = SEQ * S
    NTT = TOK // 128
    f32 = np.float32

    def asnp(v, dt=f32):
        return np.ascontiguousarray(np.asarray(v), dtype=dt)

    def vec_pc(v):
        return np.ascontiguousarray(v.reshape(-1, 128).T)

    Wz = asnp(inputs["L0_attn_Wz"])
    Wzf = np.ascontiguousarray(np.transpose(Wz, (1, 0, 2)).reshape(D, D))
    Wh = asnp(inputs["L0_attn_Wh"])
    Whf = np.ascontiguousarray(np.transpose(Wh, (1, 0, 2)).reshape(D, D))
    lng = np.ascontiguousarray(np.stack(
        [np.broadcast_to(asnp(inputs[f"L0_ffn_ln{i}_g"]), (128, D))
         for i in (1, 2, 3)], axis=1))
    lnb = np.ascontiguousarray(np.stack(
        [np.broadcast_to(asnp(inputs[f"L0_ffn_ln{i}_b"]), (128, D))
         for i in (1, 2, 3)], axis=1))
    be1 = asnp(inputs["L0_ffn_be1"])
    be1pc = np.ascontiguousarray(
        np.stack([vec_pc(be1[e]) for e in range(NE)], axis=0))
    ident = np.eye(128, dtype=ml_dtypes.bfloat16)
    identf = np.eye(128, dtype=f32)
    cm = np.zeros((128, 512), dtype=ml_dtypes.bfloat16)
    cm[:, 384:512] = np.where(np.arange(128)[None, :] <= np.arange(128)[:, None],
                              0.0, NEGINF).astype(ml_dtypes.bfloat16)
    rowidx = (BIGIDX + (np.arange(NTT)[None, :] * 128
                        + np.arange(128)[:, None])).astype(f32)
    trilS = np.triu(np.ones((128, 128)), 1).astype(ml_dtypes.bfloat16)
    iota128 = np.arange(128, dtype=f32)[:, None]
    thr8 = np.tile((np.arange(8, dtype=f32) * 512.0)[None, :], (8, 1))
    iotaU = (np.arange(16, dtype=f32) * 512.0)[None, :]
    rowplain = (np.arange(NTT)[None, :] * 128
                + np.arange(128)[:, None]).astype(f32)
    UB = PAD // 128
    ordinit = np.zeros((128, 2 * UB), dtype=f32)
    ordinit[:, 0::2] = (BIGIDX + np.arange(UB)[None, :] * 128
                        + np.arange(128)[:, None])
    shared = {
        "mx_Wz": Wzf, "mx_Wh": Whf, "mx_Wg": asnp(inputs["L0_attn_Wg"]),
        "mx_bz": vec_pc(asnp(inputs["L0_attn_bz"]).reshape(-1)),
        "mx_bh": vec_pc(asnp(inputs["L0_attn_bh"]).reshape(-1)),
        "mx_bg": vec_pc(asnp(inputs["L0_attn_bg"]).reshape(-1)),
        "f_Wqkv": asnp(inputs["L0_ffn_Wqkv"]), "f_Wo": asnp(inputs["L0_ffn_Wo"]),
        "f_Wr": asnp(inputs["L0_ffn_Wr"]),
        "f_ub": vec_pc(asnp(inputs["L0_ffn_u_bias"]).reshape(-1)),
        "f_vb": vec_pc(asnp(inputs["L0_ffn_v_bias"]).reshape(-1)),
        "f_lng": lng, "f_lnb": lnb,
        "f_Wff1": asnp(inputs["L0_ffn_Wff1"]),
        "f_bff1": vec_pc(asnp(inputs["L0_ffn_bff1"]).reshape(-1)),
        "f_Wff2": asnp(inputs["L0_ffn_Wff2"]),
        "f_bff2": asnp(inputs["L0_ffn_bff2"]).reshape(1, D),
        "f_Wg": asnp(inputs["L0_ffn_Wg"]),
        "f_We1": asnp(inputs["L0_ffn_We1"]), "f_be1": be1pc,
        "f_We2": asnp(inputs["L0_ffn_We2"]),
        "f_be2": asnp(inputs["L0_ffn_be2"]).reshape(NE, 1, D),
        "f_Wout": asnp(inputs["L0_ffn_Wout"]),
        "peT": _pe_table(S, D),
        "ident": ident, "identf": identf, "cmask": cm, "rowidx": rowidx,
        "trilS": trilS, "iota128": iota128, "thr8": thr8, "iotaU": iotaU,
        "rowplain": rowplain, "ordinit": ordinit,
    }
    x = asnp(inputs["L0_attn_x"])
    in_maps = []
    for c in range(NCORES):
        m = dict(shared)
        m["x"] = np.ascontiguousarray(x[c * SEQ:(c + 1) * SEQ].reshape(TOK, D))
        in_maps.append(m)
    return in_maps


def build(cfg):
    k = KB(cfg)
    with k.nc.Block() as block:
        for en in KB.ENGINES:
            def mk(en=en):
                def f(eng):
                    k.begin_pass(en, eng)
                    emit_program(k)
                return f
            getattr(block, en)(mk())
    return k


def kernel(**inputs):
    cfg = dict(B=16, S=2048, D=1024, NH=16, DH=64, FF=4096, NE=8)
    in_maps = prep_inputs(cfg, inputs)
    k = build(cfg)
    res = run_bass_kernel_spmd(k.nc, in_maps, core_ids=list(range(NCORES)))
    SEQ = cfg["B"] // NCORES
    out = np.concatenate(
        [r["y"].reshape(SEQ, cfg["S"], cfg["D"]) for r in res.results], axis=0)
    return out


# version 17
# speedup vs baseline: 1.5342x; 1.0244x over previous
import numpy as np
import ml_dtypes

import bass_rust
import concourse.bass as bass
import concourse.mybir as mybir
from concourse.bass import IndirectOffsetOnAxis
from concourse.bass_utils import run_bass_kernel_spmd

F32 = mybir.dt.float32
BF16 = mybir.dt.bfloat16
U32 = mybir.dt.uint32
AF = mybir.ActivationFunctionType
ALU = mybir.AluOpType
AXX = mybir.AxisListType.X

NCORES = 8
BIGIDX = 2097152.0
EPS = 1e-5
NEGINF = -1.0e30
NU_MOE = 15
PAD = NU_MOE * 512

_DBG = set()
_MOE_DBG = False


class Sem:
    __slots__ = ("h", "n")

    def __init__(self, h):
        self.h = h
        self.n = 0

    def inc(self, inst, k=1):
        if inst is not None:
            inst.then_inc(self.h, k)
        self.n += k
        return self.n


class KB:
    ENGINES = ("sync", "scalar", "vector", "tensor", "gpsimd")

    def __init__(self, cfg):
        self.cfg = cfg
        self.B, self.S, self.D = cfg["B"], cfg["S"], cfg["D"]
        self.NH, self.DH, self.FF, self.NE = cfg["NH"], cfg["DH"], cfg["FF"], cfg["NE"]
        self.SEQ = self.B // NCORES
        self.TOK = self.SEQ * self.S
        self.NT = self.S // 128
        self.NTT = self.TOK // 128
        self.DC = self.D // 128
        self.FC = self.FF // 128
        self.SEGW = 1024

        nc = bass.Bass("TRN2", target_bir_lowering=False)
        self.nc = nc
        self._sb = {}
        self._sems = {}
        self.en = None
        self.eng = None
        self.st = {}

        D, S, FF, NE, TOK, DC, FC, NT, NTT = (
            self.D, self.S, self.FF, self.NE, self.TOK, self.DC, self.FC,
            self.NT, self.NTT)

        self.inp = {}

        def di(name, shape, dt=F32):
            self.inp[name] = nc.dram_tensor(name, shape, dt, kind="ExternalInput").ap()

        di("x", [TOK, D])
        di("mx_Wz", [D, D]); di("mx_Wh", [D, D]); di("mx_Wg", [D, D])
        di("mx_bz", [128, DC]); di("mx_bh", [128, DC]); di("mx_bg", [128, DC])
        di("f_Wqkv", [D, 3 * D]); di("f_Wo", [D, D]); di("f_Wr", [D, D])
        di("f_ub", [128, DC]); di("f_vb", [128, DC])
        di("f_lng", [128, 3, D]); di("f_lnb", [128, 3, D])
        di("f_Wff1", [D, FF]); di("f_bff1", [128, FC])
        di("f_Wff2", [FF, D]); di("f_bff2", [1, D])
        di("f_Wg", [D, NE])
        di("f_We1", [NE, D, FF]); di("f_be1", [NE, 128, FC])
        di("f_We2", [NE, FF, D]); di("f_be2", [NE, 1, D])
        di("f_Wout", [D, D])
        di("peT", [D, S])
        di("ident", [128, 128], BF16)
        di("identf", [128, 128])
        di("cmask", [128, 512], BF16)
        di("rowidx", [128, NTT])
        di("trilS", [128, 128], BF16)
        di("iota128", [128, 1])
        di("thr8", [8, 8])
        di("iotaU", [1, 16])
        di("rowplain", [128, NTT])
        di("ordinit", [128, 2 * (PAD // 128)])

        self.y = nc.dram_tensor("y", [TOK, D], F32, kind="ExternalOutput").ap()

        def scr(name, shape, dt=F32):
            kind = "ExternalOutput" if name in _DBG else "Internal"
            return nc.dram_tensor(name, shape, dt, kind=kind).ap()

        self.xA = scr("xA", [TOK, D])
        self.xB = scr("xB", [TOK, D])
        self.xn_s = scr("xn_s", [TOK, D])
        self.qT_s = scr("qT_s", [self.SEQ * NT, 128, DC * 128], BF16)
        self.ctx_s = scr("ctx_s", [TOK, D])
        self.h1_s = scr("h1_s", [TOK, D])
        self.h1T_s = scr("h1T_s", [DC, 128, TOK], BF16)
        self.h2_s = scr("h2_s", [TOK, D])
        self.h2T_s = scr("h2T_s", [DC, 128, TOK], BF16)
        self.moe_s = scr("moe_s", [TOK, D])
        self.G_s = scr("G_s", [2, 128 * S], BF16)
        self.moe_dbg = scr("moe_dbg", [NE, TOK, D]) if _MOE_DBG else None
        self.We1b = scr("We1b", [NE, 8, 128, FF], BF16)
        self.We2b = scr("We2b", [NE, 8, 128, FF], BF16)
        self.ord = scr("ord_s", [PAD, 2])
        self.cnt_d = scr("cnt_d", [1, 2 * NTT * NE])
        self.B_d = scr("B_d", [1, NTT * NE])
        self.padd_d = scr("padd_d", [NE, 1])
        self.pb_d = scr("pb_d", [1, 16])

        self.S01 = nc.alloc_psum_tensor("S01", [128, 1024], F32).ap()
        self.S23 = nc.alloc_psum_tensor("S23", [128, 1024], F32).ap()
        self.S45 = nc.alloc_psum_tensor("S45", [128, 1024], F32).ap()
        self.S67 = nc.alloc_psum_tensor("S67", [128, 1024], F32).ap()
        self.PT = self.S67[:, 0:512].bitcast(BF16)
        self.PTF = self.S67[:, 0:128]
        self.CTXP = self.S67[:, 512:576]
        self.CTXP2 = self.S67[:, 576:640]
        self.GATEP = self.S67[:, 512:512 + NE]

        A = nc.alloc_sbuf_tensor

        def ap_(name, shape, dt=F32):
            return A(name, shape, dt).ap()

        self.ident_sb = ap_("ident_sb", [128, 128], BF16)
        self.identf_sb = ap_("identf_sb", [128, 128])
        self.cmask_sb = ap_("cmask_sb", [128, 512], BF16)
        self.rowidx_sb = ap_("rowidx_sb", [128, NTT])
        self.ub_sb = ap_("ub_sb", [128, DC])
        self.vb_sb = ap_("vb_sb", [128, DC])
        self.bz_sb = ap_("bz_sb", [128, DC])
        self.nbz_sb = ap_("nbz_sb", [128, DC])
        self.bh_sb = ap_("bh_sb", [128, DC])
        self.bg_sb = ap_("bg_sb", [128, DC])
        self.bff1_sb = ap_("bff1_sb", [128, FC])
        self.be1_sb = ap_("be1_sb", [128, 2, FC])
        self.bff2_sb = ap_("bff2_sb", [1, D], BF16)
        self.be2_sb = ap_("be2_sb", [1, 2, D], BF16)
        self.wg_sb = ap_("wg_sb", [128, DC, NE], BF16)
        self.ones_sb = ap_("ones_sb", [1, 128], BF16)
        self.ones8_sb = ap_("ones8_sb", [128, NE])
        self.st_sb = ap_("st_sb", [128, 64])
        self.mx2_sb = ap_("mx2_sb", [128, 4])
        self.nmx_sb = ap_("nmx_sb", [128, 2])
        self.den_sb = ap_("den_sb", [128, 2])
        self.rden_sb = ap_("rden_sb", [128, 2])
        self.epsv = ap_("epsv", [128, 1])
        self.mask8_sb = ap_("mask8_sb", [128, NTT, NE])
        self.mg8_sb = ap_("mg8_sb", [128, NTT, NE])
        self.idx_sb = ap_("idx_sb", [128, NTT, NE], U32)
        self.scr8_sb = ap_("scr8_sb", [128, 4 * NE])
        self.off_xr1 = nc.sbuf_base
        self.xr1 = ap_("xr1", [128, 2, D])
        self.off_xr2 = nc.sbuf_base
        self.xr2 = ap_("xr2", [128, 2, D])
        self.xnb = ap_("xnb", [128, 2, D], BF16)
        self.off_sq = nc.sbuf_base
        self.sq = ap_("sq", [128, D], BF16)
        self.off_lnt = nc.sbuf_base
        self.lnt = ap_("lnt", [128, D])
        self.gates_sb = ap_("gates_sb", [128, NTT])
        self.widx_u32 = ap_("widx_u32", [128, 16], U32)
        self.wq_u32 = ap_("wq_u32", [128, 8, 16], U32)
        self.uecol_u32 = ap_("uecol_u32", [128, 16], U32)
        self.be1g = ap_("be1g", [128, FC])
        self.osb = ap_("osb", [128, 2, 4])
        self.oidx_u32 = ap_("oidx_u32", [128, 8], U32)
        self.gcol = ap_("gcol", [128, 8])

        self.arena0 = nc.sbuf_base
        self.arena_kb = (nc.sbuf_top - nc.sbuf_base) // 1024
        assert self.arena_kb >= 168, f"arena too small: {self.arena_kb}KB"

    def sem(self, name):
        s = self._sems.get(name)
        if s is None:
            s = Sem(self.nc.alloc_semaphore(name))
            self._sems[name] = s
        return s

    @property
    def CHV(self):
        return self.sem("ch_vector")

    @property
    def CHS(self):
        return self.sem("ch_scalar")

    @property
    def CHG(self):
        return self.sem("ch_gpsimd")

    def sba(self, name, shape, dt, off_kb):
        t = self._sb.get(name)
        if t is None:
            t = self.nc.alloc_sbuf_tensor_at(
                name, shape, dt, offset=self.arena0 + off_kb * 1024).ap()
            self._sb[name] = t
        return t

    def begin_pass(self, en, eng):
        self.en = en
        self.eng = eng
        self.st = {}
        for s in self._sems.values():
            s.n = 0

    def on(self, en):
        return self.en == en

    def w(self, en, sem, val):
        if val > 0 and self.en == en:
            self.eng.wait_ge(sem.h, val)

    def barrier(self, deps):
        for sem, val in deps:
            if val > 0:
                self.eng.wait_ge(sem.h, val)

    def mm(self, out, lhsT, rhs, start=True, stop=True):
        if self.en == "tensor":
            return self.eng.matmul(out, lhsT, rhs, start=start, stop=stop)
        return None

    def tr(self, out, in_, fp32=False):
        if self.en == "tensor":
            return self.eng.transpose(
                out, in_, self.identf_sb if fp32 else self.ident_sb)
        return None

    def _chain_op(self, en_name, make):
        s = self.sem("ch_" + en_name)
        if s.n > 0 and self.en == en_name:
            self.eng.wait_ge(s.h, s.n)
        inst = make(self.eng) if self.en == en_name else None
        return s.inc(inst)

    def act(self, out, in_, func, **kw):
        return self._chain_op("scalar", lambda e: e.activation(out, in_, func, **kw))

    def dve(self, meth, *a, **kw):
        return self._chain_op("vector", lambda e: getattr(e, meth)(*a, **kw))

    def gp(self, meth, *a, **kw):
        return self._chain_op("gpsimd", lambda e: getattr(e, meth)(*a, **kw))

    def dma(self, q, out, in_, **kw):
        if self.en == q:
            return self.eng.dma_start(out=out, in_=in_, **kw)
        return None


def emit_ln(k, xin, xout, lng=None, lnb=None, affout=None):
    D = k.D
    m = k.st_sb[:, 0:1]; msq = k.st_sb[:, 1:2]; va = k.st_sb[:, 2:3]
    lv = k.st_sb[:, 3:4]; rstd = k.st_sb[:, 4:5]; nmr = k.st_sb[:, 5:6]
    ssum = k.st_sb[:, 6:7]; ssq = k.st_sb[:, 7:8]
    k.dve("scalar_tensor_tensor", out=k.sq[:], in0=xin, scalar=1.0,
          in1=xin, op0=ALU.mult, op1=ALU.mult, accum_out=ssq)
    k.dve("reduce_sum", out=ssum, in_=xin, axis=AXX)
    k.dve("tensor_scalar", out=m, in0=ssum, scalar1=1.0 / D, scalar2=None,
          op0=ALU.mult, op1=ALU.bypass)
    k.dve("tensor_tensor", out=msq, in0=m, in1=m, op=ALU.mult)
    tv = k.dve("tensor_scalar", out=va, in0=ssq, scalar1=1.0 / D, scalar2=msq,
               op0=ALU.mult, op1=ALU.subtract)
    k.w("scalar", k.CHV, tv)
    k.act(lv, va, AF.Ln, bias=k.epsv[:])
    ta = k.act(rstd, lv, AF.Exp, scale=-0.5)
    k.w("vector", k.CHS, ta)
    k.dve("scalar_tensor_tensor", out=nmr, in0=m, scalar=-1.0, in1=rstd,
          op0=ALU.mult, op1=ALU.mult)
    tn = k.dve("tensor_scalar", out=xout, in0=xin, scalar1=rstd, scalar2=nmr,
               op0=ALU.mult, op1=ALU.add)
    if lng is not None:
        k.dve("tensor_tensor", out=affout, in0=xout, in1=lng, op=ALU.mult)
        tn = k.dve("tensor_tensor", out=affout, in0=affout, in1=lnb, op=ALU.add)
    return tn


def emit_ct(k, src, dst, par, smt, war=()):
    DC = k.DC
    tc = k.dve("tensor_copy", out=k.xnb[:, par, :], in_=src)
    k.w("tensor", k.CHV, tc)
    k.w("tensor", k.CHV, k.st.get("ptfree", 0))
    last = None
    for c in range(DC):
        ii = k.tr(k.PT[:, c * 128:(c + 1) * 128],
                  k.xnb[:, par, :].rearrange("p (c f) -> p c f", c=DC)[:, c, :])
        if ii is not None:
            last = ii
    tt = smt.inc(last)
    k.w("vector", smt, tt)
    for sem, val in war:
        k.w("vector", sem, val)
    te = k.dve("tensor_copy", out=dst,
               in_=k.PT.rearrange("p (c f) -> p c f", c=DC))
    k.st["ptfree"] = te
    return te


def emit_stage0(k):
    sL = k.sem("c_ld")
    sG = k.sem("c_ldg")
    loads = [
        (k.ident_sb[:], k.inp["ident"]), (k.identf_sb[:], k.inp["identf"]),
        (k.cmask_sb[:], k.inp["cmask"]), (k.rowidx_sb[:], k.inp["rowidx"]),
        (k.ub_sb[:], k.inp["f_ub"]), (k.vb_sb[:], k.inp["f_vb"]),
        (k.bz_sb[:], k.inp["mx_bz"]), (k.bh_sb[:], k.inp["mx_bh"]),
        (k.bg_sb[:], k.inp["mx_bg"]), (k.bff1_sb[:], k.inp["f_bff1"]),
    ]
    t = 0
    for out, in_ in loads:
        i = k.dma("sync", out, in_)
        t = sL.inc(i, 16)
    i = k.dma("gpsimd", k.bff2_sb[:], k.inp["f_bff2"])
    sG.inc(i, 16)
    i = k.dma("gpsimd", k.wg_sb[:],
              k.inp["f_Wg"].rearrange("(c p) e -> p c e", p=128))
    tg = sG.inc(i, 16)
    k.w("gpsimd", sG, tg)
    k.gp("memset", k.ones_sb[:], 1.0)
    k.gp("memset", k.ones8_sb[:], 1.0)
    tm = k.gp("memset", k.epsv[:], EPS)
    if k.on("gpsimd"):
        k.st["breg"] = k.eng.to_reg(k.TOK - 1)
        k.st["wreg"] = k.eng.to_reg(k.NE * 128 - 1)
        k.st["wreg2"] = k.eng.to_reg(k.NE * 1024 - 1)
        k.st["ereg"] = k.eng.to_reg(k.NE - 1)
        k.st["oreg"] = k.eng.to_reg(PAD - 1)
    k.w("vector", sL, t)
    tv = k.dve("tensor_scalar", out=k.nbz_sb[:], in0=k.bz_sb[:], scalar1=-1.0,
               scalar2=None, op0=ALU.mult, op1=ALU.bypass)
    k.barrier([(sL, t), (sG, tg), (k.CHG, tm), (k.CHV, tv)])


def emit_wconv(k):
    sWC = k.sem("wc")
    t = 0
    for e in range(k.NE):
        s1 = k.inp["f_We1"][e].rearrange("(c p) f -> p c f", p=128)
        s2 = k.inp["f_We2"][e].rearrange("(c p) f -> p c f", p=128)
        for g in range(8):
            i = k.dma("gpsimd", k.We1b[e, g], s1[:, g, :])
            t = sWC.inc(i, 16)
            i = k.dma("gpsimd",
                      k.We2b[e, g].rearrange("p (c f) -> p c f", c=4),
                      s2[:, g * 4:(g + 1) * 4, :])
            t = sWC.inc(i, 16)
    k.st["wconv_t"] = t


def emit_ord(k, call):
    D, DC, FC, NE, NTT, TOK = k.D, k.DC, k.FC, k.NE, k.NTT, k.TOK
    UB = PAD // 128
    trilS = k.sba("O_tril", [128, 128], BF16, 0)
    ordini = k.sba("O_ordini", [128, 2 * UB], F32, 1)
    iota = k.sba("O_iota", [128, 1], F32, 2)
    thr8 = k.sba("O_thr8", [8, 8], F32, 24)
    iotaU = k.sba("O_iotaU", [1, 16], F32, 25)
    onesf = k.sba("O_onesf", [1, 128], F32, 3)
    onesb = k.sba("O_onesb", [128, 1], BF16, 26)
    ones832 = k.sba("O_ones", [8, 32], F32, 27)
    rowpl = k.sba("O_rowpl", [128, NTT], F32, 4)
    maskb = k.sba("O_maskb", [128, NTT * NE], BF16, 5)
    cntrow = k.sba("O_cntrow", [1, NTT * NE], F32, 6)
    Brow = k.sba("O_Brow", [1, NTT * NE], F32, 8)
    ranks = k.sba("O_ranks", [128, NTT * NE], F32, 10)
    ptall = k.sba("O_ptall", [128, NTT * NE], F32, 12)
    pt = k.sba("O_pt", [128, NTT], F32, 14)
    pcol = k.sba("O_pcol", [128, NTT], U32, 15)
    ordsrc = k.sba("O_osrc", [128, NTT, 2], F32, 16)
    cntT = k.sba("O_cntT", [8, 32], F32, 17)
    sh = k.sba("O_sh", [8, 32], F32, 18)
    xbase = k.sba("O_xbase", [8, 32], F32, 19)
    ge8 = k.sba("O_ge8", [8, 8], F32, 20)
    small = k.sba("O_small", [8, 8], F32, 21)
    rows1 = k.sba("O_rows1", [1, 64], F32, 22)
    uecolf = k.sba("O_uecolf", [128, 16], F32, 23)

    sL = k.sem("o_ld")
    sB = k.sem("o_bounce")
    smq = k.sem("o_mm")
    sSC = k.sem("o_sc")

    c_col = small[0:8, 0:1]
    pad_col = small[0:8, 1:2]
    pad_row = rows1[0:1, 0:8]
    cumpad = rows1[0:1, 8:16]
    pbase = rows1[0:1, 16:24]
    ue_row = rows1[0:1, 24:40]

    loads = [
        (trilS[:], k.inp["trilS"]), (ordini[:], k.inp["ordinit"]),
        (iota[:], k.inp["iota128"]), (thr8[0:8, :], k.inp["thr8"]),
        (iotaU[0:1, :], k.inp["iotaU"]), (rowpl[:], k.inp["rowplain"]),
    ]
    tld = 0
    for out, in_ in loads:
        i = k.dma("sync", out, in_)
        tld = sL.inc(i, 16)
    tms = k.gp("memset", onesf[0:1, :], 1.0)
    k.gp("memset", onesb[:], 1.0)
    tms = k.gp("memset", ones832[0:8, :], 1.0)
    k.w("sync", sL, tld)
    i = k.dma("sync", k.ord.rearrange("(u p) c -> p u c", p=128),
              ordini[:].rearrange("p (u c) -> p u c", c=2))
    tpre = sL.inc(i, 16)

    tmb = k.dve("tensor_copy", out=maskb[:],
                in_=k.mask8_sb.rearrange("p t e -> p (t e)"))
    k.w("tensor", k.CHV, tmb)
    k.w("tensor", sL, tld)
    k.w("tensor", k.CHG, tms)
    i1 = k.mm(k.S01[:, 0:NTT * NE], trilS[:], maskb[:], start=True, stop=True)
    i2 = k.mm(k.S23[0:1, 0:NTT * NE].rearrange("o (e t) -> o e t", e=NE),
              onesb[:],
              maskb.rearrange("p (t e) -> p e t", e=NE), start=True, stop=True)
    tmm = smq.inc(i2)
    k.w("vector", smq, tmm)
    k.dve("tensor_copy", out=ranks[:], in_=k.S01[:, 0:NTT * NE])
    tcv = k.dve("tensor_copy", out=cntrow[0:1, :], in_=k.S23[0:1, 0:NTT * NE])
    k.w("sync", k.CHV, tcv)
    i = k.dma("sync", k.cnt_d[:, 0:NTT * NE], cntrow[0:1, :])
    tb = sB.inc(i, 16)
    k.w("sync", sB, tb)
    i = k.dma("sync", cntT[0:8, :],
              k.cnt_d[:, 0:NTT * NE].rearrange("o (e t) -> (o e) t", e=NE))
    tb = sB.inc(i, 16)
    k.w("vector", sB, tb)
    k.w("vector", k.CHG, tms)
    k.dve("memset", sh[0:8, 0:1], 0.0)
    k.dve("tensor_copy", out=sh[0:8, 1:32], in_=cntT[0:8, 0:31])
    k.dve("tensor_tensor_scan", out=xbase[0:8, :], data0=ones832[0:8, :],
          data1=sh[0:8, :], initial=0.0, op0=ALU.mult, op1=ALU.add)
    k.dve("reduce_sum", out=c_col, in_=cntT[0:8, :], axis=AXX)
    k.dve("tensor_scalar", out=ge8[0:8, :], in0=thr8[0:8, :], scalar1=c_col,
          scalar2=None, op0=ALU.is_lt, op1=ALU.bypass)
    k.dve("reduce_sum", out=pad_col, in_=ge8[0:8, :], axis=AXX)
    tpc = k.dve("tensor_scalar", out=pad_col, in0=pad_col, scalar1=512.0,
                scalar2=None, op0=ALU.mult, op1=ALU.bypass)
    k.w("sync", k.CHV, tpc)
    i = k.dma("sync", k.padd_d, pad_col)
    tb = sB.inc(i, 16)
    k.w("sync", sB, tb)
    i = k.dma("sync", pad_row, k.padd_d.rearrange("e o -> o e"))
    tb = sB.inc(i, 16)
    k.w("vector", sB, tb)
    k.dve("tensor_tensor_scan", out=cumpad, data0=ones832[0:1, 0:8],
          data1=pad_row, initial=0.0, op0=ALU.mult, op1=ALU.add)
    k.dve("tensor_tensor", out=pbase, in0=cumpad, in1=pad_row, op=ALU.subtract)
    k.dve("memset", ue_row, 0.0)
    tue = 0
    for e in range(NE):
        tue = k.dve("scalar_tensor_tensor", out=ue_row, in0=iotaU[0:1, 0:16],
                    scalar=cumpad[0:1, e:e + 1], in1=ue_row,
                    op0=ALU.is_ge, op1=ALU.add)
    k.w("sync", k.CHV, tue)
    i = k.dma("sync", k.pb_d[:, 0:8], pbase)
    tb = sB.inc(i, 16)
    k.w("sync", sB, tb)
    i = k.dma("sync", pad_col, k.pb_d.rearrange("o f -> f o")[0:8, :])
    tb = sB.inc(i, 16)
    k.w("vector", sB, tb)
    tB = k.dve("tensor_scalar", out=xbase[0:8, :], in0=xbase[0:8, :],
               scalar1=pad_col, scalar2=None, op0=ALU.add, op1=ALU.bypass)
    k.w("sync", k.CHV, tB)
    i = k.dma("sync", k.B_d.rearrange("o (t e) -> e o t", e=NE),
              xbase[0:8, :].rearrange("e (o t) -> e o t", o=1),
              allow_slow_non_contiguous=True)
    tb = sB.inc(i, 16)
    k.w("sync", sB, tb)
    i = k.dma("sync", Brow[0:1, :], k.B_d)
    tb = sB.inc(i, 16)
    k.w("tensor", sB, tb)
    k.w("tensor", k.CHG, tms)
    i3 = k.mm(k.S45[:, 0:NTT * NE], onesf[0:1, :], Brow[0:1, :],
              start=True, stop=True)
    tmm = smq.inc(i3)
    k.w("vector", smq, tmm)
    k.dve("tensor_tensor", out=ptall[:], in0=k.S45[:, 0:NTT * NE],
          in1=ranks[:], op=ALU.add)
    k.dve("tensor_tensor", out=ptall[:], in0=ptall[:],
          in1=k.mask8_sb.rearrange("p t e -> p (t e)"), op=ALU.mult)
    pv = ptall.rearrange("p (t e) -> p t e", e=NE)
    k.dve("tensor_copy", out=pt[:], in_=pv[:, :, 0])
    for e in range(1, NE):
        k.dve("tensor_tensor", out=pt[:], in0=pt[:], in1=pv[:, :, e],
              op=ALU.add)
    tpt = k.dve("tensor_copy", out=pcol[:], in_=pt[:])
    k.dve("tensor_copy", out=ordsrc[:, :, 0], in_=rowpl[:])
    tos = k.dve("tensor_copy", out=ordsrc[:, :, 1], in_=k.gates_sb[:])
    k.w("gpsimd", k.CHV, max(tpt, tos))
    k.w("gpsimd", sL, tpre)
    for tau in range(NTT):
        i = None
        if k.on("gpsimd"):
            i = k.eng.indirect_dma_start(
                out=k.ord,
                out_offset=IndirectOffsetOnAxis(ap=pcol[:, tau:tau + 1], axis=0),
                in_=ordsrc[:, tau, :], in_offset=None,
                bounds_check=k.st["oreg"], oob_is_err=False)
        sSC.inc(i, 16)
    k.w("tensor", k.CHV, tue)
    i4 = k.mm(k.S45[:, 512:528], onesf[0:1, :], ue_row, start=True, stop=True)
    tmm = smq.inc(i4)
    k.w("vector", smq, tmm)
    wf = k.sba("O_wf", [128, 16], F32, 28)
    k.dve("tensor_copy", out=uecolf[:], in_=k.S45[:, 512:528])
    k.dve("tensor_copy", out=k.uecol_u32[:], in_=uecolf[:])
    k.dve("tensor_scalar", out=wf[:], in0=uecolf[:], scalar1=128.0,
          scalar2=iota[:, 0:1], op0=ALU.mult, op1=ALU.add)
    tw = k.dve("tensor_copy", out=k.widx_u32[:], in_=wf[:])
    for g in range(8):
        k.dve("tensor_scalar", out=wf[:], in0=uecolf[:], scalar1=1024.0,
              scalar2=iota[:, 0:1], op0=ALU.mult, op1=ALU.add)
        k.dve("tensor_scalar", out=wf[:], in0=wf[:], scalar1=float(g * 128),
              scalar2=None, op0=ALU.add, op1=ALU.bypass)
        tw = k.dve("tensor_copy", out=k.wq_u32[:, g, :], in_=wf[:])
    k.barrier([(sSC, sSC.n), (k.CHV, tw), (sB, sB.n), (sL, sL.n)])


def emit_Cs(k, call):
    D, DC, FC, NE, TOK = k.D, k.DC, k.FC, k.NE, k.TOK
    Wu1 = k.sba("S_w1", [128, DC, k.FF], BF16, 0)
    Wu2 = k.sba("S_w2", [128, FC, D], BF16, 64)
    hidT = k.sba("S_hid", [128, FC, 512], BF16, 128)
    h2row = k.sba("S_h2r", [128, 2, D], F32, 160)
    if "S_h2T" not in k._sb:
        k._sb["S_h2T"] = k.nc.alloc_sbuf_tensor_at(
            "S_h2T", [128, DC, 512], BF16, offset=k.off_xr1).ap()
        k._sb["S_be2g"] = k.nc.alloc_sbuf_tensor_at(
            "S_be2g", [128, D], F32, offset=k.off_lnt).ap()
        k._sb["S_be2b"] = k.nc.alloc_sbuf_tensor_at(
            "S_be2b", [1, D], BF16, offset=k.off_sq).ap()
    h2T = k._sb["S_h2T"]
    be2g = k._sb["S_be2g"]
    be2b = k._sb["S_be2b"]
    eob = k.xr2

    sW1 = k.sem("s_w1"); sW2 = k.sem("s_w2"); sBE = k.sem("s_be")
    sO = k.sem("s_ord"); sH = k.sem("s_h2")
    smq = k.sem("s_mm"); sm2 = k.sem("s_mm2"); smt = k.sem("s_mt")
    sSC = [k.sem("s_sc0"), k.sem("s_sc1")]
    stt = k.st
    stt[("s_ps", 0)] = 0
    stt[("s_ps", 1)] = 0

    We1v = k.We1b.rearrange("e g p f -> (e g p) f")
    We2v = k.We2b.rearrange("e g p f -> (e g p) f")
    be1v = k.inp["f_be1"].rearrange("e p f -> (e p) f")
    be2v = k.inp["f_be2"].rearrange("e o d -> (e o) d")

    def flush_scatters(pend):
        for col, sl, teo in pend:
            k.w("gpsimd", k.CHV, teo)
            i = None
            if k.on("gpsimd"):
                i = k.eng.indirect_dma_start(
                    out=k.moe_s,
                    out_offset=IndirectOffsetOnAxis(
                        ap=k.oidx_u32[:, col:col + 1], axis=0),
                    in_=eob[:, sl, :], in_offset=None,
                    bounds_check=k.st["breg"], oob_is_err=False)
            tsc = sSC[sl].inc(i, 16)
            stt[("s_eo", sl)] = tsc
            stt[("s_eo_sc", col)] = tsc
        del pend[:]

    pending = []
    for j in range(NU_MOE):
        k.w("gpsimd", smq, stt.get("s_w1_users", 0))
        if j == 0 and call == 0:
            k.w("gpsimd", k.sem("wc"), k.st.get("wconv_t", 0))
        tw1a = tw1b = tw2 = tbe = None
        for g in range(8):
            i = None
            if k.on("gpsimd"):
                i = k.eng.indirect_dma_start(
                    out=Wu1[:, g, :], out_offset=None, in_=We1v,
                    in_offset=IndirectOffsetOnAxis(
                        ap=k.wq_u32[:, g, j:j + 1], axis=0),
                    bounds_check=k.st["wreg2"], oob_is_err=False)
            tt = sW1.inc(i, 16)
            if g == 3:
                tw1a = tt
        tw1b = sW1.n
        k.w("gpsimd", sm2, stt.get("s_w2_users", 0))
        for g in range(8):
            i = None
            if k.on("gpsimd"):
                i = k.eng.indirect_dma_start(
                    out=Wu2[:, g * 4:(g + 1) * 4, :].rearrange(
                        "p c f -> p (c f)"),
                    out_offset=None, in_=We2v,
                    in_offset=IndirectOffsetOnAxis(
                        ap=k.wq_u32[:, g, j:j + 1], axis=0),
                    bounds_check=k.st["wreg2"], oob_is_err=False)
            tw2 = sW2.inc(i, 16)
        k.w("gpsimd", k.CHS, stt.get("s_be_users", 0))
        k.w("gpsimd", k.CHV, stt.get("s_be2_users", 0))
        i = None
        if k.on("gpsimd"):
            i = k.eng.indirect_dma_start(
                out=k.be1g[:], out_offset=None, in_=be1v,
                in_offset=IndirectOffsetOnAxis(ap=k.widx_u32[:, j:j + 1], axis=0),
                bounds_check=k.st["wreg"], oob_is_err=False)
        sBE.inc(i, 16)
        i = None
        if k.on("gpsimd"):
            i = k.eng.indirect_dma_start(
                out=be2g[:], out_offset=None, in_=be2v,
                in_offset=IndirectOffsetOnAxis(ap=k.uecol_u32[:, j:j + 1], axis=0),
                bounds_check=k.st["ereg"], oob_is_err=False)
        tbe = sBE.inc(i, 16)
        k.w("vector", sBE, tbe)
        k.w("vector", sm2, stt.get("s_w2_users", 0))
        tb2 = k.dve("tensor_copy", out=be2b[0:1, :], in_=be2g[0:1, :])
        stt["s_be2_users"] = tb2

        flush_scatters(pending)

        for it in range(4):
            pos = j * 4 + it
            sl = it % 2
            col = (j % 2) * 4 + it
            k.w("sync", k.CHV, stt.get(("s_osb", sl), 0))
            i = k.dma("sync", k.osb[:, sl, 0:2],
                      k.ord[pos * 128:(pos + 1) * 128, :])
            to = sO.inc(i, 16)
            k.w("vector", sO, to)
            k.w("vector", sSC[it % 2], stt.get(("s_eo_sc", col), 0))
            k.dve("tensor_copy", out=k.oidx_u32[:, col:col + 1],
                  in_=k.osb[:, sl, 0:1])
            tcv = k.dve("tensor_copy", out=k.gcol[:, col:col + 1],
                        in_=k.osb[:, sl, 1:2])
            stt[("s_osb", sl)] = tcv
            k.w("gpsimd", k.CHV, tcv)
            k.w("gpsimd", k.CHV, stt.get(("s_h2row", sl), 0))
            i = None
            if k.on("gpsimd"):
                i = k.eng.indirect_dma_start(
                    out=h2row[:, sl, :], out_offset=None, in_=k.h2_s,
                    in_offset=IndirectOffsetOnAxis(
                        ap=k.oidx_u32[:, col:col + 1], axis=0),
                    bounds_check=k.st["breg"], oob_is_err=False)
            th = sH.inc(i, 16)
            k.w("vector", sH, th)
            war_ = [(smq, stt.get("s_h2T_users", 0))] if it == 0 else []
            the = emit_ct(k, h2row[:, sl, :],
                          h2T[:, :, it * 128:(it + 1) * 128], sl, smt, war=war_)
            stt[("s_h2row", sl)] = the
        txn = k.st["ptfree"]

        k.w("tensor", k.CHV, txn)
        for fc in range(FC):
            r = fc % 2
            ps = k.S01[:, r * 512:(r + 1) * 512]
            k.w("tensor", sW1, tw1b)
            if fc == 0:
                k.w("tensor", sBE, tbe)
            k.w("tensor", k.CHS, stt[("s_ps", r)])
            last = None
            for kc in range(DC):
                ii = k.mm(ps, Wu1[:, kc, fc * 128:(fc + 1) * 128],
                          h2T[:, kc, :], start=(kc == 0), stop=(kc == DC - 1))
                if ii is not None:
                    last = ii
            tg = smq.inc(last)
            k.w("scalar", smq, tg)
            k.w("scalar", sBE, tbe)
            k.w("scalar", sm2, stt.get("s_hid_users", 0))
            stt[("s_ps", r)] = k.act(hidT[:, fc, :], ps, AF.Relu,
                                     bias=k.be1g[:, fc:fc + 1])
        stt["s_w1_users"] = smq.n
        stt["s_h2T_users"] = smq.n
        stt["s_be_users"] = k.CHS.n

        k.w("tensor", k.CHS, max(stt[("s_ps", 0)], stt[("s_ps", 1)]))
        k.w("tensor", sW2, tw2)
        k.w("tensor", k.CHV, tb2)
        for it in range(4):
            sl = it % 2
            col = (j % 2) * 4 + it
            ps2 = k.S23 if it % 2 == 0 else k.S45
            k.w("tensor", k.CHV, stt.get(("s_ps2", it % 2), 0))
            last = None
            for half in range(2):
                for sl2 in range(FC):
                    ii = k.mm(ps2[:, half * 512:(half + 1) * 512],
                              hidT[:, sl2, it * 128:(it + 1) * 128],
                              Wu2[:, sl2, half * 512:(half + 1) * 512],
                              start=(sl2 == 0), stop=False)
                    if ii is not None:
                        last = ii
                ii = k.mm(ps2[:, half * 512:(half + 1) * 512], k.ones_sb[:],
                          be2b[0:1, half * 512:(half + 1) * 512],
                          start=False, stop=True)
                if ii is not None:
                    last = ii
            tgm = sm2.inc(last)
            k.w("vector", sm2, tgm)
            k.w("vector", sSC[sl], stt.get(("s_eo", sl), 0))
            teo = k.dve("tensor_scalar", out=eob[:, sl, :], in0=ps2,
                        scalar1=k.gcol[:, col:col + 1], scalar2=None,
                        op0=ALU.mult, op1=ALU.bypass)
            stt[("s_ps2", it % 2)] = teo
            if it < 2:
                flush_scatters([(col, sl, teo)])
            else:
                pending.append((col, sl, teo))
        stt["s_hid_users"] = sm2.n
        stt["s_w2_users"] = sm2.n
    flush_scatters(pending)
    k.barrier([(sSC[0], sSC[0].n), (sSC[1], sSC[1].n)])


def emit_rt(k, call):
    S, DC = k.S, k.DC
    rT = k.sba("A_rT", [128, DC, S], BF16, 64)
    Wr = k.sba("A_wr", [128, DC, k.D], BF16, 96)
    peb = k.sba("A_peb", [128, 2, DC, 512], BF16, 112)
    sW = k.sem("r_wld")
    sPE = [k.sem("r_pe0"), k.sem("r_pe1")]
    smq = k.sem("r_mm")
    i = k.dma("gpsimd", Wr[:], k.inp["f_Wr"].rearrange("(c p) f -> p c f", p=128))
    tW = sW.inc(i, 16)
    ev = [0, 0]
    pe_users = [0, 0]
    tlast = 0
    for n in range(S // 512):
        par = n % 2
        k.w("gpsimd", smq, pe_users[par])
        i = k.dma("gpsimd", peb[:, par], k.inp["peT"].rearrange(
            "(c p) t -> p c t", p=128)[:, :, n * 512:(n + 1) * 512])
        tpe = sPE[par].inc(i, 16)
        k.w("tensor", sW, tW)
        k.w("tensor", sPE[par], tpe)
        for oc in range(DC):
            r = (n * DC + oc) % 2
            ps = k.S01[:, r * 512:(r + 1) * 512]
            k.w("tensor", k.CHV, ev[r])
            last = None
            for kc in range(DC):
                ii = k.mm(ps, Wr[:, kc, oc * 128:(oc + 1) * 128], peb[:, par, kc, :],
                          start=(kc == 0), stop=(kc == DC - 1))
                if ii is not None:
                    last = ii
            tg = smq.inc(last)
            k.w("vector", smq, tg)
            tlast = k.dve("tensor_copy", out=rT[:, oc, n * 512:(n + 1) * 512], in_=ps)
            ev[r] = tlast
        pe_users[par] = smq.n
    k.barrier([(k.CHV, tlast)])
    return rT


def emit_p0(k, call, s, x_in):
    S, D, DC, NT = k.S, k.D, k.DC, k.NT
    kT = k.sba("A_kT", [128, DC, S], BF16, 0)
    V = k.sba("A_v", [128, NT, D], BF16, 32)
    Wq = k.sba("A_wqkv", [128, DC, 3 * D], BF16, 96)
    xnT = k.sba("A_xnT", [128, 2, DC, 512], BF16, 144)
    qTb = k.sba("A_qTb", [128, 2, 512], BF16, 160)

    sX = [k.sem("p_x0"), k.sem("p_x1")]
    sW = k.sem("p_wld")
    smq = k.sem("p_mmq")
    smt = k.sem("p_mmt")
    sSX = [k.sem("p_sx0"), k.sem("p_sx1")]
    sSQ = [k.sem("p_sq0"), k.sem("p_sq1")]

    wsrc = k.inp["f_Wqkv"].rearrange("(c p) f -> p c f", p=128)
    i = k.dma("gpsimd", Wq[:, :, 0:2048], wsrc[:, :, 0:2048])
    sW.inc(i, 16)
    i = k.dma("gpsimd", Wq[:, :, 2048:3072], wsrc[:, :, 2048:3072])
    tW = sW.inc(i, 16)
    if call == 0 and "wconv_t" not in k.st:
        emit_wconv(k)

    xnT_users = [0, 0]
    ps_ev = [0, 0, 0, 0]
    v_ev = [0, 0]
    qsp = [0, 0]
    xr1_rd = k.st.setdefault("p_xr1_rd", [0, 0])
    xr2_rd = k.st.setdefault("p_xr2_rd", [0, 0])
    for n in range(S // 512):
        par = n % 2
        for it in range(4):
            tau = n * 4 + it
            gtau = s * NT + tau
            slot = tau % 2
            k.w("sync", k.CHV, xr1_rd[slot])
            i = k.dma("sync", k.xr1[:, slot, :],
                      x_in[gtau * 128:(gtau + 1) * 128, :])
            tx = sX[slot].inc(i, 16)
            k.w("vector", sX[slot], tx)
            k.w("vector", sSX[slot], xr2_rd[slot])
            tn = emit_ln(k, k.xr1[:, slot, :], k.xr2[:, slot, :])
            xr1_rd[slot] = tn
            k.w("sync", k.CHV, tn)
            i = k.dma("sync", k.xn_s[gtau * 128:(gtau + 1) * 128, :],
                      k.xr2[:, slot, :])
            xr2_rd[slot] = sSX[slot].inc(i, 16)
            emit_ct(k, k.xr2[:, slot, :], xnT[:, par, :, it * 128:(it + 1) * 128],
                    slot, smt, war=[(smq, xnT_users[par])] if it == 0 else ())
        txn = k.st["ptfree"]
        k.w("tensor", k.CHV, txn)
        k.w("tensor", sW, tW)
        pi = 0
        for which, base in (("q", 0), ("k", D)):
            for oc in range(DC):
                r = pi % 4
                ps = [k.S01[:, 0:512], k.S01[:, 512:1024],
                      k.S23[:, 0:512], k.S23[:, 512:1024]][r]
                k.w("tensor", k.CHV, ps_ev[r])
                last = None
                for kc in range(DC):
                    ii = k.mm(ps, Wq[:, kc, base + oc * 128:base + oc * 128 + 128],
                              xnT[:, par, kc, :],
                              start=(kc == 0), stop=(kc == DC - 1))
                    if ii is not None:
                        last = ii
                tgm = smq.inc(last)
                k.w("vector", smq, tgm)
                if which == "k":
                    ps_ev[r] = k.dve("tensor_copy",
                                     out=kT[:, oc, n * 512:(n + 1) * 512], in_=ps)
                else:
                    qs = pi % 2
                    k.w("vector", sSQ[qs], qsp[qs])
                    tqe = k.dve("tensor_copy", out=qTb[:, qs, :], in_=ps)
                    ps_ev[r] = tqe
                    k.w("sync", k.CHV, tqe)
                    qdst = k.qT_s[s * NT + n * 4: s * NT + n * 4 + 4, :,
                                  oc * 128:(oc + 1) * 128].rearrange("t p f -> p t f")
                    i = k.dma("sync", qdst,
                              qTb[:, qs, :].rearrange("p (t f) -> p t f", t=4))
                    qsp[qs] = sSQ[qs].inc(i, 16)
                pi += 1
        for it in range(4):
            for half in range(2):
                ps = k.S45[:, half * 512:(half + 1) * 512]
                k.w("tensor", k.CHV, v_ev[half])
                last = None
                for kc in range(DC):
                    ii = k.mm(ps, xnT[:, par, kc, it * 128:(it + 1) * 128],
                              Wq[:, kc, 2 * D + half * 512:2 * D + (half + 1) * 512],
                              start=(kc == 0), stop=(kc == DC - 1))
                    if ii is not None:
                        last = ii
                tgm = smq.inc(last)
                k.w("vector", smq, tgm)
                v_ev[half] = k.dve(
                    "tensor_copy",
                    out=V[:, n * 4 + it, half * 512:(half + 1) * 512], in_=ps)
        xnT_users[par] = smq.n
    k.barrier([(k.CHV, k.CHV.n), (sSX[0], sSX[0].n), (sSX[1], sSX[1].n),
               (sSQ[0], sSQ[0].n), (sSQ[1], sSQ[1].n)])
    return kT, V


def emit_attn(k, call, s, kT, V, rT):
    S, D, DC, NT, NH = k.S, k.D, k.DC, k.NT, k.NH
    SEGW = k.SEGW
    qTt = k.sba("A_qTt", [128, 2, DC, 128], BF16, 96)
    qu = k.sba("A_qu", [128, 2, DC, 128], BF16, 100)
    qv = k.sba("A_qv", [128, 2, DC, 128], BF16, 104)
    Gsb = k.sba("A_G", [128, 2, S], BF16, 108)
    bd = k.sba("A_bd", [128, 2, S], BF16, 116)
    ssb = k.sba("A_s", [128, 2, S], F32, 124)
    Psb = k.sba("A_P", [128, 2, S], BF16, 140)
    PTs = k.sba("A_PTs", [128, NT, 128], BF16, 148)
    ctx = k.sba("A_ctx", [128, 2, D], F32, 152)

    sQ = [k.sem("a_q0"), k.sem("a_q1")]
    sG = k.sem("a_g")
    sSPL = [k.sem("a_spl0"), k.sem("a_spl1")]
    sBD = [k.sem("a_bd0"), k.sem("a_bd1")]
    sAC = k.sem("a_ac")
    sTR = k.sem("a_tr")
    sPV = k.sem("a_pv")
    sCS = [k.sem("a_cs0"), k.sem("a_cs1")]
    SC = [k.S01, k.S23, k.S45]
    stt = k.st

    for t in range(NT):
        KT = (t + 1) * 128
        nseg = (KT + SEGW - 1) // SEGW
        segs = [(si * SEGW, min(SEGW, KT - si * SEGW)) for si in range(nseg)]
        gtau = s * NT + t
        tp = t % 2
        k.w("sync", k.CHV, stt.get(("a_quv", tp), 0))
        i = k.dma("sync", qTt[:, tp],
                  k.qT_s[gtau].rearrange("p (c f) -> p c f", c=DC))
        tq = sQ[tp].inc(i, 16)
        k.w("vector", sQ[tp], tq)
        k.w("vector", sG, stt.get(("a_g_users", tp), 0))
        k.w("vector", sAC, stt.get(("a_ac_users", tp), 0))
        tquv = 0
        for c in range(DC):
            k.dve("tensor_scalar", out=qu[:, tp, c, :], in0=qTt[:, tp, c, :],
                  scalar1=k.ub_sb[:, c:c + 1], scalar2=0.125,
                  op0=ALU.add, op1=ALU.mult)
            tquv = k.dve("tensor_scalar", out=qv[:, tp, c, :], in0=qTt[:, tp, c, :],
                         scalar1=k.vb_sb[:, c:c + 1], scalar2=0.125,
                         op0=ALU.add, op1=ALU.mult)
        stt[("a_quv", tp)] = tquv

        units = [(h, si) for h in range(NH) for si in range(nseg)]
        NU = len(units)
        tG = [0] * NU; tGE = [0] * NU; tAC = [0] * NU; tSE = [0] * NU
        tBDh = [0] * NH; tMKh = [0] * NH; tSPLh = [0] * NH
        tEXh = [0] * NH; tNMh = [0] * NH; tRDh = [0] * NH
        tCEh = [0] * NH; tTRh = [0] * NH

        def emit_G(u):
            h, si = units[u]
            base, w_ = segs[si]
            hc, poff = h // 2, 64 * (h % 2)
            slot = SC[u % 3]
            k.w("tensor", k.CHV, tSE[u - 3] if u >= 3 else stt.get("a_se_prev", 0))
            k.w("tensor", k.CHV, tquv)
            last = None
            for cc in range(0, w_, 512):
                cw = min(512, w_ - cc)
                ii = k.mm(slot[:, cc:cc + cw], qv[poff:poff + 64, tp, hc, :],
                          rT[poff:poff + 64, hc,
                             S - KT + base + cc:S - KT + base + cc + cw],
                          start=True, stop=True)
                if ii is not None:
                    last = ii
            tG[u] = sG.inc(last)
            k.w("scalar", sG, tG[u])
            k.w("scalar", sSPL[h % 2],
                tSPLh[h - 2] if h >= 2 else stt.get(("a_spl_prev", h % 2), 0))
            tGE[u] = k.act(Gsb[:, h % 2, base:base + w_], slot[:, 0:w_],
                           AF.Copy)

        def emit_SPILL(h):
            KTl = KT
            qh = "gpsimd" if h % 2 == 0 else "sync"
            k.w(qh, k.CHS, tGE[(h + 1) * nseg - 1])
            k.w(qh, sBD[h % 2],
                tBDh[h - 2] if h >= 2 else stt.get(("a_bd_prev", h % 2), 0))
            gslot = k.G_s[h % 2]
            wr = gslot.copy()
            base_off = wr.offset
            wr.ap = bass_rust.VecI64Pair([[KTl, 128], [1, KTl]])
            wr.offset = base_off
            i = k.dma(qh, wr, Gsb[:, h % 2, 0:KTl])
            tSPLh[h] = sSPL[h % 2].inc(i, 16)
            k.w(qh, sSPL[h % 2], tSPLh[h])
            k.w(qh, sAC,
                tAC[(h - 1) * nseg - 1] if h >= 2 else stt.get("a_ac_prev", 0))
            rd = gslot.copy()
            rd.ap = bass_rust.VecI64Pair([[KTl - 1, 128], [1, KTl]])
            rd.offset = base_off + 127
            i = k.dma(qh, bd[:, h % 2, 0:KTl], rd)
            tBDh[h] = sBD[h % 2].inc(i, 16)

        def emit_AC(u):
            h, si = units[u]
            base, w_ = segs[si]
            hc, poff = h // 2, 64 * (h % 2)
            slot = SC[u % 3]
            k.w("tensor", k.CHS, tGE[u])
            k.w("tensor", sBD[h % 2], tBDh[h])
            last = None
            for cc in range(0, w_, 512):
                cw = min(512, w_ - cc)
                is_last = (si == nseg - 1) and (cc + cw == w_)
                ii = k.mm(slot[:, cc:cc + cw], qu[poff:poff + 64, tp, hc, :],
                          kT[poff:poff + 64, hc, base + cc:base + cc + cw],
                          start=True, stop=False)
                if ii is not None:
                    last = ii
                ii = k.mm(slot[:, cc:cc + cw], k.ident_sb[:],
                          bd[:, h % 2, base + cc:base + cc + cw],
                          start=False, stop=not is_last)
                if ii is not None:
                    last = ii
                if is_last:
                    ii = k.mm(slot[:, cc:cc + cw], k.ident_sb[:],
                              k.cmask_sb[:, 512 - cw:512],
                              start=False, stop=True)
                    if ii is not None:
                        last = ii
            tAC[u] = sAC.inc(last)
            k.w("vector", sAC, tAC[u])
            if h >= 2:
                k.w("vector", k.CHS, tEXh[h - 2])
            tSE[u] = k.dve("tensor_scalar", out=ssb[:, h % 2, base:base + w_],
                           in0=slot[:, 0:w_], scalar1=1.0, scalar2=None,
                           op0=ALU.mult, op1=ALU.max,
                           accum_out=k.mx2_sb[:, si:si + 1])
            if si == nseg - 1:
                tNMh[h] = k.dve("reduce_max", out=k.nmx_sb[:, h % 2:h % 2 + 1],
                                in_=k.mx2_sb[:, 0:nseg], axis=AXX, negate=True)
                k.w("scalar", k.CHV, tNMh[h])
                if h >= 2:
                    k.w("scalar", sTR, tTRh[h - 2])
                tEXh[h] = k.act(Psb[:, h % 2, 0:KT], ssb[:, h % 2, 0:KT], AF.Exp,
                                bias=k.nmx_sb[:, h % 2:h % 2 + 1],
                                accum_out=k.den_sb[:, h % 2:h % 2 + 1])

        def emit_TPV(h):
            ntile = t + 1
            ngrp = (ntile + 7) // 8
            cp = k.CTXP if h % 2 == 0 else k.CTXP2
            k.w("tensor", k.CHS, max(tEXh[h], tCEh[h - 2] if h >= 2 else
                                     stt.get(("a_ce_prev", h % 2), 0)))
            last_pv = None
            for g in range(ngrp):
                j0, j1 = g * 8, min(g * 8 + 8, ntile)
                k.w("tensor", k.CHV, k.st.get("ptfree", 0))
                last = None
                for j in range(j0, j1):
                    ii = k.tr(k.PT[:, (j - j0) * 128:(j - j0 + 1) * 128],
                              Psb[:, h % 2, j * 128:(j + 1) * 128])
                    if ii is not None:
                        last = ii
                tg = sTR.inc(last)
                k.w("vector", sTR, tg)
                te = k.dve("tensor_copy", out=PTs[:, j0:j1, :],
                           in_=k.PT[:, 0:(j1 - j0) * 128].rearrange(
                               "p (j f) -> p j f", f=128))
                k.st["ptfree"] = te
                k.w("tensor", k.CHV, te)
                for j in range(j0, j1):
                    ii = k.mm(cp, PTs[:, j, :], V[:, j, h * 64:(h + 1) * 64],
                              start=(j == 0), stop=(j == ntile - 1))
                    if ii is not None:
                        last_pv = ii
            tTRh[h] = sTR.n
            tPVh = sPV.inc(last_pv)
            k.w("vector", k.CHS, tEXh[h])
            tRDh[h] = k.dve("reciprocal", out=k.rden_sb[:, h % 2:h % 2 + 1],
                            in_=k.den_sb[:, h % 2:h % 2 + 1])
            k.w("scalar", sPV, tPVh)
            k.w("scalar", k.CHV, tRDh[h])
            k.w("scalar", sCS[t % 2], stt.get(("a_cs", t % 2), 0))
            tCEh[h] = k.act(ctx[:, t % 2, h * 64:(h + 1) * 64], cp, AF.Copy,
                            scale=k.rden_sb[:, h % 2:h % 2 + 1])

        emit_G(0)
        if NU > 1:
            emit_G(1)
        for uu in range(min(2, NU)):
            hh, ss = units[uu]
            if ss == nseg - 1:
                emit_SPILL(hh)
        for u in range(NU):
            if u + 2 < NU:
                emit_G(u + 2)
            emit_AC(u)
            if u + 2 < NU and units[u + 2][1] == nseg - 1:
                emit_SPILL(units[u + 2][0])
            h, si = units[u]
            if si == nseg - 1 and h >= 1:
                emit_TPV(h - 1)
        emit_TPV(NH - 1)
        k.w("sync", k.CHS, tCEh[NH - 1])
        i = k.dma("sync", k.ctx_s[gtau * 128:(gtau + 1) * 128, :], ctx[:, t % 2, :])
        stt[("a_cs", t % 2)] = sCS[t % 2].inc(i, 16)
        stt["a_se_prev"] = tSE[NU - 1]
        stt[("a_spl_prev", 0)] = tSPLh[NH - 2]
        stt[("a_spl_prev", 1)] = tSPLh[NH - 1]
        stt[("a_bd_prev", 0)] = tBDh[NH - 2]
        stt[("a_bd_prev", 1)] = tBDh[NH - 1]
        stt["a_ac_prev"] = tAC[NU - 1]
        stt[("a_ce_prev", 0)] = tCEh[NH - 2]
        stt[("a_ce_prev", 1)] = tCEh[NH - 1]
        stt[("a_g_users", t % 2)] = tG[NU - 1]
        stt[("a_ac_users", t % 2)] = tAC[NU - 1]
    k.barrier([(sCS[0], sCS[0].n), (sCS[1], sCS[1].n)])


def emit_B(k, call):
    D, DC, NTT = k.D, k.DC, k.NTT
    Wo = k.sba("B_wo", [128, DC, D], BF16, 0)
    lng = k.sba("B_lng", [128, 2, D], F32, 16)
    ctr = k.sba("B_ctr", [128, 2, D], F32, 24)
    cT = k.sba("B_cT", [128, DC, 128], BF16, 32)
    h1r = k.sba("B_h1r", [128, 2, D], F32, 36)
    h1Tb = k.sba("B_h1T", [128, DC, 128], BF16, 44)

    sC = [k.sem("b_c0"), k.sem("b_c1")]
    sX = [k.sem("b_x0"), k.sem("b_x1")]
    sLg = k.sem("b_lng"); sW = k.sem("b_wld")
    smq = k.sem("b_mm"); smt = k.sem("b_mt")
    sS1 = [k.sem("b_s10"), k.sem("b_s11")]
    sST = k.sem("b_sT")

    i = k.dma("gpsimd", Wo[:], k.inp["f_Wo"].rearrange("(c p) f -> p c f", p=128))
    tW = sW.inc(i, 16)
    i = k.dma("sync", lng[:, 0, :], k.inp["f_lng"][:, 0, :])
    sLg.inc(i, 16)
    i = k.dma("sync", lng[:, 1, :], k.inp["f_lnb"][:, 0, :])
    tlg = sLg.inc(i, 16)

    stt = k.st
    for tau in range(NTT):
        sl = tau % 2
        k.w("sync", k.CHV, stt.get(("b_ctr", sl), 0))
        i = k.dma("sync", ctr[:, sl, :], k.ctx_s[tau * 128:(tau + 1) * 128, :])
        tc = sC[sl].inc(i, 16)
        k.w("sync", k.CHV, stt.get(("b_xr1", sl), 0))
        i = k.dma("sync", k.xr1[:, sl, :], k.xn_s[tau * 128:(tau + 1) * 128, :])
        tx = sX[sl].inc(i, 16)
        k.w("vector", sC[sl], tc)
        tct = emit_ct(k, ctr[:, sl, :], cT[:], sl, smt,
                      war=[(smq, stt.get("b_ct_users", 0))])
        stt[("b_ctr", sl)] = tct
        k.w("tensor", k.CHV, tct)
        k.w("tensor", sW, tW)
        for half in range(2):
            ps = k.S01[:, half * 512:(half + 1) * 512]
            k.w("tensor", k.CHV, stt.get(("b_ps", half), 0))
            last = None
            for kc in range(DC):
                ii = k.mm(ps, cT[:, kc, :], Wo[:, kc, half * 512:(half + 1) * 512],
                          start=(kc == 0), stop=(kc == DC - 1))
                if ii is not None:
                    last = ii
            tgm = smq.inc(last)
            k.w("vector", smq, tgm)
            k.w("vector", sX[sl], tx)
            tu = k.dve("tensor_tensor",
                       out=k.xr2[:, sl, half * 512:(half + 1) * 512], in0=ps,
                       in1=k.xr1[:, sl, half * 512:(half + 1) * 512], op=ALU.add)
            stt[("b_ps", half)] = tu
            stt[("b_xr1", sl)] = tu
        stt["b_ct_users"] = smq.n
        k.w("vector", sLg, tlg)
        k.w("vector", sS1[sl], stt.get(("b_h1r", sl), 0))
        tn = emit_ln(k, k.xr2[:, sl, :], k.lnt[:],
                     lng=lng[:, 0, :], lnb=lng[:, 1, :], affout=h1r[:, sl, :])
        k.w("sync", k.CHV, tn)
        i = k.dma("sync", k.h1_s[tau * 128:(tau + 1) * 128, :], h1r[:, sl, :])
        tsp = sS1[sl].inc(i, 16)
        stt[("b_h1r", sl)] = tsp
        the = emit_ct(k, h1r[:, sl, :], h1Tb[:], sl, smt,
                      war=[(sST, stt.get("b_h1T_sp", 0))])
        k.w("sync", k.CHV, the)
        i = k.dma("sync",
                  k.h1T_s.rearrange("c p t -> p c t")[:, :, tau * 128:(tau + 1) * 128],
                  h1Tb[:])
        stt["b_h1T_sp"] = sST.inc(i, 16)
    k.barrier([(sS1[0], sS1[0].n), (sS1[1], sS1[1].n), (sST, sST.n)])


def emit_B2(k, call):
    D, DC, FC, NE, TOK = k.D, k.DC, k.FC, k.NE, k.TOK
    W1 = k.sba("F_w1", [128, DC, k.FF], BF16, 0)
    W2 = k.sba("F_w2", [128, FC, D], BF16, 64)
    hidT = k.sba("F_hid", [128, FC, 256], BF16, 128)
    h1T = k.sba("F_h1T", [128, 2, DC, 256], BF16, 144)
    lng = k.sba("F_lng", [128, 2, D], F32, 152)
    h2Tb = k.sba("F_h2T", [128, DC, 128], BF16, 160)

    sH = [k.sem("f_h0"), k.sem("f_h1")]
    sX = [k.sem("f_x0"), k.sem("f_x1")]
    sW1s = k.sem("f_w1"); sW2s = k.sem("f_w2"); sLg = k.sem("f_lng")
    smq = k.sem("f_mm"); sm2 = k.sem("f_mm2")
    smt = k.sem("f_mt"); smg = k.sem("f_gmm")
    sS2 = [k.sem("f_s20"), k.sem("f_s21")]
    sST = k.sem("f_sT")

    w1src = k.inp["f_Wff1"].rearrange("(c p) f -> p c f", p=128)
    i = k.dma("gpsimd", W1[:, :, 0:2048], w1src[:, :, 0:2048])
    sW1s.inc(i, 16)
    i = k.dma("gpsimd", W1[:, :, 2048:4096], w1src[:, :, 2048:4096])
    tW1 = sW1s.inc(i, 16)
    i = k.dma("gpsimd", W2[:], k.inp["f_Wff2"].rearrange("(c p) f -> p c f", p=128))
    tW2 = sW2s.inc(i, 16)
    i = k.dma("sync", lng[:, 0, :], k.inp["f_lng"][:, 1, :])
    sLg.inc(i, 16)
    i = k.dma("sync", lng[:, 1, :], k.inp["f_lnb"][:, 1, :])
    tlg = sLg.inc(i, 16)

    stt = k.st
    stt[("f_ps", 0)] = 0
    stt[("f_ps", 1)] = 0
    for st_ in range(TOK // 256):
        hpar = st_ % 2
        k.w("sync", smq, stt.get(("f_h1T_users", hpar), 0))
        i = k.dma("sync", h1T[:, hpar],
                  k.h1T_s.rearrange("c p t -> p c t")[:, :, st_ * 256:(st_ + 1) * 256])
        th = sH[hpar].inc(i, 16)
        k.w("tensor", sH[hpar], th)
        k.w("tensor", sW1s, tW1)
        for fc in range(FC):
            r = fc % 2
            ps = k.S01[:, r * 512:r * 512 + 256]
            k.w("tensor", k.CHS, stt[("f_ps", r)])
            last = None
            for kc in range(DC):
                ii = k.mm(ps, W1[:, kc, fc * 128:(fc + 1) * 128], h1T[:, hpar, kc, :],
                          start=(kc == 0), stop=(kc == DC - 1))
                if ii is not None:
                    last = ii
            tg = smq.inc(last)
            k.w("scalar", smq, tg)
            k.w("scalar", sm2, stt.get("f_hid_users", 0))
            stt[("f_ps", r)] = k.act(hidT[:, fc, :], ps, AF.Relu,
                                     bias=k.bff1_sb[:, fc:fc + 1])
        stt[("f_h1T_users", hpar)] = smq.n
        k.w("tensor", k.CHS, max(stt[("f_ps", 0)], stt[("f_ps", 1)]))
        k.w("tensor", sW2s, tW2)
        for it in range(2):
            tau = st_ * 2 + it
            sl = tau % 2
            ps2 = k.S23 if it % 2 == 0 else k.S45
            k.w("sync", k.CHV, stt.get(("f_xr1", sl), 0))
            k.w("sync", sS2[sl], stt.get(("f_xr1s", sl), 0))
            i = k.dma("sync", k.xr1[:, sl, :], k.h1_s[tau * 128:(tau + 1) * 128, :])
            tx = sX[sl].inc(i, 16)
            k.w("tensor", k.CHV, stt.get(("f_ps2", it % 2), 0))
            last = None
            for half in range(2):
                for sl2 in range(FC):
                    ii = k.mm(ps2[:, half * 512:(half + 1) * 512],
                              hidT[:, sl2, it * 128:(it + 1) * 128],
                              W2[:, sl2, half * 512:(half + 1) * 512],
                              start=(sl2 == 0), stop=False)
                    if ii is not None:
                        last = ii
                ii = k.mm(ps2[:, half * 512:(half + 1) * 512], k.ones_sb[:],
                          k.bff2_sb[:, half * 512:(half + 1) * 512],
                          start=False, stop=True)
                if ii is not None:
                    last = ii
            tgm = sm2.inc(last)
            k.w("vector", sm2, tgm)
            k.w("vector", sX[sl], tx)
            k.w("vector", sS2[sl], stt.get(("f_xr2", sl), 0))
            tu2 = k.dve("tensor_tensor", out=k.xr2[:, sl, :], in0=ps2,
                        in1=k.xr1[:, sl, :], op=ALU.add)
            stt[("f_ps2", it % 2)] = tu2
            stt[("f_xr1", sl)] = tu2
            k.w("vector", sLg, tlg)
            tn = emit_ln(k, k.xr2[:, sl, :], k.lnt[:],
                         lng=lng[:, 0, :], lnb=lng[:, 1, :],
                         affout=k.xr1[:, sl, :])
            stt[("f_xr1", sl)] = tn
            k.w("sync", k.CHV, tn)
            i = k.dma("sync", k.h2_s[tau * 128:(tau + 1) * 128, :], k.xr1[:, sl, :])
            th2 = sS2[sl].inc(i, 16)
            stt[("f_xr2", sl)] = th2
            stt[("f_xr1s", sl)] = th2
            the = emit_ct(k, k.xr1[:, sl, :], h2Tb[:], sl, smt,
                          war=[(sST, stt.get("f_h2T_sp", 0)),
                               (smg, stt.get("f_h2T_users", 0))])
            stt[("f_xr1", sl)] = max(stt[("f_xr1", sl)], the)
            k.w("sync", k.CHV, the)
            i = k.dma("sync",
                      k.h2T_s.rearrange("c p t -> p c t")[:, :, tau * 128:(tau + 1) * 128],
                      h2Tb[:])
            stt["f_h2T_sp"] = sST.inc(i, 16)
            stt["f_hid_users"] = sm2.n
            k.w("tensor", k.CHV, max(the, stt.get("f_gate_done", 0)))
            last = None
            for kc in range(DC):
                ii = k.mm(k.GATEP, h2Tb[:, kc, :], k.wg_sb[:, kc, :],
                          start=(kc == 0), stop=(kc == DC - 1))
                if ii is not None:
                    last = ii
            tgm = smg.inc(last)
            stt["f_h2T_users"] = tgm
            k.w("vector", smg, tgm)
            nl = k.st_sb[:, 8:9]; dsum = k.st_sb[:, 9:10]; gate = k.st_sb[:, 10:11]
            eq8 = k.scr8_sb[:, 0:NE]; cum8 = k.scr8_sb[:, NE:2 * NE]
            c18 = k.scr8_sb[:, 2 * NE:3 * NE]; pr8 = k.scr8_sb[:, 3 * NE:4 * NE]
            tv1 = k.dve("reduce_max", out=nl, in_=k.GATEP, axis=AXX, negate=True)
            k.w("scalar", k.CHV, tv1)
            ta1 = k.act(pr8, k.GATEP, AF.Exp, bias=nl, accum_out=dsum)
            k.w("vector", k.CHS, ta1)
            k.dve("reciprocal", out=gate, in_=dsum)
            k.dve("tensor_scalar", out=eq8, in0=k.GATEP, scalar1=nl,
                  scalar2=0.0, op0=ALU.add, op1=ALU.is_equal)
            k.dve("tensor_tensor_scan", out=cum8, data0=k.ones8_sb[:],
                  data1=eq8, initial=0.0, op0=ALU.mult, op1=ALU.add)
            k.dve("tensor_scalar", out=c18, in0=cum8, scalar1=1.0,
                  scalar2=None, op0=ALU.is_equal, op1=ALU.bypass)
            k.dve("tensor_tensor", out=k.mask8_sb[:, tau, :], in0=eq8,
                  in1=c18, op=ALU.mult)
            stt["f_gate_done"] = k.dve(
                "tensor_copy", out=k.gates_sb[:, tau:tau + 1], in_=gate)
    k.barrier([(sS2[0], sS2[0].n), (sS2[1], sS2[1].n), (sST, sST.n),
               (k.CHV, stt["f_gate_done"])])


def emit_C(k, call):
    D, DC, FC, NE, TOK = k.D, k.DC, k.FC, k.NE, k.TOK
    We1 = k.sba("C_w1", [128, DC, k.FF], BF16, 0)
    We2 = k.sba("C_w2", [128, FC, D], BF16, 64)
    hidT = k.sba("C_hid", [128, FC, 512], BF16, 128)
    if "C_h2T" not in k._sb:
        k._sb["C_h2T"] = k.nc.alloc_sbuf_tensor_at(
            "C_h2T", [128, DC, 512], BF16, offset=k.off_xr1).ap()
    h2T = k._sb["C_h2T"]
    eob = k.xr2

    sL = k.sem("c2_ld"); sW1 = k.sem("c2_w1"); sW2 = k.sem("c2_w2")
    sB = k.sem("c2_b")
    smq = k.sem("c2_mm"); sm2 = k.sem("c2_mm2")
    sSC = [k.sem("c2_sc0"), k.sem("c2_sc1")]
    stt = k.st
    stt[("c_ps", 0)] = 0
    stt[("c_ps", 1)] = 0

    def ldw1(e):
        src = k.inp["f_We1"][e].rearrange("(c p) f -> p c f", p=128)
        i = k.dma("gpsimd", We1[:, :, 0:2048], src[:, :, 0:2048])
        sW1.inc(i, 16)
        i = k.dma("gpsimd", We1[:, :, 2048:4096], src[:, :, 2048:4096])
        return sW1.inc(i, 16)

    def ldw2(e):
        i = k.dma("gpsimd", We2[:],
                  k.inp["f_We2"][e].rearrange("(c p) f -> p c f", p=128))
        return sW2.inc(i, 16)

    def ldb(e):
        i = k.dma("gpsimd", k.be1_sb[:, e % 2, :], k.inp["f_be1"][e])
        sB.inc(i, 16)
        i = k.dma("gpsimd", k.be2_sb[:, e % 2, :], k.inp["f_be2"][e])
        return sB.inc(i, 16)

    tw1 = ldw1(0)
    tb = ldb(0)
    for e in range(NE):
        k.w("gpsimd", sm2, stt.get("c_eout_done", 0))
        tw2 = ldw2(e)
        for st_ in range(TOK // 512):
            k.w("sync", smq, stt.get("c_h2T_users", 0))
            i = k.dma("sync", h2T[:],
                      k.h2T_s.rearrange("c p t -> p c t")[:, :, st_ * 512:(st_ + 1) * 512])
            th = sL.inc(i, 16)
            k.w("tensor", sL, th)
            k.w("tensor", sW1, tw1)
            k.w("tensor", sB, tb)
            for fc in range(FC):
                r = fc % 2
                ps = k.S01[:, r * 512:(r + 1) * 512]
                k.w("tensor", k.CHS, stt[("c_ps", r)])
                last = None
                for kc in range(DC):
                    ii = k.mm(ps, We1[:, kc, fc * 128:(fc + 1) * 128], h2T[:, kc, :],
                              start=(kc == 0), stop=(kc == DC - 1))
                    if ii is not None:
                        last = ii
                tg = smq.inc(last)
                k.w("scalar", smq, tg)
                k.w("scalar", sm2, stt.get("c_hid_users", 0))
                stt[("c_ps", r)] = k.act(hidT[:, fc, :], ps, AF.Relu,
                                         bias=k.be1_sb[:, e % 2, fc:fc + 1])
            stt["c_h2T_users"] = smq.n
            k.w("tensor", k.CHS, max(stt[("c_ps", 0)], stt[("c_ps", 1)]))
            k.w("tensor", sW2, tw2)
            for it in range(4):
                tau = st_ * 4 + it
                sl = tau % 2
                ps2 = k.S23 if it % 2 == 0 else k.S45
                k.w("tensor", k.CHV, stt.get(("c_ps2", it % 2), 0))
                last = None
                for half in range(2):
                    for sl2 in range(FC):
                        ii = k.mm(ps2[:, half * 512:(half + 1) * 512],
                                  hidT[:, sl2, it * 128:(it + 1) * 128],
                                  We2[:, sl2, half * 512:(half + 1) * 512],
                                  start=(sl2 == 0), stop=False)
                        if ii is not None:
                            last = ii
                    ii = k.mm(ps2[:, half * 512:(half + 1) * 512], k.ones_sb[:],
                              k.be2_sb[:, e % 2, half * 512:(half + 1) * 512],
                              start=False, stop=True)
                    if ii is not None:
                        last = ii
                tgm = sm2.inc(last)
                k.w("vector", sm2, tgm)
                k.w("vector", sSC[sl], stt.get(("c_eo", sl), 0))
                teo = k.dve("tensor_scalar", out=eob[:, sl, :], in0=ps2,
                            scalar1=k.mg8_sb[:, tau, e:e + 1], scalar2=None,
                            op0=ALU.mult, op1=ALU.bypass)
                stt[("c_ps2", it % 2)] = teo
                k.w("gpsimd", k.CHV, teo)
                i = None
                if k.on("gpsimd") and _MOE_DBG:
                    di = k.eng.indirect_dma_start(
                        out=k.moe_dbg.rearrange("e t d -> (e t) d"),
                        out_offset=IndirectOffsetOnAxis(
                            ap=k.idx_sb[:, tau, e:e + 1], axis=0),
                        in_=eob[:, sl, :], in_offset=None,
                        element_offset=e * k.TOK * k.D,
                        bounds_check=k.st["breg"], oob_is_err=False)
                else:
                    di = None
                if _MOE_DBG:
                    sSC[sl].inc(di, 16)
                if k.on("gpsimd"):
                    i = k.eng.indirect_dma_start(
                        out=k.moe_s,
                        out_offset=IndirectOffsetOnAxis(
                            ap=k.idx_sb[:, tau, e:e + 1], axis=0),
                        in_=eob[:, sl, :], in_offset=None,
                        bounds_check=k.st["breg"], oob_is_err=False)
                stt[("c_eo", sl)] = sSC[sl].inc(i, 16)
            stt["c_eout_done"] = sm2.n
        if e + 1 < NE:
            k.w("gpsimd", smq, stt["c_h2T_users"])
            tw1 = ldw1(e + 1)
            k.w("gpsimd", k.CHS, max(stt[("c_ps", 0)], stt[("c_ps", 1)]))
            tb = ldb(e + 1)
    k.barrier([(sSC[0], sSC[0].n), (sSC[1], sSC[1].n)])


def emit_D(k, call, x_in, x_out):
    D, DC, NTT = k.D, k.DC, k.NTT
    Wout = k.sba("D_wout", [128, DC, D], BF16, 0)
    lng = k.sba("D_lng", [128, 2, D], F32, 16)
    x3r = k.sba("D_x3", [128, 2, D], F32, 24)
    u3r = k.sba("D_u3", [128, 2, D], F32, 32)
    h3r = k.sba("D_h3", [128, 2, D], F32, 40)
    h3Tb = k.sba("D_h3T", [128, DC, 128], BF16, 48)

    s1 = [k.sem("d_a0"), k.sem("d_a1")]
    s2 = [k.sem("d_b0"), k.sem("d_b1")]
    s3 = [k.sem("d_c0"), k.sem("d_c1")]
    sLg = k.sem("d_lng"); sW = k.sem("d_wld")
    smq = k.sem("d_mm"); smt = k.sem("d_mt")
    sSP = [k.sem("d_o0"), k.sem("d_o1")]

    i = k.dma("gpsimd", Wout[:], k.inp["f_Wout"].rearrange("(c p) f -> p c f", p=128))
    tW = sW.inc(i, 16)
    i = k.dma("sync", lng[:, 0, :], k.inp["f_lng"][:, 2, :])
    sLg.inc(i, 16)
    i = k.dma("sync", lng[:, 1, :], k.inp["f_lnb"][:, 2, :])
    tlg = sLg.inc(i, 16)

    stt = k.st
    for tau in range(NTT):
        sl = tau % 2
        k.w("sync", k.CHV, stt.get(("d_xr", sl), 0))
        i = k.dma("sync", k.xr1[:, sl, :], k.h2_s[tau * 128:(tau + 1) * 128, :])
        t1 = s1[sl].inc(i, 16)
        i = k.dma("sync", k.xr2[:, sl, :], k.moe_s[tau * 128:(tau + 1) * 128, :])
        t2 = s2[sl].inc(i, 16)
        k.w("sync", k.CHV, stt.get(("d_x3", sl), 0))
        i = k.dma("sync", x3r[:, sl, :], x_in[tau * 128:(tau + 1) * 128, :])
        t3 = s3[sl].inc(i, 16)
        k.w("vector", s1[sl], t1)
        k.w("vector", s2[sl], t2)
        k.w("vector", sSP[sl], stt.get(("d_u3", sl), 0))
        tu3 = k.dve("tensor_tensor", out=u3r[:, sl, :], in0=k.xr1[:, sl, :],
                    in1=k.xr2[:, sl, :], op=ALU.add)
        stt[("d_xr", sl)] = tu3
        k.w("vector", sLg, tlg)
        emit_ln(k, u3r[:, sl, :], k.lnt[:],
                lng=lng[:, 0, :], lnb=lng[:, 1, :], affout=h3r[:, sl, :])
        the = emit_ct(k, h3r[:, sl, :], h3Tb[:], sl, smt,
                      war=[(smq, stt.get("d_h3T_users", 0))])
        k.w("tensor", k.CHV, the)
        k.w("tensor", sW, tW)
        tf = 0
        for half in range(2):
            ps = k.S01[:, half * 512:(half + 1) * 512]
            k.w("tensor", k.CHV, stt.get(("d_ps", half), 0))
            last = None
            for kc in range(DC):
                ii = k.mm(ps, h3Tb[:, kc, :],
                          Wout[:, kc, half * 512:(half + 1) * 512],
                          start=(kc == 0), stop=(kc == DC - 1))
                if ii is not None:
                    last = ii
            tgm = smq.inc(last)
            k.w("vector", smq, tgm)
            k.w("vector", s3[sl], t3)
            tf = k.dve("scalar_tensor_tensor",
                       out=u3r[:, sl, half * 512:(half + 1) * 512], in0=ps,
                       scalar=0.5, in1=x3r[:, sl, half * 512:(half + 1) * 512],
                       op0=ALU.mult, op1=ALU.add)
            stt[("d_ps", half)] = tf
            stt[("d_x3", sl)] = tf
        stt["d_h3T_users"] = smq.n
        k.w("sync", k.CHV, tf)
        i = k.dma("sync", x_out[tau * 128:(tau + 1) * 128, :], u3r[:, sl, :])
        stt[("d_u3", sl)] = sSP[sl].inc(i, 16)
    k.barrier([(sSP[0], sSP[0].n), (sSP[1], sSP[1].n)])


def emit_ffn(k, call, x_in, x_out):
    rT = emit_rt(k, call)
    for s in range(k.SEQ):
        kT, V = emit_p0(k, call, s, x_in)
        emit_attn(k, call, s, kT, V, rT)
    emit_B(k, call)
    emit_B2(k, call)
    emit_ord(k, call)
    emit_Cs(k, call)
    emit_D(k, call, x_in, x_out)


def emit_mixer(k, x_in, x_out):
    S, D, DC, NT = k.S, k.D, k.DC, k.NT
    Wz = k.sba("M_wz", [128, DC, D], BF16, 0)
    Wh = k.sba("M_wh", [128, DC, D], BF16, 16)
    xnT = k.sba("M_xnT", [128, 2, DC, 512], BF16, 32)
    asb = k.sba("M_a", [128, 2, 512], F32, 48)
    zsb = k.sba("M_z", [128, 2, 512], F32, 52)
    hsb = k.sba("M_h", [128, DC, S], F32, 56)
    hbf = k.sba("M_hbf", [128, DC, S], BF16, 120)
    Wg = k.sba("M_wg", [128, DC, D], BF16, 0)
    gsb = k.sba("M_g", [128, 2, 512], F32, 16)
    mox = k.sba("M_mo", [128, 4, D], F32, 24)

    sX = [k.sem("m_x0"), k.sem("m_x1")]
    sWzh = k.sem("m_wzh"); sWg = k.sem("m_wgl")
    smq = k.sem("m_mm"); smt = k.sem("m_mt")
    sSP = [k.sem("m_o0"), k.sem("m_o1"), k.sem("m_o2"), k.sem("m_o3")]
    stt = k.st

    for s in range(k.SEQ):
        k.w("gpsimd", smq, stt.get("m_wg_users", 0))
        k.w("gpsimd", smt, stt.get("m_gate_tr", 0))
        k.w("gpsimd", k.CHV, stt.get("m_gate_dve", 0))
        k.w("gpsimd", k.CHS, stt.get("m_gate_act", 0))
        for it_ in range(4):
            k.w("gpsimd", sSP[it_], stt.get(("m_mo", it_), 0))
        i = k.dma("gpsimd", Wz[:], k.inp["mx_Wz"].rearrange("(c p) f -> p c f", p=128))
        sWzh.inc(i, 16)
        i = k.dma("gpsimd", Wh[:], k.inp["mx_Wh"].rearrange("(c p) f -> p c f", p=128))
        tWzh = sWzh.inc(i, 16)
        xnT_users = [0, 0]
        for n in range(S // 512):
            par = n % 2
            for it in range(4):
                tau = n * 4 + it
                gtau = s * NT + tau
                slot = tau % 2
                k.w("sync", k.CHV, stt.get(("m_xr1", slot), 0))
                i = k.dma("sync", k.xr1[:, slot, :],
                          x_in[gtau * 128:(gtau + 1) * 128, :])
                tx = sX[slot].inc(i, 16)
                k.w("vector", sX[slot], tx)
                tn = emit_ln(k, k.xr1[:, slot, :], k.xr2[:, slot, :])
                stt[("m_xr1", slot)] = tn
                war_ = [(smq, xnT_users[par])] if it == 0 else []
                if n == 0 and it == 0:
                    war_ += [(sSP[2], stt.get(("m_mo", 2), 0)),
                             (sSP[3], stt.get(("m_mo", 3), 0))]
                emit_ct(k, k.xr2[:, slot, :],
                        xnT[:, par, :, it * 128:(it + 1) * 128], slot, smt,
                        war=war_)
            txn = k.st["ptfree"]
            k.w("tensor", k.CHV, txn)
            k.w("tensor", sWzh, tWzh)
            for oc in range(DC):
                r = oc % 2
                psz = k.S01[:, r * 512:(r + 1) * 512]
                psh = k.S23[:, r * 512:(r + 1) * 512]
                k.w("tensor", k.CHV, stt.get(("m_ps", r), 0))
                lastz = None
                for kc in range(DC):
                    ii = k.mm(psz, Wz[:, kc, oc * 128:(oc + 1) * 128],
                              xnT[:, par, kc, :], start=(kc == 0), stop=(kc == DC - 1))
                    if ii is not None:
                        lastz = ii
                tz = smq.inc(lastz)
                lasth = None
                for kc in range(DC):
                    ii = k.mm(psh, Wh[:, kc, oc * 128:(oc + 1) * 128],
                              xnT[:, par, kc, :], start=(kc == 0), stop=(kc == DC - 1))
                    if ii is not None:
                        lasth = ii
                th_ = smq.inc(lasth)
                k.w("scalar", smq, tz)
                k.w("scalar", k.CHV, stt.get(("m_ab", r), 0))
                k.act(asb[:, r, :], psz, AF.Sigmoid, scale=-1.0,
                      bias=k.nbz_sb[:, oc:oc + 1])
                ta2 = k.act(zsb[:, r, :], psz, AF.Sigmoid, bias=k.bz_sb[:, oc:oc + 1])
                k.w("vector", k.CHS, ta2)
                k.w("vector", smq, th_)
                k.dve("scalar_tensor_tensor", out=zsb[:, r, :], in0=psh,
                      scalar=k.bh_sb[:, oc:oc + 1], in1=zsb[:, r, :],
                      op0=ALU.add, op1=ALU.mult)
                init = 0.0 if n == 0 else hsb[:, oc, n * 512 - 1:n * 512]
                tsc = k.dve("tensor_tensor_scan",
                            out=hsb[:, oc, n * 512:(n + 1) * 512],
                            data0=asb[:, r, :], data1=zsb[:, r, :],
                            initial=init, op0=ALU.mult, op1=ALU.add)
                stt[("m_ps", r)] = tsc
                stt[("m_ab", r)] = tsc
            xnT_users[par] = smq.n
        tcast = 0
        for oc in range(DC):
            tcast = k.dve("tensor_copy", out=hbf[:, oc, :], in_=hsb[:, oc, :])
        k.w("gpsimd", smq, smq.n)
        i = k.dma("gpsimd", Wg[:], k.inp["mx_Wg"].rearrange("(c p) f -> p c f", p=128))
        tWg = sWg.inc(i, 16)
        for n in range(S // 512):
            for oc in range(DC):
                r = oc % 2
                psz = k.S45[:, r * 512:(r + 1) * 512]
                k.w("tensor", k.CHV, max(tcast, stt.get(("m_g", r), 0)))
                k.w("tensor", sWg, tWg)
                last = None
                for kc in range(DC):
                    ii = k.mm(psz, Wg[:, kc, oc * 128:(oc + 1) * 128],
                              hbf[:, kc, n * 512:(n + 1) * 512],
                              start=(kc == 0), stop=(kc == DC - 1))
                    if ii is not None:
                        last = ii
                tg = smq.inc(last)
                stt["m_wg_users"] = tg
                k.w("scalar", smq, tg)
                k.w("scalar", k.CHV, stt.get(("m_g", r), 0))
                tga = k.act(gsb[:, r, :], psz, AF.Sigmoid, bias=k.bg_sb[:, oc:oc + 1])
                k.w("vector", k.CHS, tga)
                tgh = k.dve("tensor_tensor", out=gsb[:, r, :], in0=gsb[:, r, :],
                            in1=hsb[:, oc, n * 512:(n + 1) * 512], op=ALU.mult)
                k.w("tensor", k.CHV, tgh)
                te = 0
                for it in range(4):
                    tau = n * 4 + it
                    k.w("tensor", k.CHV, k.st.get("ptfree", 0))
                    ii = k.tr(k.PTF, gsb[:, r, it * 128:(it + 1) * 128], fp32=True)
                    tt = smt.inc(ii)
                    k.w("vector", smt, tt)
                    if oc == 0:
                        k.w("vector", sSP[it], stt.get(("m_mo", it), 0))
                    te = k.dve("tensor_copy",
                               out=mox[:, it, oc * 128:(oc + 1) * 128], in_=k.PTF)
                    k.st["ptfree"] = te
                stt[("m_g", r)] = te
            for it in range(4):
                tau = n * 4 + it
                gtau = s * NT + tau
                slot = tau % 2
                k.w("sync", k.CHV, stt.get(("m_xr1", slot), 0))
                i = k.dma("sync", k.xr1[:, slot, :],
                          x_in[gtau * 128:(gtau + 1) * 128, :])
                tx = sX[slot].inc(i, 16)
                k.w("vector", sX[slot], tx)
                tmo = k.dve("tensor_tensor", out=mox[:, it, :], in0=mox[:, it, :],
                            in1=k.xr1[:, slot, :], op=ALU.add)
                stt[("m_xr1", slot)] = tmo
                k.w("sync", k.CHV, tmo)
                i = k.dma("sync", x_out[gtau * 128:(gtau + 1) * 128, :],
                          mox[:, it, :])
                stt[("m_mo", it)] = sSP[it].inc(i, 16)
        stt["m_gate_dve"] = k.CHV.n
        stt["m_gate_act"] = k.CHS.n
        stt["m_gate_tr"] = smt.n
    k.barrier([(sSP[0], sSP[0].n), (sSP[1], sSP[1].n),
               (sSP[2], sSP[2].n), (sSP[3], sSP[3].n)])


_NSTAGE = 3


def emit_program(k):
    emit_stage0(k)
    emit_ffn(k, 0, k.inp["x"], k.xA)
    if _NSTAGE >= 2:
        emit_mixer(k, k.xA, k.xB)
    if _NSTAGE >= 3:
        emit_ffn(k, 1, k.xB, k.y)


def _pe_table(S, D):
    pos = np.arange(S - 1, -1, -1, dtype=np.float64)
    inv_freq = 1.0 / 10000.0 ** (np.arange(0, D, 2, dtype=np.float64) / D)
    sinusoid = pos[:, None] * inv_freq[None, :]
    pe = np.concatenate([np.sin(sinusoid), np.cos(sinusoid)], axis=-1)
    return np.ascontiguousarray(pe.T.astype(np.float32))


def prep_inputs(cfg, inputs):
    B, S, D = cfg["B"], cfg["S"], cfg["D"]
    NE = cfg["NE"]
    SEQ = B // NCORES
    TOK = SEQ * S
    NTT = TOK // 128
    f32 = np.float32

    def asnp(v, dt=f32):
        return np.ascontiguousarray(np.asarray(v), dtype=dt)

    def vec_pc(v):
        return np.ascontiguousarray(v.reshape(-1, 128).T)

    Wz = asnp(inputs["L0_attn_Wz"])
    Wzf = np.ascontiguousarray(np.transpose(Wz, (1, 0, 2)).reshape(D, D))
    Wh = asnp(inputs["L0_attn_Wh"])
    Whf = np.ascontiguousarray(np.transpose(Wh, (1, 0, 2)).reshape(D, D))
    lng = np.ascontiguousarray(np.stack(
        [np.broadcast_to(asnp(inputs[f"L0_ffn_ln{i}_g"]), (128, D))
         for i in (1, 2, 3)], axis=1))
    lnb = np.ascontiguousarray(np.stack(
        [np.broadcast_to(asnp(inputs[f"L0_ffn_ln{i}_b"]), (128, D))
         for i in (1, 2, 3)], axis=1))
    be1 = asnp(inputs["L0_ffn_be1"])
    be1pc = np.ascontiguousarray(
        np.stack([vec_pc(be1[e]) for e in range(NE)], axis=0))
    ident = np.eye(128, dtype=ml_dtypes.bfloat16)
    identf = np.eye(128, dtype=f32)
    cm = np.zeros((128, 512), dtype=ml_dtypes.bfloat16)
    cm[:, 384:512] = np.where(np.arange(128)[None, :] <= np.arange(128)[:, None],
                              0.0, NEGINF).astype(ml_dtypes.bfloat16)
    rowidx = (BIGIDX + (np.arange(NTT)[None, :] * 128
                        + np.arange(128)[:, None])).astype(f32)
    trilS = np.triu(np.ones((128, 128)), 1).astype(ml_dtypes.bfloat16)
    iota128 = np.arange(128, dtype=f32)[:, None]
    thr8 = np.tile((np.arange(8, dtype=f32) * 512.0)[None, :], (8, 1))
    iotaU = (np.arange(16, dtype=f32) * 512.0)[None, :]
    rowplain = (np.arange(NTT)[None, :] * 128
                + np.arange(128)[:, None]).astype(f32)
    UB = PAD // 128
    ordinit = np.zeros((128, 2 * UB), dtype=f32)
    ordinit[:, 0::2] = (BIGIDX + np.arange(UB)[None, :] * 128
                        + np.arange(128)[:, None])
    shared = {
        "mx_Wz": Wzf, "mx_Wh": Whf, "mx_Wg": asnp(inputs["L0_attn_Wg"]),
        "mx_bz": vec_pc(asnp(inputs["L0_attn_bz"]).reshape(-1)),
        "mx_bh": vec_pc(asnp(inputs["L0_attn_bh"]).reshape(-1)),
        "mx_bg": vec_pc(asnp(inputs["L0_attn_bg"]).reshape(-1)),
        "f_Wqkv": asnp(inputs["L0_ffn_Wqkv"]), "f_Wo": asnp(inputs["L0_ffn_Wo"]),
        "f_Wr": asnp(inputs["L0_ffn_Wr"]),
        "f_ub": vec_pc(asnp(inputs["L0_ffn_u_bias"]).reshape(-1)),
        "f_vb": vec_pc(asnp(inputs["L0_ffn_v_bias"]).reshape(-1)),
        "f_lng": lng, "f_lnb": lnb,
        "f_Wff1": asnp(inputs["L0_ffn_Wff1"]),
        "f_bff1": vec_pc(asnp(inputs["L0_ffn_bff1"]).reshape(-1)),
        "f_Wff2": asnp(inputs["L0_ffn_Wff2"]),
        "f_bff2": asnp(inputs["L0_ffn_bff2"]).reshape(1, D),
        "f_Wg": asnp(inputs["L0_ffn_Wg"]),
        "f_We1": asnp(inputs["L0_ffn_We1"]), "f_be1": be1pc,
        "f_We2": asnp(inputs["L0_ffn_We2"]),
        "f_be2": asnp(inputs["L0_ffn_be2"]).reshape(NE, 1, D),
        "f_Wout": asnp(inputs["L0_ffn_Wout"]),
        "peT": _pe_table(S, D),
        "ident": ident, "identf": identf, "cmask": cm, "rowidx": rowidx,
        "trilS": trilS, "iota128": iota128, "thr8": thr8, "iotaU": iotaU,
        "rowplain": rowplain, "ordinit": ordinit,
    }
    x = asnp(inputs["L0_attn_x"])
    in_maps = []
    for c in range(NCORES):
        m = dict(shared)
        m["x"] = np.ascontiguousarray(x[c * SEQ:(c + 1) * SEQ].reshape(TOK, D))
        in_maps.append(m)
    return in_maps


def build(cfg):
    k = KB(cfg)
    with k.nc.Block() as block:
        for en in KB.ENGINES:
            def mk(en=en):
                def f(eng):
                    k.begin_pass(en, eng)
                    emit_program(k)
                return f
            getattr(block, en)(mk())
    return k


def kernel(**inputs):
    cfg = dict(B=16, S=2048, D=1024, NH=16, DH=64, FF=4096, NE=8)
    in_maps = prep_inputs(cfg, inputs)
    k = build(cfg)
    res = run_bass_kernel_spmd(k.nc, in_maps, core_ids=list(range(NCORES)))
    SEQ = cfg["B"] // NCORES
    out = np.concatenate(
        [r["y"].reshape(SEQ, cfg["S"], cfg["D"]) for r in res.results], axis=0)
    return out
